# version 01
# baseline (speedup 1.0000x reference)
import contextlib
import numpy as np
import ml_dtypes
import concourse.bass as bass
import concourse.mybir as mybir
from concourse.alu_op_type import AluOpType
from concourse.bass_utils import run_bass_kernel_spmd

F32 = mybir.dt.float32
BF16 = mybir.dt.bfloat16
AF = mybir.ActivationFunctionType
AX = mybir.AxisListType
OP = AluOpType

D = 1024
S = 2048
H = 16
DK = 64
NCORES = 8
INTER = 4096
NE = 16

ENGS = ("sync", "gpsimd", "vector", "scalar", "tensor")


class Sem:
    __slots__ = ("h", "step", "n")

    def __init__(self, h, step):
        self.h = h
        self.step = step
        self.n = 0


class Prog:

    def __init__(self, nc):
        self.nc = nc
        self.q = {e: [] for e in ENGS}
        self.stack = contextlib.ExitStack()
        self.sem = {}
        for e in ENGS:
            step = 16 if e in ("sync", "gpsimd") else 1
            self.sem[e] = Sem(self.stack.enter_context(nc.semaphore("S" + e)), step)

    def n(self, eng):
        return self.sem[eng].n

    def sb(self, name, shape, dtype=F32):
        return self.stack.enter_context(self.nc.sbuf_tensor(name, shape, dtype))

    def cc(self, val):
        if not hasattr(self, "_cpool"):
            self._cpool = self.sb("cpool", [128, 16], F32)
            self._cmap = {}
        if val not in self._cmap:
            col = len(self._cmap)
            self._cmap[val] = col
            self.emit("vector", lambda e, col=col, val=val: e.memset(self._cpool[:, col:col + 1], val))
        c = self._cmap[val]
        return self._cpool[:, c:c + 1]

    def ps(self, name, shape, dtype=F32):
        return self.stack.enter_context(self.nc.psum_tensor(name, shape, dtype))

    def emit(self, eng, fn, waits=()):
        sem = self.sem[eng]
        wl = []
        for e2, n in waits:
            s2 = self.sem[e2]
            assert n <= s2.n, f"wait on future value of {e2}: {n} > {s2.n}"
            if n > 0:
                wl.append((s2.h, n))
        wl = tuple(wl)

        if eng in ("vector", "scalar"):
            def cl(e, fn=fn, wl=wl, sem=sem):
                for h, n in wl:
                    e.wait_ge(h, n)
                fn(e)
                e.drain().then_inc(sem.h, sem.step)
        else:
            def cl(e, fn=fn, wl=wl, sem=sem):
                for h, n in wl:
                    e.wait_ge(h, n)
                fn(e).then_inc(sem.h, sem.step)

        self.q[eng].append(cl)
        sem.n += sem.step
        return sem.n

    def wother(self, eng):
        return [(e, self.sem[e].n) for e in ENGS if e != eng]

    def finalize(self):
        snap = [(e, self.sem[e].n) for e in ENGS]

        def fin(e, snap=snap):
            for eng, n in snap:
                if n > 0:
                    e.wait_ge(self.sem[eng].h, n)
        self.q["gpsimd"].append(fin)
        with self.nc.Block() as block:
            for eng in ENGS:
                lst = self.q[eng]
                if lst:
                    getattr(block, eng)(lambda e, lst=lst: [c(e) for c in lst])
        self.stack.close()


def np_bf16(a):
    return np.asarray(a).astype(ml_dtypes.bfloat16)


def consts_np():
    return {
        "ident_f32": np.eye(128, dtype=np.float32),
        "ident_bf": np_bf16(np.eye(128)),
        "tri_bf": np_bf16(np.tril(np.ones((128, 128)))),
        "ones_col_bf": np_bf16(np.ones((128, 1))),
        "ones_row_f32": np.ones((1, 128), dtype=np.float32),
    }


CONSTS = consts_np()


def declare_io(nc, ins, outs):
    aps = {}
    for n, (shp, dt) in ins.items():
        aps[n] = nc.dram_tensor(n, list(shp), dt, kind="ExternalInput").ap()
    for n, (shp, dt) in outs.items():
        aps[n] = nc.dram_tensor(n, list(shp), dt, kind="ExternalOutput").ap()
    return aps



def emit_weight_T(p, w_dram, O, I, wt_sb, wstage, psT, idf):
    nIC, nOC = I // 128, O // 128
    nbuf = max(1, wstage.shape[1] // I)
    for oc in range(nOC):
        st = wstage[:, (oc % nbuf) * I:(oc % nbuf) * I + I]
        p.emit("sync", lambda e, oc=oc, st=st: e.dma_start(
            out=st, in_=w_dram[oc * 128:(oc + 1) * 128, :]),
            waits=[("tensor", p.n("tensor"))])
        for ic in range(nIC):
            p.emit("tensor", lambda e, ic=ic, st=st: e.transpose(
                psT[:, ic * 128:(ic + 1) * 128], st[:, ic * 128:(ic + 1) * 128], idf[:]),
                waits=[("sync", p.n("sync")), ("vector", p.n("vector")), ("scalar", p.n("scalar"))])
        p.emit("vector", lambda e, oc=oc: e.tensor_copy(
            wt_sb[:].rearrange("p (ic o) -> p ic o", o=O)[:, :, oc * 128:(oc + 1) * 128],
            psT[:, 0:nIC * 128].rearrange("p (ic q) -> p ic q", q=128)),
            waits=[("tensor", p.n("tensor"))])


def emit_bcast_row(p, vec_dram, n, out_tile, row_sb, ones1, psB):
    p.emit("sync", lambda e, n=n: e.dma_start(
        out=row_sb[0:1, 0:n], in_=vec_dram[0:n].rearrange("(a b) -> a b", a=1)),
        waits=[("tensor", p.n("tensor"))])
    for c in range((n + 511) // 512):
        w = min(512, n - c * 512)
        p.emit("tensor", lambda e, c=c, w=w: e.matmul(
            psB[:, c * 512:c * 512 + w], ones1[0:1, :], row_sb[0:1, c * 512:c * 512 + w],
            start=True, stop=True),
            waits=[("sync", p.n("sync")), ("vector", p.n("vector")), ("scalar", p.n("scalar"))])
    p.emit("vector", lambda e, n=n: e.tensor_copy(out_tile[:, 0:n], psB[:, 0:n]),
           waits=[("tensor", p.n("tensor"))])


def hsl(sb, h, c0, w, rows=64):
    oc = h // 2
    r = (h % 2) * 64
    return sb[r:r + rows, oc * 2048 + c0:oc * 2048 + c0 + w]



def build_ln(nc, want_f32):
    ins = {"x": ((S, D), F32), "ident_f32": ((128, 128), F32)}
    outs = {"xn_bf": ((D, S), BF16)}
    if want_f32:
        outs["xn_f32"] = ((D, S), F32)
    aps = declare_io(nc, ins, outs)
    p = Prog(nc)

    xsb = p.sb("xsb", [128, 16 * 1024], F32)
    idf = p.sb("idf", [128, 128], F32)
    sc = p.sb("sc", [128, 1024], F32)
    stats = p.sb("stats", [128, 16], F32)
    xnbf = p.sb("xnbf", [128, 8 * 2048], BF16)
    xnf = p.sb("xnf", [128, 8 * 2048], F32) if want_f32 else None
    psA = p.ps("psA", [128, 1024], F32)
    psB = p.ps("psB", [128, 1024], F32)

    p.emit("sync", lambda e: e.dma_start(out=idf[:], in_=aps["ident_f32"]))
    for t in range(16):
        p.emit("sync", lambda e, t=t: e.dma_start(
            out=xsb[:, t * 1024:(t + 1) * 1024], in_=aps["x"][t * 128:(t + 1) * 128, :]))

    for t in range(16):
        xt = xsb[:, t * 1024:(t + 1) * 1024]
        p.emit("vector", lambda e, xt=xt: e.tensor_reduce(stats[:, 0:1], xt, axis=AX.X, op=OP.add),
               waits=[("sync", 16 * (t + 2))])
        p.emit("vector", lambda e, xt=xt: e.tensor_tensor(sc[:], xt, xt, op=OP.mult))
        p.emit("vector", lambda e: e.tensor_reduce(stats[:, 1:2], sc[:], axis=AX.X, op=OP.add))
        p.emit("vector", lambda e: e.tensor_scalar(stats[:, 2:3], stats[:, 0:1], 1.0 / D, None, op0=OP.mult))
        p.emit("vector", lambda e: e.tensor_tensor(stats[:, 3:4], stats[:, 2:3], stats[:, 2:3], op=OP.mult))
        rD = p.cc(1.0 / D)
        p.emit("vector", lambda e, rD=rD: e.tensor_scalar(stats[:, 4:5], stats[:, 1:2], rD, stats[:, 3:4], op0=OP.mult, op1=OP.subtract))
        eps_ln = p.cc(1e-5)
        p.emit("scalar", lambda e, eps_ln=eps_ln: e.activation(stats[:, 5:6], stats[:, 4:5], AF.Ln, bias=eps_ln, scale=1.0),
               waits=[("vector", p.n("vector"))])
        p.emit("scalar", lambda e: e.activation(stats[:, 6:7], stats[:, 5:6], AF.Exp, bias=0.0, scale=-0.5))
        p.emit("vector", lambda e, xt=xt: e.tensor_scalar(
            sc[:], xt, stats[:, 2:3], stats[:, 6:7], op0=OP.subtract, op1=OP.mult),
            waits=[("scalar", p.n("scalar"))])
        pst = psA if t % 2 == 0 else psB
        for c in range(8):
            p.emit("tensor", lambda e, c=c, pst=pst: e.transpose(
                pst[:, c * 128:(c + 1) * 128], sc[:, c * 128:(c + 1) * 128], idf[:]),
                waits=[("vector", p.n("vector"))])
        p.emit("vector", lambda e, t=t, pst=pst: e.tensor_copy(
            xnbf[:].rearrange("p (c s) -> p c s", s=S)[:, :, t * 128:(t + 1) * 128],
            pst[:].rearrange("p (c q) -> p c q", q=128)),
            waits=[("tensor", p.n("tensor"))])
        if want_f32:
            p.emit("vector", lambda e, t=t, pst=pst: e.tensor_copy(
                xnf[:].rearrange("p (c s) -> p c s", s=S)[:, :, t * 128:(t + 1) * 128],
                pst[:].rearrange("p (c q) -> p c q", q=128)))

    for c in range(8):
        p.emit("gpsimd", lambda e, c=c: e.dma_start(
            out=aps["xn_bf"][c * 128:(c + 1) * 128, :], in_=xnbf[:, c * S:(c + 1) * S]),
            waits=[("vector", p.n("vector"))])
        if want_f32:
            p.emit("gpsimd", lambda e, c=c: e.dma_start(
                out=aps["xn_f32"][c * 128:(c + 1) * 128, :], in_=xnf[:, c * S:(c + 1) * S]))
    p.finalize()
    return nc



def build_comb(nc):
    aps = declare_io(nc, {"x": ((S, D), F32), "a": ((S, D), F32), "b": ((S, D), F32)},
                     {"o": ((S, D), F32)})
    p = Prog(nc)
    xa = p.sb("xa", [128, 2 * 1024], F32)
    xb = p.sb("xb", [128, 2 * 1024], F32)
    xc = p.sb("xc", [128, 2 * 1024], F32)
    xo = p.sb("xo", [128, 2 * 1024], F32)
    for t in range(16):
        b = (t % 2) * 1024
        p.emit("sync", lambda e, t=t, b=b: e.dma_start(out=xa[:, b:b + 1024], in_=aps["x"][t * 128:(t + 1) * 128, :]),
               waits=[("vector", p.n("vector"))])
        p.emit("sync", lambda e, t=t, b=b: e.dma_start(out=xb[:, b:b + 1024], in_=aps["a"][t * 128:(t + 1) * 128, :]))
        p.emit("sync", lambda e, t=t, b=b: e.dma_start(out=xc[:, b:b + 1024], in_=aps["b"][t * 128:(t + 1) * 128, :]))
        p.emit("vector", lambda e, b=b: e.tensor_tensor(xo[:, b:b + 1024], xa[:, b:b + 1024], xb[:, b:b + 1024], op=OP.add),
               waits=[("sync", p.n("sync")), ("gpsimd", max(0, p.n("gpsimd") - 16))])
        p.emit("vector", lambda e, b=b: e.tensor_tensor(xo[:, b:b + 1024], xo[:, b:b + 1024], xc[:, b:b + 1024], op=OP.add))
        p.emit("gpsimd", lambda e, t=t, b=b: e.dma_start(out=aps["o"][t * 128:(t + 1) * 128, :], in_=xo[:, b:b + 1024]),
               waits=[("vector", p.n("vector"))])
    p.finalize()
    return nc



def build_ffn0(nc):
    aps = declare_io(nc, {
        "xn_bf": ((D, S), BF16),
        "W1": ((D, INTER), F32), "b1": ((INTER,), F32),
        "W2": ((INTER, D), F32), "b2": ((D,), F32),
        "ones_row_f32": ((1, 128), F32), "ident_f32": ((128, 128), F32),
    }, {"f0": ((S, D), F32)})
    p = Prog(nc)

    xn = p.sb("xn", [128, 8 * 2048], BF16)
    w1h = p.sb("w1h", [128, 8 * 2048], BF16)
    w2h = p.sb("w2h", [128, 16 * 1024], BF16)
    hbuf = p.sb("hbuf", [128, 16 * 512], BF16)
    b1c = p.sb("b1c", [128, 32], F32)
    acc = p.sb("acc", [128, 16 * 1024], F32)
    ones1 = p.sb("ones1", [1, 128], F32)
    b2t = p.sb("b2t", [128, 1024], F32)
    brow = p.sb("brow", [1, 1024], F32)
    outb = p.sb("outb", [128, 2 * 1024], F32)
    psA = p.ps("psA", [128, 512], F32)
    psB = p.ps("psB", [128, 512], F32)
    psC = p.ps("psC", [128, 1024], F32)
    psD = p.ps("psD", [128, 1024], F32)

    p.emit("sync", lambda e: e.dma_start(out=ones1[:], in_=aps["ones_row_f32"]))
    for c in range(8):
        p.emit("sync", lambda e, c=c: e.dma_start(out=xn[:, c * 2048:(c + 1) * 2048],
                                                  in_=aps["xn_bf"][c * 128:(c + 1) * 128, :]))
    b1st = p.sb("b1st", [32, 128], F32)
    idf2 = p.sb("idf2", [128, 128], F32)
    p.emit("sync", lambda e: e.dma_start(out=b1st[:], in_=aps["b1"].rearrange("(ic p) -> ic p", p=128)))
    p.emit("gpsimd", lambda e: e.dma_start(out=idf2[:], in_=aps["ident_f32"]))
    p.emit("tensor", lambda e: e.transpose(psC[:, 0:32], b1st[:, 0:128], idf2[0:32, 0:32]),
           waits=[("sync", p.n("sync")), ("gpsimd", p.n("gpsimd"))])
    p.emit("vector", lambda e: e.tensor_copy(b1c[:], psC[:, 0:32]), waits=[("tensor", p.n("tensor"))])
    emit_bcast_row(p, aps["b2"], 1024, b2t, brow, ones1, psC)

    for ih in range(2):
        for dc in range(8):
            p.emit("gpsimd", lambda e, dc=dc, ih=ih: e.dma_start(
                out=w1h[:, dc * 2048:(dc + 1) * 2048],
                in_=aps["W1"][dc * 128:(dc + 1) * 128, ih * 2048:(ih + 1) * 2048]),
                waits=[("tensor", p.n("tensor"))])
        for ic in range(16):
            p.emit("gpsimd", lambda e, ic=ic, ih=ih: e.dma_start(
                out=w2h[:, ic * 1024:(ic + 1) * 1024],
                in_=aps["W2"][ih * 2048 + ic * 128:ih * 2048 + (ic + 1) * 128, :]),
                waits=[("tensor", p.n("tensor"))])
        for qb in range(4):
            for ic in range(16):
                pst = psA if ic % 2 == 0 else psB
                for dc in range(8):
                    p.emit("tensor", lambda e, ic=ic, dc=dc, qb=qb, pst=pst: e.matmul(
                        pst[:], w1h[:, dc * 2048 + ic * 128:dc * 2048 + (ic + 1) * 128],
                        xn[:, dc * 2048 + qb * 512:dc * 2048 + (qb + 1) * 512],
                        start=(dc == 0), stop=(dc == 7)),
                        waits=[("gpsimd", p.n("gpsimd")), ("sync", p.n("sync")), ("scalar", p.n("scalar"))])
                icg = ih * 16 + ic
                p.emit("scalar", lambda e, ic=ic, icg=icg, pst=pst: e.activation(
                    hbuf[:, ic * 512:(ic + 1) * 512], pst[:], AF.Relu,
                    bias=b1c[:, icg:icg + 1], scale=1.0),
                    waits=[("tensor", p.n("tensor"))])
            for tsub in range(4):
                t = qb * 4 + tsub
                pst = psC if tsub % 2 == 0 else psD
                for ic in range(16):
                    for oc in range(2):
                        p.emit("tensor", lambda e, ic=ic, oc=oc, tsub=tsub, pst=pst: e.matmul(
                            pst[:, oc * 512:(oc + 1) * 512],
                            hbuf[:, ic * 512 + tsub * 128:ic * 512 + (tsub + 1) * 128],
                            w2h[:, ic * 1024 + oc * 512:ic * 1024 + (oc + 1) * 512],
                            start=(ic == 0), stop=(ic == 15)),
                            waits=[("scalar", p.n("scalar")), ("vector", p.n("vector"))])
                if ih == 0:
                    p.emit("vector", lambda e, t=t, pst=pst: e.tensor_copy(
                        acc[:, t * 1024:(t + 1) * 1024], pst[:]), waits=[("tensor", p.n("tensor"))])
                else:
                    p.emit("vector", lambda e, t=t, pst=pst: e.tensor_tensor(
                        acc[:, t * 1024:(t + 1) * 1024], acc[:, t * 1024:(t + 1) * 1024], pst[:], op=OP.add),
                        waits=[("tensor", p.n("tensor"))])
                    ob = outb[:, (t % 2) * 1024:(t % 2) * 1024 + 1024]
                    p.emit("vector", lambda e, ob=ob, t=t: e.tensor_tensor(
                        ob, acc[:, t * 1024:(t + 1) * 1024], b2t[:], op=OP.add),
                        waits=[("gpsimd", max(0, p.n("gpsimd") - 16))])
                    p.emit("gpsimd", lambda e, t=t, ob=ob: e.dma_start(
                        out=aps["f0"][t * 128:(t + 1) * 128, :], in_=ob),
                        waits=[("vector", p.n("vector"))])
    p.finalize()
    return nc



def build_attn0(nc):
    aps = declare_io(nc, {
        "xn_bf": ((D, S), BF16),
        "Wq": ((D, D), F32), "Wk": ((D, D), F32), "Wv": ((D, D), F32), "Wo": ((D, D), F32),
        "b_o": ((D,), F32), "rel_k": ((65, 64), F32), "rel_v": ((65, 64), F32),
        "ident_f32": ((128, 128), F32), "ident_bf": ((128, 128), BF16),
        "tri_bf": ((128, 128), BF16), "ones_col_bf": ((128, 1), BF16),
        "ones_row_f32": ((1, 128), F32),
    }, {"a0": ((S, D), F32)})
    pscrP = nc.dram_tensor("pscrP", [128 * 319 + 256], F32).ap()
    pscrB = nc.dram_tensor("pscrB", [128 * 193 + 256], BF16).ap()

    p = Prog(nc)
    xn = p.sb("xn", [128, 8 * 2048], BF16)
    wt = p.sb("wt", [128, 8 * 1024], BF16)
    wstage = p.sb("wstage", [128, 2 * 1024], F32)
    Q = p.sb("Q", [128, 8 * 2048], BF16)
    K = p.sb("K", [128, 8 * 2048], BF16)
    V = p.sb("V", [128, 16 * 1024], BF16)
    attn = p.sb("attn", [128, 2112], BF16)
    attnT = p.sb("attnT", [128, 16 * 512], BF16)
    ctxfm = p.sb("ctxfm", [128, 8 * 512], BF16)
    Wwin = p.sb("Wwin", [128, 192], F32)
    QRt = p.sb("QRt", [128, 65], F32)
    Ptab = p.sb("Ptab", [128, 319], F32)
    Dg = p.sb("Dg", [128, 65], BF16)
    Atile = p.sb("Atile", [128, 65], BF16)
    Arhs = p.sb("Arhs", [128, 128], BF16)
    crn = p.sb("crn", [128, 128], BF16)
    st8 = p.sb("st8", [128, 16], F32)
    relkT = p.sb("relkT", [128, 65], BF16)
    relvN = p.sb("relvN", [128, 64], BF16)
    relst = p.sb("relst", [128, 64], F32)
    tri = p.sb("tri", [128, 128], BF16)
    onec = p.sb("onec", [128, 1], BF16)
    idf = p.sb("idf", [128, 128], F32)
    idb = p.sb("idb", [128, 128], BF16)
    ones1 = p.sb("ones1", [1, 128], F32)
    bot = p.sb("bot", [128, 1024], F32)
    brow = p.sb("brow", [1, 1024], F32)
    outb = p.sb("outb", [128, 2 * 1024], F32)

    psS = p.ps("psS", [128, 2048], F32)
    psT = p.ps("psT", [128, 2048], BF16)
    psC = p.ps("psC", [128, 512], F32)
    psX = p.ps("psX", [128, 512], F32)

    for nm, dst in [("ident_f32", idf), ("ident_bf", idb), ("tri_bf", tri),
                    ("ones_col_bf", onec), ("ones_row_f32", ones1)]:
        p.emit("sync", lambda e, nm=nm, dst=dst: e.dma_start(out=dst[:], in_=aps[nm]))
    p.emit("sync", lambda e: e.dma_start(out=relst[0:65, :], in_=aps["rel_k"]))
    for c in range(8):
        p.emit("sync", lambda e, c=c: e.dma_start(out=xn[:, c * 2048:(c + 1) * 2048],
                                                  in_=aps["xn_bf"][c * 128:(c + 1) * 128, :]))
    p.emit("gpsimd", lambda e: e.dma_start(out=relvN[0:65, :], in_=aps["rel_v"]))
    p.emit("tensor", lambda e: e.transpose(psX[0:64, 0:65], relst[0:65, 0:64], idf[0:65, 0:65]),
           waits=[("sync", p.n("sync"))])
    p.emit("vector", lambda e: e.tensor_copy(relkT[0:64, 0:65], psX[0:64, 0:65]),
           waits=[("tensor", p.n("tensor"))])
    p.emit("vector", lambda e: e.tensor_copy(relkT[64:128, 0:65], psX[0:64, 0:65]))
    emit_bcast_row(p, aps["b_o"], 1024, bot, brow, ones1, psS)

    def proj_fm(out_sb, scale):
        for oc in range(8):
            for qb in range(4):
                pst = psS[:, (qb % 2) * 512:(qb % 2) * 512 + 512]
                for dc in range(8):
                    p.emit("tensor", lambda e, oc=oc, qb=qb, dc=dc, pst=pst: e.matmul(
                        pst, wt[:, dc * 1024 + oc * 128:dc * 1024 + (oc + 1) * 128],
                        xn[:, dc * 2048 + qb * 512:dc * 2048 + (qb + 1) * 512],
                        start=(dc == 0), stop=(dc == 7)),
                        waits=[("vector", p.n("vector")), ("scalar", p.n("scalar")), ("sync", p.n("sync"))])
                p.emit("scalar", lambda e, oc=oc, qb=qb, pst=pst, scale=scale: e.activation(
                    out_sb[:, oc * 2048 + qb * 512:oc * 2048 + (qb + 1) * 512], pst,
                    AF.Copy, bias=0.0, scale=scale),
                    waits=[("tensor", p.n("tensor"))])

    def proj_tm(out_sb):
        for t in range(16):
            for oh in range(2):
                pst = psS[:, (oh % 2) * 512:(oh % 2) * 512 + 512]
                for dc in range(8):
                    p.emit("tensor", lambda e, t=t, oh=oh, dc=dc, pst=pst: e.matmul(
                        pst, xn[:, dc * 2048 + t * 128:dc * 2048 + (t + 1) * 128],
                        wt[:, dc * 1024 + oh * 512:dc * 1024 + (oh + 1) * 512],
                        start=(dc == 0), stop=(dc == 7)),
                        waits=[("vector", p.n("vector")), ("scalar", p.n("scalar")), ("sync", p.n("sync"))])
                p.emit("scalar", lambda e, t=t, oh=oh, pst=pst: e.activation(
                    out_sb[:, t * 1024 + oh * 512:t * 1024 + (oh + 1) * 512], pst,
                    AF.Copy, bias=0.0, scale=1.0),
                    waits=[("tensor", p.n("tensor"))])

    emit_weight_T(p, aps["Wq"], D, D, wt, wstage, psS, idf)
    proj_fm(Q, 0.125)
    emit_weight_T(p, aps["Wk"], D, D, wt, wstage, psS, idf)
    proj_fm(K, 1.0)
    emit_weight_T(p, aps["Wv"], D, D, wt, wstage, psS, idf)
    proj_tm(V)
    emit_weight_T(p, aps["Wo"], D, D, wt, wstage, psS, idf)

    p.emit("vector", lambda e: e.memset(attn[:, 0:32], 0.0))
    p.emit("vector", lambda e: e.memset(attn[:, 2080:2112], 0.0))

    for qb in range(4):
        for h in range(H):
            for qtl in range(4):
                qt = qb * 4 + qtl
                q0 = qt * 128
                for kb in range(4):
                    p.emit("tensor", lambda e, h=h, qt=qt, kb=kb: e.matmul(
                        psS[:, kb * 512:(kb + 1) * 512],
                        hsl(Q, h, qt * 128, 128), hsl(K, h, kb * 512, 512),
                        start=True, stop=True),
                        waits=[("vector", p.n("vector")), ("scalar", p.n("scalar"))])
                p.emit("tensor", lambda e, h=h, qt=qt: e.matmul(
                    psX[:, 0:65], hsl(Q, h, qt * 128, 128),
                    relkT[(h % 2) * 64:(h % 2) * 64 + 64, 0:65],
                    start=True, stop=True))
                p.emit("vector", lambda e: e.tensor_copy(QRt[:], psX[:, 0:65]),
                       waits=[("tensor", p.n("tensor"))])
                p.emit("vector", lambda e: e.tensor_copy(Ptab[:, 0:127], QRt[:, 0:1].broadcast_to([128, 127])))
                p.emit("vector", lambda e: e.tensor_copy(Ptab[:, 127:192], QRt[:]))
                p.emit("vector", lambda e: e.tensor_copy(Ptab[:, 192:319], QRt[:, 64:65].broadcast_to([128, 127])))
                p.emit("sync", lambda e: e.dma_start(
                    out=pscrP[0:128 * 319].rearrange("(q w) -> q w", w=319), in_=Ptab[:]),
                    waits=[("vector", p.n("vector"))])
                p.emit("sync", lambda e: e.dma_start(
                    out=Wwin[:], in_=pscrP[127:127 + 128 * 318].rearrange("(q w) -> q w", w=318)[:, 0:192]))
                wadd = [("sync", p.n("sync")), ("tensor", p.n("tensor"))]
                if qt == 0:
                    p.emit("vector", lambda e: e.tensor_tensor(psS[:, 0:160], psS[:, 0:160], Wwin[:, 32:192], op=OP.add), waits=wadd)
                elif qt == 15:
                    p.emit("vector", lambda e: e.tensor_tensor(psS[:, 1888:2048], psS[:, 1888:2048], Wwin[:, 0:160], op=OP.add), waits=wadd)
                else:
                    p.emit("vector", lambda e, q0=q0: e.tensor_tensor(psS[:, q0 - 32:q0 + 160], psS[:, q0 - 32:q0 + 160], Wwin[:, 0:192], op=OP.add), waits=wadd)
                if qt > 0:
                    p.emit("vector", lambda e, q0=q0: e.tensor_scalar(psS[:, 0:q0 - 32], psS[:, 0:q0 - 32], QRt[:, 0:1], None, op0=OP.add))
                if qt < 15:
                    p.emit("vector", lambda e, q0=q0: e.tensor_scalar(psS[:, q0 + 160:2048], psS[:, q0 + 160:2048], QRt[:, 64:65], None, op0=OP.add))
                p.emit("vector", lambda e: e.tensor_reduce(st8[:, 0:1], psS[:], axis=AX.X, op=OP.max, negate=True))
                p.emit("scalar", lambda e: e.activation(attn[:, 32:2080], psS[:], AF.Exp,
                                                        bias=st8[:, 0:1], scale=1.0, accum_out=st8[:, 1:2]),
                       waits=[("vector", p.n("vector"))])
                p.emit("vector", lambda e: e.reciprocal(st8[:, 2:3], st8[:, 1:2]),
                       waits=[("scalar", p.n("scalar"))])
                p.emit("vector", lambda e: e.tensor_scalar(attn[:, 32:2080], attn[:, 32:2080], st8[:, 2:3], None, op0=OP.mult))
                p.emit("vector", lambda e, q0=q0: e.tensor_tensor(crn[:], attn[:, q0:q0 + 128], tri[:], op=OP.mult))
                p.emit("vector", lambda e: e.tensor_reduce(st8[:, 3:4], crn[:], axis=AX.X, op=OP.add))
                p.emit("sync", lambda e, q0=q0: e.dma_start(
                    out=pscrB[0:128 * 192].rearrange("(q w) -> q w", w=192), in_=attn[:, q0:q0 + 192]),
                    waits=[("vector", p.n("vector"))])
                p.emit("sync", lambda e: e.dma_start(
                    out=Dg[:], in_=pscrB[0:128 * 193].rearrange("(q w) -> q w", w=193)[:, 0:65]))
                for c in range(16):
                    p.emit("tensor", lambda e, c=c: e.transpose(
                        psT[:, c * 128:(c + 1) * 128], attn[:, 32 + c * 128:32 + (c + 1) * 128], idb[:]),
                        waits=[("vector", p.n("vector"))])
                p.emit("vector", lambda e, qtl=qtl: e.tensor_copy(
                    attnT[:].rearrange("p (c q) -> p c q", q=512)[:, :, qtl * 128:(qtl + 1) * 128],
                    psT[:].rearrange("p (c q) -> p c q", q=128)),
                    waits=[("tensor", p.n("tensor"))])
                nfull = qt - 1
                for c in range(max(0, nfull)):
                    p.emit("tensor", lambda e, c=c, qtl=qtl: e.matmul(
                        psX[:, 128:129], attnT[:, c * 512 + qtl * 128:c * 512 + (qtl + 1) * 128],
                        onec[:], start=(c == 0), stop=False),
                        waits=[("vector", p.n("vector"))])
                if qt >= 1:
                    p.emit("tensor", lambda e, qt=qt, qtl=qtl, st=(nfull <= 0): e.matmul(
                        psX[:, 128:129],
                        attnT[0:96, (qt - 1) * 512 + qtl * 128:(qt - 1) * 512 + (qtl + 1) * 128],
                        onec[0:96, :], start=st, stop=True),
                        waits=[("vector", p.n("vector"))])
                aw = [("tensor", p.n("tensor")), ("sync", p.n("sync"))]
                if qt >= 1:
                    p.emit("vector", lambda e: e.tensor_tensor(st8[:, 4:5], psX[:, 128:129], st8[:, 3:4], op=OP.add), waits=aw)
                else:
                    p.emit("vector", lambda e: e.tensor_copy(st8[:, 4:5], st8[:, 3:4]), waits=aw)
                p.emit("vector", lambda e: e.tensor_reduce(st8[:, 5:6], Dg[:, 1:64], axis=AX.X, op=OP.add))
                p.emit("vector", lambda e: e.tensor_tensor(st8[:, 6:7], st8[:, 4:5], st8[:, 5:6], op=OP.add))
                p.emit("vector", lambda e: e.tensor_scalar(st8[:, 7:8], st8[:, 6:7], -1.0, 1.0, op0=OP.mult, op1=OP.add))
                p.emit("vector", lambda e: e.tensor_copy(Atile[:, 1:64], Dg[:, 1:64]))
                p.emit("vector", lambda e: e.tensor_copy(Atile[:, 0:1], st8[:, 4:5]))
                p.emit("vector", lambda e: e.tensor_copy(Atile[:, 64:65], st8[:, 7:8]))
                p.emit("tensor", lambda e: e.transpose(psT[0:65, 0:128], Atile[:], idb[:]),
                       waits=[("vector", p.n("vector"))])
                p.emit("vector", lambda e: e.tensor_copy(Arhs[0:65, 0:128], psT[0:65, 0:128]),
                       waits=[("tensor", p.n("tensor"))])
                p.emit("tensor", lambda e, qtl=qtl: e.matmul(
                    psC[0:64, qtl * 128:(qtl + 1) * 128], relvN[0:65, 0:64], Arhs[0:65, 0:128],
                    start=(qtl == 0), stop=False),
                    waits=[("vector", p.n("vector"))])
            for c in range(16):
                p.emit("tensor", lambda e, h=h, c=c: e.matmul(
                    psC[0:64, :], V[:, c * 1024 + h * 64:c * 1024 + (h + 1) * 64],
                    attnT[:, c * 512:(c + 1) * 512], start=False, stop=(c == 15)),
                    waits=[("vector", p.n("vector"))])
            p.emit("vector", lambda e, h=h: e.tensor_copy(
                ctxfm[(h % 2) * 64:(h % 2) * 64 + 64, (h // 2) * 512:(h // 2 + 1) * 512], psC[0:64, :]),
                waits=[("tensor", p.n("tensor"))])
        for tsub in range(4):
            t = qb * 4 + tsub
            for oh in range(2):
                pst = psS[:, oh * 512:(oh + 1) * 512]
                for dc in range(8):
                    p.emit("tensor", lambda e, tsub=tsub, oh=oh, dc=dc, pst=pst: e.matmul(
                        pst, ctxfm[:, dc * 512 + tsub * 128:dc * 512 + (tsub + 1) * 128],
                        wt[:, dc * 1024 + oh * 512:dc * 1024 + (oh + 1) * 512],
                        start=(dc == 0), stop=(dc == 7)),
                        waits=[("vector", p.n("vector")), ("scalar", p.n("scalar"))])
            ob = outb[:, (t % 2) * 1024:(t % 2) * 1024 + 1024]
            p.emit("vector", lambda e, ob=ob: e.tensor_tensor(ob, psS[:, 0:1024], bot[:], op=OP.add),
                   waits=[("tensor", p.n("tensor")), ("gpsimd", max(0, p.n("gpsimd") - 16))])
            p.emit("gpsimd", lambda e, t=t, ob=ob: e.dma_start(
                out=aps["a0"][t * 128:(t + 1) * 128, :], in_=ob),
                waits=[("vector", p.n("vector"))])
    p.finalize()
    return nc



def build_attn1(nc):
    aps = declare_io(nc, {
        "xn_bf": ((D, S), BF16),
        "Wqkv": ((3 * D, D), F32), "bqkv": ((3 * D,), F32),
        "Wproj": ((D, D), F32), "bproj": ((D,), F32),
        "qn_w": ((DK,), F32), "kn_w": ((DK,), F32),
        "ident_f32": ((128, 128), F32), "ident_bf": ((128, 128), BF16),
        "ones_row_f32": ((1, 128), F32),
    }, {"a1": ((S, D), F32)})
    p = Prog(nc)

    xn = p.sb("xn", [128, 8 * 2048], BF16)
    wt = p.sb("wt", [128, 8 * 1024], BF16)
    wstage = p.sb("wstage", [128, 1024], F32)
    Q = p.sb("Q", [128, 8 * 2048], BF16)
    K = p.sb("K", [128, 8 * 2048], BF16)
    V = p.sb("V", [128, 16 * 1024], BF16)
    attn = p.sb("attn", [128, 2048], BF16)
    attnT = p.sb("attnT", [128, 16 * 512], BF16)
    ctxfm = p.sb("ctxfm", [128, 8 * 512], BF16)
    qkb = p.sb("qkb", [128, 1024], BF16)
    rst = p.sb("rst", [128, 16], F32)
    wnt = p.sb("wnt", [128, 1024], F32)
    st8 = p.sb("st8", [128, 16], F32)
    idf = p.sb("idf", [128, 128], F32)
    idb = p.sb("idb", [128, 128], BF16)
    ones1 = p.sb("ones1", [1, 128], F32)
    bqt = p.sb("bqt", [128, 1024], F32)
    bpt = p.sb("bpt", [128, 1024], F32)
    brow = p.sb("brow", [1, 1024], F32)
    wrow = p.sb("wrow", [1, 64], F32)
    outb = p.sb("outb", [128, 2 * 1024], F32)
    qkt = outb[:, 0:1024]
    sq = outb[:, 1024:2048]

    psS = p.ps("psS", [128, 2048], F32)
    psT = p.ps("psT", [128, 2048], BF16)
    psC = p.ps("psC", [128, 512], F32)

    for nm, dst in [("ident_f32", idf), ("ident_bf", idb), ("ones_row_f32", ones1)]:
        p.emit("sync", lambda e, nm=nm, dst=dst: e.dma_start(out=dst[:], in_=aps[nm]))
    for c in range(8):
        p.emit("sync", lambda e, c=c: e.dma_start(out=xn[:, c * 2048:(c + 1) * 2048],
                                                  in_=aps["xn_bf"][c * 128:(c + 1) * 128, :]))
    emit_bcast_row(p, aps["bproj"], 1024, bpt, brow, ones1, psS)

    def emit_wn_tile(nm):
        p.emit("sync", lambda e, nm=nm: e.dma_start(out=wrow[0:1, :], in_=aps[nm][0:DK].rearrange("(a b) -> a b", a=1)),
               waits=[("tensor", p.n("tensor"))])
        p.emit("tensor", lambda e: e.matmul(psS[:, 0:64], ones1[0:1, :], wrow[0:1, :], start=True, stop=True),
               waits=[("sync", p.n("sync")), ("vector", p.n("vector"))])
        for j in range(16):
            p.emit("vector", lambda e, j=j: e.tensor_copy(
                wnt[:, j * 64:(j + 1) * 64], psS[:, 0:64]),
                waits=[("tensor", p.n("tensor"))])

    for part, dest in ((0, Q), (1, K), (2, V)):
        emit_bcast_row(p, aps["bqkv"][part * D:(part + 1) * D, 0:1024] if False else aps["bqkv"][part * D:(part + 1) * D], 1024,
                       bqt, brow, ones1, psS)
        if part == 0:
            emit_wn_tile("qn_w")
        elif part == 1:
            emit_wn_tile("kn_w")
        emit_weight_T(p, aps["Wqkv"][part * D:(part + 1) * D, :], D, D, wt, wstage, psS, idf)
        for t in range(16):
            for oh in range(2):
                pst = psS[:, oh * 512:oh * 512 + 512]
                for dc in range(8):
                    p.emit("tensor", lambda e, t=t, oh=oh, dc=dc, pst=pst: e.matmul(
                        pst, xn[:, dc * 2048 + t * 128:dc * 2048 + (t + 1) * 128],
                        wt[:, dc * 1024 + oh * 512:dc * 1024 + (oh + 1) * 512],
                        start=(dc == 0), stop=(dc == 7)),
                        waits=[("vector", p.n("vector")), ("scalar", p.n("scalar")), ("sync", p.n("sync"))])
            if part == 2:
                p.emit("vector", lambda e, t=t: e.tensor_tensor(
                    V[:, t * 1024:(t + 1) * 1024], psS[:, 0:1024], bqt[:], op=OP.add),
                    waits=[("tensor", p.n("tensor"))])
            else:
                p.emit("vector", lambda e: e.tensor_tensor(
                    qkt[:], psS[:, 0:1024], bqt[:], op=OP.add),
                    waits=[("tensor", p.n("tensor"))])
                p.emit("vector", lambda e: e.tensor_tensor(sq[:], qkt[:], qkt[:], op=OP.mult))
                p.emit("vector", lambda e: e.tensor_reduce(
                    rst[:, 0:16], sq[:].rearrange("p (g d) -> p g d", d=64), axis=AX.X, op=OP.add))
                eps_r = p.cc(1e-6)
                p.emit("scalar", lambda e, eps_r=eps_r: e.activation(rst[:, 0:16], rst[:, 0:16], AF.Ln, bias=eps_r, scale=1.0 / DK),
                       waits=[("vector", p.n("vector"))])
                p.emit("scalar", lambda e: e.activation(rst[:, 0:16], rst[:, 0:16], AF.Exp, bias=0.0, scale=-0.5))
                for g in range(16):
                    p.emit("vector", lambda e, g=g: e.tensor_scalar(
                        qkt[:, g * 64:(g + 1) * 64], qkt[:, g * 64:(g + 1) * 64], rst[:, g:g + 1], None, op0=OP.mult),
                        waits=([("scalar", p.n("scalar"))] if g == 0 else []))
                sc2 = 0.125 if part == 0 else 1.0
                p.emit("vector", lambda e, sc2=sc2: e.scalar_tensor_tensor(
                    qkb[:], qkt[:], sc2, wnt[:], op0=OP.mult, op1=OP.mult))
                for c in range(8):
                    p.emit("tensor", lambda e, c=c: e.transpose(
                        psT[:, c * 128:(c + 1) * 128], qkb[:, c * 128:(c + 1) * 128], idb[:]),
                        waits=[("vector", p.n("vector"))])
                p.emit("vector", lambda e, t=t, dest=dest: e.tensor_copy(
                    dest[:].rearrange("p (c s) -> p c s", s=S)[:, :, t * 128:(t + 1) * 128],
                    psT[:, 0:1024].rearrange("p (c q) -> p c q", q=128)),
                    waits=[("tensor", p.n("tensor"))])
    emit_weight_T(p, aps["Wproj"], D, D, wt, wstage, psS, idf)

    for qb in range(4):
        for h in range(H):
            for qtl in range(4):
                qt = qb * 4 + qtl
                for kb in range(4):
                    p.emit("tensor", lambda e, h=h, qt=qt, kb=kb: e.matmul(
                        psS[:, kb * 512:(kb + 1) * 512],
                        hsl(Q, h, qt * 128, 128), hsl(K, h, kb * 512, 512),
                        start=True, stop=True),
                        waits=[("vector", p.n("vector")), ("scalar", p.n("scalar"))])
                p.emit("vector", lambda e: e.tensor_reduce(st8[:, 0:1], psS[:], axis=AX.X, op=OP.max, negate=True),
                       waits=[("tensor", p.n("tensor"))])
                p.emit("scalar", lambda e: e.activation(attn[:], psS[:], AF.Exp,
                                                        bias=st8[:, 0:1], scale=1.0, accum_out=st8[:, 1:2]),
                       waits=[("vector", p.n("vector"))])
                p.emit("vector", lambda e: e.reciprocal(st8[:, 2:3], st8[:, 1:2]),
                       waits=[("scalar", p.n("scalar"))])
                p.emit("vector", lambda e: e.tensor_scalar(attn[:], attn[:], st8[:, 2:3], None, op0=OP.mult))
                for c in range(16):
                    p.emit("tensor", lambda e, c=c: e.transpose(
                        psT[:, c * 128:(c + 1) * 128], attn[:, c * 128:(c + 1) * 128], idb[:]),
                        waits=[("vector", p.n("vector"))])
                p.emit("vector", lambda e, qtl=qtl: e.tensor_copy(
                    attnT[:].rearrange("p (c q) -> p c q", q=512)[:, :, qtl * 128:(qtl + 1) * 128],
                    psT[:].rearrange("p (c q) -> p c q", q=128)),
                    waits=[("tensor", p.n("tensor"))])
            for c in range(16):
                p.emit("tensor", lambda e, h=h, c=c: e.matmul(
                    psC[0:64, :], V[:, c * 1024 + h * 64:c * 1024 + (h + 1) * 64],
                    attnT[:, c * 512:(c + 1) * 512], start=(c == 0), stop=(c == 15)),
                    waits=[("vector", p.n("vector"))])
            p.emit("vector", lambda e, h=h: e.tensor_copy(
                ctxfm[(h % 2) * 64:(h % 2) * 64 + 64, (h // 2) * 512:(h // 2 + 1) * 512], psC[0:64, :]),
                waits=[("tensor", p.n("tensor"))])
        for tsub in range(4):
            t = qb * 4 + tsub
            for oh in range(2):
                pst = psS[:, oh * 512:(oh + 1) * 512]
                for dc in range(8):
                    p.emit("tensor", lambda e, tsub=tsub, oh=oh, dc=dc, pst=pst: e.matmul(
                        pst, ctxfm[:, dc * 512 + tsub * 128:dc * 512 + (tsub + 1) * 128],
                        wt[:, dc * 1024 + oh * 512:dc * 1024 + (oh + 1) * 512],
                        start=(dc == 0), stop=(dc == 7)),
                        waits=[("vector", p.n("vector")), ("scalar", p.n("scalar"))])
            ob = outb[:, (t % 2) * 1024:(t % 2) * 1024 + 1024]
            p.emit("vector", lambda e, ob=ob: e.tensor_tensor(ob, psS[:, 0:1024], bpt[:], op=OP.add),
                   waits=[("tensor", p.n("tensor")), ("gpsimd", max(0, p.n("gpsimd") - 16))])
            p.emit("gpsimd", lambda e, t=t, ob=ob: e.dma_start(
                out=aps["a1"][t * 128:(t + 1) * 128, :], in_=ob),
                waits=[("vector", p.n("vector"))])
    p.finalize()
    return nc



def build_moe(nc):
    aps = declare_io(nc, {
        "xn_bf": ((D, S), BF16), "xn_f32": ((D, S), F32),
        "router_w": ((NE, D), F32), "router_b": ((NE,), F32),
        "expert_w": ((NE, D, D), F32),
        "ident_f32": ((128, 128), F32), "ones_row_f32": ((1, 128), F32),
    }, {"m1": ((S, D), F32)})
    p = Prog(nc)

    xn = p.sb("xn", [128, 8 * 2048], BF16)
    xf = p.sb("xf", [128, 1024], F32)
    rwt = p.sb("rwt", [128, 8 * 16], F32)
    rstage = p.sb("rstage", [128, 128], F32)
    rbt = p.sb("rbt", [128, 16], F32)
    brow = p.sb("brow", [1, 1024], F32)
    ones1 = p.sb("ones1", [1, 128], F32)
    logit = p.sb("logit", [128, 16], F32)
    probs = p.sb("probs", [128, 16], F32)
    pwork = p.sb("pwork", [128, 16], F32)
    msk = p.sb("msk", [128, 16], F32)
    st8 = p.sb("st8", [128, 16], F32)
    gates = p.sb("gates", [128, 16 * 16], F32)
    racct = p.sb("racct", [128, 16], F32)
    epst = p.sb("epst", [128, 16], F32)
    we = p.sb("we", [128, 2 * 8 * 1024], BF16)
    moeacc = p.sb("moeacc", [128, 16 * 1024], F32)
    outb = p.sb("outb", [128, 2 * 1024], F32)
    idf = p.sb("idf", [128, 128], F32)

    psR = p.ps("psR", [128, 512], F32)
    psE = p.ps("psE", [128, 1024], F32)
    psF = p.ps("psF", [128, 1024], F32)

    p.emit("sync", lambda e: e.dma_start(out=idf[:], in_=aps["ident_f32"]))
    p.emit("sync", lambda e: e.dma_start(out=ones1[:], in_=aps["ones_row_f32"]))
    for c in range(8):
        p.emit("sync", lambda e, c=c: e.dma_start(out=xn[:, c * 2048:(c + 1) * 2048],
                                                  in_=aps["xn_bf"][c * 128:(c + 1) * 128, :]))
    for c in range(8):
        p.emit("sync", lambda e, c=c: e.dma_start(out=rstage[0:16, :], in_=aps["router_w"][:, c * 128:(c + 1) * 128]),
               waits=[("tensor", p.n("tensor"))])
        p.emit("tensor", lambda e, c=c: e.transpose(psR[:, 0:16], rstage[0:16, 0:128], idf[0:16, 0:16]),
               waits=[("sync", p.n("sync")), ("vector", p.n("vector"))])
        p.emit("vector", lambda e, c=c: e.tensor_copy(rwt[:, c * 16:(c + 1) * 16], psR[:, 0:16]),
               waits=[("tensor", p.n("tensor"))])
    emit_bcast_row(p, aps["router_b"], 16, rbt, brow, ones1, psR)

    for t in range(16):
        for c in range(8):
            p.emit("sync", lambda e, t=t, c=c: e.dma_start(
                out=xf[:, c * 128:(c + 1) * 128],
                in_=aps["xn_f32"][c * 128:(c + 1) * 128, t * 128:(t + 1) * 128]),
                waits=[("tensor", p.n("tensor"))])
        for c in range(8):
            p.emit("tensor", lambda e, c=c: e.matmul(
                psR[:, 0:16], xf[:, c * 128:(c + 1) * 128], rwt[:, c * 16:(c + 1) * 16],
                start=(c == 0), stop=(c == 7)),
                waits=[("sync", p.n("sync")), ("vector", p.n("vector"))])
        p.emit("vector", lambda e: e.tensor_tensor(logit[:], psR[:, 0:16], rbt[:], op=OP.add),
               waits=[("tensor", p.n("tensor"))])
        p.emit("vector", lambda e: e.tensor_reduce(st8[:, 0:1], logit[:], axis=AX.X, op=OP.max, negate=True))
        p.emit("scalar", lambda e: e.activation(probs[:], logit[:], AF.Exp, bias=st8[:, 0:1], scale=1.0, accum_out=st8[:, 1:2]),
               waits=[("vector", p.n("vector"))])
        p.emit("vector", lambda e: e.reciprocal(st8[:, 2:3], st8[:, 1:2]), waits=[("scalar", p.n("scalar"))])
        p.emit("vector", lambda e: e.tensor_scalar(probs[:], probs[:], st8[:, 2:3], None, op0=OP.mult))
        p.emit("vector", lambda e: e.tensor_copy(pwork[:], probs[:]))
        for it in range(6):
            p.emit("vector", lambda e: e.tensor_reduce(st8[:, 3:4], pwork[:], axis=AX.X, op=OP.max))
            if it == 0:
                p.emit("vector", lambda e: e.tensor_copy(st8[:, 4:5], st8[:, 3:4]))
            else:
                p.emit("vector", lambda e: e.tensor_tensor(st8[:, 4:5], st8[:, 4:5], st8[:, 3:4], op=OP.add))
            if it < 5:
                p.emit("vector", lambda e: e.tensor_scalar(msk[:], pwork[:], st8[:, 3:4], None, op0=OP.is_ge))
                p.emit("vector", lambda e: e.scalar_tensor_tensor(pwork[:], msk[:], -1e30, pwork[:], op0=OP.mult, op1=OP.add))
        p.emit("vector", lambda e: e.tensor_scalar(msk[:], probs[:], st8[:, 3:4], None, op0=OP.is_ge))
        p.emit("vector", lambda e: e.tensor_scalar(st8[:, 5:6], st8[:, 4:5], 1e-10, None, op0=OP.add))
        p.emit("vector", lambda e: e.reciprocal(st8[:, 6:7], st8[:, 5:6]))
        p.emit("vector", lambda e, t=t: e.scalar_tensor_tensor(
            gates[:, t * 16:(t + 1) * 16], probs[:], st8[:, 6:7], msk[:], op0=OP.mult, op1=OP.mult))
        p.emit("vector", lambda e, t=t: e.tensor_reduce(st8[:, 7:8], gates[:, t * 16:(t + 1) * 16], axis=AX.X, op=OP.add))
        p.emit("vector", lambda e: e.tensor_scalar(st8[:, 8:9], st8[:, 7:8], 1e-10, None, op0=OP.max))
        p.emit("vector", lambda e, t=t: e.reciprocal(racct[:, t:t + 1], st8[:, 8:9]))
        p.emit("vector", lambda e, t=t: e.tensor_scalar(epst[:, t:t + 1], st8[:, 7:8], 1e-10, None, op0=OP.mult))

    for eix in range(NE):
        wslot = we[:, (eix % 2) * 8192:(eix % 2) * 8192 + 8192]
        for dc in range(8):
            p.emit("gpsimd", lambda e, eix=eix, dc=dc, wslot=wslot: e.dma_start(
                out=wslot[:, dc * 1024:(dc + 1) * 1024],
                in_=aps["expert_w"][eix, dc * 128:(dc + 1) * 128, :]),
                waits=[("tensor", max(0, p.n("tensor") - 256))])
        for t in range(16):
            pst = psE if t % 2 == 0 else psF
            for oh in range(2):
                for dc in range(8):
                    p.emit("tensor", lambda e, t=t, oh=oh, dc=dc, pst=pst, wslot=wslot: e.matmul(
                        pst[:, oh * 512:(oh + 1) * 512],
                        xn[:, dc * 2048 + t * 128:dc * 2048 + (t + 1) * 128],
                        wslot[:, dc * 1024 + oh * 512:dc * 1024 + (oh + 1) * 512],
                        start=(dc == 0), stop=(dc == 7)),
                        waits=[("gpsimd", p.n("gpsimd")), ("vector", p.n("vector"))])
            if eix == 0:
                p.emit("vector", lambda e, t=t, eix=eix, pst=pst: e.tensor_scalar(
                    moeacc[:, t * 1024:(t + 1) * 1024], pst[:], gates[:, t * 16 + eix:t * 16 + eix + 1], None, op0=OP.mult),
                    waits=[("tensor", p.n("tensor"))])
            else:
                p.emit("vector", lambda e, t=t, eix=eix, pst=pst: e.scalar_tensor_tensor(
                    moeacc[:, t * 1024:(t + 1) * 1024], pst[:], gates[:, t * 16 + eix:t * 16 + eix + 1],
                    moeacc[:, t * 1024:(t + 1) * 1024], op0=OP.mult, op1=OP.add),
                    waits=[("tensor", p.n("tensor"))])
    for t in range(16):
        ob = outb[:, (t % 2) * 1024:(t % 2) * 1024 + 1024]
        p.emit("vector", lambda e, t=t, ob=ob: e.tensor_scalar(
            ob, moeacc[:, t * 1024:(t + 1) * 1024], racct[:, t:t + 1], epst[:, t:t + 1], op0=OP.mult, op1=OP.add),
            waits=[("gpsimd", max(0, p.n("gpsimd") - 16))])
        p.emit("gpsimd", lambda e, t=t, ob=ob: e.dma_start(
            out=aps["m1"][t * 128:(t + 1) * 128, :], in_=ob),
            waits=[("vector", p.n("vector"))])
    p.finalize()
    return nc



def _run(build_fn, per_core_inputs, n_cores=NCORES, **bkw):
    nc = bass.Bass("TRN2", target_bir_lowering=False)
    build_fn(nc, **bkw) if bkw else build_fn(nc)
    res = run_bass_kernel_spmd(nc, per_core_inputs, core_ids=list(range(n_cores)))
    return res.results


def kernel(L0_attn_x, L0_attn_Wq, L0_attn_Wk, L0_attn_Wv, L0_attn_Wo, L0_attn_b_o,
           L0_attn_rel_k_emb, L0_attn_rel_v_emb, L0_ffn_W1, L0_ffn_b1, L0_ffn_W2,
           L0_ffn_b2, L1_attn_Wqkv, L1_attn_bqkv, L1_attn_Wproj, L1_attn_bproj,
           L1_attn_qn_w, L1_attn_kn_w, L1_ffn_router_w, L1_ffn_router_b,
           L1_ffn_expert_w):
    C = CONSTS
    x = np.ascontiguousarray(np.asarray(L0_attn_x, dtype=np.float32))

    def bc(arrs):
        out = []
        for i in range(NCORES):
            d = {}
            for k, v in arrs.items():
                d[k] = v[i] if isinstance(v, list) else v
            out.append(d)
        return out

    xs = [np.ascontiguousarray(x[b]) for b in range(NCORES)]

    r = _run(lambda nc: build_ln(nc, False), bc({"x": xs, "ident_f32": C["ident_f32"]}))
    xn0 = [r[i]["xn_bf"] for i in range(NCORES)]
    r = _run(build_attn0, bc({
        "xn_bf": xn0, "Wq": L0_attn_Wq, "Wk": L0_attn_Wk, "Wv": L0_attn_Wv,
        "Wo": L0_attn_Wo, "b_o": L0_attn_b_o, "rel_k": L0_attn_rel_k_emb,
        "rel_v": L0_attn_rel_v_emb, "ident_f32": C["ident_f32"], "ident_bf": C["ident_bf"],
        "tri_bf": C["tri_bf"], "ones_col_bf": C["ones_col_bf"], "ones_row_f32": C["ones_row_f32"]}))
    a0 = [r[i]["a0"] for i in range(NCORES)]
    r = _run(build_ffn0, bc({
        "xn_bf": xn0, "W1": L0_ffn_W1, "b1": L0_ffn_b1, "W2": L0_ffn_W2, "b2": L0_ffn_b2,
        "ones_row_f32": C["ones_row_f32"], "ident_f32": C["ident_f32"]}))
    f0 = [r[i]["f0"] for i in range(NCORES)]
    r = _run(build_comb, bc({"x": xs, "a": a0, "b": f0}))
    x1 = [r[i]["o"] for i in range(NCORES)]

    r = _run(lambda nc: build_ln(nc, True), bc({"x": x1, "ident_f32": C["ident_f32"]}))
    xn1 = [r[i]["xn_bf"] for i in range(NCORES)]
    xn1f = [r[i]["xn_f32"] for i in range(NCORES)]
    r = _run(build_attn1, bc({
        "xn_bf": xn1, "Wqkv": L1_attn_Wqkv, "bqkv": L1_attn_bqkv, "Wproj": L1_attn_Wproj,
        "bproj": L1_attn_bproj, "qn_w": L1_attn_qn_w, "kn_w": L1_attn_kn_w,
        "ident_f32": C["ident_f32"], "ident_bf": C["ident_bf"], "ones_row_f32": C["ones_row_f32"]}))
    a1 = [r[i]["a1"] for i in range(NCORES)]
    r = _run(build_moe, bc({
        "xn_bf": xn1, "xn_f32": xn1f, "router_w": L1_ffn_router_w,
        "router_b": L1_ffn_router_b, "expert_w": L1_ffn_expert_w,
        "ident_f32": C["ident_f32"], "ones_row_f32": C["ones_row_f32"]}))
    m1 = [r[i]["m1"] for i in range(NCORES)]
    r = _run(build_comb, bc({"x": x1, "a": a1, "b": m1}))
    out = np.stack([np.asarray(r[i]["o"], dtype=np.float32) for i in range(NCORES)], axis=0)
    return out


# version 02
# speedup vs baseline: 1.0042x; 1.0042x over previous
import contextlib
import numpy as np
import ml_dtypes
import concourse.bass as bass
import concourse.mybir as mybir
from concourse.alu_op_type import AluOpType
from concourse.bass_utils import run_bass_kernel_spmd

F32 = mybir.dt.float32
BF16 = mybir.dt.bfloat16
AF = mybir.ActivationFunctionType
AX = mybir.AxisListType
OP = AluOpType

D = 1024
S = 2048
H = 16
DK = 64
NCORES = 8
INTER = 4096
NE = 16

ENGS = ("sync", "gpsimd", "vector", "scalar", "tensor")


class Sem:
    __slots__ = ("h", "step", "n")

    def __init__(self, h, step):
        self.h = h
        self.step = step
        self.n = 0


class Prog:

    def __init__(self, nc):
        self.nc = nc
        self.q = {e: [] for e in ENGS}
        self.stack = contextlib.ExitStack()
        self.scopes = [self.stack]
        self._uid = 0
        self._barrier = {}
        self.sem = {}
        for e in ENGS:
            step = 16 if e in ("sync", "gpsimd") else 1
            self.sem[e] = Sem(self.stack.enter_context(nc.semaphore("S" + e)), step)
        self._cpool = self.stack.enter_context(nc.sbuf_tensor("cpool", [128, 16], F32))
        self._cmap = {}

    def n(self, eng):
        return self.sem[eng].n

    def sb(self, name, shape, dtype=F32):
        self._uid += 1
        return self.scopes[-1].enter_context(
            self.nc.sbuf_tensor(f"{name}_{self._uid}", shape, dtype))

    @contextlib.contextmanager
    def phase(self, name):
        st = contextlib.ExitStack()
        self.scopes.append(st)
        try:
            yield
        finally:
            self.scopes.pop()
            st.close()
            snap = [(e, self.sem[e].n) for e in ENGS]
            for e in ENGS:
                self._barrier[e] = snap

    def cc(self, val):
        if val not in self._cmap:
            col = len(self._cmap)
            self._cmap[val] = col
            self.emit("vector", lambda e, col=col, val=val: e.memset(self._cpool[:, col:col + 1], val))
        c = self._cmap[val]
        return self._cpool[:, c:c + 1]

    def ps(self, name, shape, dtype=F32):
        self._uid += 1
        return self.scopes[-1].enter_context(
            self.nc.psum_tensor(f"{name}_{self._uid}", shape, dtype))

    def emit(self, eng, fn, waits=()):
        sem = self.sem[eng]
        waits = list(waits)
        if eng in self._barrier:
            waits.extend((e2, n) for (e2, n) in self._barrier.pop(eng) if e2 != eng)
        wl = []
        for e2, n in waits:
            s2 = self.sem[e2]
            assert n <= s2.n, f"wait on future value of {e2}: {n} > {s2.n}"
            if n > 0:
                wl.append((s2.h, n))
        byh = {}
        for h, n in wl:
            byh[h] = max(n, byh.get(h, 0))
        wl = tuple(byh.items())

        if eng in ("vector", "scalar"):
            def cl(e, fn=fn, wl=wl, sem=sem):
                for h, n in wl:
                    e.wait_ge(h, n)
                fn(e)
                e.drain().then_inc(sem.h, sem.step)
        else:
            def cl(e, fn=fn, wl=wl, sem=sem):
                for h, n in wl:
                    e.wait_ge(h, n)
                fn(e).then_inc(sem.h, sem.step)

        self.q[eng].append(cl)
        sem.n += sem.step
        return sem.n

    def wother(self, eng):
        return [(e, self.sem[e].n) for e in ENGS if e != eng]

    def finalize(self):
        snap = [(e, self.sem[e].n) for e in ENGS]

        def fin(e, snap=snap):
            for eng, n in snap:
                if n > 0:
                    e.wait_ge(self.sem[eng].h, n)
        self.q["gpsimd"].append(fin)
        with self.nc.Block() as block:
            for eng in ENGS:
                lst = self.q[eng]
                if lst:
                    getattr(block, eng)(lambda e, lst=lst: [c(e) for c in lst])
        self.stack.close()


def np_bf16(a):
    return np.asarray(a).astype(ml_dtypes.bfloat16)


def consts_np():
    return {
        "ident_f32": np.eye(128, dtype=np.float32),
        "ident_bf": np_bf16(np.eye(128)),
        "tri_bf": np_bf16(np.tril(np.ones((128, 128)))),
        "ones_col_bf": np_bf16(np.ones((128, 1))),
        "ones_row_f32": np.ones((1, 128), dtype=np.float32),
    }


CONSTS = consts_np()


def declare_io(nc, ins, outs):
    aps = {}
    for n, (shp, dt) in ins.items():
        aps[n] = nc.dram_tensor(n, list(shp), dt, kind="ExternalInput").ap()
    for n, (shp, dt) in outs.items():
        aps[n] = nc.dram_tensor(n, list(shp), dt, kind="ExternalOutput").ap()
    return aps



def emit_weight_T(p, w_dram, O, I, wt_sb, wstage, psT, idf):
    nIC, nOC = I // 128, O // 128
    nbuf = max(1, wstage.shape[1] // I)
    for oc in range(nOC):
        st = wstage[:, (oc % nbuf) * I:(oc % nbuf) * I + I]
        p.emit("sync", lambda e, oc=oc, st=st: e.dma_start(
            out=st, in_=w_dram[oc * 128:(oc + 1) * 128, :]),
            waits=[("tensor", p.n("tensor"))])
        for ic in range(nIC):
            p.emit("tensor", lambda e, ic=ic, st=st: e.transpose(
                psT[:, ic * 128:(ic + 1) * 128], st[:, ic * 128:(ic + 1) * 128], idf[:]),
                waits=[("sync", p.n("sync")), ("vector", p.n("vector")), ("scalar", p.n("scalar"))])
        p.emit("vector", lambda e, oc=oc: e.tensor_copy(
            wt_sb[:].rearrange("p (ic o) -> p ic o", o=O)[:, :, oc * 128:(oc + 1) * 128],
            psT[:, 0:nIC * 128].rearrange("p (ic q) -> p ic q", q=128)),
            waits=[("tensor", p.n("tensor"))])


def emit_bcast_row(p, vec_dram, n, out_tile, row_sb, ones1, psB):
    p.emit("sync", lambda e, n=n: e.dma_start(
        out=row_sb[0:1, 0:n], in_=vec_dram[0:n].rearrange("(a b) -> a b", a=1)),
        waits=[("tensor", p.n("tensor"))])
    for c in range((n + 511) // 512):
        w = min(512, n - c * 512)
        p.emit("tensor", lambda e, c=c, w=w: e.matmul(
            psB[:, c * 512:c * 512 + w], ones1[0:1, :], row_sb[0:1, c * 512:c * 512 + w],
            start=True, stop=True),
            waits=[("sync", p.n("sync")), ("vector", p.n("vector")), ("scalar", p.n("scalar"))])
    p.emit("vector", lambda e, n=n: e.tensor_copy(out_tile[:, 0:n], psB[:, 0:n]),
           waits=[("tensor", p.n("tensor"))])


def hsl(sb, h, c0, w, rows=64):
    oc = h // 2
    r = (h % 2) * 64
    return sb[r:r + rows, oc * 2048 + c0:oc * 2048 + c0 + w]



def phase_ln(p, io, env, want_f32):
    aps = io
    idf = env["idf"]
    xsb = p.sb("xsb", [128, 16 * 1024], F32)
    sc = p.sb("sc", [128, 1024], F32)
    stats = p.sb("stats", [128, 16], F32)
    xnbf = p.sb("xnbf", [128, 8 * 2048], BF16)
    xnf = p.sb("xnf", [128, 8 * 2048], F32) if want_f32 else None
    psA = p.ps("psA", [128, 1024], F32)
    psB = p.ps("psB", [128, 1024], F32)

    ld0 = p.n("sync")
    for t in range(16):
        p.emit("sync", lambda e, t=t: e.dma_start(
            out=xsb[:, t * 1024:(t + 1) * 1024], in_=aps["x"][t * 128:(t + 1) * 128, :]))

    for t in range(16):
        xt = xsb[:, t * 1024:(t + 1) * 1024]
        p.emit("vector", lambda e, xt=xt: e.tensor_reduce(stats[:, 0:1], xt, axis=AX.X, op=OP.add),
               waits=[("sync", ld0 + 16 * (t + 1))])
        p.emit("vector", lambda e, xt=xt: e.tensor_tensor(sc[:], xt, xt, op=OP.mult))
        p.emit("vector", lambda e: e.tensor_reduce(stats[:, 1:2], sc[:], axis=AX.X, op=OP.add))
        p.emit("vector", lambda e: e.tensor_scalar(stats[:, 2:3], stats[:, 0:1], 1.0 / D, None, op0=OP.mult))
        p.emit("vector", lambda e: e.tensor_tensor(stats[:, 3:4], stats[:, 2:3], stats[:, 2:3], op=OP.mult))
        rD = p.cc(1.0 / D)
        p.emit("vector", lambda e, rD=rD: e.tensor_scalar(stats[:, 4:5], stats[:, 1:2], rD, stats[:, 3:4], op0=OP.mult, op1=OP.subtract))
        eps_ln = p.cc(1e-5)
        p.emit("scalar", lambda e, eps_ln=eps_ln: e.activation(stats[:, 5:6], stats[:, 4:5], AF.Ln, bias=eps_ln, scale=1.0),
               waits=[("vector", p.n("vector"))])
        p.emit("scalar", lambda e: e.activation(stats[:, 6:7], stats[:, 5:6], AF.Exp, bias=0.0, scale=-0.5))
        p.emit("vector", lambda e, xt=xt: e.tensor_scalar(
            sc[:], xt, stats[:, 2:3], stats[:, 6:7], op0=OP.subtract, op1=OP.mult),
            waits=[("scalar", p.n("scalar"))])
        pst = psA if t % 2 == 0 else psB
        for c in range(8):
            p.emit("tensor", lambda e, c=c, pst=pst: e.transpose(
                pst[:, c * 128:(c + 1) * 128], sc[:, c * 128:(c + 1) * 128], idf[:]),
                waits=[("vector", p.n("vector"))])
        p.emit("vector", lambda e, t=t, pst=pst: e.tensor_copy(
            xnbf[:].rearrange("p (c s) -> p c s", s=S)[:, :, t * 128:(t + 1) * 128],
            pst[:].rearrange("p (c q) -> p c q", q=128)),
            waits=[("tensor", p.n("tensor"))])
        if want_f32:
            p.emit("vector", lambda e, t=t, pst=pst: e.tensor_copy(
                xnf[:].rearrange("p (c s) -> p c s", s=S)[:, :, t * 128:(t + 1) * 128],
                pst[:].rearrange("p (c q) -> p c q", q=128)))

    for c in range(8):
        p.emit("gpsimd", lambda e, c=c: e.dma_start(
            out=aps["xn_bf"][c * 128:(c + 1) * 128, :], in_=xnbf[:, c * S:(c + 1) * S]),
            waits=[("vector", p.n("vector"))])
        if want_f32:
            p.emit("gpsimd", lambda e, c=c: e.dma_start(
                out=aps["xn_f32"][c * 128:(c + 1) * 128, :], in_=xnf[:, c * S:(c + 1) * S]))



def phase_comb(p, io):
    aps = io
    xa = p.sb("xa", [128, 2 * 1024], F32)
    xb = p.sb("xb", [128, 2 * 1024], F32)
    xc = p.sb("xc", [128, 2 * 1024], F32)
    xo = p.sb("xo", [128, 2 * 1024], F32)
    for t in range(16):
        b = (t % 2) * 1024
        p.emit("sync", lambda e, t=t, b=b: e.dma_start(out=xa[:, b:b + 1024], in_=aps["x"][t * 128:(t + 1) * 128, :]),
               waits=[("vector", p.n("vector"))])
        p.emit("sync", lambda e, t=t, b=b: e.dma_start(out=xb[:, b:b + 1024], in_=aps["a"][t * 128:(t + 1) * 128, :]))
        p.emit("sync", lambda e, t=t, b=b: e.dma_start(out=xc[:, b:b + 1024], in_=aps["b"][t * 128:(t + 1) * 128, :]))
        p.emit("vector", lambda e, b=b: e.tensor_tensor(xo[:, b:b + 1024], xa[:, b:b + 1024], xb[:, b:b + 1024], op=OP.add),
               waits=[("sync", p.n("sync")), ("gpsimd", max(0, p.n("gpsimd") - 16))])
        p.emit("vector", lambda e, b=b: e.tensor_tensor(xo[:, b:b + 1024], xo[:, b:b + 1024], xc[:, b:b + 1024], op=OP.add))
        p.emit("gpsimd", lambda e, t=t, b=b: e.dma_start(out=aps["o"][t * 128:(t + 1) * 128, :], in_=xo[:, b:b + 1024]),
               waits=[("vector", p.n("vector"))])



def phase_ffn0(p, io, env):
    aps = io
    ones1 = env["ones1"]
    xn = p.sb("xn", [128, 8 * 2048], BF16)
    w1h = p.sb("w1h", [128, 8 * 2048], BF16)
    w2h = p.sb("w2h", [128, 16 * 1024], BF16)
    hbuf = p.sb("hbuf", [128, 16 * 512], BF16)
    b1c = p.sb("b1c", [128, 32], F32)
    acc = p.sb("acc", [128, 16 * 1024], F32)
    b2t = p.sb("b2t", [128, 1024], F32)
    brow = p.sb("brow", [1, 1024], F32)
    outb = p.sb("outb", [128, 2 * 1024], F32)
    psA = p.ps("psA", [128, 512], F32)
    psB = p.ps("psB", [128, 512], F32)
    psC = p.ps("psC", [128, 1024], F32)
    psD = p.ps("psD", [128, 1024], F32)

    for c in range(8):
        p.emit("sync", lambda e, c=c: e.dma_start(out=xn[:, c * 2048:(c + 1) * 2048],
                                                  in_=aps["xn_bf"][c * 128:(c + 1) * 128, :]))
    b1st = p.sb("b1st", [32, 128], F32)
    idf2 = env["idf"]
    p.emit("sync", lambda e: e.dma_start(out=b1st[:], in_=aps["b1"].rearrange("(ic p) -> ic p", p=128)))
    p.emit("tensor", lambda e: e.transpose(psC[:, 0:32], b1st[:, 0:128], idf2[0:32, 0:32]),
           waits=[("sync", p.n("sync")), ("gpsimd", p.n("gpsimd"))])
    p.emit("vector", lambda e: e.tensor_copy(b1c[:], psC[:, 0:32]), waits=[("tensor", p.n("tensor"))])
    emit_bcast_row(p, aps["b2"], 1024, b2t, brow, ones1, psC)

    for ih in range(2):
        for dc in range(8):
            p.emit("gpsimd", lambda e, dc=dc, ih=ih: e.dma_start(
                out=w1h[:, dc * 2048:(dc + 1) * 2048],
                in_=aps["W1"][dc * 128:(dc + 1) * 128, ih * 2048:(ih + 1) * 2048]),
                waits=[("tensor", p.n("tensor"))])
        for ic in range(16):
            p.emit("gpsimd", lambda e, ic=ic, ih=ih: e.dma_start(
                out=w2h[:, ic * 1024:(ic + 1) * 1024],
                in_=aps["W2"][ih * 2048 + ic * 128:ih * 2048 + (ic + 1) * 128, :]),
                waits=[("tensor", p.n("tensor"))])
        for qb in range(4):
            for ic in range(16):
                pst = psA if ic % 2 == 0 else psB
                for dc in range(8):
                    p.emit("tensor", lambda e, ic=ic, dc=dc, qb=qb, pst=pst: e.matmul(
                        pst[:], w1h[:, dc * 2048 + ic * 128:dc * 2048 + (ic + 1) * 128],
                        xn[:, dc * 2048 + qb * 512:dc * 2048 + (qb + 1) * 512],
                        start=(dc == 0), stop=(dc == 7)),
                        waits=[("gpsimd", p.n("gpsimd")), ("sync", p.n("sync")), ("scalar", p.n("scalar"))])
                icg = ih * 16 + ic
                p.emit("scalar", lambda e, ic=ic, icg=icg, pst=pst: e.activation(
                    hbuf[:, ic * 512:(ic + 1) * 512], pst[:], AF.Relu,
                    bias=b1c[:, icg:icg + 1], scale=1.0),
                    waits=[("tensor", p.n("tensor"))])
            for tsub in range(4):
                t = qb * 4 + tsub
                pst = psC if tsub % 2 == 0 else psD
                for ic in range(16):
                    for oc in range(2):
                        p.emit("tensor", lambda e, ic=ic, oc=oc, tsub=tsub, pst=pst: e.matmul(
                            pst[:, oc * 512:(oc + 1) * 512],
                            hbuf[:, ic * 512 + tsub * 128:ic * 512 + (tsub + 1) * 128],
                            w2h[:, ic * 1024 + oc * 512:ic * 1024 + (oc + 1) * 512],
                            start=(ic == 0), stop=(ic == 15)),
                            waits=[("scalar", p.n("scalar")), ("vector", p.n("vector"))])
                if ih == 0:
                    p.emit("vector", lambda e, t=t, pst=pst: e.tensor_copy(
                        acc[:, t * 1024:(t + 1) * 1024], pst[:]), waits=[("tensor", p.n("tensor"))])
                else:
                    p.emit("vector", lambda e, t=t, pst=pst: e.tensor_tensor(
                        acc[:, t * 1024:(t + 1) * 1024], acc[:, t * 1024:(t + 1) * 1024], pst[:], op=OP.add),
                        waits=[("tensor", p.n("tensor"))])
                    ob = outb[:, (t % 2) * 1024:(t % 2) * 1024 + 1024]
                    p.emit("vector", lambda e, ob=ob, t=t: e.tensor_tensor(
                        ob, acc[:, t * 1024:(t + 1) * 1024], b2t[:], op=OP.add),
                        waits=[("gpsimd", max(0, p.n("gpsimd") - 16))])
                    p.emit("gpsimd", lambda e, t=t, ob=ob: e.dma_start(
                        out=aps["f0"][t * 128:(t + 1) * 128, :], in_=ob),
                        waits=[("vector", p.n("vector"))])



def phase_attn0(p, io, env):
    aps = io
    pscrP = io["pscrP"]
    pscrB = io["pscrB"]
    idf = env["idf"]
    idb = env["idb"]
    tri = env["tri"]
    onec = env["onec"]
    ones1 = env["ones1"]
    xn = p.sb("xn", [128, 8 * 2048], BF16)
    wt = p.sb("wt", [128, 8 * 1024], BF16)
    wstage = p.sb("wstage", [128, 2 * 1024], F32)
    Q = p.sb("Q", [128, 8 * 2048], BF16)
    K = p.sb("K", [128, 8 * 2048], BF16)
    V = p.sb("V", [128, 16 * 1024], BF16)
    attn = p.sb("attn", [128, 2112], BF16)
    attnT = p.sb("attnT", [128, 16 * 512], BF16)
    ctxfm = p.sb("ctxfm", [128, 8 * 512], BF16)
    Wwin = p.sb("Wwin", [128, 192], F32)
    QRt = p.sb("QRt", [128, 65], F32)
    Ptab = p.sb("Ptab", [128, 319], F32)
    Dg = p.sb("Dg", [128, 65], BF16)
    Atile = p.sb("Atile", [128, 65], BF16)
    Arhs = p.sb("Arhs", [128, 128], BF16)
    crn = p.sb("crn", [128, 128], BF16)
    st8 = p.sb("st8", [128, 16], F32)
    relkT = p.sb("relkT", [128, 65], BF16)
    relvN = p.sb("relvN", [128, 64], BF16)
    relst = p.sb("relst", [128, 64], F32)
    bot = p.sb("bot", [128, 1024], F32)
    brow = p.sb("brow", [1, 1024], F32)
    outb = p.sb("outb", [128, 2 * 1024], F32)

    psS = p.ps("psS", [128, 2048], F32)
    psT = p.ps("psT", [128, 2048], BF16)
    psC = p.ps("psC", [128, 512], F32)
    psX = p.ps("psX", [128, 512], F32)

    p.emit("sync", lambda e: e.dma_start(out=relst[0:65, :], in_=aps["rel_k"]))
    for c in range(8):
        p.emit("sync", lambda e, c=c: e.dma_start(out=xn[:, c * 2048:(c + 1) * 2048],
                                                  in_=aps["xn_bf"][c * 128:(c + 1) * 128, :]))
    p.emit("gpsimd", lambda e: e.dma_start(out=relvN[0:65, :], in_=aps["rel_v"]))
    p.emit("tensor", lambda e: e.transpose(psX[0:64, 0:65], relst[0:65, 0:64], idf[0:65, 0:65]),
           waits=[("sync", p.n("sync"))])
    p.emit("vector", lambda e: e.tensor_copy(relkT[0:64, 0:65], psX[0:64, 0:65]),
           waits=[("tensor", p.n("tensor"))])
    p.emit("vector", lambda e: e.tensor_copy(relkT[64:128, 0:65], psX[0:64, 0:65]))
    emit_bcast_row(p, aps["b_o"], 1024, bot, brow, ones1, psS)

    def proj_fm(out_sb, scale):
        for oc in range(8):
            for qb in range(4):
                pst = psS[:, (qb % 2) * 512:(qb % 2) * 512 + 512]
                for dc in range(8):
                    p.emit("tensor", lambda e, oc=oc, qb=qb, dc=dc, pst=pst: e.matmul(
                        pst, wt[:, dc * 1024 + oc * 128:dc * 1024 + (oc + 1) * 128],
                        xn[:, dc * 2048 + qb * 512:dc * 2048 + (qb + 1) * 512],
                        start=(dc == 0), stop=(dc == 7)),
                        waits=[("vector", p.n("vector")), ("scalar", p.n("scalar")), ("sync", p.n("sync"))])
                p.emit("scalar", lambda e, oc=oc, qb=qb, pst=pst, scale=scale: e.activation(
                    out_sb[:, oc * 2048 + qb * 512:oc * 2048 + (qb + 1) * 512], pst,
                    AF.Copy, bias=0.0, scale=scale),
                    waits=[("tensor", p.n("tensor"))])

    def proj_tm(out_sb):
        for t in range(16):
            for oh in range(2):
                pst = psS[:, (oh % 2) * 512:(oh % 2) * 512 + 512]
                for dc in range(8):
                    p.emit("tensor", lambda e, t=t, oh=oh, dc=dc, pst=pst: e.matmul(
                        pst, xn[:, dc * 2048 + t * 128:dc * 2048 + (t + 1) * 128],
                        wt[:, dc * 1024 + oh * 512:dc * 1024 + (oh + 1) * 512],
                        start=(dc == 0), stop=(dc == 7)),
                        waits=[("vector", p.n("vector")), ("scalar", p.n("scalar")), ("sync", p.n("sync"))])
                p.emit("scalar", lambda e, t=t, oh=oh, pst=pst: e.activation(
                    out_sb[:, t * 1024 + oh * 512:t * 1024 + (oh + 1) * 512], pst,
                    AF.Copy, bias=0.0, scale=1.0),
                    waits=[("tensor", p.n("tensor"))])

    emit_weight_T(p, aps["Wq"], D, D, wt, wstage, psS, idf)
    proj_fm(Q, 0.125)
    emit_weight_T(p, aps["Wk"], D, D, wt, wstage, psS, idf)
    proj_fm(K, 1.0)
    emit_weight_T(p, aps["Wv"], D, D, wt, wstage, psS, idf)
    proj_tm(V)
    emit_weight_T(p, aps["Wo"], D, D, wt, wstage, psS, idf)

    p.emit("vector", lambda e: e.memset(attn[:, 0:32], 0.0))
    p.emit("vector", lambda e: e.memset(attn[:, 2080:2112], 0.0))

    for qb in range(4):
        for h in range(H):
            for qtl in range(4):
                qt = qb * 4 + qtl
                q0 = qt * 128
                for kb in range(4):
                    p.emit("tensor", lambda e, h=h, qt=qt, kb=kb: e.matmul(
                        psS[:, kb * 512:(kb + 1) * 512],
                        hsl(Q, h, qt * 128, 128), hsl(K, h, kb * 512, 512),
                        start=True, stop=True),
                        waits=[("vector", p.n("vector")), ("scalar", p.n("scalar"))])
                p.emit("tensor", lambda e, h=h, qt=qt: e.matmul(
                    psX[:, 0:65], hsl(Q, h, qt * 128, 128),
                    relkT[(h % 2) * 64:(h % 2) * 64 + 64, 0:65],
                    start=True, stop=True))
                p.emit("vector", lambda e: e.tensor_copy(QRt[:], psX[:, 0:65]),
                       waits=[("tensor", p.n("tensor"))])
                p.emit("vector", lambda e: e.tensor_copy(Ptab[:, 0:127], QRt[:, 0:1].broadcast_to([128, 127])))
                p.emit("vector", lambda e: e.tensor_copy(Ptab[:, 127:192], QRt[:]))
                p.emit("vector", lambda e: e.tensor_copy(Ptab[:, 192:319], QRt[:, 64:65].broadcast_to([128, 127])))
                p.emit("sync", lambda e: e.dma_start(
                    out=pscrP[0:128 * 319].rearrange("(q w) -> q w", w=319), in_=Ptab[:]),
                    waits=[("vector", p.n("vector"))])
                p.emit("sync", lambda e: e.dma_start(
                    out=Wwin[:], in_=pscrP[127:127 + 128 * 318].rearrange("(q w) -> q w", w=318)[:, 0:192]))
                wadd = [("sync", p.n("sync")), ("tensor", p.n("tensor"))]
                if qt == 0:
                    p.emit("vector", lambda e: e.tensor_tensor(psS[:, 0:160], psS[:, 0:160], Wwin[:, 32:192], op=OP.add), waits=wadd)
                elif qt == 15:
                    p.emit("vector", lambda e: e.tensor_tensor(psS[:, 1888:2048], psS[:, 1888:2048], Wwin[:, 0:160], op=OP.add), waits=wadd)
                else:
                    p.emit("vector", lambda e, q0=q0: e.tensor_tensor(psS[:, q0 - 32:q0 + 160], psS[:, q0 - 32:q0 + 160], Wwin[:, 0:192], op=OP.add), waits=wadd)
                if qt > 0:
                    p.emit("vector", lambda e, q0=q0: e.tensor_scalar(psS[:, 0:q0 - 32], psS[:, 0:q0 - 32], QRt[:, 0:1], None, op0=OP.add))
                if qt < 15:
                    p.emit("vector", lambda e, q0=q0: e.tensor_scalar(psS[:, q0 + 160:2048], psS[:, q0 + 160:2048], QRt[:, 64:65], None, op0=OP.add))
                p.emit("vector", lambda e: e.tensor_reduce(st8[:, 0:1], psS[:], axis=AX.X, op=OP.max, negate=True))
                p.emit("scalar", lambda e: e.activation(attn[:, 32:2080], psS[:], AF.Exp,
                                                        bias=st8[:, 0:1], scale=1.0, accum_out=st8[:, 1:2]),
                       waits=[("vector", p.n("vector"))])
                p.emit("vector", lambda e: e.reciprocal(st8[:, 2:3], st8[:, 1:2]),
                       waits=[("scalar", p.n("scalar"))])
                p.emit("vector", lambda e: e.tensor_scalar(attn[:, 32:2080], attn[:, 32:2080], st8[:, 2:3], None, op0=OP.mult))
                p.emit("vector", lambda e, q0=q0: e.tensor_tensor(crn[:], attn[:, q0:q0 + 128], tri[:], op=OP.mult))
                p.emit("vector", lambda e: e.tensor_reduce(st8[:, 3:4], crn[:], axis=AX.X, op=OP.add))
                p.emit("sync", lambda e, q0=q0: e.dma_start(
                    out=pscrB[0:128 * 192].rearrange("(q w) -> q w", w=192), in_=attn[:, q0:q0 + 192]),
                    waits=[("vector", p.n("vector"))])
                p.emit("sync", lambda e: e.dma_start(
                    out=Dg[:], in_=pscrB[0:128 * 193].rearrange("(q w) -> q w", w=193)[:, 0:65]))
                for c in range(16):
                    p.emit("tensor", lambda e, c=c: e.transpose(
                        psT[:, c * 128:(c + 1) * 128], attn[:, 32 + c * 128:32 + (c + 1) * 128], idb[:]),
                        waits=[("vector", p.n("vector"))])
                p.emit("vector", lambda e, qtl=qtl: e.tensor_copy(
                    attnT[:].rearrange("p (c q) -> p c q", q=512)[:, :, qtl * 128:(qtl + 1) * 128],
                    psT[:].rearrange("p (c q) -> p c q", q=128)),
                    waits=[("tensor", p.n("tensor"))])
                nfull = qt - 1
                for c in range(max(0, nfull)):
                    p.emit("tensor", lambda e, c=c, qtl=qtl: e.matmul(
                        psX[:, 128:129], attnT[:, c * 512 + qtl * 128:c * 512 + (qtl + 1) * 128],
                        onec[:], start=(c == 0), stop=False),
                        waits=[("vector", p.n("vector"))])
                if qt >= 1:
                    p.emit("tensor", lambda e, qt=qt, qtl=qtl, st=(nfull <= 0): e.matmul(
                        psX[:, 128:129],
                        attnT[0:96, (qt - 1) * 512 + qtl * 128:(qt - 1) * 512 + (qtl + 1) * 128],
                        onec[0:96, :], start=st, stop=True),
                        waits=[("vector", p.n("vector"))])
                aw = [("tensor", p.n("tensor")), ("sync", p.n("sync"))]
                if qt >= 1:
                    p.emit("vector", lambda e: e.tensor_tensor(st8[:, 4:5], psX[:, 128:129], st8[:, 3:4], op=OP.add), waits=aw)
                else:
                    p.emit("vector", lambda e: e.tensor_copy(st8[:, 4:5], st8[:, 3:4]), waits=aw)
                p.emit("vector", lambda e: e.tensor_reduce(st8[:, 5:6], Dg[:, 1:64], axis=AX.X, op=OP.add))
                p.emit("vector", lambda e: e.tensor_tensor(st8[:, 6:7], st8[:, 4:5], st8[:, 5:6], op=OP.add))
                p.emit("vector", lambda e: e.tensor_scalar(st8[:, 7:8], st8[:, 6:7], -1.0, 1.0, op0=OP.mult, op1=OP.add))
                p.emit("vector", lambda e: e.tensor_copy(Atile[:, 1:64], Dg[:, 1:64]))
                p.emit("vector", lambda e: e.tensor_copy(Atile[:, 0:1], st8[:, 4:5]))
                p.emit("vector", lambda e: e.tensor_copy(Atile[:, 64:65], st8[:, 7:8]))
                p.emit("tensor", lambda e: e.transpose(psT[0:65, 0:128], Atile[:], idb[:]),
                       waits=[("vector", p.n("vector"))])
                p.emit("vector", lambda e: e.tensor_copy(Arhs[0:65, 0:128], psT[0:65, 0:128]),
                       waits=[("tensor", p.n("tensor"))])
                p.emit("tensor", lambda e, qtl=qtl: e.matmul(
                    psC[0:64, qtl * 128:(qtl + 1) * 128], relvN[0:65, 0:64], Arhs[0:65, 0:128],
                    start=(qtl == 0), stop=False),
                    waits=[("vector", p.n("vector"))])
            for c in range(16):
                p.emit("tensor", lambda e, h=h, c=c: e.matmul(
                    psC[0:64, :], V[:, c * 1024 + h * 64:c * 1024 + (h + 1) * 64],
                    attnT[:, c * 512:(c + 1) * 512], start=False, stop=(c == 15)),
                    waits=[("vector", p.n("vector"))])
            p.emit("vector", lambda e, h=h: e.tensor_copy(
                ctxfm[(h % 2) * 64:(h % 2) * 64 + 64, (h // 2) * 512:(h // 2 + 1) * 512], psC[0:64, :]),
                waits=[("tensor", p.n("tensor"))])
        for tsub in range(4):
            t = qb * 4 + tsub
            for oh in range(2):
                pst = psS[:, oh * 512:(oh + 1) * 512]
                for dc in range(8):
                    p.emit("tensor", lambda e, tsub=tsub, oh=oh, dc=dc, pst=pst: e.matmul(
                        pst, ctxfm[:, dc * 512 + tsub * 128:dc * 512 + (tsub + 1) * 128],
                        wt[:, dc * 1024 + oh * 512:dc * 1024 + (oh + 1) * 512],
                        start=(dc == 0), stop=(dc == 7)),
                        waits=[("vector", p.n("vector")), ("scalar", p.n("scalar"))])
            ob = outb[:, (t % 2) * 1024:(t % 2) * 1024 + 1024]
            p.emit("vector", lambda e, ob=ob: e.tensor_tensor(ob, psS[:, 0:1024], bot[:], op=OP.add),
                   waits=[("tensor", p.n("tensor")), ("gpsimd", max(0, p.n("gpsimd") - 16))])
            p.emit("gpsimd", lambda e, t=t, ob=ob: e.dma_start(
                out=aps["a0"][t * 128:(t + 1) * 128, :], in_=ob),
                waits=[("vector", p.n("vector"))])



def phase_attn1(p, io, env):
    aps = io
    idf = env["idf"]
    idb = env["idb"]
    ones1 = env["ones1"]
    xn = p.sb("xn", [128, 8 * 2048], BF16)
    wt = p.sb("wt", [128, 8 * 1024], BF16)
    wstage = p.sb("wstage", [128, 1024], F32)
    Q = p.sb("Q", [128, 8 * 2048], BF16)
    K = p.sb("K", [128, 8 * 2048], BF16)
    V = p.sb("V", [128, 16 * 1024], BF16)
    attn = p.sb("attn", [128, 2048], BF16)
    attnT = p.sb("attnT", [128, 16 * 512], BF16)
    ctxfm = p.sb("ctxfm", [128, 8 * 512], BF16)
    qkb = p.sb("qkb", [128, 1024], BF16)
    rst = p.sb("rst", [128, 16], F32)
    wnt = p.sb("wnt", [128, 1024], F32)
    st8 = p.sb("st8", [128, 16], F32)
    bqt = p.sb("bqt", [128, 1024], F32)
    bpt = p.sb("bpt", [128, 1024], F32)
    brow = p.sb("brow", [1, 1024], F32)
    wrow = p.sb("wrow", [1, 64], F32)
    outb = p.sb("outb", [128, 2 * 1024], F32)
    qkt = outb[:, 0:1024]
    sq = outb[:, 1024:2048]

    psS = p.ps("psS", [128, 2048], F32)
    psT = p.ps("psT", [128, 2048], BF16)
    psC = p.ps("psC", [128, 512], F32)

    for c in range(8):
        p.emit("sync", lambda e, c=c: e.dma_start(out=xn[:, c * 2048:(c + 1) * 2048],
                                                  in_=aps["xn_bf"][c * 128:(c + 1) * 128, :]))
    emit_bcast_row(p, aps["bproj"], 1024, bpt, brow, ones1, psS)

    def emit_wn_tile(nm):
        p.emit("sync", lambda e, nm=nm: e.dma_start(out=wrow[0:1, :], in_=aps[nm][0:DK].rearrange("(a b) -> a b", a=1)),
               waits=[("tensor", p.n("tensor"))])
        p.emit("tensor", lambda e: e.matmul(psS[:, 0:64], ones1[0:1, :], wrow[0:1, :], start=True, stop=True),
               waits=[("sync", p.n("sync")), ("vector", p.n("vector"))])
        for j in range(16):
            p.emit("vector", lambda e, j=j: e.tensor_copy(
                wnt[:, j * 64:(j + 1) * 64], psS[:, 0:64]),
                waits=[("tensor", p.n("tensor"))])

    for part, dest in ((0, Q), (1, K), (2, V)):
        emit_bcast_row(p, aps["bqkv"][part * D:(part + 1) * D, 0:1024] if False else aps["bqkv"][part * D:(part + 1) * D], 1024,
                       bqt, brow, ones1, psS)
        if part == 0:
            emit_wn_tile("qn_w")
        elif part == 1:
            emit_wn_tile("kn_w")
        emit_weight_T(p, aps["Wqkv"][part * D:(part + 1) * D, :], D, D, wt, wstage, psS, idf)
        for t in range(16):
            for oh in range(2):
                pst = psS[:, oh * 512:oh * 512 + 512]
                for dc in range(8):
                    p.emit("tensor", lambda e, t=t, oh=oh, dc=dc, pst=pst: e.matmul(
                        pst, xn[:, dc * 2048 + t * 128:dc * 2048 + (t + 1) * 128],
                        wt[:, dc * 1024 + oh * 512:dc * 1024 + (oh + 1) * 512],
                        start=(dc == 0), stop=(dc == 7)),
                        waits=[("vector", p.n("vector")), ("scalar", p.n("scalar")), ("sync", p.n("sync"))])
            if part == 2:
                p.emit("vector", lambda e, t=t: e.tensor_tensor(
                    V[:, t * 1024:(t + 1) * 1024], psS[:, 0:1024], bqt[:], op=OP.add),
                    waits=[("tensor", p.n("tensor"))])
            else:
                p.emit("vector", lambda e: e.tensor_tensor(
                    qkt[:], psS[:, 0:1024], bqt[:], op=OP.add),
                    waits=[("tensor", p.n("tensor"))])
                p.emit("vector", lambda e: e.tensor_tensor(sq[:], qkt[:], qkt[:], op=OP.mult))
                p.emit("vector", lambda e: e.tensor_reduce(
                    rst[:, 0:16], sq[:].rearrange("p (g d) -> p g d", d=64), axis=AX.X, op=OP.add))
                eps_r = p.cc(1e-6)
                p.emit("scalar", lambda e, eps_r=eps_r: e.activation(rst[:, 0:16], rst[:, 0:16], AF.Ln, bias=eps_r, scale=1.0 / DK),
                       waits=[("vector", p.n("vector"))])
                p.emit("scalar", lambda e: e.activation(rst[:, 0:16], rst[:, 0:16], AF.Exp, bias=0.0, scale=-0.5))
                for g in range(16):
                    p.emit("vector", lambda e, g=g: e.tensor_scalar(
                        qkt[:, g * 64:(g + 1) * 64], qkt[:, g * 64:(g + 1) * 64], rst[:, g:g + 1], None, op0=OP.mult),
                        waits=([("scalar", p.n("scalar"))] if g == 0 else []))
                sc2 = 0.125 if part == 0 else 1.0
                p.emit("vector", lambda e, sc2=sc2: e.scalar_tensor_tensor(
                    qkb[:], qkt[:], sc2, wnt[:], op0=OP.mult, op1=OP.mult))
                for c in range(8):
                    p.emit("tensor", lambda e, c=c: e.transpose(
                        psT[:, c * 128:(c + 1) * 128], qkb[:, c * 128:(c + 1) * 128], idb[:]),
                        waits=[("vector", p.n("vector"))])
                p.emit("vector", lambda e, t=t, dest=dest: e.tensor_copy(
                    dest[:].rearrange("p (c s) -> p c s", s=S)[:, :, t * 128:(t + 1) * 128],
                    psT[:, 0:1024].rearrange("p (c q) -> p c q", q=128)),
                    waits=[("tensor", p.n("tensor"))])
    emit_weight_T(p, aps["Wproj"], D, D, wt, wstage, psS, idf)

    for qb in range(4):
        for h in range(H):
            for qtl in range(4):
                qt = qb * 4 + qtl
                for kb in range(4):
                    p.emit("tensor", lambda e, h=h, qt=qt, kb=kb: e.matmul(
                        psS[:, kb * 512:(kb + 1) * 512],
                        hsl(Q, h, qt * 128, 128), hsl(K, h, kb * 512, 512),
                        start=True, stop=True),
                        waits=[("vector", p.n("vector")), ("scalar", p.n("scalar"))])
                p.emit("vector", lambda e: e.tensor_reduce(st8[:, 0:1], psS[:], axis=AX.X, op=OP.max, negate=True),
                       waits=[("tensor", p.n("tensor"))])
                p.emit("scalar", lambda e: e.activation(attn[:], psS[:], AF.Exp,
                                                        bias=st8[:, 0:1], scale=1.0, accum_out=st8[:, 1:2]),
                       waits=[("vector", p.n("vector"))])
                p.emit("vector", lambda e: e.reciprocal(st8[:, 2:3], st8[:, 1:2]),
                       waits=[("scalar", p.n("scalar"))])
                p.emit("vector", lambda e: e.tensor_scalar(attn[:], attn[:], st8[:, 2:3], None, op0=OP.mult))
                for c in range(16):
                    p.emit("tensor", lambda e, c=c: e.transpose(
                        psT[:, c * 128:(c + 1) * 128], attn[:, c * 128:(c + 1) * 128], idb[:]),
                        waits=[("vector", p.n("vector"))])
                p.emit("vector", lambda e, qtl=qtl: e.tensor_copy(
                    attnT[:].rearrange("p (c q) -> p c q", q=512)[:, :, qtl * 128:(qtl + 1) * 128],
                    psT[:].rearrange("p (c q) -> p c q", q=128)),
                    waits=[("tensor", p.n("tensor"))])
            for c in range(16):
                p.emit("tensor", lambda e, h=h, c=c: e.matmul(
                    psC[0:64, :], V[:, c * 1024 + h * 64:c * 1024 + (h + 1) * 64],
                    attnT[:, c * 512:(c + 1) * 512], start=(c == 0), stop=(c == 15)),
                    waits=[("vector", p.n("vector"))])
            p.emit("vector", lambda e, h=h: e.tensor_copy(
                ctxfm[(h % 2) * 64:(h % 2) * 64 + 64, (h // 2) * 512:(h // 2 + 1) * 512], psC[0:64, :]),
                waits=[("tensor", p.n("tensor"))])
        for tsub in range(4):
            t = qb * 4 + tsub
            for oh in range(2):
                pst = psS[:, oh * 512:(oh + 1) * 512]
                for dc in range(8):
                    p.emit("tensor", lambda e, tsub=tsub, oh=oh, dc=dc, pst=pst: e.matmul(
                        pst, ctxfm[:, dc * 512 + tsub * 128:dc * 512 + (tsub + 1) * 128],
                        wt[:, dc * 1024 + oh * 512:dc * 1024 + (oh + 1) * 512],
                        start=(dc == 0), stop=(dc == 7)),
                        waits=[("vector", p.n("vector")), ("scalar", p.n("scalar"))])
            ob = outb[:, (t % 2) * 1024:(t % 2) * 1024 + 1024]
            p.emit("vector", lambda e, ob=ob: e.tensor_tensor(ob, psS[:, 0:1024], bpt[:], op=OP.add),
                   waits=[("tensor", p.n("tensor")), ("gpsimd", max(0, p.n("gpsimd") - 16))])
            p.emit("gpsimd", lambda e, t=t, ob=ob: e.dma_start(
                out=aps["a1"][t * 128:(t + 1) * 128, :], in_=ob),
                waits=[("vector", p.n("vector"))])



def phase_moe(p, io, env):
    aps = io
    idf = env["idf"]
    ones1 = env["ones1"]
    xn = p.sb("xn", [128, 8 * 2048], BF16)
    xf = p.sb("xf", [128, 1024], F32)
    rwt = p.sb("rwt", [128, 8 * 16], F32)
    rstage = p.sb("rstage", [128, 128], F32)
    rbt = p.sb("rbt", [128, 16], F32)
    brow = p.sb("brow", [1, 1024], F32)
    logit = p.sb("logit", [128, 16], F32)
    probs = p.sb("probs", [128, 16], F32)
    pwork = p.sb("pwork", [128, 16], F32)
    msk = p.sb("msk", [128, 16], F32)
    st8 = p.sb("st8", [128, 16], F32)
    gates = p.sb("gates", [128, 16 * 16], F32)
    racct = p.sb("racct", [128, 16], F32)
    epst = p.sb("epst", [128, 16], F32)
    we = p.sb("we", [128, 2 * 8 * 1024], BF16)
    moeacc = p.sb("moeacc", [128, 16 * 1024], F32)
    outb = p.sb("outb", [128, 2 * 1024], F32)

    psR = p.ps("psR", [128, 512], F32)
    psE = p.ps("psE", [128, 1024], F32)
    psF = p.ps("psF", [128, 1024], F32)

    for c in range(8):
        p.emit("sync", lambda e, c=c: e.dma_start(out=xn[:, c * 2048:(c + 1) * 2048],
                                                  in_=aps["xn_bf"][c * 128:(c + 1) * 128, :]))
    for c in range(8):
        p.emit("sync", lambda e, c=c: e.dma_start(out=rstage[0:16, :], in_=aps["router_w"][:, c * 128:(c + 1) * 128]),
               waits=[("tensor", p.n("tensor"))])
        p.emit("tensor", lambda e, c=c: e.transpose(psR[:, 0:16], rstage[0:16, 0:128], idf[0:16, 0:16]),
               waits=[("sync", p.n("sync")), ("vector", p.n("vector"))])
        p.emit("vector", lambda e, c=c: e.tensor_copy(rwt[:, c * 16:(c + 1) * 16], psR[:, 0:16]),
               waits=[("tensor", p.n("tensor"))])
    emit_bcast_row(p, aps["router_b"], 16, rbt, brow, ones1, psR)

    for t in range(16):
        for c in range(8):
            p.emit("sync", lambda e, t=t, c=c: e.dma_start(
                out=xf[:, c * 128:(c + 1) * 128],
                in_=aps["xn_f32"][c * 128:(c + 1) * 128, t * 128:(t + 1) * 128]),
                waits=[("tensor", p.n("tensor"))])
        for c in range(8):
            p.emit("tensor", lambda e, c=c: e.matmul(
                psR[:, 0:16], xf[:, c * 128:(c + 1) * 128], rwt[:, c * 16:(c + 1) * 16],
                start=(c == 0), stop=(c == 7)),
                waits=[("sync", p.n("sync")), ("vector", p.n("vector"))])
        p.emit("vector", lambda e: e.tensor_tensor(logit[:], psR[:, 0:16], rbt[:], op=OP.add),
               waits=[("tensor", p.n("tensor"))])
        p.emit("vector", lambda e: e.tensor_reduce(st8[:, 0:1], logit[:], axis=AX.X, op=OP.max, negate=True))
        p.emit("scalar", lambda e: e.activation(probs[:], logit[:], AF.Exp, bias=st8[:, 0:1], scale=1.0, accum_out=st8[:, 1:2]),
               waits=[("vector", p.n("vector"))])
        p.emit("vector", lambda e: e.reciprocal(st8[:, 2:3], st8[:, 1:2]), waits=[("scalar", p.n("scalar"))])
        p.emit("vector", lambda e: e.tensor_scalar(probs[:], probs[:], st8[:, 2:3], None, op0=OP.mult))
        p.emit("vector", lambda e: e.tensor_copy(pwork[:], probs[:]))
        for it in range(6):
            p.emit("vector", lambda e: e.tensor_reduce(st8[:, 3:4], pwork[:], axis=AX.X, op=OP.max))
            if it == 0:
                p.emit("vector", lambda e: e.tensor_copy(st8[:, 4:5], st8[:, 3:4]))
            else:
                p.emit("vector", lambda e: e.tensor_tensor(st8[:, 4:5], st8[:, 4:5], st8[:, 3:4], op=OP.add))
            if it < 5:
                p.emit("vector", lambda e: e.tensor_scalar(msk[:], pwork[:], st8[:, 3:4], None, op0=OP.is_ge))
                p.emit("vector", lambda e: e.scalar_tensor_tensor(pwork[:], msk[:], -1e30, pwork[:], op0=OP.mult, op1=OP.add))
        p.emit("vector", lambda e: e.tensor_scalar(msk[:], probs[:], st8[:, 3:4], None, op0=OP.is_ge))
        p.emit("vector", lambda e: e.tensor_scalar(st8[:, 5:6], st8[:, 4:5], 1e-10, None, op0=OP.add))
        p.emit("vector", lambda e: e.reciprocal(st8[:, 6:7], st8[:, 5:6]))
        p.emit("vector", lambda e, t=t: e.scalar_tensor_tensor(
            gates[:, t * 16:(t + 1) * 16], probs[:], st8[:, 6:7], msk[:], op0=OP.mult, op1=OP.mult))
        p.emit("vector", lambda e, t=t: e.tensor_reduce(st8[:, 7:8], gates[:, t * 16:(t + 1) * 16], axis=AX.X, op=OP.add))
        p.emit("vector", lambda e: e.tensor_scalar(st8[:, 8:9], st8[:, 7:8], 1e-10, None, op0=OP.max))
        p.emit("vector", lambda e, t=t: e.reciprocal(racct[:, t:t + 1], st8[:, 8:9]))
        p.emit("vector", lambda e, t=t: e.tensor_scalar(epst[:, t:t + 1], st8[:, 7:8], 1e-10, None, op0=OP.mult))

    for eix in range(NE):
        wslot = we[:, (eix % 2) * 8192:(eix % 2) * 8192 + 8192]
        for dc in range(8):
            p.emit("gpsimd", lambda e, eix=eix, dc=dc, wslot=wslot: e.dma_start(
                out=wslot[:, dc * 1024:(dc + 1) * 1024],
                in_=aps["expert_w"][eix, dc * 128:(dc + 1) * 128, :]),
                waits=[("tensor", max(0, p.n("tensor") - 256))])
        for t in range(16):
            pst = psE if t % 2 == 0 else psF
            for oh in range(2):
                for dc in range(8):
                    p.emit("tensor", lambda e, t=t, oh=oh, dc=dc, pst=pst, wslot=wslot: e.matmul(
                        pst[:, oh * 512:(oh + 1) * 512],
                        xn[:, dc * 2048 + t * 128:dc * 2048 + (t + 1) * 128],
                        wslot[:, dc * 1024 + oh * 512:dc * 1024 + (oh + 1) * 512],
                        start=(dc == 0), stop=(dc == 7)),
                        waits=[("gpsimd", p.n("gpsimd")), ("vector", p.n("vector"))])
            if eix == 0:
                p.emit("vector", lambda e, t=t, eix=eix, pst=pst: e.tensor_scalar(
                    moeacc[:, t * 1024:(t + 1) * 1024], pst[:], gates[:, t * 16 + eix:t * 16 + eix + 1], None, op0=OP.mult),
                    waits=[("tensor", p.n("tensor"))])
            else:
                p.emit("vector", lambda e, t=t, eix=eix, pst=pst: e.scalar_tensor_tensor(
                    moeacc[:, t * 1024:(t + 1) * 1024], pst[:], gates[:, t * 16 + eix:t * 16 + eix + 1],
                    moeacc[:, t * 1024:(t + 1) * 1024], op0=OP.mult, op1=OP.add),
                    waits=[("tensor", p.n("tensor"))])
    for t in range(16):
        ob = outb[:, (t % 2) * 1024:(t % 2) * 1024 + 1024]
        p.emit("vector", lambda e, t=t, ob=ob: e.tensor_scalar(
            ob, moeacc[:, t * 1024:(t + 1) * 1024], racct[:, t:t + 1], epst[:, t:t + 1], op0=OP.mult, op1=OP.add),
            waits=[("gpsimd", max(0, p.n("gpsimd") - 16))])
        p.emit("gpsimd", lambda e, t=t, ob=ob: e.dma_start(
            out=aps["m1"][t * 128:(t + 1) * 128, :], in_=ob),
            waits=[("vector", p.n("vector"))])



def build_fused(nc, debug=False):
    ins = {
        "x": ((S, D), F32),
        "Wq": ((D, D), F32), "Wk": ((D, D), F32), "Wv": ((D, D), F32), "Wo": ((D, D), F32),
        "b_o": ((D,), F32), "rel_k": ((65, 64), F32), "rel_v": ((65, 64), F32),
        "W1": ((D, INTER), F32), "b1": ((INTER,), F32),
        "W2": ((INTER, D), F32), "b2": ((D,), F32),
        "Wqkv": ((3 * D, D), F32), "bqkv": ((3 * D,), F32),
        "Wproj": ((D, D), F32), "bproj": ((D,), F32),
        "qn_w": ((DK,), F32), "kn_w": ((DK,), F32),
        "router_w": ((NE, D), F32), "router_b": ((NE,), F32),
        "expert_w": ((NE, D, D), F32),
        "ident_f32": ((128, 128), F32), "ident_bf": ((128, 128), BF16),
        "tri_bf": ((128, 128), BF16), "ones_col_bf": ((128, 1), BF16),
        "ones_row_f32": ((1, 128), F32),
    }
    outs = {"out": ((S, D), F32)}
    aps = declare_io(nc, ins, outs)
    for nm, shp, dt in [("xn_d", (D, S), BF16), ("xnf_d", (D, S), F32),
                        ("a_d", (S, D), F32), ("f_d", (S, D), F32),
                        ("x1_d", (S, D), F32), ("a1_d", (S, D), F32),
                        ("m1_d", (S, D), F32),
                        ("pscrP", (128 * 319 + 256,), F32),
                        ("pscrB", (128 * 193 + 256,), BF16)]:
        kind = "ExternalOutput" if (debug and nm not in ("pscrP", "pscrB")) else "Internal"
        aps[nm] = nc.dram_tensor(nm, list(shp), dt, kind=kind).ap()

    p = Prog(nc)
    env = {
        "idf": p.sb("idf", [128, 128], F32),
        "idb": p.sb("idb", [128, 128], BF16),
        "tri": p.sb("tri", [128, 128], BF16),
        "onec": p.sb("onec", [128, 1], BF16),
        "ones1": p.sb("ones1", [1, 128], F32),
    }
    for nm, key in [("idf", "ident_f32"), ("idb", "ident_bf"), ("tri", "tri_bf"),
                    ("onec", "ones_col_bf"), ("ones1", "ones_row_f32")]:
        p.emit("sync", lambda e, nm=nm, key=key: e.dma_start(out=env[nm][:], in_=aps[key]))

    with p.phase("ln0"):
        phase_ln(p, {"x": aps["x"], "xn_bf": aps["xn_d"]}, env, want_f32=False)
    with p.phase("attn0"):
        phase_attn0(p, {"xn_bf": aps["xn_d"], "Wq": aps["Wq"], "Wk": aps["Wk"],
                        "Wv": aps["Wv"], "Wo": aps["Wo"], "b_o": aps["b_o"],
                        "rel_k": aps["rel_k"], "rel_v": aps["rel_v"], "a0": aps["a_d"],
                        "pscrP": aps["pscrP"], "pscrB": aps["pscrB"]}, env)
    with p.phase("ffn0"):
        phase_ffn0(p, {"xn_bf": aps["xn_d"], "W1": aps["W1"], "b1": aps["b1"],
                       "W2": aps["W2"], "b2": aps["b2"], "f0": aps["f_d"]}, env)
    with p.phase("comb0"):
        phase_comb(p, {"x": aps["x"], "a": aps["a_d"], "b": aps["f_d"], "o": aps["x1_d"]})
    with p.phase("ln1"):
        phase_ln(p, {"x": aps["x1_d"], "xn_bf": aps["xn_d"], "xn_f32": aps["xnf_d"]},
                 env, want_f32=True)
    with p.phase("attn1"):
        phase_attn1(p, {"xn_bf": aps["xn_d"], "Wqkv": aps["Wqkv"], "bqkv": aps["bqkv"],
                        "Wproj": aps["Wproj"], "bproj": aps["bproj"], "qn_w": aps["qn_w"],
                        "kn_w": aps["kn_w"], "a1": aps["a1_d"]}, env)
    with p.phase("moe"):
        phase_moe(p, {"xn_bf": aps["xn_d"], "xn_f32": aps["xnf_d"],
                      "router_w": aps["router_w"], "router_b": aps["router_b"],
                      "expert_w": aps["expert_w"], "m1": aps["m1_d"]}, env)
    with p.phase("comb1"):
        phase_comb(p, {"x": aps["x1_d"], "a": aps["a1_d"], "b": aps["m1_d"], "o": aps["out"]})
    p.finalize()
    return nc


def kernel(L0_attn_x, L0_attn_Wq, L0_attn_Wk, L0_attn_Wv, L0_attn_Wo, L0_attn_b_o,
           L0_attn_rel_k_emb, L0_attn_rel_v_emb, L0_ffn_W1, L0_ffn_b1, L0_ffn_W2,
           L0_ffn_b2, L1_attn_Wqkv, L1_attn_bqkv, L1_attn_Wproj, L1_attn_bproj,
           L1_attn_qn_w, L1_attn_kn_w, L1_ffn_router_w, L1_ffn_router_b,
           L1_ffn_expert_w):
    C = CONSTS
    x = np.ascontiguousarray(np.asarray(L0_attn_x, dtype=np.float32))
    shared = {
        "Wq": L0_attn_Wq, "Wk": L0_attn_Wk, "Wv": L0_attn_Wv, "Wo": L0_attn_Wo,
        "b_o": L0_attn_b_o, "rel_k": L0_attn_rel_k_emb, "rel_v": L0_attn_rel_v_emb,
        "W1": L0_ffn_W1, "b1": L0_ffn_b1, "W2": L0_ffn_W2, "b2": L0_ffn_b2,
        "Wqkv": L1_attn_Wqkv, "bqkv": L1_attn_bqkv, "Wproj": L1_attn_Wproj,
        "bproj": L1_attn_bproj, "qn_w": L1_attn_qn_w, "kn_w": L1_attn_kn_w,
        "router_w": L1_ffn_router_w, "router_b": L1_ffn_router_b,
        "expert_w": L1_ffn_expert_w,
    }
    shared = {k: np.ascontiguousarray(np.asarray(v, dtype=np.float32)) for k, v in shared.items()}
    shared.update(C)
    in_maps = []
    for b in range(NCORES):
        d = dict(shared)
        d["x"] = np.ascontiguousarray(x[b])
        in_maps.append(d)
    nc = bass.Bass("TRN2", target_bir_lowering=False)
    build_fused(nc)
    res = run_bass_kernel_spmd(nc, in_maps, core_ids=list(range(NCORES)))
    return np.stack([np.asarray(res.results[i]["out"], dtype=np.float32)
                     for i in range(NCORES)], axis=0)


# version 04
# speedup vs baseline: 1.2644x; 1.2592x over previous
import contextlib
import numpy as np
import ml_dtypes
import concourse.bass as bass
import concourse.mybir as mybir
from concourse.alu_op_type import AluOpType
from concourse.bass_utils import run_bass_kernel_spmd

F32 = mybir.dt.float32
BF16 = mybir.dt.bfloat16
AF = mybir.ActivationFunctionType
AX = mybir.AxisListType
OP = AluOpType

D = 1024
S = 2048
H = 16
DK = 64
NCORES = 8
INTER = 4096
NE = 16

ENGS = ("sync", "gpsimd", "vector", "scalar", "tensor")


class Sem:
    __slots__ = ("h", "step", "n")

    def __init__(self, h, step):
        self.h = h
        self.step = step
        self.n = 0


class Prog:

    def __init__(self, nc):
        self.nc = nc
        self.q = {e: [] for e in ENGS}
        self.stack = contextlib.ExitStack()
        self.scopes = [self.stack]
        self._uid = 0
        self._barrier = {}
        self.sem = {}
        for e in ENGS:
            step = 16 if e in ("sync", "gpsimd") else 1
            self.sem[e] = Sem(self.stack.enter_context(nc.semaphore("S" + e)), step)
        self._cpool = self.stack.enter_context(nc.sbuf_tensor("cpool", [128, 16], F32))
        self._cmap = {}

    def n(self, eng):
        return self.sem[eng].n

    def sb(self, name, shape, dtype=F32):
        self._uid += 1
        return self.scopes[-1].enter_context(
            self.nc.sbuf_tensor(f"{name}_{self._uid}", shape, dtype))

    @contextlib.contextmanager
    def phase(self, name):
        st = contextlib.ExitStack()
        self.scopes.append(st)
        try:
            yield
        finally:
            self.scopes.pop()
            st.close()
            snap = [(e, self.sem[e].n) for e in ENGS]
            for e in ENGS:
                self._barrier[e] = snap

    def cc(self, val):
        if val not in self._cmap:
            col = len(self._cmap)
            self._cmap[val] = col
            self.emit("vector", lambda e, col=col, val=val: e.memset(self._cpool[:, col:col + 1], val))
        c = self._cmap[val]
        return self._cpool[:, c:c + 1]

    def ps(self, name, shape, dtype=F32):
        self._uid += 1
        return self.scopes[-1].enter_context(
            self.nc.psum_tensor(f"{name}_{self._uid}", shape, dtype))

    def emit(self, eng, fn, waits=()):
        sem = self.sem[eng]
        waits = list(waits)
        if eng in self._barrier:
            waits.extend((e2, n) for (e2, n) in self._barrier.pop(eng) if e2 != eng)
        wl = []
        for e2, n in waits:
            s2 = self.sem[e2]
            assert n <= s2.n, f"wait on future value of {e2}: {n} > {s2.n}"
            if n > 0:
                wl.append((s2.h, n))
        byh = {}
        for h, n in wl:
            byh[h] = max(n, byh.get(h, 0))
        wl = tuple(byh.items())

        if eng in ("vector", "scalar"):
            def cl(e, fn=fn, wl=wl, sem=sem):
                for h, n in wl:
                    e.wait_ge(h, n)
                fn(e)
                e.drain().then_inc(sem.h, sem.step)
        else:
            def cl(e, fn=fn, wl=wl, sem=sem):
                for h, n in wl:
                    e.wait_ge(h, n)
                fn(e).then_inc(sem.h, sem.step)

        self.q[eng].append(cl)
        sem.n += sem.step
        return sem.n

    def wother(self, eng):
        return [(e, self.sem[e].n) for e in ENGS if e != eng]

    def finalize(self):
        snap = [(e, self.sem[e].n) for e in ENGS]

        def fin(e, snap=snap):
            for eng, n in snap:
                if n > 0:
                    e.wait_ge(self.sem[eng].h, n)
        self.q["gpsimd"].append(fin)
        with self.nc.Block() as block:
            for eng in ENGS:
                lst = self.q[eng]
                if lst:
                    getattr(block, eng)(lambda e, lst=lst: [c(e) for c in lst])
        self.stack.close()


def np_bf16(a):
    return np.asarray(a).astype(ml_dtypes.bfloat16)


def consts_np():
    return {
        "ident_f32": np.eye(128, dtype=np.float32),
        "ident_bf": np_bf16(np.eye(128)),
        "tri_bf": np_bf16(np.tril(np.ones((128, 128)))),
        "ones_col_bf": np_bf16(np.ones((128, 1))),
        "ones_row_f32": np.ones((1, 128), dtype=np.float32),
    }


CONSTS = consts_np()


def declare_io(nc, ins, outs):
    aps = {}
    for n, (shp, dt) in ins.items():
        aps[n] = nc.dram_tensor(n, list(shp), dt, kind="ExternalInput").ap()
    for n, (shp, dt) in outs.items():
        aps[n] = nc.dram_tensor(n, list(shp), dt, kind="ExternalOutput").ap()
    return aps



def emit_weight_T(p, w_dram, O, I, wt_sb, wstage, psT, idf):
    nIC, nOC = I // 128, O // 128
    nbuf = max(1, wstage.shape[1] // I)
    for oc in range(nOC):
        st = wstage[:, (oc % nbuf) * I:(oc % nbuf) * I + I]
        p.emit("sync", lambda e, oc=oc, st=st: e.dma_start(
            out=st, in_=w_dram[oc * 128:(oc + 1) * 128, :]),
            waits=[("tensor", p.n("tensor"))])
        for ic in range(nIC):
            p.emit("tensor", lambda e, ic=ic, st=st: e.transpose(
                psT[:, ic * 128:(ic + 1) * 128], st[:, ic * 128:(ic + 1) * 128], idf[:]),
                waits=[("sync", p.n("sync")), ("vector", p.n("vector")), ("scalar", p.n("scalar"))])
        p.emit("vector", lambda e, oc=oc: e.tensor_copy(
            wt_sb[:].rearrange("p (ic o) -> p ic o", o=O)[:, :, oc * 128:(oc + 1) * 128],
            psT[:, 0:nIC * 128].rearrange("p (ic q) -> p ic q", q=128)),
            waits=[("tensor", p.n("tensor"))])


def emit_bcast_row(p, vec_dram, n, out_tile, row_sb, ones1, psB):
    p.emit("sync", lambda e, n=n: e.dma_start(
        out=row_sb[0:1, 0:n], in_=vec_dram[0:n].rearrange("(a b) -> a b", a=1)),
        waits=[("tensor", p.n("tensor"))])
    for c in range((n + 511) // 512):
        w = min(512, n - c * 512)
        p.emit("tensor", lambda e, c=c, w=w: e.matmul(
            psB[:, c * 512:c * 512 + w], ones1[0:1, :], row_sb[0:1, c * 512:c * 512 + w],
            start=True, stop=True),
            waits=[("sync", p.n("sync")), ("vector", p.n("vector")), ("scalar", p.n("scalar"))])
    p.emit("vector", lambda e, n=n: e.tensor_copy(out_tile[:, 0:n], psB[:, 0:n]),
           waits=[("tensor", p.n("tensor"))])


def hsl(sb, h, c0, w, rows=64):
    oc = h // 2
    r = (h % 2) * 64
    return sb[r:r + rows, oc * 2048 + c0:oc * 2048 + c0 + w]



def phase_ln(p, io, env, want_f32):
    aps = io
    idf = env["idf"]
    xsb = p.sb("xsb", [128, 16 * 1024], F32)
    sc = p.sb("sc", [128, 1024], F32)
    stats = p.sb("stats", [128, 16], F32)
    xnbf = p.sb("xnbf", [128, 8 * 2048], BF16)
    xnf = p.sb("xnf", [128, 8 * 2048], F32) if want_f32 else None
    psA = p.ps("psA", [128, 1024], F32)
    psB = p.ps("psB", [128, 1024], F32)

    ld0 = p.n("sync")
    for t in range(16):
        p.emit("sync", lambda e, t=t: e.dma_start(
            out=xsb[:, t * 1024:(t + 1) * 1024], in_=aps["x"][t * 128:(t + 1) * 128, :]))

    for t in range(16):
        xt = xsb[:, t * 1024:(t + 1) * 1024]
        p.emit("vector", lambda e, xt=xt: e.tensor_reduce(stats[:, 0:1], xt, axis=AX.X, op=OP.add),
               waits=[("sync", ld0 + 16 * (t + 1))])
        p.emit("vector", lambda e, xt=xt: e.tensor_tensor(sc[:], xt, xt, op=OP.mult))
        p.emit("vector", lambda e: e.tensor_reduce(stats[:, 1:2], sc[:], axis=AX.X, op=OP.add))
        p.emit("vector", lambda e: e.tensor_scalar(stats[:, 2:3], stats[:, 0:1], 1.0 / D, None, op0=OP.mult))
        p.emit("vector", lambda e: e.tensor_tensor(stats[:, 3:4], stats[:, 2:3], stats[:, 2:3], op=OP.mult))
        rD = p.cc(1.0 / D)
        p.emit("vector", lambda e, rD=rD: e.tensor_scalar(stats[:, 4:5], stats[:, 1:2], rD, stats[:, 3:4], op0=OP.mult, op1=OP.subtract))
        eps_ln = p.cc(1e-5)
        p.emit("scalar", lambda e, eps_ln=eps_ln: e.activation(stats[:, 5:6], stats[:, 4:5], AF.Ln, bias=eps_ln, scale=1.0),
               waits=[("vector", p.n("vector"))])
        p.emit("scalar", lambda e: e.activation(stats[:, 6:7], stats[:, 5:6], AF.Exp, bias=0.0, scale=-0.5))
        p.emit("vector", lambda e, xt=xt: e.tensor_scalar(
            sc[:], xt, stats[:, 2:3], stats[:, 6:7], op0=OP.subtract, op1=OP.mult),
            waits=[("scalar", p.n("scalar"))])
        pst = psA if t % 2 == 0 else psB
        for c in range(8):
            p.emit("tensor", lambda e, c=c, pst=pst: e.transpose(
                pst[:, c * 128:(c + 1) * 128], sc[:, c * 128:(c + 1) * 128], idf[:]),
                waits=[("vector", p.n("vector"))])
        p.emit("vector", lambda e, t=t, pst=pst: e.tensor_copy(
            xnbf[:].rearrange("p (c s) -> p c s", s=S)[:, :, t * 128:(t + 1) * 128],
            pst[:].rearrange("p (c q) -> p c q", q=128)),
            waits=[("tensor", p.n("tensor"))])
        if want_f32:
            p.emit("vector", lambda e, t=t, pst=pst: e.tensor_copy(
                xnf[:].rearrange("p (c s) -> p c s", s=S)[:, :, t * 128:(t + 1) * 128],
                pst[:].rearrange("p (c q) -> p c q", q=128)))

    for c in range(8):
        p.emit("gpsimd", lambda e, c=c: e.dma_start(
            out=aps["xn_bf"][c * 128:(c + 1) * 128, :], in_=xnbf[:, c * S:(c + 1) * S]),
            waits=[("vector", p.n("vector"))])
        if want_f32:
            p.emit("gpsimd", lambda e, c=c: e.dma_start(
                out=aps["xn_f32"][c * 128:(c + 1) * 128, :], in_=xnf[:, c * S:(c + 1) * S]))



def phase_comb(p, io):
    aps = io
    xa = p.sb("xa", [128, 2 * 1024], F32)
    xb = p.sb("xb", [128, 2 * 1024], F32)
    xc = p.sb("xc", [128, 2 * 1024], F32)
    xo = p.sb("xo", [128, 2 * 1024], F32)
    for t in range(16):
        b = (t % 2) * 1024
        p.emit("sync", lambda e, t=t, b=b: e.dma_start(out=xa[:, b:b + 1024], in_=aps["x"][t * 128:(t + 1) * 128, :]),
               waits=[("vector", p.n("vector"))])
        p.emit("sync", lambda e, t=t, b=b: e.dma_start(out=xb[:, b:b + 1024], in_=aps["a"][t * 128:(t + 1) * 128, :]))
        p.emit("sync", lambda e, t=t, b=b: e.dma_start(out=xc[:, b:b + 1024], in_=aps["b"][t * 128:(t + 1) * 128, :]))
        p.emit("vector", lambda e, b=b: e.tensor_tensor(xo[:, b:b + 1024], xa[:, b:b + 1024], xb[:, b:b + 1024], op=OP.add),
               waits=[("sync", p.n("sync")), ("gpsimd", max(0, p.n("gpsimd") - 16))])
        p.emit("vector", lambda e, b=b: e.tensor_tensor(xo[:, b:b + 1024], xo[:, b:b + 1024], xc[:, b:b + 1024], op=OP.add))
        p.emit("gpsimd", lambda e, t=t, b=b: e.dma_start(out=aps["o"][t * 128:(t + 1) * 128, :], in_=xo[:, b:b + 1024]),
               waits=[("vector", p.n("vector"))])



def phase_ffn0(p, io, env):
    aps = io
    ones1 = env["ones1"]
    xn = p.sb("xn", [128, 8 * 2048], BF16)
    w1h = p.sb("w1h", [128, 8 * 2048], BF16)
    w2h = p.sb("w2h", [128, 16 * 1024], BF16)
    hbuf = p.sb("hbuf", [128, 16 * 512], BF16)
    b1c = p.sb("b1c", [128, 32], F32)
    acc = p.sb("acc", [128, 16 * 1024], F32)
    b2t = p.sb("b2t", [128, 1024], F32)
    brow = p.sb("brow", [1, 1024], F32)
    outb = p.sb("outb", [128, 2 * 1024], F32)
    psA = p.ps("psA", [128, 512], F32)
    psB = p.ps("psB", [128, 512], F32)
    psC = p.ps("psC", [128, 1024], F32)
    psD = p.ps("psD", [128, 1024], F32)

    for c in range(8):
        p.emit("sync", lambda e, c=c: e.dma_start(out=xn[:, c * 2048:(c + 1) * 2048],
                                                  in_=aps["xn_bf"][c * 128:(c + 1) * 128, :]))
    b1st = p.sb("b1st", [32, 128], F32)
    idf2 = env["idf"]
    p.emit("sync", lambda e: e.dma_start(out=b1st[:], in_=aps["b1"].rearrange("(ic p) -> ic p", p=128)))
    p.emit("tensor", lambda e: e.transpose(psC[:, 0:32], b1st[:, 0:128], idf2[0:32, 0:32]),
           waits=[("sync", p.n("sync")), ("gpsimd", p.n("gpsimd"))])
    p.emit("vector", lambda e: e.tensor_copy(b1c[:], psC[:, 0:32]), waits=[("tensor", p.n("tensor"))])
    emit_bcast_row(p, aps["b2"], 1024, b2t, brow, ones1, psC)

    for ih in range(2):
        for dc in range(8):
            p.emit("gpsimd", lambda e, dc=dc, ih=ih: e.dma_start(
                out=w1h[:, dc * 2048:(dc + 1) * 2048],
                in_=aps["W1"][dc * 128:(dc + 1) * 128, ih * 2048:(ih + 1) * 2048]),
                waits=[("tensor", p.n("tensor"))])
        for ic in range(16):
            p.emit("gpsimd", lambda e, ic=ic, ih=ih: e.dma_start(
                out=w2h[:, ic * 1024:(ic + 1) * 1024],
                in_=aps["W2"][ih * 2048 + ic * 128:ih * 2048 + (ic + 1) * 128, :]),
                waits=[("tensor", p.n("tensor"))])
        for qb in range(4):
            for ic in range(16):
                pst = psA if ic % 2 == 0 else psB
                for dc in range(8):
                    p.emit("tensor", lambda e, ic=ic, dc=dc, qb=qb, pst=pst: e.matmul(
                        pst[:], w1h[:, dc * 2048 + ic * 128:dc * 2048 + (ic + 1) * 128],
                        xn[:, dc * 2048 + qb * 512:dc * 2048 + (qb + 1) * 512],
                        start=(dc == 0), stop=(dc == 7)),
                        waits=[("gpsimd", p.n("gpsimd")), ("sync", p.n("sync")), ("scalar", p.n("scalar"))])
                icg = ih * 16 + ic
                p.emit("scalar", lambda e, ic=ic, icg=icg, pst=pst: e.activation(
                    hbuf[:, ic * 512:(ic + 1) * 512], pst[:], AF.Relu,
                    bias=b1c[:, icg:icg + 1], scale=1.0),
                    waits=[("tensor", p.n("tensor"))])
            for tsub in range(4):
                t = qb * 4 + tsub
                pst = psC if tsub % 2 == 0 else psD
                for ic in range(16):
                    for oc in range(2):
                        p.emit("tensor", lambda e, ic=ic, oc=oc, tsub=tsub, pst=pst: e.matmul(
                            pst[:, oc * 512:(oc + 1) * 512],
                            hbuf[:, ic * 512 + tsub * 128:ic * 512 + (tsub + 1) * 128],
                            w2h[:, ic * 1024 + oc * 512:ic * 1024 + (oc + 1) * 512],
                            start=(ic == 0), stop=(ic == 15)),
                            waits=[("scalar", p.n("scalar")), ("vector", p.n("vector"))])
                if ih == 0:
                    p.emit("vector", lambda e, t=t, pst=pst: e.tensor_copy(
                        acc[:, t * 1024:(t + 1) * 1024], pst[:]), waits=[("tensor", p.n("tensor"))])
                else:
                    p.emit("vector", lambda e, t=t, pst=pst: e.tensor_tensor(
                        acc[:, t * 1024:(t + 1) * 1024], acc[:, t * 1024:(t + 1) * 1024], pst[:], op=OP.add),
                        waits=[("tensor", p.n("tensor"))])
                    ob = outb[:, (t % 2) * 1024:(t % 2) * 1024 + 1024]
                    p.emit("vector", lambda e, ob=ob, t=t: e.tensor_tensor(
                        ob, acc[:, t * 1024:(t + 1) * 1024], b2t[:], op=OP.add),
                        waits=[("gpsimd", max(0, p.n("gpsimd") - 16))])
                    p.emit("gpsimd", lambda e, t=t, ob=ob: e.dma_start(
                        out=aps["f0"][t * 128:(t + 1) * 128, :], in_=ob),
                        waits=[("vector", p.n("vector"))])



def phase_attn0(p, io, env):
    aps = io
    pscrP = io["pscrP"]
    pscrB = io["pscrB"]
    idf = env["idf"]
    idb = env["idb"]
    tri = env["tri"]
    onec = env["onec"]
    ones1 = env["ones1"]
    xn = p.sb("xn", [128, 8 * 2048], BF16)
    wt = p.sb("wt", [128, 8 * 1024], BF16)
    wstage = p.sb("wstage", [128, 2 * 1024], F32)
    Q = p.sb("Q", [128, 8 * 2048], BF16)
    K = p.sb("K", [128, 8 * 2048], BF16)
    V = p.sb("V", [128, 16 * 1024], BF16)
    attn = p.sb("attn", [128, 2112], BF16)
    attnT = p.sb("attnT", [128, 16 * 512], BF16)
    ctxfm = p.sb("ctxfm", [128, 8 * 512], BF16)
    Wwin = p.sb("Wwin", [128, 192], F32)
    QRt = p.sb("QRt", [128, 65], F32)
    Ptab = p.sb("Ptab", [128, 319], F32)
    Dg = p.sb("Dg", [128, 65], BF16)
    Atile = p.sb("Atile", [128, 65], BF16)
    Arhs = p.sb("Arhs", [128, 128], BF16)
    crn = p.sb("crn", [128, 128], BF16)
    st8 = p.sb("st8", [128, 16], F32)
    relkT = p.sb("relkT", [128, 65], BF16)
    relvN = p.sb("relvN", [128, 64], BF16)
    relst = p.sb("relst", [128, 64], F32)
    bot = p.sb("bot", [128, 1024], F32)
    brow = p.sb("brow", [1, 1024], F32)
    outb = p.sb("outb", [128, 2 * 1024], F32)

    psS = p.ps("psS", [128, 2048], F32)
    psT = p.ps("psT", [128, 2048], BF16)
    psC = p.ps("psC", [128, 512], F32)
    psX = p.ps("psX", [128, 512], F32)

    p.emit("sync", lambda e: e.dma_start(out=relst[0:65, :], in_=aps["rel_k"]))
    for c in range(8):
        p.emit("sync", lambda e, c=c: e.dma_start(out=xn[:, c * 2048:(c + 1) * 2048],
                                                  in_=aps["xn_bf"][c * 128:(c + 1) * 128, :]))
    p.emit("gpsimd", lambda e: e.dma_start(out=relvN[0:65, :], in_=aps["rel_v"]))
    p.emit("tensor", lambda e: e.transpose(psX[0:64, 0:65], relst[0:65, 0:64], idf[0:65, 0:65]),
           waits=[("sync", p.n("sync"))])
    p.emit("vector", lambda e: e.tensor_copy(relkT[0:64, 0:65], psX[0:64, 0:65]),
           waits=[("tensor", p.n("tensor"))])
    p.emit("vector", lambda e: e.tensor_copy(relkT[64:128, 0:65], psX[0:64, 0:65]))
    emit_bcast_row(p, aps["b_o"], 1024, bot, brow, ones1, psS)

    def proj_fm(out_sb, scale):
        for oc in range(8):
            for qb in range(4):
                pst = psS[:, (qb % 2) * 512:(qb % 2) * 512 + 512]
                for dc in range(8):
                    p.emit("tensor", lambda e, oc=oc, qb=qb, dc=dc, pst=pst: e.matmul(
                        pst, wt[:, dc * 1024 + oc * 128:dc * 1024 + (oc + 1) * 128],
                        xn[:, dc * 2048 + qb * 512:dc * 2048 + (qb + 1) * 512],
                        start=(dc == 0), stop=(dc == 7)),
                        waits=[("vector", p.n("vector")), ("scalar", p.n("scalar")), ("sync", p.n("sync"))])
                p.emit("scalar", lambda e, oc=oc, qb=qb, pst=pst, scale=scale: e.activation(
                    out_sb[:, oc * 2048 + qb * 512:oc * 2048 + (qb + 1) * 512], pst,
                    AF.Copy, bias=0.0, scale=scale),
                    waits=[("tensor", p.n("tensor"))])

    def proj_tm(out_sb):
        for t in range(16):
            for oh in range(2):
                pst = psS[:, (oh % 2) * 512:(oh % 2) * 512 + 512]
                for dc in range(8):
                    p.emit("tensor", lambda e, t=t, oh=oh, dc=dc, pst=pst: e.matmul(
                        pst, xn[:, dc * 2048 + t * 128:dc * 2048 + (t + 1) * 128],
                        wt[:, dc * 1024 + oh * 512:dc * 1024 + (oh + 1) * 512],
                        start=(dc == 0), stop=(dc == 7)),
                        waits=[("vector", p.n("vector")), ("scalar", p.n("scalar")), ("sync", p.n("sync"))])
                p.emit("scalar", lambda e, t=t, oh=oh, pst=pst: e.activation(
                    out_sb[:, t * 1024 + oh * 512:t * 1024 + (oh + 1) * 512], pst,
                    AF.Copy, bias=0.0, scale=1.0),
                    waits=[("tensor", p.n("tensor"))])

    emit_weight_T(p, aps["Wq"], D, D, wt, wstage, psS, idf)
    proj_fm(Q, 0.125)
    emit_weight_T(p, aps["Wk"], D, D, wt, wstage, psS, idf)
    proj_fm(K, 1.0)
    emit_weight_T(p, aps["Wv"], D, D, wt, wstage, psS, idf)
    proj_tm(V)
    emit_weight_T(p, aps["Wo"], D, D, wt, wstage, psS, idf)

    p.emit("vector", lambda e: e.memset(attn[:, 0:32], 0.0))
    p.emit("vector", lambda e: e.memset(attn[:, 2080:2112], 0.0))

    for qb in range(4):
        for h in range(H):
            for qtl in range(4):
                qt = qb * 4 + qtl
                q0 = qt * 128
                for kb in range(4):
                    p.emit("tensor", lambda e, h=h, qt=qt, kb=kb: e.matmul(
                        psS[:, kb * 512:(kb + 1) * 512],
                        hsl(Q, h, qt * 128, 128), hsl(K, h, kb * 512, 512),
                        start=True, stop=True),
                        waits=[("vector", p.n("vector")), ("scalar", p.n("scalar"))])
                p.emit("tensor", lambda e, h=h, qt=qt: e.matmul(
                    psX[:, 0:65], hsl(Q, h, qt * 128, 128),
                    relkT[(h % 2) * 64:(h % 2) * 64 + 64, 0:65],
                    start=True, stop=True))
                p.emit("vector", lambda e: e.tensor_copy(QRt[:], psX[:, 0:65]),
                       waits=[("tensor", p.n("tensor"))])
                p.emit("vector", lambda e: e.tensor_copy(Ptab[:, 0:127], QRt[:, 0:1].broadcast_to([128, 127])))
                p.emit("vector", lambda e: e.tensor_copy(Ptab[:, 127:192], QRt[:]))
                p.emit("vector", lambda e: e.tensor_copy(Ptab[:, 192:319], QRt[:, 64:65].broadcast_to([128, 127])))
                p.emit("sync", lambda e: e.dma_start(
                    out=pscrP[0:128 * 319].rearrange("(q w) -> q w", w=319), in_=Ptab[:]),
                    waits=[("vector", p.n("vector"))])
                p.emit("sync", lambda e: e.dma_start(
                    out=Wwin[:], in_=pscrP[127:127 + 128 * 318].rearrange("(q w) -> q w", w=318)[:, 0:192]))
                wadd = [("sync", p.n("sync")), ("tensor", p.n("tensor"))]
                if qt == 0:
                    p.emit("vector", lambda e: e.tensor_tensor(psS[:, 0:160], psS[:, 0:160], Wwin[:, 32:192], op=OP.add), waits=wadd)
                elif qt == 15:
                    p.emit("vector", lambda e: e.tensor_tensor(psS[:, 1888:2048], psS[:, 1888:2048], Wwin[:, 0:160], op=OP.add), waits=wadd)
                else:
                    p.emit("vector", lambda e, q0=q0: e.tensor_tensor(psS[:, q0 - 32:q0 + 160], psS[:, q0 - 32:q0 + 160], Wwin[:, 0:192], op=OP.add), waits=wadd)
                if qt > 0:
                    p.emit("vector", lambda e, q0=q0: e.tensor_scalar(psS[:, 0:q0 - 32], psS[:, 0:q0 - 32], QRt[:, 0:1], None, op0=OP.add))
                if qt < 15:
                    p.emit("vector", lambda e, q0=q0: e.tensor_scalar(psS[:, q0 + 160:2048], psS[:, q0 + 160:2048], QRt[:, 64:65], None, op0=OP.add))
                p.emit("vector", lambda e: e.tensor_reduce(st8[:, 0:1], psS[:], axis=AX.X, op=OP.max, negate=True))
                p.emit("scalar", lambda e: e.activation(attn[:, 32:2080], psS[:], AF.Exp,
                                                        bias=st8[:, 0:1], scale=1.0, accum_out=st8[:, 1:2]),
                       waits=[("vector", p.n("vector"))])
                p.emit("vector", lambda e: e.reciprocal(st8[:, 2:3], st8[:, 1:2]),
                       waits=[("scalar", p.n("scalar"))])
                p.emit("vector", lambda e: e.tensor_scalar(attn[:, 32:2080], attn[:, 32:2080], st8[:, 2:3], None, op0=OP.mult))
                p.emit("vector", lambda e, q0=q0: e.tensor_tensor(crn[:], attn[:, q0:q0 + 128], tri[:], op=OP.mult))
                p.emit("vector", lambda e: e.tensor_reduce(st8[:, 3:4], crn[:], axis=AX.X, op=OP.add))
                p.emit("sync", lambda e, q0=q0: e.dma_start(
                    out=pscrB[0:128 * 192].rearrange("(q w) -> q w", w=192), in_=attn[:, q0:q0 + 192]),
                    waits=[("vector", p.n("vector"))])
                p.emit("sync", lambda e: e.dma_start(
                    out=Dg[:], in_=pscrB[0:128 * 193].rearrange("(q w) -> q w", w=193)[:, 0:65]))
                for c in range(16):
                    p.emit("tensor", lambda e, c=c: e.transpose(
                        psT[:, c * 128:(c + 1) * 128], attn[:, 32 + c * 128:32 + (c + 1) * 128], idb[:]),
                        waits=[("vector", p.n("vector"))])
                p.emit("vector", lambda e, qtl=qtl: e.tensor_copy(
                    attnT[:].rearrange("p (c q) -> p c q", q=512)[:, :, qtl * 128:(qtl + 1) * 128],
                    psT[:].rearrange("p (c q) -> p c q", q=128)),
                    waits=[("tensor", p.n("tensor"))])
                nfull = qt - 1
                for c in range(max(0, nfull)):
                    p.emit("tensor", lambda e, c=c, qtl=qtl: e.matmul(
                        psX[:, 128:129], attnT[:, c * 512 + qtl * 128:c * 512 + (qtl + 1) * 128],
                        onec[:], start=(c == 0), stop=False),
                        waits=[("vector", p.n("vector"))])
                if qt >= 1:
                    p.emit("tensor", lambda e, qt=qt, qtl=qtl, st=(nfull <= 0): e.matmul(
                        psX[:, 128:129],
                        attnT[0:96, (qt - 1) * 512 + qtl * 128:(qt - 1) * 512 + (qtl + 1) * 128],
                        onec[0:96, :], start=st, stop=True),
                        waits=[("vector", p.n("vector"))])
                aw = [("tensor", p.n("tensor")), ("sync", p.n("sync"))]
                if qt >= 1:
                    p.emit("vector", lambda e: e.tensor_tensor(st8[:, 4:5], psX[:, 128:129], st8[:, 3:4], op=OP.add), waits=aw)
                else:
                    p.emit("vector", lambda e: e.tensor_copy(st8[:, 4:5], st8[:, 3:4]), waits=aw)
                p.emit("vector", lambda e: e.tensor_reduce(st8[:, 5:6], Dg[:, 1:64], axis=AX.X, op=OP.add))
                p.emit("vector", lambda e: e.tensor_tensor(st8[:, 6:7], st8[:, 4:5], st8[:, 5:6], op=OP.add))
                p.emit("vector", lambda e: e.tensor_scalar(st8[:, 7:8], st8[:, 6:7], -1.0, 1.0, op0=OP.mult, op1=OP.add))
                p.emit("vector", lambda e: e.tensor_copy(Atile[:, 1:64], Dg[:, 1:64]))
                p.emit("vector", lambda e: e.tensor_copy(Atile[:, 0:1], st8[:, 4:5]))
                p.emit("vector", lambda e: e.tensor_copy(Atile[:, 64:65], st8[:, 7:8]))
                p.emit("tensor", lambda e: e.transpose(psT[0:65, 0:128], Atile[:], idb[:]),
                       waits=[("vector", p.n("vector"))])
                p.emit("vector", lambda e: e.tensor_copy(Arhs[0:65, 0:128], psT[0:65, 0:128]),
                       waits=[("tensor", p.n("tensor"))])
                p.emit("tensor", lambda e, qtl=qtl: e.matmul(
                    psC[0:64, qtl * 128:(qtl + 1) * 128], relvN[0:65, 0:64], Arhs[0:65, 0:128],
                    start=(qtl == 0), stop=False),
                    waits=[("vector", p.n("vector"))])
            for c in range(16):
                p.emit("tensor", lambda e, h=h, c=c: e.matmul(
                    psC[0:64, :], V[:, c * 1024 + h * 64:c * 1024 + (h + 1) * 64],
                    attnT[:, c * 512:(c + 1) * 512], start=False, stop=(c == 15)),
                    waits=[("vector", p.n("vector"))])
            p.emit("vector", lambda e, h=h: e.tensor_copy(
                ctxfm[(h % 2) * 64:(h % 2) * 64 + 64, (h // 2) * 512:(h // 2 + 1) * 512], psC[0:64, :]),
                waits=[("tensor", p.n("tensor"))])
        for tsub in range(4):
            t = qb * 4 + tsub
            for oh in range(2):
                pst = psS[:, oh * 512:(oh + 1) * 512]
                for dc in range(8):
                    p.emit("tensor", lambda e, tsub=tsub, oh=oh, dc=dc, pst=pst: e.matmul(
                        pst, ctxfm[:, dc * 512 + tsub * 128:dc * 512 + (tsub + 1) * 128],
                        wt[:, dc * 1024 + oh * 512:dc * 1024 + (oh + 1) * 512],
                        start=(dc == 0), stop=(dc == 7)),
                        waits=[("vector", p.n("vector")), ("scalar", p.n("scalar"))])
            ob = outb[:, (t % 2) * 1024:(t % 2) * 1024 + 1024]
            p.emit("vector", lambda e, ob=ob: e.tensor_tensor(ob, psS[:, 0:1024], bot[:], op=OP.add),
                   waits=[("tensor", p.n("tensor")), ("gpsimd", max(0, p.n("gpsimd") - 16))])
            p.emit("gpsimd", lambda e, t=t, ob=ob: e.dma_start(
                out=aps["a0"][t * 128:(t + 1) * 128, :], in_=ob),
                waits=[("vector", p.n("vector"))])



def phase_attn1(p, io, env):
    aps = io
    idf = env["idf"]
    idb = env["idb"]
    ones1 = env["ones1"]
    xn = p.sb("xn", [128, 8 * 2048], BF16)
    wt = p.sb("wt", [128, 8 * 1024], BF16)
    wstage = p.sb("wstage", [128, 1024], F32)
    Q = p.sb("Q", [128, 8 * 2048], BF16)
    K = p.sb("K", [128, 8 * 2048], BF16)
    V = p.sb("V", [128, 16 * 1024], BF16)
    attn = p.sb("attn", [128, 2048], BF16)
    attnT = p.sb("attnT", [128, 16 * 512], BF16)
    ctxfm = p.sb("ctxfm", [128, 8 * 512], BF16)
    qkb = p.sb("qkb", [128, 1024], BF16)
    rst = p.sb("rst", [128, 16], F32)
    wnt = p.sb("wnt", [128, 1024], F32)
    st8 = p.sb("st8", [128, 16], F32)
    bqt = p.sb("bqt", [128, 1024], F32)
    bpt = p.sb("bpt", [128, 1024], F32)
    brow = p.sb("brow", [1, 1024], F32)
    wrow = p.sb("wrow", [1, 64], F32)
    outb = p.sb("outb", [128, 2 * 1024], F32)
    qkt = outb[:, 0:1024]
    sq = outb[:, 1024:2048]

    psS = p.ps("psS", [128, 2048], F32)
    psT = p.ps("psT", [128, 2048], BF16)
    psC = p.ps("psC", [128, 512], F32)

    for c in range(8):
        p.emit("sync", lambda e, c=c: e.dma_start(out=xn[:, c * 2048:(c + 1) * 2048],
                                                  in_=aps["xn_bf"][c * 128:(c + 1) * 128, :]))
    emit_bcast_row(p, aps["bproj"], 1024, bpt, brow, ones1, psS)

    def emit_wn_tile(nm):
        p.emit("sync", lambda e, nm=nm: e.dma_start(out=wrow[0:1, :], in_=aps[nm][0:DK].rearrange("(a b) -> a b", a=1)),
               waits=[("tensor", p.n("tensor"))])
        p.emit("tensor", lambda e: e.matmul(psS[:, 0:64], ones1[0:1, :], wrow[0:1, :], start=True, stop=True),
               waits=[("sync", p.n("sync")), ("vector", p.n("vector"))])
        for j in range(16):
            p.emit("vector", lambda e, j=j: e.tensor_copy(
                wnt[:, j * 64:(j + 1) * 64], psS[:, 0:64]),
                waits=[("tensor", p.n("tensor"))])

    for part, dest in ((0, Q), (1, K), (2, V)):
        emit_bcast_row(p, aps["bqkv"][part * D:(part + 1) * D, 0:1024] if False else aps["bqkv"][part * D:(part + 1) * D], 1024,
                       bqt, brow, ones1, psS)
        if part == 0:
            emit_wn_tile("qn_w")
        elif part == 1:
            emit_wn_tile("kn_w")
        emit_weight_T(p, aps["Wqkv"][part * D:(part + 1) * D, :], D, D, wt, wstage, psS, idf)
        for t in range(16):
            for oh in range(2):
                pst = psS[:, oh * 512:oh * 512 + 512]
                for dc in range(8):
                    p.emit("tensor", lambda e, t=t, oh=oh, dc=dc, pst=pst: e.matmul(
                        pst, xn[:, dc * 2048 + t * 128:dc * 2048 + (t + 1) * 128],
                        wt[:, dc * 1024 + oh * 512:dc * 1024 + (oh + 1) * 512],
                        start=(dc == 0), stop=(dc == 7)),
                        waits=[("vector", p.n("vector")), ("scalar", p.n("scalar")), ("sync", p.n("sync"))])
            if part == 2:
                p.emit("vector", lambda e, t=t: e.tensor_tensor(
                    V[:, t * 1024:(t + 1) * 1024], psS[:, 0:1024], bqt[:], op=OP.add),
                    waits=[("tensor", p.n("tensor"))])
            else:
                p.emit("vector", lambda e: e.tensor_tensor(
                    qkt[:], psS[:, 0:1024], bqt[:], op=OP.add),
                    waits=[("tensor", p.n("tensor"))])
                p.emit("vector", lambda e: e.tensor_tensor(sq[:], qkt[:], qkt[:], op=OP.mult))
                p.emit("vector", lambda e: e.tensor_reduce(
                    rst[:, 0:16], sq[:].rearrange("p (g d) -> p g d", d=64), axis=AX.X, op=OP.add))
                eps_r = p.cc(1e-6)
                p.emit("scalar", lambda e, eps_r=eps_r: e.activation(rst[:, 0:16], rst[:, 0:16], AF.Ln, bias=eps_r, scale=1.0 / DK),
                       waits=[("vector", p.n("vector"))])
                p.emit("scalar", lambda e: e.activation(rst[:, 0:16], rst[:, 0:16], AF.Exp, bias=0.0, scale=-0.5))
                for g in range(16):
                    p.emit("vector", lambda e, g=g: e.tensor_scalar(
                        qkt[:, g * 64:(g + 1) * 64], qkt[:, g * 64:(g + 1) * 64], rst[:, g:g + 1], None, op0=OP.mult),
                        waits=([("scalar", p.n("scalar"))] if g == 0 else []))
                sc2 = 0.125 if part == 0 else 1.0
                p.emit("vector", lambda e, sc2=sc2: e.scalar_tensor_tensor(
                    qkb[:], qkt[:], sc2, wnt[:], op0=OP.mult, op1=OP.mult))
                for c in range(8):
                    p.emit("tensor", lambda e, c=c: e.transpose(
                        psT[:, c * 128:(c + 1) * 128], qkb[:, c * 128:(c + 1) * 128], idb[:]),
                        waits=[("vector", p.n("vector"))])
                p.emit("vector", lambda e, t=t, dest=dest: e.tensor_copy(
                    dest[:].rearrange("p (c s) -> p c s", s=S)[:, :, t * 128:(t + 1) * 128],
                    psT[:, 0:1024].rearrange("p (c q) -> p c q", q=128)),
                    waits=[("tensor", p.n("tensor"))])
    emit_weight_T(p, aps["Wproj"], D, D, wt, wstage, psS, idf)

    for qb in range(4):
        for h in range(H):
            for qtl in range(4):
                qt = qb * 4 + qtl
                for kb in range(4):
                    p.emit("tensor", lambda e, h=h, qt=qt, kb=kb: e.matmul(
                        psS[:, kb * 512:(kb + 1) * 512],
                        hsl(Q, h, qt * 128, 128), hsl(K, h, kb * 512, 512),
                        start=True, stop=True),
                        waits=[("vector", p.n("vector")), ("scalar", p.n("scalar"))])
                p.emit("vector", lambda e: e.tensor_reduce(st8[:, 0:1], psS[:], axis=AX.X, op=OP.max, negate=True),
                       waits=[("tensor", p.n("tensor"))])
                p.emit("scalar", lambda e: e.activation(attn[:], psS[:], AF.Exp,
                                                        bias=st8[:, 0:1], scale=1.0, accum_out=st8[:, 1:2]),
                       waits=[("vector", p.n("vector"))])
                p.emit("vector", lambda e: e.reciprocal(st8[:, 2:3], st8[:, 1:2]),
                       waits=[("scalar", p.n("scalar"))])
                p.emit("vector", lambda e: e.tensor_scalar(attn[:], attn[:], st8[:, 2:3], None, op0=OP.mult))
                for c in range(16):
                    p.emit("tensor", lambda e, c=c: e.transpose(
                        psT[:, c * 128:(c + 1) * 128], attn[:, c * 128:(c + 1) * 128], idb[:]),
                        waits=[("vector", p.n("vector"))])
                p.emit("vector", lambda e, qtl=qtl: e.tensor_copy(
                    attnT[:].rearrange("p (c q) -> p c q", q=512)[:, :, qtl * 128:(qtl + 1) * 128],
                    psT[:].rearrange("p (c q) -> p c q", q=128)),
                    waits=[("tensor", p.n("tensor"))])
            for c in range(16):
                p.emit("tensor", lambda e, h=h, c=c: e.matmul(
                    psC[0:64, :], V[:, c * 1024 + h * 64:c * 1024 + (h + 1) * 64],
                    attnT[:, c * 512:(c + 1) * 512], start=(c == 0), stop=(c == 15)),
                    waits=[("vector", p.n("vector"))])
            p.emit("vector", lambda e, h=h: e.tensor_copy(
                ctxfm[(h % 2) * 64:(h % 2) * 64 + 64, (h // 2) * 512:(h // 2 + 1) * 512], psC[0:64, :]),
                waits=[("tensor", p.n("tensor"))])
        for tsub in range(4):
            t = qb * 4 + tsub
            for oh in range(2):
                pst = psS[:, oh * 512:(oh + 1) * 512]
                for dc in range(8):
                    p.emit("tensor", lambda e, tsub=tsub, oh=oh, dc=dc, pst=pst: e.matmul(
                        pst, ctxfm[:, dc * 512 + tsub * 128:dc * 512 + (tsub + 1) * 128],
                        wt[:, dc * 1024 + oh * 512:dc * 1024 + (oh + 1) * 512],
                        start=(dc == 0), stop=(dc == 7)),
                        waits=[("vector", p.n("vector")), ("scalar", p.n("scalar"))])
            ob = outb[:, (t % 2) * 1024:(t % 2) * 1024 + 1024]
            p.emit("vector", lambda e, ob=ob: e.tensor_tensor(ob, psS[:, 0:1024], bpt[:], op=OP.add),
                   waits=[("tensor", p.n("tensor")), ("gpsimd", max(0, p.n("gpsimd") - 16))])
            p.emit("gpsimd", lambda e, t=t, ob=ob: e.dma_start(
                out=aps["a1"][t * 128:(t + 1) * 128, :], in_=ob),
                waits=[("vector", p.n("vector"))])



def phase_moe(p, io, env):
    aps = io
    idf = env["idf"]
    ones1 = env["ones1"]
    xn = p.sb("xn", [128, 8 * 2048], BF16)
    xf = p.sb("xf", [128, 1024], F32)
    rwt = p.sb("rwt", [128, 8 * 16], F32)
    rstage = p.sb("rstage", [128, 128], F32)
    rbt = p.sb("rbt", [128, 16], F32)
    brow = p.sb("brow", [1, 1024], F32)
    logit = p.sb("logit", [128, 16], F32)
    probs = p.sb("probs", [128, 16], F32)
    pwork = p.sb("pwork", [128, 16], F32)
    msk = p.sb("msk", [128, 16], F32)
    st8 = p.sb("st8", [128, 16], F32)
    gates = p.sb("gates", [128, 16 * 16], F32)
    racct = p.sb("racct", [128, 16], F32)
    epst = p.sb("epst", [128, 16], F32)
    we = p.sb("we", [128, 2 * 8 * 1024], BF16)
    moeacc = p.sb("moeacc", [128, 16 * 1024], F32)
    outb = p.sb("outb", [128, 2 * 1024], F32)

    psR = p.ps("psR", [128, 512], F32)
    psE = p.ps("psE", [128, 1024], F32)
    psF = p.ps("psF", [128, 1024], F32)

    for c in range(8):
        p.emit("sync", lambda e, c=c: e.dma_start(out=xn[:, c * 2048:(c + 1) * 2048],
                                                  in_=aps["xn_bf"][c * 128:(c + 1) * 128, :]))
    for c in range(8):
        p.emit("sync", lambda e, c=c: e.dma_start(out=rstage[0:16, :], in_=aps["router_w"][:, c * 128:(c + 1) * 128]),
               waits=[("tensor", p.n("tensor"))])
        p.emit("tensor", lambda e, c=c: e.transpose(psR[:, 0:16], rstage[0:16, 0:128], idf[0:16, 0:16]),
               waits=[("sync", p.n("sync")), ("vector", p.n("vector"))])
        p.emit("vector", lambda e, c=c: e.tensor_copy(rwt[:, c * 16:(c + 1) * 16], psR[:, 0:16]),
               waits=[("tensor", p.n("tensor"))])
    emit_bcast_row(p, aps["router_b"], 16, rbt, brow, ones1, psR)

    for t in range(16):
        for c in range(8):
            p.emit("sync", lambda e, t=t, c=c: e.dma_start(
                out=xf[:, c * 128:(c + 1) * 128],
                in_=aps["xn_f32"][c * 128:(c + 1) * 128, t * 128:(t + 1) * 128]),
                waits=[("tensor", p.n("tensor"))])
        for c in range(8):
            p.emit("tensor", lambda e, c=c: e.matmul(
                psR[:, 0:16], xf[:, c * 128:(c + 1) * 128], rwt[:, c * 16:(c + 1) * 16],
                start=(c == 0), stop=(c == 7)),
                waits=[("sync", p.n("sync")), ("vector", p.n("vector"))])
        p.emit("vector", lambda e: e.tensor_tensor(logit[:], psR[:, 0:16], rbt[:], op=OP.add),
               waits=[("tensor", p.n("tensor"))])
        p.emit("vector", lambda e: e.tensor_reduce(st8[:, 0:1], logit[:], axis=AX.X, op=OP.max, negate=True))
        p.emit("scalar", lambda e: e.activation(probs[:], logit[:], AF.Exp, bias=st8[:, 0:1], scale=1.0, accum_out=st8[:, 1:2]),
               waits=[("vector", p.n("vector"))])
        p.emit("vector", lambda e: e.reciprocal(st8[:, 2:3], st8[:, 1:2]), waits=[("scalar", p.n("scalar"))])
        p.emit("vector", lambda e: e.tensor_scalar(probs[:], probs[:], st8[:, 2:3], None, op0=OP.mult))
        p.emit("vector", lambda e: e.tensor_copy(pwork[:], probs[:]))
        for it in range(6):
            p.emit("vector", lambda e: e.tensor_reduce(st8[:, 3:4], pwork[:], axis=AX.X, op=OP.max))
            if it == 0:
                p.emit("vector", lambda e: e.tensor_copy(st8[:, 4:5], st8[:, 3:4]))
            else:
                p.emit("vector", lambda e: e.tensor_tensor(st8[:, 4:5], st8[:, 4:5], st8[:, 3:4], op=OP.add))
            if it < 5:
                p.emit("vector", lambda e: e.tensor_scalar(msk[:], pwork[:], st8[:, 3:4], None, op0=OP.is_ge))
                p.emit("vector", lambda e: e.scalar_tensor_tensor(pwork[:], msk[:], -1e30, pwork[:], op0=OP.mult, op1=OP.add))
        p.emit("vector", lambda e: e.tensor_scalar(msk[:], probs[:], st8[:, 3:4], None, op0=OP.is_ge))
        p.emit("vector", lambda e: e.tensor_scalar(st8[:, 5:6], st8[:, 4:5], 1e-10, None, op0=OP.add))
        p.emit("vector", lambda e: e.reciprocal(st8[:, 6:7], st8[:, 5:6]))
        p.emit("vector", lambda e, t=t: e.scalar_tensor_tensor(
            gates[:, t * 16:(t + 1) * 16], probs[:], st8[:, 6:7], msk[:], op0=OP.mult, op1=OP.mult))
        p.emit("vector", lambda e, t=t: e.tensor_reduce(st8[:, 7:8], gates[:, t * 16:(t + 1) * 16], axis=AX.X, op=OP.add))
        p.emit("vector", lambda e: e.tensor_scalar(st8[:, 8:9], st8[:, 7:8], 1e-10, None, op0=OP.max))
        p.emit("vector", lambda e, t=t: e.reciprocal(racct[:, t:t + 1], st8[:, 8:9]))
        p.emit("vector", lambda e, t=t: e.tensor_scalar(epst[:, t:t + 1], st8[:, 7:8], 1e-10, None, op0=OP.mult))

    for eix in range(NE):
        wslot = we[:, (eix % 2) * 8192:(eix % 2) * 8192 + 8192]
        for dc in range(8):
            p.emit("gpsimd", lambda e, eix=eix, dc=dc, wslot=wslot: e.dma_start(
                out=wslot[:, dc * 1024:(dc + 1) * 1024],
                in_=aps["expert_w"][eix, dc * 128:(dc + 1) * 128, :]),
                waits=[("tensor", max(0, p.n("tensor") - 256))])
        for t in range(16):
            pst = psE if t % 2 == 0 else psF
            for oh in range(2):
                for dc in range(8):
                    p.emit("tensor", lambda e, t=t, oh=oh, dc=dc, pst=pst, wslot=wslot: e.matmul(
                        pst[:, oh * 512:(oh + 1) * 512],
                        xn[:, dc * 2048 + t * 128:dc * 2048 + (t + 1) * 128],
                        wslot[:, dc * 1024 + oh * 512:dc * 1024 + (oh + 1) * 512],
                        start=(dc == 0), stop=(dc == 7)),
                        waits=[("gpsimd", p.n("gpsimd")), ("vector", p.n("vector"))])
            if eix == 0:
                p.emit("vector", lambda e, t=t, eix=eix, pst=pst: e.tensor_scalar(
                    moeacc[:, t * 1024:(t + 1) * 1024], pst[:], gates[:, t * 16 + eix:t * 16 + eix + 1], None, op0=OP.mult),
                    waits=[("tensor", p.n("tensor"))])
            else:
                p.emit("vector", lambda e, t=t, eix=eix, pst=pst: e.scalar_tensor_tensor(
                    moeacc[:, t * 1024:(t + 1) * 1024], pst[:], gates[:, t * 16 + eix:t * 16 + eix + 1],
                    moeacc[:, t * 1024:(t + 1) * 1024], op0=OP.mult, op1=OP.add),
                    waits=[("tensor", p.n("tensor"))])
    for t in range(16):
        ob = outb[:, (t % 2) * 1024:(t % 2) * 1024 + 1024]
        p.emit("vector", lambda e, t=t, ob=ob: e.tensor_scalar(
            ob, moeacc[:, t * 1024:(t + 1) * 1024], racct[:, t:t + 1], epst[:, t:t + 1], op0=OP.mult, op1=OP.add),
            waits=[("gpsimd", max(0, p.n("gpsimd") - 16))])
        p.emit("gpsimd", lambda e, t=t, ob=ob: e.dma_start(
            out=aps["m1"][t * 128:(t + 1) * 128, :], in_=ob),
            waits=[("vector", p.n("vector"))])



def _proj_common(p, wt, xn, psS, out_sb, scale, to_tm, tok):
    ready = [("vector", p.n("vector")), ("sync", p.n("sync"))]
    if not to_tm:
        for oc in range(8):
            for qb in range(4):
                bank = qb % 2
                pst = psS[:, bank * 512:bank * 512 + 512]
                for dc in range(8):
                    w = ([("scalar", tok["se_bank"][bank])] + (ready if (oc == 0 and qb == 0) else [])) if dc == 0 else []
                    p.emit("tensor", lambda e, oc=oc, qb=qb, dc=dc, pst=pst: e.matmul(
                        pst, wt[:, dc * 1024 + oc * 128:dc * 1024 + (oc + 1) * 128],
                        xn[:, dc * 2048 + qb * 512:dc * 2048 + (qb + 1) * 512],
                        start=(dc == 0), stop=(dc == 7)), waits=w)
                t_mm = p.n("tensor")
                tok["se_bank"][bank] = p.emit("scalar", lambda e, oc=oc, qb=qb, pst=pst, scale=scale: e.activation(
                    out_sb[:, oc * 2048 + qb * 512:oc * 2048 + (qb + 1) * 512], pst,
                    AF.Copy, bias=0.0, scale=scale), waits=[("tensor", t_mm)])
    else:
        for t in range(16):
            for oh in range(2):
                bank = oh % 2
                pst = psS[:, bank * 512:bank * 512 + 512]
                for dc in range(8):
                    w = ([("scalar", tok["se_bank"][bank])] + (ready if (t == 0 and oh == 0) else [])) if dc == 0 else []
                    p.emit("tensor", lambda e, t=t, oh=oh, dc=dc, pst=pst: e.matmul(
                        pst, xn[:, dc * 2048 + t * 128:dc * 2048 + (t + 1) * 128],
                        wt[:, dc * 1024 + oh * 512:dc * 1024 + (oh + 1) * 512],
                        start=(dc == 0), stop=(dc == 7)), waits=w)
                t_mm = p.n("tensor")
                tok["se_bank"][bank] = p.emit("scalar", lambda e, t=t, oh=oh, pst=pst: e.activation(
                    out_sb[:, t * 1024 + oh * 512:t * 1024 + (oh + 1) * 512], pst,
                    AF.Copy, bias=0.0, scale=1.0), waits=[("tensor", t_mm)])


def _kmax_chain(p, Ksb, Ksq, psS, onec, Krow, col, h, tok):
    r = (h % 2) * 64
    oc = h // 2
    t_sq = p.emit("scalar", lambda e: e.activation(
        Ksq[r:r + 64, :], Ksb[r:r + 64, oc * 2048:(oc + 1) * 2048], AF.Square, bias=0.0, scale=1.0),
        waits=[("scalar", tok.get("k_ready", 0))] if False else [])
    for half in range(2):
        for kb in range(2):
            p.emit("tensor", lambda e, half=half, kb=kb: e.matmul(
                psS[0:1, 1024 + kb * 512:1024 + (kb + 1) * 512],
                onec[r:r + 64, :].bitcast(BF16) if False else onec[r:r + 64, :],
                Ksq[r:r + 64, half * 1024 + kb * 512:half * 1024 + (kb + 1) * 512],
                start=True, stop=True),
                waits=[("scalar", t_sq), ("vector", tok.get("kred", 0))])
        t_mm = p.n("tensor")
        tok["kred"] = p.emit("vector", lambda e, half=half, col=col: e.tensor_reduce(
            Krow[0:1, 16 + half:17 + half], psS[0:1, 1024:2048], axis=AX.X, op=OP.max),
            waits=[("tensor", t_mm)])
    tok["kred"] = p.emit("vector", lambda e, col=col: e.tensor_tensor(
        Krow[0:1, col:col + 1], Krow[0:1, 16:17], Krow[0:1, 17:18], op=OP.max))


def phase_attn0_v2(p, io, env):
    aps = io
    idf = env["idf"]
    idb = env["idb"]
    tri = env["tri"]
    onec = env["onec"]
    ones1 = env["ones1"]

    xn = p.sb("xn", [128, 8 * 2048], BF16)
    wt = p.sb("wt", [128, 8 * 1024], BF16)
    wstage = p.sb("wstage", [128, 1024], F32)
    Q = p.sb("Q", [128, 8 * 2048], BF16)
    K = p.sb("K", [128, 8 * 2048], BF16)
    V = p.sb("V", [128, 16 * 1024], BF16)
    attn = p.sb("attn", [128, 2112], BF16)
    attnT = p.sb("attnT", [128, 16 * 512], BF16)
    ctxfm = p.sb("ctxfm", [128, 8 * 512], BF16)
    Wwin4 = p.sb("Wwin4", [128, 4 * 192], F32)
    QRt4 = p.sb("QRt4", [128, 4 * 65], F32)
    Ptab4 = p.sb("Ptab4", [128, 4 * 319], F32)
    Dg4 = p.sb("Dg4", [128, 4 * 65], BF16)
    Atile4 = p.sb("Atile4", [128, 4 * 65], BF16)
    Arhs4 = p.sb("Arhs4", [128, 512], BF16)
    crn = p.sb("crn", [128, 128], BF16)
    st8 = p.sb("st8", [128, 48], F32)
    Qsq = p.sb("Qsq", [128, 512], BF16)
    KRcols = p.sb("KRcols", [128, 16], F32)
    Krow = p.sb("Krow", [1, 32], F32)
    relkT = p.sb("relkT", [128, 65], BF16)
    relvN = p.sb("relvN", [128, 64], BF16)
    relst = p.sb("relst", [128, 65], F32)
    rsrow = p.sb("rsrow", [4, 128], F32)
    rsrow1 = p.sb("rsrow1", [1, 512], F32)
    rsb = p.sb("rsb", [128, 512], F32)
    bot = p.sb("bot", [128, 1024], F32)
    outb = p.sb("outb", [128, 2 * 1024], F32)
    brow = outb[0:1, 0:1024]
    Ksq = outb[:].bitcast(BF16)[:, 0:2048]

    psS = p.ps("psS", [128, 2048], F32)
    psT = p.ps("psT", [128, 2048], BF16)
    psX = p.ps("psX", [128, 512], F32)
    psC = p.ps("psC", [128, 512], F32)

    for c in range(8):
        p.emit("sync", lambda e, c=c: e.dma_start(out=xn[:, c * 2048:(c + 1) * 2048],
                                                  in_=aps["xn_bf"][c * 128:(c + 1) * 128, :]))

    ST8_S1, ST8_S2, ST8_S3 = 0, 4, 8
    ST8_C0 = 16
    ST8_NM = 20
    ST8_BL, ST8_BR = 24, 28
    ST8_S4, ST8_RS = 32, 36
    ST8_A0, ST8_T1, ST8_T2 = 40, 44, 45

    t_ld = p.emit("sync", lambda e: e.dma_start(out=relst[0:65, 0:64], in_=aps["rel_k"]))
    p.emit("gpsimd", lambda e: e.dma_start(out=relvN[0:65, :], in_=aps["rel_v"]))
    t_tr = p.emit("tensor", lambda e: e.transpose(psX[0:64, 0:65], relst[0:65, 0:64], idf[0:65, 0:65]),
                  waits=[("sync", t_ld)])
    p.emit("vector", lambda e: e.tensor_copy(relkT[0:64, 0:65], psX[0:64, 0:65]),
           waits=[("tensor", t_tr)])
    t_rk = p.emit("vector", lambda e: e.tensor_copy(relkT[64:128, 0:65], psX[0:64, 0:65]))
    p.emit("vector", lambda e: e.tensor_tensor(relst[0:65, 0:65][0:65, 0:64] if False else relst[0:65, 0:64], relst[0:65, 0:64], relst[0:65, 0:64], op=OP.mult))
    t_rs = p.emit("vector", lambda e: e.tensor_reduce(relst[0:65, 64:65], relst[0:65, 0:64], axis=AX.X, op=OP.add))
    t_tr2 = p.emit("tensor", lambda e: e.transpose(psX[0:1, 70:135], relst[0:65, 64:65], idf[0:65, 0:65]),
                   waits=[("vector", t_rs)])
    t_rm = p.emit("vector", lambda e: e.tensor_reduce(Krow[0:1, 20:21], psX[0:1, 70:135], axis=AX.X, op=OP.max),
                  waits=[("tensor", t_tr2)])
    t_rm2 = p.emit("scalar", lambda e: e.activation(Krow[0:1, 20:21], Krow[0:1, 20:21], AF.Ln, bias=0.0, scale=1.0),
                   waits=[("vector", t_rm)])
    t_rm3 = p.emit("scalar", lambda e: e.activation(Krow[0:1, 20:21], Krow[0:1, 20:21], AF.Exp, bias=0.0, scale=0.5))
    emit_bcast_row(p, aps["b_o"], 1024, bot, brow, ones1, psS)
    p.emit("vector", lambda e: e.memset(attn[:, 0:32], 0.0))
    p.emit("vector", lambda e: e.memset(attn[:, 2080:2112], 0.0))

    tok = {"se_bank": [0, 0], "kred": 0}
    emit_weight_T(p, aps["Wq"], D, D, wt, wstage, psS, idf)
    _proj_common(p, wt, xn, psS, Q, 0.125, False, tok)
    emit_weight_T(p, aps["Wk"], D, D, wt, wstage, psS, idf)
    _proj_common(p, wt, xn, psS, K, 1.0, False, tok)
    import os as _os
    TMX = not _os.environ.get("KK_MBOUND")
    for h in ([] if TMX else range(H)):
        _kmax_chain(p, K, Ksq, psS, onec, Krow, h, h, tok)
    emit_weight_T(p, aps["Wv"], D, D, wt, wstage, psS, idf)
    _proj_common(p, wt, xn, psS, V, 1.0, True, tok)
    emit_weight_T(p, aps["Wo"], D, D, wt, wstage, psS, idf)
    if TMX:
        tok_krc = p.n("vector")
    else:
        t1 = p.emit("scalar", lambda e: e.activation(Krow[0:1, 0:16], Krow[0:1, 0:16], AF.Ln, bias=0.0, scale=1.0),
                    waits=[("vector", tok["kred"]), ("scalar", t_rm3)])
        p.emit("scalar", lambda e: e.activation(Krow[0:1, 0:16], Krow[0:1, 0:16], AF.Exp, bias=0.0, scale=0.5))
        t2 = p.emit("vector", lambda e: e.tensor_tensor(Krow[0:1, 0:16], Krow[0:1, 0:16],
                                                        Krow[0:1, 20:21].broadcast_to([1, 16]), op=OP.add),
                    waits=[("scalar", p.n("scalar"))])
        t3 = p.emit("vector", lambda e: e.tensor_scalar(Krow[0:1, 0:16], Krow[0:1, 0:16], -1.0, None, op0=OP.mult))
        t4 = p.emit("tensor", lambda e: e.matmul(psX[:, 140:156], ones1[0:1, :], Krow[0:1, 0:16],
                                                 start=True, stop=True), waits=[("vector", t3)])
        tok_krc = p.emit("vector", lambda e: e.tensor_copy(KRcols[:], psX[:, 140:156]), waits=[("tensor", t4)])

    tk = {"exp": 0, "corner": [0, 0], "transp": [0, 0], "dgw": [0, 0],
          "psT_evac": 0, "psX_ve": tok_krc, "psX_se": 0, "ctxmult": 0,
          "rsbevac": 0, "woevac": 0, "wread": 0, "wadd_last": 0, "atile_prev": 0}
    bufc = [0]
    import os
    NQB = int(os.environ.get("KK_NQB", "4"))
    NH = int(os.environ.get("KK_NH", str(H)))
    for qb in range(NQB):
        for h in range(NH):
            r = (h % 2) * 64
            oc = h // 2
            SKIPQR = bool(os.environ.get("KK_SKIP_QR"))
            if SKIPQR:
                p.emit("vector", lambda e: e.memset(st8[:, ST8_NM:ST8_NM + 4], -10.0))
                p.emit("vector", lambda e: e.memset(st8[:, ST8_BL:ST8_BL + 4], -10.0))
                t_bias = p.emit("vector", lambda e: e.memset(st8[:, ST8_BR:ST8_BR + 4], -10.0))
                tk["wread"] = p.n("sync")
            TRUEMAX0 = not os.environ.get("KK_MBOUND")
            t_qsq = None if (SKIPQR or TRUEMAX0) else p.emit("scalar", lambda e, r=r, oc=oc, qb=qb: e.activation(
                Qsq[r:r + 64, :], Q[r:r + 64, oc * 2048 + qb * 512:oc * 2048 + (qb + 1) * 512],
                AF.Square, bias=0.0, scale=1.0),
                waits=[("tensor", p.n("tensor"))] if (h > 0 or qb > 0) else [])
            for qtl in ([] if SKIPQR else range(4)):
                qt = qb * 4 + qtl
                p.emit("tensor", lambda e, h=h, qt=qt, qtl=qtl: e.matmul(
                    psX[:, qtl * 65:(qtl + 1) * 65], hsl(Q, h, qt * 128, 128),
                    relkT[(h % 2) * 64:(h % 2) * 64 + 64, 0:65], start=True, stop=True),
                    waits=([("vector", tk["psX_ve"]), ("scalar", tk["psX_se"])] if qtl == 0 else []))
            QRLVL = int(os.environ.get("KK_QRLVL", "4"))
            TRUEMAX = not os.environ.get("KK_MBOUND")
            for qtl in ([] if (SKIPQR or QRLVL < 2 or TRUEMAX) else range(4)):
                p.emit("tensor", lambda e, r=r, qtl=qtl: e.matmul(
                    psX[:, 260 + qtl:261 + qtl], Qsq[r:r + 64, qtl * 128:(qtl + 1) * 128],
                    onec[r:r + 64, :], start=True, stop=True),
                    waits=([("scalar", t_qsq)] if qtl == 0 else []))
            t_qr_mm = p.n("tensor")
            if SKIPQR:
                pass
            else:
              t_qrt = p.emit("vector", lambda e: e.tensor_copy(QRt4[:], psX[:, 0:260]),
                           waits=[("tensor", t_qr_mm)])
              if QRLVL < 2:
                p.emit("vector", lambda e: e.memset(st8[:, ST8_NM:ST8_NM + 4], -10.0))
                p.emit("vector", lambda e: e.memset(st8[:, ST8_BL:ST8_BL + 4], -10.0))
                t_bias = p.emit("vector", lambda e: e.memset(st8[:, ST8_BR:ST8_BR + 4], -10.0))
                tk["wread"] = p.n("sync")
                raise_skip = True
              if QRLVL >= 2 and not TRUEMAX:
               t_mln = p.emit("scalar", lambda e: e.activation(st8[:, ST8_T1:ST8_T1 + 4], psX[:, 260:264],
                                                            AF.Ln, bias=0.0, scale=1.0),
                           waits=[("tensor", t_qr_mm)])
               tk["psX_se"] = t_mln
               t_msq = p.emit("scalar", lambda e: e.activation(st8[:, ST8_T1:ST8_T1 + 4], st8[:, ST8_T1:ST8_T1 + 4],
                                                            AF.Exp, bias=0.0, scale=0.5))
               t_nm = p.emit("vector", lambda e, h=h: e.tensor_scalar(
                st8[:, ST8_NM:ST8_NM + 4], st8[:, ST8_T1:ST8_T1 + 4], KRcols[:, h:h + 1], None, op0=OP.mult),
                waits=[("scalar", t_msq)])
              qr4 = QRt4[:].rearrange("p (q r) -> p q r", r=65)
              nm4 = st8[:, ST8_NM:ST8_NM + 4].rearrange("p (q o) -> p q o", o=1)
              if QRLVL >= 3:
               if not TRUEMAX:
                p.emit("vector", lambda e, qr4=qr4, nm4=nm4: e.tensor_tensor(
                 st8[:, ST8_BL:ST8_BL + 4].rearrange("p (q o) -> p q o", o=1), qr4[:, :, 0:1], nm4, op=OP.add))
                t_bias = p.emit("vector", lambda e, qr4=qr4, nm4=nm4: e.tensor_tensor(
                 st8[:, ST8_BR:ST8_BR + 4].rearrange("p (q o) -> p q o", o=1), qr4[:, :, 64:65], nm4, op=OP.add))
               else:
                t_bias = p.n("vector")
               pt4 = Ptab4[:].rearrange("p (q c) -> p q c", c=319)
               p.emit("vector", lambda e, pt4=pt4, qr4=qr4: e.tensor_copy(
                pt4[:, :, 0:127], qr4[:, :, 0:1].broadcast_to([128, 4, 127])))
               p.emit("vector", lambda e, pt4=pt4: e.tensor_copy(
                pt4[:, :, 127:192], QRt4[:].rearrange("p (q r) -> p q r", r=65)))
               t_pb = p.emit("vector", lambda e, pt4=pt4, qr4=qr4: e.tensor_copy(
                pt4[:, :, 192:319], qr4[:, :, 64:65].broadcast_to([128, 4, 127])))
              else:
                t_bias = p.emit("vector", lambda e: e.memset(st8[:, ST8_BL:ST8_BL + 8], -10.0))
              for qtl in (range(4) if QRLVL >= 4 else []):
                off = qtl * 40960
                p.emit("sync", lambda e, qtl=qtl, off=off: e.dma_start(
                    out=aps["pscrP"][off:off + 128 * 319].rearrange("(q w) -> q w", w=319),
                    in_=Ptab4[:, qtl * 319:(qtl + 1) * 319]),
                    waits=([("vector", t_pb)] if qtl == 0 else []))
                p.emit("sync", lambda e, qtl=qtl, off=off: e.dma_start(
                    out=Wwin4[:, qtl * 192:(qtl + 1) * 192],
                    in_=aps["pscrP"][off + 127:off + 127 + 128 * 318].rearrange("(q w) -> q w", w=318)[:, 0:192]),
                    waits=([("vector", tk["wadd_last"])] if qtl == 0 else []))
              tk["wread"] = p.n("sync")
              if QRLVL < 4:
                for qtl in range(4):
                    p.emit("vector", lambda e, qtl=qtl: e.memset(Wwin4[:, qtl * 192:(qtl + 1) * 192], 0.0))
                tk["wread"] = p.n("sync")
            if qb == 0:
                p.emit("vector", lambda e: e.memset(st8[:, ST8_S1:ST8_S1 + 1], 0.0))
                tk["psX_ve"] = p.emit("vector", lambda e: e.memset(psX[:, 264:265], 0.0),
                                      waits=[("tensor", p.n("tensor"))])
            if qb == 3:
                p.emit("vector", lambda e: e.memset(st8[:, ST8_S3 + 3:ST8_S3 + 4], 0.0))
            NQT = int(os.environ.get("KK_NQT", "4"))
            BARE = bool(os.environ.get("KK_BARE"))
            for qtl in range(NQT):
                qt = qb * 4 + qtl
                q0 = qt * 128
                buf = 0
                bufc[0] += 1
                a0_ = 0
                for kb in range(4):
                    w = []
                    if kb == 0:
                        w = [("scalar", tk["exp"])]
                        if h == 0 and qtl == 0:
                            w.append(("vector", tk["woevac"]))
                        if qtl == 0:
                            w.append(("vector", tk["rsbevac"]))
                            w.append(("scalar", tk["rsbevac_se"] if "rsbevac_se" in tk else 0))
                    p.emit("tensor", lambda e, h=h, qt=qt, kb=kb: e.matmul(
                        psS[:, kb * 512:(kb + 1) * 512],
                        hsl(Q, h, qt * 128, 128), hsl(K, h, kb * 512, 512),
                        start=True, stop=True), waits=w)
                t_sc = p.n("tensor")
                if SKIPQR:
                    t_wadd = p.emit("vector", lambda e: e.tensor_copy(st8[:, ST8_T2:ST8_T2 + 1], st8[:, ST8_T2:ST8_T2 + 1]),
                                    waits=[("tensor", t_sc)])
                elif qt == 0:
                    t_wadd = p.emit("vector", lambda e: e.tensor_tensor(
                        psS[:, 0:160], psS[:, 0:160], Wwin4[:, 32:192], op=OP.add),
                        waits=[("sync", tk["wread"]), ("tensor", t_sc)])
                elif qt == 15:
                    t_wadd = p.emit("vector", lambda e: e.tensor_tensor(
                        psS[:, 1888:2048], psS[:, 1888:2048], Wwin4[:, 3 * 192:3 * 192 + 160], op=OP.add),
                        waits=[("sync", tk["wread"]), ("tensor", t_sc)])
                else:
                    t_wadd = p.emit("vector", lambda e, q0=q0, qtl=qtl: e.tensor_tensor(
                        psS[:, q0 - 32:q0 + 160], psS[:, q0 - 32:q0 + 160],
                        Wwin4[:, qtl * 192:(qtl + 1) * 192], op=OP.add),
                        waits=[("sync", tk["wread"]), ("tensor", t_sc), ("vector", t_bias)])
                tk["wadd_last"] = t_wadd
                if TRUEMAX:
                    p.emit("vector", lambda e, qtl=qtl: e.tensor_reduce(
                        st8[:, ST8_NM + qtl:ST8_NM + qtl + 1], psS[:], axis=AX.X, op=OP.max, negate=True))
                    p.emit("vector", lambda e, qtl=qtl: e.tensor_tensor(
                        st8[:, ST8_BL + qtl:ST8_BL + qtl + 1], QRt4[:, qtl * 65:qtl * 65 + 1],
                        st8[:, ST8_NM + qtl:ST8_NM + qtl + 1], op=OP.add))
                    t_wadd = p.emit("vector", lambda e, qtl=qtl: e.tensor_tensor(
                        st8[:, ST8_BR + qtl:ST8_BR + qtl + 1], QRt4[:, qtl * 65 + 64:qtl * 65 + 65],
                        st8[:, ST8_NM + qtl:ST8_NM + qtl + 1], op=OP.add))
                obuf = 1 - buf
                expw = [("vector", t_wadd), ("vector", tk["corner"][buf]),
                        ("tensor", tk["transp"][buf]), ("sync", tk["dgw"][buf])]
                if os.environ.get("KK_EXP1"):
                    p.emit("scalar", lambda e, a0_=a0_, qtl=qtl: e.activation(
                        attn[:, a0_ + 32:a0_ + 32 + 2048], psS[:], AF.Exp,
                        bias=st8[:, ST8_NM + qtl:ST8_NM + qtl + 1], scale=1.0,
                        accum_out=st8[:, ST8_S2 + qtl:ST8_S2 + qtl + 1]), waits=expw)
                    tk["exp"] = p.n("scalar")
                else:
                  if qt > 0:
                    p.emit("scalar", lambda e, q0=q0, a0_=a0_, qtl=qtl: e.activation(
                        attn[:, a0_ + 32:a0_ + 32 + q0 - 32], psS[:, 0:q0 - 32], AF.Exp,
                        bias=st8[:, ST8_BL + qtl:ST8_BL + qtl + 1], scale=1.0,
                        accum_out=st8[:, ST8_S1 + qtl:ST8_S1 + qtl + 1]), waits=expw)
                    expw = []
                  lo = max(0, q0 - 32)
                  hi = min(2048, q0 + 160)
                  p.emit("scalar", lambda e, lo=lo, hi=hi, a0_=a0_, qtl=qtl: e.activation(
                    attn[:, a0_ + 32 + lo:a0_ + 32 + hi], psS[:, lo:hi], AF.Exp,
                    bias=st8[:, ST8_NM + qtl:ST8_NM + qtl + 1], scale=1.0,
                    accum_out=st8[:, ST8_S2 + qtl:ST8_S2 + qtl + 1]), waits=expw)
                  if qt < 15:
                    p.emit("scalar", lambda e, hi=hi, a0_=a0_, qtl=qtl: e.activation(
                        attn[:, a0_ + 32 + hi:a0_ + 32 + 2048], psS[:, hi:2048], AF.Exp,
                        bias=st8[:, ST8_BR + qtl:ST8_BR + qtl + 1], scale=1.0,
                        accum_out=st8[:, ST8_S3 + qtl:ST8_S3 + qtl + 1]), waits=[])
                  tk["exp"] = p.n("scalar")
                if os.environ.get("KK_SKIP_A"):
                    tk["corner"][buf] = p.n("vector")
                    tk["dgw"][buf] = p.n("sync")
                else:
                  p.emit("vector", lambda e, q0=q0, a0_=a0_: e.tensor_tensor(
                    crn[:], attn[:, a0_ + q0:a0_ + q0 + 128], tri[:], op=OP.mult),
                    waits=[("scalar", tk["exp"])])
                  tk["corner"][buf] = p.emit("vector", lambda e, qtl=qtl: e.tensor_reduce(
                    st8[:, ST8_C0 + qtl:ST8_C0 + qtl + 1], crn[:], axis=AX.X, op=OP.add))
                  offb = qtl * 24832
                  p.emit("sync", lambda e, q0=q0, a0_=a0_, offb=offb: e.dma_start(
                    out=aps["pscrB"][offb:offb + 128 * 192].rearrange("(q w) -> q w", w=192),
                    in_=attn[:, a0_ + q0:a0_ + q0 + 192]),
                    waits=[("scalar", tk["exp"])])
                  p.emit("sync", lambda e, qtl=qtl, offb=offb: e.dma_start(
                    out=Dg4[:, qtl * 65:(qtl + 1) * 65],
                    in_=aps["pscrB"][offb:offb + 128 * 193].rearrange("(q w) -> q w", w=193)[:, 0:65]),
                    waits=([("vector", tk["atile_prev"])] if qtl == 0 else []))
                  tk["dgw"][buf] = p.n("sync")
                if BARE:
                    continue
                for c in range(16):
                    w = []
                    if c == 0:
                        w = [("scalar", tk["exp"]), ("vector", tk["psT_evac"])]
                    p.emit("tensor", lambda e, c=c, a0_=a0_: e.transpose(
                        psT[:, c * 128:(c + 1) * 128], attn[:, a0_ + 32 + c * 128:a0_ + 32 + (c + 1) * 128],
                        idb[:]), waits=w)
                tk["transp"][buf] = p.n("tensor")
                tk["psT_evac"] = p.emit("vector", lambda e, qtl=qtl: e.tensor_copy(
                    attnT[:].rearrange("p (c q) -> p c q", q=512)[:, :, qtl * 128:(qtl + 1) * 128],
                    psT[:].rearrange("p (c q) -> p c q", q=128)),
                    waits=[("tensor", tk["transp"][buf])])
            if BARE:
                continue
            t_evacs = tk["psT_evac"]
            SKIPA = bool(os.environ.get("KK_SKIP_A"))
            for qtl in ([] if SKIPA else range(4)):
                qt = qb * 4 + qtl
                nfull = qt - 1
                first = True
                for c in range(max(0, nfull)):
                    p.emit("tensor", lambda e, c=c, qtl=qtl: e.matmul(
                        psX[:, 264 + qtl:265 + qtl],
                        attnT[:, c * 512 + qtl * 128:c * 512 + (qtl + 1) * 128],
                        onec[:], start=(c == 0), stop=False),
                        waits=([("vector", t_evacs)] if first else []))
                    first = False
                if qt >= 1:
                    p.emit("tensor", lambda e, qt=qt, qtl=qtl, st=(nfull <= 0): e.matmul(
                        psX[:, 264 + qtl:265 + qtl],
                        attnT[0:96, (qt - 1) * 512 + qtl * 128:(qt - 1) * 512 + (qtl + 1) * 128],
                        onec[0:96, :], start=st, stop=True),
                        waits=([("vector", t_evacs)] if first else []))
            t_a0 = p.n("tensor")
            p.emit("vector", lambda e: e.tensor_tensor(st8[:, ST8_S4:ST8_S4 + 4], st8[:, ST8_S1:ST8_S1 + 4],
                                                       st8[:, ST8_S2:ST8_S2 + 4], op=OP.add),
                   waits=[("scalar", tk["exp"])])
            p.emit("vector", lambda e: e.tensor_tensor(st8[:, ST8_S4:ST8_S4 + 4], st8[:, ST8_S4:ST8_S4 + 4],
                                                       st8[:, ST8_S3:ST8_S3 + 4], op=OP.add))
            t_rs4 = p.emit("vector", lambda e: e.reciprocal(st8[:, ST8_RS:ST8_RS + 4], st8[:, ST8_S4:ST8_S4 + 4]))
            if SKIPA:
                t_atile = p.n("vector")
                tk["psX_ve"] = max(tk["psX_ve"], t_atile)
                tk["atile_prev"] = t_atile
                t_arhs = tk["psT_evac"]
            dg4v = Dg4[:].rearrange("p (q r) -> p q r", r=65)
            at4v = Atile4[:].rearrange("p (q r) -> p q r", r=65)
            if not SKIPA:
              p.emit("vector", lambda e: e.tensor_tensor(
                st8[:, ST8_A0:ST8_A0 + 4], psX[:, 264:268], st8[:, ST8_C0:ST8_C0 + 4], op=OP.add),
                waits=[("tensor", t_a0), ("sync", tk["dgw"][0]), ("sync", tk["dgw"][1])])
            if not SKIPA:
              p.emit("vector", lambda e, dg4v=dg4v: e.tensor_reduce(
                  st8[:, ST8_T1:ST8_T1 + 4], dg4v[:, :, 1:64], axis=AX.X, op=OP.add))
            if not SKIPA:
              p.emit("vector", lambda e: e.tensor_tensor(st8[:, ST8_T1:ST8_T1 + 4], st8[:, ST8_T1:ST8_T1 + 4],
                                                         st8[:, ST8_A0:ST8_A0 + 4], op=OP.add))
            if not SKIPA:
              p.emit("vector", lambda e: e.tensor_tensor(st8[:, ST8_T1:ST8_T1 + 4], st8[:, ST8_S4:ST8_S4 + 4],
                                                         st8[:, ST8_T1:ST8_T1 + 4], op=OP.subtract))
            if not SKIPA:
              p.emit("vector", lambda e, dg4v=dg4v, at4v=at4v: e.tensor_copy(at4v[:, :, 1:64], dg4v[:, :, 1:64]))
            if not SKIPA:
              p.emit("vector", lambda e, at4v=at4v: e.tensor_copy(
                  at4v[:, :, 0:1], st8[:, ST8_A0:ST8_A0 + 4].rearrange("p (q o) -> p q o", o=1)))
            if not SKIPA:
                t_atile = p.emit("vector", lambda e, at4v=at4v: e.tensor_copy(
                    at4v[:, :, 64:65], st8[:, ST8_T1:ST8_T1 + 4].rearrange("p (q o) -> p q o", o=1)))
                tk["psX_ve"] = t_atile
                tk["atile_prev"] = t_atile
            if not SKIPA:
              for qtl in range(4):
                p.emit("tensor", lambda e, qtl=qtl: e.transpose(
                    psT[0:65, qtl * 128:(qtl + 1) * 128], Atile4[:, qtl * 65:(qtl + 1) * 65], idb[:]),
                    waits=([("vector", t_atile)] if qtl == 0 else []))
              t_atr = p.n("tensor")
              t_arhs = p.emit("vector", lambda e: e.tensor_copy(Arhs4[0:65, :], psT[0:65, 0:512]),
                            waits=[("tensor", t_atr)])
              tk["psT_evac"] = t_arhs
              for qtl in range(4):
                w = [("vector", t_arhs)] if qtl == 0 else []
                if qtl == 0:
                    w.append(("vector", tk["ctxmult"]))
                p.emit("tensor", lambda e, qtl=qtl: e.matmul(
                    psC[0:64, qtl * 128:(qtl + 1) * 128], relvN[0:65, 0:64],
                    Arhs4[0:65, qtl * 128:(qtl + 1) * 128], start=(qtl == 0), stop=False), waits=w)
            if os.environ.get("KK_NOAV"):
                continue
            for c in range(16):
                w = []
                if SKIPA and c == 0:
                    w = [("vector", max(tk["ctxmult"], t_evacs))]
                p.emit("tensor", lambda e, h=h, c=c, st=(SKIPA and c == 0): e.matmul(
                    psC[0:64, :], V[:, c * 1024 + h * 64:c * 1024 + (h + 1) * 64],
                    attnT[:, c * 512:(c + 1) * 512], start=st, stop=(c == 15)), waits=w)
            t_av = p.n("tensor")
            if os.environ.get("KK_SKIP_RS"):
                tk["rsbevac_se"] = p.n("scalar")
                tk["ctxmult"] = p.emit("vector", lambda e, h=h: e.tensor_copy(
                    ctxfm[(h % 2) * 64:(h % 2) * 64 + 64, (h // 2) * 512:(h // 2 + 1) * 512],
                    psC[0:64, :]), waits=[("tensor", t_av)])
                tk["rsbevac"] = tk["ctxmult"]
                continue
            t_rstr = p.emit("tensor", lambda e: e.transpose(psX[0:4, 300:428], st8[:, ST8_RS:ST8_RS + 4],
                                                            idf[:]),
                            waits=[("vector", t_atile)])
            t_rsrow = p.emit("vector", lambda e: e.tensor_copy(rsrow[0:4, :], psX[0:4, 300:428]),
                             waits=[("tensor", t_rstr), ("sync", tk.get("flat_prev", 0))])
            tk["psX_ve"] = t_rsrow
            t_flat = p.emit("sync", lambda e: e.dma_start(out=rsrow1[0:1, :], in_=rsrow[0:4, :]),
                            waits=[("vector", t_rsrow), ("tensor", tk.get("rsmm_prev", 0))])
            tk["flat_prev"] = t_flat
            for qtl in range(4):
                w = [("sync", t_flat)] if qtl == 0 else []
                if qtl == 0:
                    w.append(("scalar", tk["exp"]))
                p.emit("tensor", lambda e, qtl=qtl: e.matmul(
                    psS[:, qtl * 128:(qtl + 1) * 128], ones1[0:1, :],
                    rsrow1[0:1, qtl * 128:(qtl + 1) * 128], start=True, stop=True), waits=w)
            t_rsmm = p.n("tensor")
            tk["rsmm_prev"] = t_rsmm
            tk["rsbevac_se"] = p.emit("scalar", lambda e: e.activation(rsb[:], psS[:, 0:512], AF.Copy,
                                                                       bias=0.0, scale=1.0),
                                      waits=[("tensor", t_rsmm)])
            tk["ctxmult"] = p.emit("vector", lambda e, h=h: e.tensor_tensor(
                ctxfm[(h % 2) * 64:(h % 2) * 64 + 64, (h // 2) * 512:(h // 2 + 1) * 512],
                psC[0:64, :], rsb[0:64, :], op=OP.mult),
                waits=[("scalar", tk["rsbevac_se"]), ("tensor", t_av)])
            tk["rsbevac"] = tk["ctxmult"]
        if os.environ.get("KK_BARE") or os.environ.get("KK_NOAV") or os.environ.get("KK_NOWO"):
            continue
        t_allctx = tk["ctxmult"]
        t_woev = 0
        for tsub in range(4):
            t = qb * 4 + tsub
            for oh in range(2):
                pst = psS[:, oh * 512:(oh + 1) * 512]
                for dc in range(8):
                    w = []
                    if oh == 0 and dc == 0:
                        w = [("vector", max(t_allctx, t_woev))]
                        if tsub == 0:
                            w.append(("scalar", tk["rsbevac_se"]))
                    p.emit("tensor", lambda e, tsub=tsub, oh=oh, dc=dc, pst=pst: e.matmul(
                        pst, ctxfm[:, dc * 512 + tsub * 128:dc * 512 + (tsub + 1) * 128],
                        wt[:, dc * 1024 + oh * 512:dc * 1024 + (oh + 1) * 512],
                        start=(dc == 0), stop=(dc == 7)), waits=w)
            t_wo = p.n("tensor")
            ob = outb[:, (tsub % 2) * 1024:(tsub % 2) * 1024 + 1024]
            p.emit("vector", lambda e, ob=ob: e.tensor_tensor(ob, psS[:, 0:1024], bot[:], op=OP.add),
                   waits=[("tensor", t_wo), ("gpsimd", max(0, p.n("gpsimd") - 16))])
            t_woev = p.n("vector")
            tk["woevac"] = t_woev
            p.emit("gpsimd", lambda e, t=t, ob=ob: e.dma_start(
                out=aps["a0"][t * 128:(t + 1) * 128, :], in_=ob),
                waits=[("vector", tk["woevac"])])



def phase_attn1_v2(p, io, env):
    aps = io
    idf = env["idf"]
    idb = env["idb"]
    ones1 = env["ones1"]

    xn = p.sb("xn", [128, 8 * 2048], BF16)
    wt = p.sb("wt", [128, 8 * 1024], BF16)
    wstage = p.sb("wstage", [128, 1024], F32)
    Q = p.sb("Q", [128, 8 * 2048], BF16)
    K = p.sb("K", [128, 8 * 2048], BF16)
    V = p.sb("V", [128, 16 * 1024], BF16)
    attn = p.sb("attn", [128, 2048], BF16)
    attnT = p.sb("attnT", [128, 16 * 512], BF16)
    ctxfm = p.sb("ctxfm", [128, 8 * 512], BF16)
    qkb = p.sb("qkb", [128, 1024], BF16)
    rst = p.sb("rst", [128, 16], F32)
    wnt = p.sb("wnt", [128, 1024], F32)
    st8 = p.sb("st8", [128, 16], F32)
    bqt = p.sb("bqt", [128, 1024], F32)
    bpt = p.sb("bpt", [128, 1024], F32)
    rsrow = p.sb("rsrow", [4, 128], F32)
    rsrow1 = p.sb("rsrow1", [1, 512], F32)
    rsb = p.sb("rsb", [128, 512], F32)
    wrow = p.sb("wrow", [1, 64], F32)
    outb = p.sb("outb", [128, 2 * 1024], F32)
    qkt = outb[:, 0:1024]
    sq = outb[:, 1024:2048]
    brow = outb[0:1, 0:1024]

    psS = p.ps("psS", [128, 2048], F32)
    psT = p.ps("psT", [128, 2048], BF16)
    psC = p.ps("psC", [128, 512], F32)
    psX = p.ps("psX", [128, 512], F32)

    ST8_NM, ST8_S4, ST8_RS = 0, 4, 8

    for c in range(8):
        p.emit("sync", lambda e, c=c: e.dma_start(out=xn[:, c * 2048:(c + 1) * 2048],
                                                  in_=aps["xn_bf"][c * 128:(c + 1) * 128, :]))
    emit_bcast_row(p, aps["bproj"], 1024, bpt, brow, ones1, psS)

    def emit_wn_tile(nm):
        p.emit("sync", lambda e, nm=nm: e.dma_start(out=wrow[0:1, :], in_=aps[nm][0:DK].rearrange("(a b) -> a b", a=1)),
               waits=[("tensor", p.n("tensor"))])
        p.emit("tensor", lambda e: e.matmul(psS[:, 0:64], ones1[0:1, :], wrow[0:1, :], start=True, stop=True),
               waits=[("sync", p.n("sync")), ("vector", p.n("vector"))])
        for j in range(16):
            p.emit("vector", lambda e, j=j: e.tensor_copy(
                wnt[:, j * 64:(j + 1) * 64], psS[:, 0:64]),
                waits=[("tensor", p.n("tensor"))] if j == 0 else [])

    for part, dest in ((0, Q), (1, K), (2, V)):
        emit_bcast_row(p, aps["bqkv"][part * D:(part + 1) * D], 1024, bqt, brow, ones1, psS)
        if part == 0:
            emit_wn_tile("qn_w")
        elif part == 1:
            emit_wn_tile("kn_w")
        emit_weight_T(p, aps["Wqkv"][part * D:(part + 1) * D, :], D, D, wt, wstage, psS, idf)
        t_ps = p.n("vector")
        t_pse = p.n("scalar")
        t_tr_ev = p.n("vector")
        for t in range(16):
            for oh in range(2):
                for dc in range(8):
                    w = []
                    if oh == 0 and dc == 0:
                        w = [("vector", t_ps), ("scalar", t_pse), ("sync", p.n("sync"))]
                    p.emit("tensor", lambda e, t=t, oh=oh, dc=dc: e.matmul(
                        psS[:, oh * 512:oh * 512 + 512],
                        xn[:, dc * 2048 + t * 128:dc * 2048 + (t + 1) * 128],
                        wt[:, dc * 1024 + oh * 512:dc * 1024 + (oh + 1) * 512],
                        start=(dc == 0), stop=(dc == 7)), waits=w)
            t_mm = p.n("tensor")
            if part == 2:
                t_ps = p.emit("vector", lambda e, t=t: e.tensor_tensor(
                    V[:, t * 1024:(t + 1) * 1024], psS[:, 0:1024], bqt[:], op=OP.add),
                    waits=[("tensor", t_mm)])
            else:
                p.emit("vector", lambda e: e.tensor_tensor(qkt[:], psS[:, 0:1024], bqt[:], op=OP.add),
                       waits=[("tensor", t_mm)])
                p.emit("vector", lambda e: e.tensor_tensor(sq[:], qkt[:], qkt[:], op=OP.mult))
                t_ps = p.emit("vector", lambda e: e.tensor_reduce(
                    rst[:, 0:16], sq[:].rearrange("p (g d) -> p g d", d=64), axis=AX.X, op=OP.add))
                eps_r = p.cc(1e-6)
                p.emit("scalar", lambda e, eps_r=eps_r: e.activation(rst[:, 0:16], rst[:, 0:16], AF.Ln, bias=eps_r, scale=1.0 / DK),
                       waits=[("vector", t_ps)])
                t_pse = p.emit("scalar", lambda e: e.activation(rst[:, 0:16], rst[:, 0:16], AF.Exp, bias=0.0, scale=-0.5))
                p.emit("vector", lambda e: e.tensor_tensor(
                    qkt[:].rearrange("p (g d) -> p g d", d=64),
                    qkt[:].rearrange("p (g d) -> p g d", d=64),
                    rst[:, 0:16].rearrange("p (g o) -> p g o", o=1).broadcast_to([128, 16, 64]),
                    op=OP.mult), waits=[("scalar", t_pse)])
                sc2 = 0.125 if part == 0 else 1.0
                t_ps = p.emit("vector", lambda e, sc2=sc2: e.scalar_tensor_tensor(
                    qkb[:], qkt[:], sc2, wnt[:], op0=OP.mult, op1=OP.mult))
                for c in range(8):
                    w = []
                    if c == 0:
                        w = [("vector", t_ps)]
                        if t_tr_ev:
                            w.append(("vector", t_tr_ev))
                    p.emit("tensor", lambda e, c=c: e.transpose(
                        psT[:, c * 128:(c + 1) * 128], qkb[:, c * 128:(c + 1) * 128], idb[:]), waits=w)
                t_trp = p.n("tensor")
                t_tr_ev = p.emit("vector", lambda e, t=t, dest=dest: e.tensor_copy(
                    dest[:].rearrange("p (c s) -> p c s", s=S)[:, :, t * 128:(t + 1) * 128],
                    psT[:, 0:1024].rearrange("p (c q) -> p c q", q=128)),
                    waits=[("tensor", t_trp)])
    emit_weight_T(p, aps["Wproj"], D, D, wt, wstage, psS, idf)

    tk = {"exp": 0, "transp": 0, "psT_evac": p.n("vector"), "ctxmult": 0,
          "rsbevac_se": p.n("scalar"), "woevac": p.n("vector"),
          "flat_prev": p.n("sync"), "rsmm_prev": p.n("tensor")}
    for qb in range(4):
        for h in range(H):
            for qtl in range(4):
                qt = qb * 4 + qtl
                for kb in range(4):
                    w = []
                    if kb == 0:
                        w = [("scalar", tk["exp"])]
                        if qtl == 0:
                            w.append(("vector", tk["woevac"]))
                            w.append(("scalar", tk["rsbevac_se"]))
                    p.emit("tensor", lambda e, h=h, qt=qt, kb=kb: e.matmul(
                        psS[:, kb * 512:(kb + 1) * 512],
                        hsl(Q, h, qt * 128, 128), hsl(K, h, kb * 512, 512),
                        start=True, stop=True), waits=w)
                t_sc = p.n("tensor")
                t_mx = p.emit("vector", lambda e, qtl=qtl: e.tensor_reduce(
                    st8[:, ST8_NM + qtl:ST8_NM + qtl + 1], psS[:], axis=AX.X, op=OP.max, negate=True),
                    waits=[("tensor", t_sc)])
                p.emit("scalar", lambda e, qtl=qtl: e.activation(
                    attn[:], psS[:], AF.Exp, bias=st8[:, ST8_NM + qtl:ST8_NM + qtl + 1], scale=1.0,
                    accum_out=st8[:, ST8_S4 + qtl:ST8_S4 + qtl + 1]),
                    waits=[("vector", t_mx), ("tensor", tk["transp"])])
                tk["exp"] = p.n("scalar")
                for c in range(16):
                    w = []
                    if c == 0:
                        w = [("scalar", tk["exp"]), ("vector", tk["psT_evac"])]
                    p.emit("tensor", lambda e, c=c: e.transpose(
                        psT[:, c * 128:(c + 1) * 128], attn[:, c * 128:(c + 1) * 128], idb[:]), waits=w)
                tk["transp"] = p.n("tensor")
                tk["psT_evac"] = p.emit("vector", lambda e, qtl=qtl: e.tensor_copy(
                    attnT[:].rearrange("p (c q) -> p c q", q=512)[:, :, qtl * 128:(qtl + 1) * 128],
                    psT[:].rearrange("p (c q) -> p c q", q=128)),
                    waits=[("tensor", tk["transp"])])
            t_evacs = tk["psT_evac"]
            t_rs4 = p.emit("vector", lambda e: e.reciprocal(st8[:, ST8_RS:ST8_RS + 4], st8[:, ST8_S4:ST8_S4 + 4]),
                           waits=[("scalar", tk["exp"])])
            for c in range(16):
                w = []
                if c == 0:
                    w = [("vector", max(tk["ctxmult"], t_evacs))]
                p.emit("tensor", lambda e, h=h, c=c, st=(c == 0): e.matmul(
                    psC[0:64, :], V[:, c * 1024 + h * 64:c * 1024 + (h + 1) * 64],
                    attnT[:, c * 512:(c + 1) * 512], start=st, stop=(c == 15)), waits=w)
            t_av = p.n("tensor")
            t_rstr = p.emit("tensor", lambda e: e.transpose(psX[0:4, 300:428], st8[:, ST8_RS:ST8_RS + 4], idf[:]),
                            waits=[("vector", t_rs4)])
            t_rsrow = p.emit("vector", lambda e: e.tensor_copy(rsrow[0:4, :], psX[0:4, 300:428]),
                             waits=[("tensor", t_rstr), ("sync", tk["flat_prev"])])
            t_flat = p.emit("sync", lambda e: e.dma_start(out=rsrow1[0:1, :], in_=rsrow[0:4, :]),
                            waits=[("vector", t_rsrow), ("tensor", tk["rsmm_prev"])])
            tk["flat_prev"] = t_flat
            for qtl in range(4):
                w = []
                if qtl == 0:
                    w = [("sync", t_flat), ("scalar", tk["exp"])]
                p.emit("tensor", lambda e, qtl=qtl: e.matmul(
                    psS[:, qtl * 128:(qtl + 1) * 128], ones1[0:1, :],
                    rsrow1[0:1, qtl * 128:(qtl + 1) * 128], start=True, stop=True), waits=w)
            t_rsmm = p.n("tensor")
            tk["rsmm_prev"] = t_rsmm
            tk["rsbevac_se"] = p.emit("scalar", lambda e: e.activation(rsb[:], psS[:, 0:512], AF.Copy,
                                                                      bias=0.0, scale=1.0),
                                      waits=[("tensor", t_rsmm)])
            tk["ctxmult"] = p.emit("vector", lambda e, h=h: e.tensor_tensor(
                ctxfm[(h % 2) * 64:(h % 2) * 64 + 64, (h // 2) * 512:(h // 2 + 1) * 512],
                psC[0:64, :], rsb[0:64, :], op=OP.mult),
                waits=[("scalar", tk["rsbevac_se"]), ("tensor", t_av)])
        t_allctx = tk["ctxmult"]
        t_woev = 0
        for tsub in range(4):
            t = qb * 4 + tsub
            for oh in range(2):
                pst = psS[:, oh * 512:(oh + 1) * 512]
                for dc in range(8):
                    w = []
                    if oh == 0 and dc == 0:
                        w = [("vector", max(t_allctx, t_woev))]
                        if tsub == 0:
                            w.append(("scalar", tk["rsbevac_se"]))
                    p.emit("tensor", lambda e, tsub=tsub, oh=oh, dc=dc, pst=pst: e.matmul(
                        pst, ctxfm[:, dc * 512 + tsub * 128:dc * 512 + (tsub + 1) * 128],
                        wt[:, dc * 1024 + oh * 512:dc * 1024 + (oh + 1) * 512],
                        start=(dc == 0), stop=(dc == 7)), waits=w)
            t_wo = p.n("tensor")
            ob = outb[:, (tsub % 2) * 1024:(tsub % 2) * 1024 + 1024]
            p.emit("vector", lambda e, ob=ob: e.tensor_tensor(ob, psS[:, 0:1024], bpt[:], op=OP.add),
                   waits=[("tensor", t_wo), ("gpsimd", max(0, p.n("gpsimd") - 16))])
            t_woev = p.n("vector")
            tk["woevac"] = t_woev
            p.emit("gpsimd", lambda e, t=t, ob=ob: e.dma_start(
                out=aps["a1"][t * 128:(t + 1) * 128, :], in_=ob),
                waits=[("vector", t_woev)])



def build_fused(nc, debug=False):
    ins = {
        "x": ((S, D), F32),
        "Wq": ((D, D), F32), "Wk": ((D, D), F32), "Wv": ((D, D), F32), "Wo": ((D, D), F32),
        "b_o": ((D,), F32), "rel_k": ((65, 64), F32), "rel_v": ((65, 64), F32),
        "W1": ((D, INTER), F32), "b1": ((INTER,), F32),
        "W2": ((INTER, D), F32), "b2": ((D,), F32),
        "Wqkv": ((3 * D, D), F32), "bqkv": ((3 * D,), F32),
        "Wproj": ((D, D), F32), "bproj": ((D,), F32),
        "qn_w": ((DK,), F32), "kn_w": ((DK,), F32),
        "router_w": ((NE, D), F32), "router_b": ((NE,), F32),
        "expert_w": ((NE, D, D), F32),
        "ident_f32": ((128, 128), F32), "ident_bf": ((128, 128), BF16),
        "tri_bf": ((128, 128), BF16), "ones_col_bf": ((128, 1), BF16),
        "ones_row_f32": ((1, 128), F32),
    }
    outs = {"out": ((S, D), F32)}
    aps = declare_io(nc, ins, outs)
    for nm, shp, dt in [("xn_d", (D, S), BF16), ("xnf_d", (D, S), F32),
                        ("a_d", (S, D), F32), ("f_d", (S, D), F32),
                        ("x1_d", (S, D), F32), ("a1_d", (S, D), F32),
                        ("m1_d", (S, D), F32),
                        ("pscrP", (4 * 40960 + 256,), F32),
                        ("pscrB", (4 * 24832 + 256,), BF16)]:
        kind = "ExternalOutput" if (debug and nm not in ("pscrP", "pscrB")) else "Internal"
        aps[nm] = nc.dram_tensor(nm, list(shp), dt, kind=kind).ap()

    p = Prog(nc)
    env = {
        "idf": p.sb("idf", [128, 128], F32),
        "idb": p.sb("idb", [128, 128], BF16),
        "tri": p.sb("tri", [128, 128], BF16),
        "onec": p.sb("onec", [128, 1], BF16),
        "ones1": p.sb("ones1", [1, 128], F32),
    }
    for nm, key in [("idf", "ident_f32"), ("idb", "ident_bf"), ("tri", "tri_bf"),
                    ("onec", "ones_col_bf"), ("ones1", "ones_row_f32")]:
        p.emit("sync", lambda e, nm=nm, key=key: e.dma_start(out=env[nm][:], in_=aps[key]))

    with p.phase("ln0"):
        phase_ln(p, {"x": aps["x"], "xn_bf": aps["xn_d"]}, env, want_f32=False)
    with p.phase("attn0"):
        phase_attn0_v2(p, {"xn_bf": aps["xn_d"], "Wq": aps["Wq"], "Wk": aps["Wk"],
                        "Wv": aps["Wv"], "Wo": aps["Wo"], "b_o": aps["b_o"],
                        "rel_k": aps["rel_k"], "rel_v": aps["rel_v"], "a0": aps["a_d"],
                        "pscrP": aps["pscrP"], "pscrB": aps["pscrB"]}, env)
    with p.phase("ffn0"):
        phase_ffn0(p, {"xn_bf": aps["xn_d"], "W1": aps["W1"], "b1": aps["b1"],
                       "W2": aps["W2"], "b2": aps["b2"], "f0": aps["f_d"]}, env)
    with p.phase("comb0"):
        phase_comb(p, {"x": aps["x"], "a": aps["a_d"], "b": aps["f_d"], "o": aps["x1_d"]})
    with p.phase("ln1"):
        phase_ln(p, {"x": aps["x1_d"], "xn_bf": aps["xn_d"], "xn_f32": aps["xnf_d"]},
                 env, want_f32=True)
    with p.phase("attn1"):
        phase_attn1_v2(p, {"xn_bf": aps["xn_d"], "Wqkv": aps["Wqkv"], "bqkv": aps["bqkv"],
                        "Wproj": aps["Wproj"], "bproj": aps["bproj"], "qn_w": aps["qn_w"],
                        "kn_w": aps["kn_w"], "a1": aps["a1_d"]}, env)
    with p.phase("moe"):
        phase_moe(p, {"xn_bf": aps["xn_d"], "xn_f32": aps["xnf_d"],
                      "router_w": aps["router_w"], "router_b": aps["router_b"],
                      "expert_w": aps["expert_w"], "m1": aps["m1_d"]}, env)
    with p.phase("comb1"):
        phase_comb(p, {"x": aps["x1_d"], "a": aps["a1_d"], "b": aps["m1_d"], "o": aps["out"]})
    p.finalize()
    return nc


def kernel(L0_attn_x, L0_attn_Wq, L0_attn_Wk, L0_attn_Wv, L0_attn_Wo, L0_attn_b_o,
           L0_attn_rel_k_emb, L0_attn_rel_v_emb, L0_ffn_W1, L0_ffn_b1, L0_ffn_W2,
           L0_ffn_b2, L1_attn_Wqkv, L1_attn_bqkv, L1_attn_Wproj, L1_attn_bproj,
           L1_attn_qn_w, L1_attn_kn_w, L1_ffn_router_w, L1_ffn_router_b,
           L1_ffn_expert_w):
    C = CONSTS
    x = np.ascontiguousarray(np.asarray(L0_attn_x, dtype=np.float32))
    shared = {
        "Wq": L0_attn_Wq, "Wk": L0_attn_Wk, "Wv": L0_attn_Wv, "Wo": L0_attn_Wo,
        "b_o": L0_attn_b_o, "rel_k": L0_attn_rel_k_emb, "rel_v": L0_attn_rel_v_emb,
        "W1": L0_ffn_W1, "b1": L0_ffn_b1, "W2": L0_ffn_W2, "b2": L0_ffn_b2,
        "Wqkv": L1_attn_Wqkv, "bqkv": L1_attn_bqkv, "Wproj": L1_attn_Wproj,
        "bproj": L1_attn_bproj, "qn_w": L1_attn_qn_w, "kn_w": L1_attn_kn_w,
        "router_w": L1_ffn_router_w, "router_b": L1_ffn_router_b,
        "expert_w": L1_ffn_expert_w,
    }
    shared = {k: np.ascontiguousarray(np.asarray(v, dtype=np.float32)) for k, v in shared.items()}
    shared.update(C)
    in_maps = []
    for b in range(NCORES):
        d = dict(shared)
        d["x"] = np.ascontiguousarray(x[b])
        in_maps.append(d)
    nc = bass.Bass("TRN2", target_bir_lowering=False)
    build_fused(nc)
    res = run_bass_kernel_spmd(nc, in_maps, core_ids=list(range(NCORES)))
    return np.stack([np.asarray(res.results[i]["out"], dtype=np.float32)
                     for i in range(NCORES)], axis=0)


# version 05
# speedup vs baseline: 1.3803x; 1.0916x over previous
import contextlib
import numpy as np
import ml_dtypes
import concourse.bass as bass
import concourse.mybir as mybir
from concourse.alu_op_type import AluOpType
from concourse.bass_utils import run_bass_kernel_spmd

F32 = mybir.dt.float32
BF16 = mybir.dt.bfloat16
AF = mybir.ActivationFunctionType
AX = mybir.AxisListType
OP = AluOpType

D = 1024
S = 2048
H = 16
DK = 64
NCORES = 8
INTER = 4096
NE = 16

ENGS = ("sync", "gpsimd", "vector", "scalar", "tensor")


class Sem:
    __slots__ = ("h", "step", "n")

    def __init__(self, h, step):
        self.h = h
        self.step = step
        self.n = 0


class Prog:

    def __init__(self, nc):
        self.nc = nc
        self.q = {e: [] for e in ENGS}
        self.stack = contextlib.ExitStack()
        self.scopes = [self.stack]
        self._uid = 0
        self._barrier = {}
        self.sem = {}
        for e in ENGS:
            step = 16 if e in ("sync", "gpsimd") else 1
            self.sem[e] = Sem(self.stack.enter_context(nc.semaphore("S" + e)), step)
        self._cpool = self.stack.enter_context(nc.sbuf_tensor("cpool", [128, 16], F32))
        self._cmap = {}

    def n(self, eng):
        return self.sem[eng].n

    def sb(self, name, shape, dtype=F32):
        self._uid += 1
        return self.scopes[-1].enter_context(
            self.nc.sbuf_tensor(f"{name}_{self._uid}", shape, dtype))

    @contextlib.contextmanager
    def phase(self, name):
        st = contextlib.ExitStack()
        self.scopes.append(st)
        try:
            yield
        finally:
            self.scopes.pop()
            st.close()
            snap = [(e, self.sem[e].n) for e in ENGS]
            for e in ENGS:
                self._barrier[e] = snap

    def cc(self, val):
        if val not in self._cmap:
            col = len(self._cmap)
            self._cmap[val] = col
            self.emit("vector", lambda e, col=col, val=val: e.memset(self._cpool[:, col:col + 1], val))
        c = self._cmap[val]
        return self._cpool[:, c:c + 1]

    def ps(self, name, shape, dtype=F32):
        self._uid += 1
        return self.scopes[-1].enter_context(
            self.nc.psum_tensor(f"{name}_{self._uid}", shape, dtype))

    def emit(self, eng, fn, waits=()):
        sem = self.sem[eng]
        waits = list(waits)
        if eng in self._barrier:
            waits.extend((e2, n) for (e2, n) in self._barrier.pop(eng) if e2 != eng)
        wl = []
        for e2, n in waits:
            s2 = self.sem[e2]
            assert n <= s2.n, f"wait on future value of {e2}: {n} > {s2.n}"
            if n > 0:
                wl.append((s2.h, n))
        byh = {}
        for h, n in wl:
            byh[h] = max(n, byh.get(h, 0))
        wl = tuple(byh.items())

        if eng in ("vector", "scalar"):
            def cl(e, fn=fn, wl=wl, sem=sem):
                for h, n in wl:
                    e.wait_ge(h, n)
                fn(e)
                e.drain().then_inc(sem.h, sem.step)
        else:
            def cl(e, fn=fn, wl=wl, sem=sem):
                for h, n in wl:
                    e.wait_ge(h, n)
                fn(e).then_inc(sem.h, sem.step)

        self.q[eng].append(cl)
        sem.n += sem.step
        return sem.n

    def wother(self, eng):
        return [(e, self.sem[e].n) for e in ENGS if e != eng]

    def finalize(self):
        snap = [(e, self.sem[e].n) for e in ENGS]

        def fin(e, snap=snap):
            for eng, n in snap:
                if n > 0:
                    e.wait_ge(self.sem[eng].h, n)
        self.q["gpsimd"].append(fin)
        with self.nc.Block() as block:
            for eng in ENGS:
                lst = self.q[eng]
                if lst:
                    getattr(block, eng)(lambda e, lst=lst: [c(e) for c in lst])
        self.stack.close()


def np_bf16(a):
    return np.asarray(a).astype(ml_dtypes.bfloat16)


def consts_np():
    return {
        "ident_f32": np.eye(128, dtype=np.float32),
        "ident_bf": np_bf16(np.eye(128)),
        "tri_bf": np_bf16(np.tril(np.ones((128, 128)))),
        "ones_col_bf": np_bf16(np.ones((128, 1))),
        "ones_row_f32": np.ones((1, 128), dtype=np.float32),
    }


CONSTS = consts_np()


def declare_io(nc, ins, outs):
    aps = {}
    for n, (shp, dt) in ins.items():
        aps[n] = nc.dram_tensor(n, list(shp), dt, kind="ExternalInput").ap()
    for n, (shp, dt) in outs.items():
        aps[n] = nc.dram_tensor(n, list(shp), dt, kind="ExternalOutput").ap()
    return aps



def emit_weight_T(p, w_dram, O, I, wt_sb, wstage, psT, idf):
    nIC, nOC = I // 128, O // 128
    nbuf = max(1, wstage.shape[1] // I)
    for oc in range(nOC):
        st = wstage[:, (oc % nbuf) * I:(oc % nbuf) * I + I]
        p.emit("sync", lambda e, oc=oc, st=st: e.dma_start(
            out=st, in_=w_dram[oc * 128:(oc + 1) * 128, :]),
            waits=[("tensor", p.n("tensor"))])
        for ic in range(nIC):
            p.emit("tensor", lambda e, ic=ic, st=st: e.transpose(
                psT[:, ic * 128:(ic + 1) * 128], st[:, ic * 128:(ic + 1) * 128], idf[:]),
                waits=[("sync", p.n("sync")), ("vector", p.n("vector")), ("scalar", p.n("scalar"))])
        p.emit("vector", lambda e, oc=oc: e.tensor_copy(
            wt_sb[:].rearrange("p (ic o) -> p ic o", o=O)[:, :, oc * 128:(oc + 1) * 128],
            psT[:, 0:nIC * 128].rearrange("p (ic q) -> p ic q", q=128)),
            waits=[("tensor", p.n("tensor"))])


def emit_bcast_row(p, vec_dram, n, out_tile, row_sb, ones1, psB):
    p.emit("sync", lambda e, n=n: e.dma_start(
        out=row_sb[0:1, 0:n], in_=vec_dram[0:n].rearrange("(a b) -> a b", a=1)),
        waits=[("tensor", p.n("tensor"))])
    for c in range((n + 511) // 512):
        w = min(512, n - c * 512)
        p.emit("tensor", lambda e, c=c, w=w: e.matmul(
            psB[:, c * 512:c * 512 + w], ones1[0:1, :], row_sb[0:1, c * 512:c * 512 + w],
            start=True, stop=True),
            waits=[("sync", p.n("sync")), ("vector", p.n("vector")), ("scalar", p.n("scalar"))])
    p.emit("vector", lambda e, n=n: e.tensor_copy(out_tile[:, 0:n], psB[:, 0:n]),
           waits=[("tensor", p.n("tensor"))])


def hsl(sb, h, c0, w, rows=64):
    oc = h // 2
    r = (h % 2) * 64
    return sb[r:r + rows, oc * 2048 + c0:oc * 2048 + c0 + w]



def phase_ln(p, io, env, want_f32):
    aps = io
    idf = env["idf"]
    xsb = p.sb("xsb", [128, 16 * 1024], F32)
    sc = p.sb("sc", [128, 1024], F32)
    stats = p.sb("stats", [128, 16], F32)
    xnbf = p.sb("xnbf", [128, 8 * 2048], BF16)
    xnf = p.sb("xnf", [128, 8 * 2048], F32) if want_f32 else None
    psA = p.ps("psA", [128, 1024], F32)
    psB = p.ps("psB", [128, 1024], F32)

    ld0 = p.n("sync")
    for t in range(16):
        p.emit("sync", lambda e, t=t: e.dma_start(
            out=xsb[:, t * 1024:(t + 1) * 1024], in_=aps["x"][t * 128:(t + 1) * 128, :]))

    for t in range(16):
        xt = xsb[:, t * 1024:(t + 1) * 1024]
        p.emit("vector", lambda e, xt=xt: e.tensor_reduce(stats[:, 0:1], xt, axis=AX.X, op=OP.add),
               waits=[("sync", ld0 + 16 * (t + 1))])
        p.emit("vector", lambda e, xt=xt: e.tensor_tensor(sc[:], xt, xt, op=OP.mult))
        p.emit("vector", lambda e: e.tensor_reduce(stats[:, 1:2], sc[:], axis=AX.X, op=OP.add))
        p.emit("vector", lambda e: e.tensor_scalar(stats[:, 2:3], stats[:, 0:1], 1.0 / D, None, op0=OP.mult))
        p.emit("vector", lambda e: e.tensor_tensor(stats[:, 3:4], stats[:, 2:3], stats[:, 2:3], op=OP.mult))
        rD = p.cc(1.0 / D)
        p.emit("vector", lambda e, rD=rD: e.tensor_scalar(stats[:, 4:5], stats[:, 1:2], rD, stats[:, 3:4], op0=OP.mult, op1=OP.subtract))
        eps_ln = p.cc(1e-5)
        p.emit("scalar", lambda e, eps_ln=eps_ln: e.activation(stats[:, 5:6], stats[:, 4:5], AF.Ln, bias=eps_ln, scale=1.0),
               waits=[("vector", p.n("vector"))])
        p.emit("scalar", lambda e: e.activation(stats[:, 6:7], stats[:, 5:6], AF.Exp, bias=0.0, scale=-0.5))
        p.emit("vector", lambda e, xt=xt: e.tensor_scalar(
            sc[:], xt, stats[:, 2:3], stats[:, 6:7], op0=OP.subtract, op1=OP.mult),
            waits=[("scalar", p.n("scalar"))])
        pst = psA if t % 2 == 0 else psB
        for c in range(8):
            p.emit("tensor", lambda e, c=c, pst=pst: e.transpose(
                pst[:, c * 128:(c + 1) * 128], sc[:, c * 128:(c + 1) * 128], idf[:]),
                waits=[("vector", p.n("vector"))])
        p.emit("vector", lambda e, t=t, pst=pst: e.tensor_copy(
            xnbf[:].rearrange("p (c s) -> p c s", s=S)[:, :, t * 128:(t + 1) * 128],
            pst[:].rearrange("p (c q) -> p c q", q=128)),
            waits=[("tensor", p.n("tensor"))])
        if want_f32:
            p.emit("vector", lambda e, t=t, pst=pst: e.tensor_copy(
                xnf[:].rearrange("p (c s) -> p c s", s=S)[:, :, t * 128:(t + 1) * 128],
                pst[:].rearrange("p (c q) -> p c q", q=128)))

    for c in range(8):
        p.emit("gpsimd", lambda e, c=c: e.dma_start(
            out=aps["xn_bf"][c * 128:(c + 1) * 128, :], in_=xnbf[:, c * S:(c + 1) * S]),
            waits=[("vector", p.n("vector"))])
        if want_f32:
            p.emit("gpsimd", lambda e, c=c: e.dma_start(
                out=aps["xn_f32"][c * 128:(c + 1) * 128, :], in_=xnf[:, c * S:(c + 1) * S]))



def phase_comb(p, io):
    aps = io
    xa = p.sb("xa", [128, 2 * 1024], F32)
    xb = p.sb("xb", [128, 2 * 1024], F32)
    xc = p.sb("xc", [128, 2 * 1024], F32)
    xo = p.sb("xo", [128, 2 * 1024], F32)
    for t in range(16):
        b = (t % 2) * 1024
        p.emit("sync", lambda e, t=t, b=b: e.dma_start(out=xa[:, b:b + 1024], in_=aps["x"][t * 128:(t + 1) * 128, :]),
               waits=[("vector", p.n("vector"))])
        p.emit("sync", lambda e, t=t, b=b: e.dma_start(out=xb[:, b:b + 1024], in_=aps["a"][t * 128:(t + 1) * 128, :]))
        p.emit("sync", lambda e, t=t, b=b: e.dma_start(out=xc[:, b:b + 1024], in_=aps["b"][t * 128:(t + 1) * 128, :]))
        p.emit("vector", lambda e, b=b: e.tensor_tensor(xo[:, b:b + 1024], xa[:, b:b + 1024], xb[:, b:b + 1024], op=OP.add),
               waits=[("sync", p.n("sync")), ("gpsimd", max(0, p.n("gpsimd") - 16))])
        p.emit("vector", lambda e, b=b: e.tensor_tensor(xo[:, b:b + 1024], xo[:, b:b + 1024], xc[:, b:b + 1024], op=OP.add))
        p.emit("gpsimd", lambda e, t=t, b=b: e.dma_start(out=aps["o"][t * 128:(t + 1) * 128, :], in_=xo[:, b:b + 1024]),
               waits=[("vector", p.n("vector"))])



def phase_ffn0(p, io, env):
    aps = io
    ones1 = env["ones1"]
    xn = p.sb("xn", [128, 8 * 2048], BF16)
    w1h = p.sb("w1h", [128, 8 * 2048], BF16)
    w2h = p.sb("w2h", [128, 16 * 1024], BF16)
    hbuf = p.sb("hbuf", [128, 16 * 512], BF16)
    b1c = p.sb("b1c", [128, 32], F32)
    acc = p.sb("acc", [128, 16 * 1024], F32)
    b2t = p.sb("b2t", [128, 1024], F32)
    brow = p.sb("brow", [1, 1024], F32)
    outb = p.sb("outb", [128, 2 * 1024], F32)
    psA = p.ps("psA", [128, 512], F32)
    psB = p.ps("psB", [128, 512], F32)
    psC = p.ps("psC", [128, 1024], F32)
    psD = p.ps("psD", [128, 1024], F32)

    for c in range(8):
        p.emit("sync", lambda e, c=c: e.dma_start(out=xn[:, c * 2048:(c + 1) * 2048],
                                                  in_=aps["xn_bf"][c * 128:(c + 1) * 128, :]))
    b1st = p.sb("b1st", [32, 128], F32)
    idf2 = env["idf"]
    p.emit("sync", lambda e: e.dma_start(out=b1st[:], in_=aps["b1"].rearrange("(ic p) -> ic p", p=128)))
    p.emit("tensor", lambda e: e.transpose(psC[:, 0:32], b1st[:, 0:128], idf2[0:32, 0:32]),
           waits=[("sync", p.n("sync")), ("gpsimd", p.n("gpsimd"))])
    p.emit("vector", lambda e: e.tensor_copy(b1c[:], psC[:, 0:32]), waits=[("tensor", p.n("tensor"))])
    emit_bcast_row(p, aps["b2"], 1024, b2t, brow, ones1, psC)

    for ih in range(2):
        for dc in range(8):
            p.emit("gpsimd", lambda e, dc=dc, ih=ih: e.dma_start(
                out=w1h[:, dc * 2048:(dc + 1) * 2048],
                in_=aps["W1"][dc * 128:(dc + 1) * 128, ih * 2048:(ih + 1) * 2048]),
                waits=[("tensor", p.n("tensor"))])
        for ic in range(16):
            p.emit("gpsimd", lambda e, ic=ic, ih=ih: e.dma_start(
                out=w2h[:, ic * 1024:(ic + 1) * 1024],
                in_=aps["W2"][ih * 2048 + ic * 128:ih * 2048 + (ic + 1) * 128, :]),
                waits=[("tensor", p.n("tensor"))])
        for qb in range(4):
            for ic in range(16):
                pst = psA if ic % 2 == 0 else psB
                for dc in range(8):
                    p.emit("tensor", lambda e, ic=ic, dc=dc, qb=qb, pst=pst: e.matmul(
                        pst[:], w1h[:, dc * 2048 + ic * 128:dc * 2048 + (ic + 1) * 128],
                        xn[:, dc * 2048 + qb * 512:dc * 2048 + (qb + 1) * 512],
                        start=(dc == 0), stop=(dc == 7)),
                        waits=[("gpsimd", p.n("gpsimd")), ("sync", p.n("sync")), ("scalar", p.n("scalar"))])
                icg = ih * 16 + ic
                p.emit("scalar", lambda e, ic=ic, icg=icg, pst=pst: e.activation(
                    hbuf[:, ic * 512:(ic + 1) * 512], pst[:], AF.Relu,
                    bias=b1c[:, icg:icg + 1], scale=1.0),
                    waits=[("tensor", p.n("tensor"))])
            for tsub in range(4):
                t = qb * 4 + tsub
                pst = psC if tsub % 2 == 0 else psD
                for ic in range(16):
                    for oc in range(2):
                        p.emit("tensor", lambda e, ic=ic, oc=oc, tsub=tsub, pst=pst: e.matmul(
                            pst[:, oc * 512:(oc + 1) * 512],
                            hbuf[:, ic * 512 + tsub * 128:ic * 512 + (tsub + 1) * 128],
                            w2h[:, ic * 1024 + oc * 512:ic * 1024 + (oc + 1) * 512],
                            start=(ic == 0), stop=(ic == 15)),
                            waits=[("scalar", p.n("scalar")), ("vector", p.n("vector"))])
                if ih == 0:
                    p.emit("vector", lambda e, t=t, pst=pst: e.tensor_copy(
                        acc[:, t * 1024:(t + 1) * 1024], pst[:]), waits=[("tensor", p.n("tensor"))])
                else:
                    p.emit("vector", lambda e, t=t, pst=pst: e.tensor_tensor(
                        acc[:, t * 1024:(t + 1) * 1024], acc[:, t * 1024:(t + 1) * 1024], pst[:], op=OP.add),
                        waits=[("tensor", p.n("tensor"))])
                    ob = outb[:, (t % 2) * 1024:(t % 2) * 1024 + 1024]
                    p.emit("vector", lambda e, ob=ob, t=t: e.tensor_tensor(
                        ob, acc[:, t * 1024:(t + 1) * 1024], b2t[:], op=OP.add),
                        waits=[("gpsimd", max(0, p.n("gpsimd") - 16))])
                    p.emit("gpsimd", lambda e, t=t, ob=ob: e.dma_start(
                        out=aps["f0"][t * 128:(t + 1) * 128, :], in_=ob),
                        waits=[("vector", p.n("vector"))])



def phase_attn0(p, io, env):
    aps = io
    pscrP = io["pscrP"]
    pscrB = io["pscrB"]
    idf = env["idf"]
    idb = env["idb"]
    tri = env["tri"]
    onec = env["onec"]
    ones1 = env["ones1"]
    xn = p.sb("xn", [128, 8 * 2048], BF16)
    wt = p.sb("wt", [128, 8 * 1024], BF16)
    wstage = p.sb("wstage", [128, 2 * 1024], F32)
    Q = p.sb("Q", [128, 8 * 2048], BF16)
    K = p.sb("K", [128, 8 * 2048], BF16)
    V = p.sb("V", [128, 16 * 1024], BF16)
    attn = p.sb("attn", [128, 2112], BF16)
    attnT = p.sb("attnT", [128, 16 * 512], BF16)
    ctxfm = p.sb("ctxfm", [128, 8 * 512], BF16)
    Wwin = p.sb("Wwin", [128, 192], F32)
    QRt = p.sb("QRt", [128, 65], F32)
    Ptab = p.sb("Ptab", [128, 319], F32)
    Dg = p.sb("Dg", [128, 65], BF16)
    Atile = p.sb("Atile", [128, 65], BF16)
    Arhs = p.sb("Arhs", [128, 128], BF16)
    crn = p.sb("crn", [128, 128], BF16)
    st8 = p.sb("st8", [128, 16], F32)
    relkT = p.sb("relkT", [128, 65], BF16)
    relvN = p.sb("relvN", [128, 64], BF16)
    relst = p.sb("relst", [128, 64], F32)
    bot = p.sb("bot", [128, 1024], F32)
    brow = p.sb("brow", [1, 1024], F32)
    outb = p.sb("outb", [128, 2 * 1024], F32)

    psS = p.ps("psS", [128, 2048], F32)
    psT = p.ps("psT", [128, 2048], BF16)
    psC = p.ps("psC", [128, 512], F32)
    psX = p.ps("psX", [128, 512], F32)

    p.emit("sync", lambda e: e.dma_start(out=relst[0:65, :], in_=aps["rel_k"]))
    for c in range(8):
        p.emit("sync", lambda e, c=c: e.dma_start(out=xn[:, c * 2048:(c + 1) * 2048],
                                                  in_=aps["xn_bf"][c * 128:(c + 1) * 128, :]))
    p.emit("gpsimd", lambda e: e.dma_start(out=relvN[0:65, :], in_=aps["rel_v"]))
    p.emit("tensor", lambda e: e.transpose(psX[0:64, 0:65], relst[0:65, 0:64], idf[0:65, 0:65]),
           waits=[("sync", p.n("sync"))])
    p.emit("vector", lambda e: e.tensor_copy(relkT[0:64, 0:65], psX[0:64, 0:65]),
           waits=[("tensor", p.n("tensor"))])
    p.emit("vector", lambda e: e.tensor_copy(relkT[64:128, 0:65], psX[0:64, 0:65]))
    emit_bcast_row(p, aps["b_o"], 1024, bot, brow, ones1, psS)

    def proj_fm(out_sb, scale):
        for oc in range(8):
            for qb in range(4):
                pst = psS[:, (qb % 2) * 512:(qb % 2) * 512 + 512]
                for dc in range(8):
                    p.emit("tensor", lambda e, oc=oc, qb=qb, dc=dc, pst=pst: e.matmul(
                        pst, wt[:, dc * 1024 + oc * 128:dc * 1024 + (oc + 1) * 128],
                        xn[:, dc * 2048 + qb * 512:dc * 2048 + (qb + 1) * 512],
                        start=(dc == 0), stop=(dc == 7)),
                        waits=[("vector", p.n("vector")), ("scalar", p.n("scalar")), ("sync", p.n("sync"))])
                p.emit("scalar", lambda e, oc=oc, qb=qb, pst=pst, scale=scale: e.activation(
                    out_sb[:, oc * 2048 + qb * 512:oc * 2048 + (qb + 1) * 512], pst,
                    AF.Copy, bias=0.0, scale=scale),
                    waits=[("tensor", p.n("tensor"))])

    def proj_tm(out_sb):
        for t in range(16):
            for oh in range(2):
                pst = psS[:, (oh % 2) * 512:(oh % 2) * 512 + 512]
                for dc in range(8):
                    p.emit("tensor", lambda e, t=t, oh=oh, dc=dc, pst=pst: e.matmul(
                        pst, xn[:, dc * 2048 + t * 128:dc * 2048 + (t + 1) * 128],
                        wt[:, dc * 1024 + oh * 512:dc * 1024 + (oh + 1) * 512],
                        start=(dc == 0), stop=(dc == 7)),
                        waits=[("vector", p.n("vector")), ("scalar", p.n("scalar")), ("sync", p.n("sync"))])
                p.emit("scalar", lambda e, t=t, oh=oh, pst=pst: e.activation(
                    out_sb[:, t * 1024 + oh * 512:t * 1024 + (oh + 1) * 512], pst,
                    AF.Copy, bias=0.0, scale=1.0),
                    waits=[("tensor", p.n("tensor"))])

    emit_weight_T(p, aps["Wq"], D, D, wt, wstage, psS, idf)
    proj_fm(Q, 0.125)
    emit_weight_T(p, aps["Wk"], D, D, wt, wstage, psS, idf)
    proj_fm(K, 1.0)
    emit_weight_T(p, aps["Wv"], D, D, wt, wstage, psS, idf)
    proj_tm(V)
    emit_weight_T(p, aps["Wo"], D, D, wt, wstage, psS, idf)

    p.emit("vector", lambda e: e.memset(attn[:, 0:32], 0.0))
    p.emit("vector", lambda e: e.memset(attn[:, 2080:2112], 0.0))

    for qb in range(4):
        for h in range(H):
            for qtl in range(4):
                qt = qb * 4 + qtl
                q0 = qt * 128
                for kb in range(4):
                    p.emit("tensor", lambda e, h=h, qt=qt, kb=kb: e.matmul(
                        psS[:, kb * 512:(kb + 1) * 512],
                        hsl(Q, h, qt * 128, 128), hsl(K, h, kb * 512, 512),
                        start=True, stop=True),
                        waits=[("vector", p.n("vector")), ("scalar", p.n("scalar"))])
                p.emit("tensor", lambda e, h=h, qt=qt: e.matmul(
                    psX[:, 0:65], hsl(Q, h, qt * 128, 128),
                    relkT[(h % 2) * 64:(h % 2) * 64 + 64, 0:65],
                    start=True, stop=True))
                p.emit("vector", lambda e: e.tensor_copy(QRt[:], psX[:, 0:65]),
                       waits=[("tensor", p.n("tensor"))])
                p.emit("vector", lambda e: e.tensor_copy(Ptab[:, 0:127], QRt[:, 0:1].broadcast_to([128, 127])))
                p.emit("vector", lambda e: e.tensor_copy(Ptab[:, 127:192], QRt[:]))
                p.emit("vector", lambda e: e.tensor_copy(Ptab[:, 192:319], QRt[:, 64:65].broadcast_to([128, 127])))
                p.emit("sync", lambda e: e.dma_start(
                    out=pscrP[0:128 * 319].rearrange("(q w) -> q w", w=319), in_=Ptab[:]),
                    waits=[("vector", p.n("vector"))])
                p.emit("sync", lambda e: e.dma_start(
                    out=Wwin[:], in_=pscrP[127:127 + 128 * 318].rearrange("(q w) -> q w", w=318)[:, 0:192]))
                wadd = [("sync", p.n("sync")), ("tensor", p.n("tensor"))]
                if qt == 0:
                    p.emit("vector", lambda e: e.tensor_tensor(psS[:, 0:160], psS[:, 0:160], Wwin[:, 32:192], op=OP.add), waits=wadd)
                elif qt == 15:
                    p.emit("vector", lambda e: e.tensor_tensor(psS[:, 1888:2048], psS[:, 1888:2048], Wwin[:, 0:160], op=OP.add), waits=wadd)
                else:
                    p.emit("vector", lambda e, q0=q0: e.tensor_tensor(psS[:, q0 - 32:q0 + 160], psS[:, q0 - 32:q0 + 160], Wwin[:, 0:192], op=OP.add), waits=wadd)
                if qt > 0:
                    p.emit("vector", lambda e, q0=q0: e.tensor_scalar(psS[:, 0:q0 - 32], psS[:, 0:q0 - 32], QRt[:, 0:1], None, op0=OP.add))
                if qt < 15:
                    p.emit("vector", lambda e, q0=q0: e.tensor_scalar(psS[:, q0 + 160:2048], psS[:, q0 + 160:2048], QRt[:, 64:65], None, op0=OP.add))
                p.emit("vector", lambda e: e.tensor_reduce(st8[:, 0:1], psS[:], axis=AX.X, op=OP.max, negate=True))
                p.emit("scalar", lambda e: e.activation(attn[:, 32:2080], psS[:], AF.Exp,
                                                        bias=st8[:, 0:1], scale=1.0, accum_out=st8[:, 1:2]),
                       waits=[("vector", p.n("vector"))])
                p.emit("vector", lambda e: e.reciprocal(st8[:, 2:3], st8[:, 1:2]),
                       waits=[("scalar", p.n("scalar"))])
                p.emit("vector", lambda e: e.tensor_scalar(attn[:, 32:2080], attn[:, 32:2080], st8[:, 2:3], None, op0=OP.mult))
                p.emit("vector", lambda e, q0=q0: e.tensor_tensor(crn[:], attn[:, q0:q0 + 128], tri[:], op=OP.mult))
                p.emit("vector", lambda e: e.tensor_reduce(st8[:, 3:4], crn[:], axis=AX.X, op=OP.add))
                p.emit("sync", lambda e, q0=q0: e.dma_start(
                    out=pscrB[0:128 * 192].rearrange("(q w) -> q w", w=192), in_=attn[:, q0:q0 + 192]),
                    waits=[("vector", p.n("vector"))])
                p.emit("sync", lambda e: e.dma_start(
                    out=Dg[:], in_=pscrB[0:128 * 193].rearrange("(q w) -> q w", w=193)[:, 0:65]))
                for c in range(16):
                    p.emit("tensor", lambda e, c=c: e.transpose(
                        psT[:, c * 128:(c + 1) * 128], attn[:, 32 + c * 128:32 + (c + 1) * 128], idb[:]),
                        waits=[("vector", p.n("vector"))])
                p.emit("vector", lambda e, qtl=qtl: e.tensor_copy(
                    attnT[:].rearrange("p (c q) -> p c q", q=512)[:, :, qtl * 128:(qtl + 1) * 128],
                    psT[:].rearrange("p (c q) -> p c q", q=128)),
                    waits=[("tensor", p.n("tensor"))])
                nfull = qt - 1
                for c in range(max(0, nfull)):
                    p.emit("tensor", lambda e, c=c, qtl=qtl: e.matmul(
                        psX[:, 128:129], attnT[:, c * 512 + qtl * 128:c * 512 + (qtl + 1) * 128],
                        onec[:], start=(c == 0), stop=False),
                        waits=[("vector", p.n("vector"))])
                if qt >= 1:
                    p.emit("tensor", lambda e, qt=qt, qtl=qtl, st=(nfull <= 0): e.matmul(
                        psX[:, 128:129],
                        attnT[0:96, (qt - 1) * 512 + qtl * 128:(qt - 1) * 512 + (qtl + 1) * 128],
                        onec[0:96, :], start=st, stop=True),
                        waits=[("vector", p.n("vector"))])
                aw = [("tensor", p.n("tensor")), ("sync", p.n("sync"))]
                if qt >= 1:
                    p.emit("vector", lambda e: e.tensor_tensor(st8[:, 4:5], psX[:, 128:129], st8[:, 3:4], op=OP.add), waits=aw)
                else:
                    p.emit("vector", lambda e: e.tensor_copy(st8[:, 4:5], st8[:, 3:4]), waits=aw)
                p.emit("vector", lambda e: e.tensor_reduce(st8[:, 5:6], Dg[:, 1:64], axis=AX.X, op=OP.add))
                p.emit("vector", lambda e: e.tensor_tensor(st8[:, 6:7], st8[:, 4:5], st8[:, 5:6], op=OP.add))
                p.emit("vector", lambda e: e.tensor_scalar(st8[:, 7:8], st8[:, 6:7], -1.0, 1.0, op0=OP.mult, op1=OP.add))
                p.emit("vector", lambda e: e.tensor_copy(Atile[:, 1:64], Dg[:, 1:64]))
                p.emit("vector", lambda e: e.tensor_copy(Atile[:, 0:1], st8[:, 4:5]))
                p.emit("vector", lambda e: e.tensor_copy(Atile[:, 64:65], st8[:, 7:8]))
                p.emit("tensor", lambda e: e.transpose(psT[0:65, 0:128], Atile[:], idb[:]),
                       waits=[("vector", p.n("vector"))])
                p.emit("vector", lambda e: e.tensor_copy(Arhs[0:65, 0:128], psT[0:65, 0:128]),
                       waits=[("tensor", p.n("tensor"))])
                p.emit("tensor", lambda e, qtl=qtl: e.matmul(
                    psC[0:64, qtl * 128:(qtl + 1) * 128], relvN[0:65, 0:64], Arhs[0:65, 0:128],
                    start=(qtl == 0), stop=False),
                    waits=[("vector", p.n("vector"))])
            for c in range(16):
                p.emit("tensor", lambda e, h=h, c=c: e.matmul(
                    psC[0:64, :], V[:, c * 1024 + h * 64:c * 1024 + (h + 1) * 64],
                    attnT[:, c * 512:(c + 1) * 512], start=False, stop=(c == 15)),
                    waits=[("vector", p.n("vector"))])
            p.emit("vector", lambda e, h=h: e.tensor_copy(
                ctxfm[(h % 2) * 64:(h % 2) * 64 + 64, (h // 2) * 512:(h // 2 + 1) * 512], psC[0:64, :]),
                waits=[("tensor", p.n("tensor"))])
        for tsub in range(4):
            t = qb * 4 + tsub
            for oh in range(2):
                pst = psS[:, oh * 512:(oh + 1) * 512]
                for dc in range(8):
                    p.emit("tensor", lambda e, tsub=tsub, oh=oh, dc=dc, pst=pst: e.matmul(
                        pst, ctxfm[:, dc * 512 + tsub * 128:dc * 512 + (tsub + 1) * 128],
                        wt[:, dc * 1024 + oh * 512:dc * 1024 + (oh + 1) * 512],
                        start=(dc == 0), stop=(dc == 7)),
                        waits=[("vector", p.n("vector")), ("scalar", p.n("scalar"))])
            ob = outb[:, (t % 2) * 1024:(t % 2) * 1024 + 1024]
            p.emit("vector", lambda e, ob=ob: e.tensor_tensor(ob, psS[:, 0:1024], bot[:], op=OP.add),
                   waits=[("tensor", p.n("tensor")), ("gpsimd", max(0, p.n("gpsimd") - 16))])
            p.emit("gpsimd", lambda e, t=t, ob=ob: e.dma_start(
                out=aps["a0"][t * 128:(t + 1) * 128, :], in_=ob),
                waits=[("vector", p.n("vector"))])



def phase_attn1(p, io, env):
    aps = io
    idf = env["idf"]
    idb = env["idb"]
    ones1 = env["ones1"]
    xn = p.sb("xn", [128, 8 * 2048], BF16)
    wt = p.sb("wt", [128, 8 * 1024], BF16)
    wstage = p.sb("wstage", [128, 1024], F32)
    Q = p.sb("Q", [128, 8 * 2048], BF16)
    K = p.sb("K", [128, 8 * 2048], BF16)
    V = p.sb("V", [128, 16 * 1024], BF16)
    attn = p.sb("attn", [128, 2048], BF16)
    attnT = p.sb("attnT", [128, 16 * 512], BF16)
    ctxfm = p.sb("ctxfm", [128, 8 * 512], BF16)
    qkb = p.sb("qkb", [128, 1024], BF16)
    rst = p.sb("rst", [128, 16], F32)
    wnt = p.sb("wnt", [128, 1024], F32)
    st8 = p.sb("st8", [128, 16], F32)
    bqt = p.sb("bqt", [128, 1024], F32)
    bpt = p.sb("bpt", [128, 1024], F32)
    brow = p.sb("brow", [1, 1024], F32)
    wrow = p.sb("wrow", [1, 64], F32)
    outb = p.sb("outb", [128, 2 * 1024], F32)
    qkt = outb[:, 0:1024]
    sq = outb[:, 1024:2048]

    psS = p.ps("psS", [128, 2048], F32)
    psT = p.ps("psT", [128, 2048], BF16)
    psC = p.ps("psC", [128, 512], F32)

    for c in range(8):
        p.emit("sync", lambda e, c=c: e.dma_start(out=xn[:, c * 2048:(c + 1) * 2048],
                                                  in_=aps["xn_bf"][c * 128:(c + 1) * 128, :]))
    emit_bcast_row(p, aps["bproj"], 1024, bpt, brow, ones1, psS)

    def emit_wn_tile(nm):
        p.emit("sync", lambda e, nm=nm: e.dma_start(out=wrow[0:1, :], in_=aps[nm][0:DK].rearrange("(a b) -> a b", a=1)),
               waits=[("tensor", p.n("tensor"))])
        p.emit("tensor", lambda e: e.matmul(psS[:, 0:64], ones1[0:1, :], wrow[0:1, :], start=True, stop=True),
               waits=[("sync", p.n("sync")), ("vector", p.n("vector"))])
        for j in range(16):
            p.emit("vector", lambda e, j=j: e.tensor_copy(
                wnt[:, j * 64:(j + 1) * 64], psS[:, 0:64]),
                waits=[("tensor", p.n("tensor"))])

    for part, dest in ((0, Q), (1, K), (2, V)):
        emit_bcast_row(p, aps["bqkv"][part * D:(part + 1) * D, 0:1024] if False else aps["bqkv"][part * D:(part + 1) * D], 1024,
                       bqt, brow, ones1, psS)
        if part == 0:
            emit_wn_tile("qn_w")
        elif part == 1:
            emit_wn_tile("kn_w")
        emit_weight_T(p, aps["Wqkv"][part * D:(part + 1) * D, :], D, D, wt, wstage, psS, idf)
        for t in range(16):
            for oh in range(2):
                pst = psS[:, oh * 512:oh * 512 + 512]
                for dc in range(8):
                    p.emit("tensor", lambda e, t=t, oh=oh, dc=dc, pst=pst: e.matmul(
                        pst, xn[:, dc * 2048 + t * 128:dc * 2048 + (t + 1) * 128],
                        wt[:, dc * 1024 + oh * 512:dc * 1024 + (oh + 1) * 512],
                        start=(dc == 0), stop=(dc == 7)),
                        waits=[("vector", p.n("vector")), ("scalar", p.n("scalar")), ("sync", p.n("sync"))])
            if part == 2:
                p.emit("vector", lambda e, t=t: e.tensor_tensor(
                    V[:, t * 1024:(t + 1) * 1024], psS[:, 0:1024], bqt[:], op=OP.add),
                    waits=[("tensor", p.n("tensor"))])
            else:
                p.emit("vector", lambda e: e.tensor_tensor(
                    qkt[:], psS[:, 0:1024], bqt[:], op=OP.add),
                    waits=[("tensor", p.n("tensor"))])
                p.emit("vector", lambda e: e.tensor_tensor(sq[:], qkt[:], qkt[:], op=OP.mult))
                p.emit("vector", lambda e: e.tensor_reduce(
                    rst[:, 0:16], sq[:].rearrange("p (g d) -> p g d", d=64), axis=AX.X, op=OP.add))
                eps_r = p.cc(1e-6)
                p.emit("scalar", lambda e, eps_r=eps_r: e.activation(rst[:, 0:16], rst[:, 0:16], AF.Ln, bias=eps_r, scale=1.0 / DK),
                       waits=[("vector", p.n("vector"))])
                p.emit("scalar", lambda e: e.activation(rst[:, 0:16], rst[:, 0:16], AF.Exp, bias=0.0, scale=-0.5))
                for g in range(16):
                    p.emit("vector", lambda e, g=g: e.tensor_scalar(
                        qkt[:, g * 64:(g + 1) * 64], qkt[:, g * 64:(g + 1) * 64], rst[:, g:g + 1], None, op0=OP.mult),
                        waits=([("scalar", p.n("scalar"))] if g == 0 else []))
                sc2 = 0.125 if part == 0 else 1.0
                p.emit("vector", lambda e, sc2=sc2: e.scalar_tensor_tensor(
                    qkb[:], qkt[:], sc2, wnt[:], op0=OP.mult, op1=OP.mult))
                for c in range(8):
                    p.emit("tensor", lambda e, c=c: e.transpose(
                        psT[:, c * 128:(c + 1) * 128], qkb[:, c * 128:(c + 1) * 128], idb[:]),
                        waits=[("vector", p.n("vector"))])
                p.emit("vector", lambda e, t=t, dest=dest: e.tensor_copy(
                    dest[:].rearrange("p (c s) -> p c s", s=S)[:, :, t * 128:(t + 1) * 128],
                    psT[:, 0:1024].rearrange("p (c q) -> p c q", q=128)),
                    waits=[("tensor", p.n("tensor"))])
    emit_weight_T(p, aps["Wproj"], D, D, wt, wstage, psS, idf)

    for qb in range(4):
        for h in range(H):
            for qtl in range(4):
                qt = qb * 4 + qtl
                for kb in range(4):
                    p.emit("tensor", lambda e, h=h, qt=qt, kb=kb: e.matmul(
                        psS[:, kb * 512:(kb + 1) * 512],
                        hsl(Q, h, qt * 128, 128), hsl(K, h, kb * 512, 512),
                        start=True, stop=True),
                        waits=[("vector", p.n("vector")), ("scalar", p.n("scalar"))])
                p.emit("vector", lambda e: e.tensor_reduce(st8[:, 0:1], psS[:], axis=AX.X, op=OP.max, negate=True),
                       waits=[("tensor", p.n("tensor"))])
                p.emit("scalar", lambda e: e.activation(attn[:], psS[:], AF.Exp,
                                                        bias=st8[:, 0:1], scale=1.0, accum_out=st8[:, 1:2]),
                       waits=[("vector", p.n("vector"))])
                p.emit("vector", lambda e: e.reciprocal(st8[:, 2:3], st8[:, 1:2]),
                       waits=[("scalar", p.n("scalar"))])
                p.emit("vector", lambda e: e.tensor_scalar(attn[:], attn[:], st8[:, 2:3], None, op0=OP.mult))
                for c in range(16):
                    p.emit("tensor", lambda e, c=c: e.transpose(
                        psT[:, c * 128:(c + 1) * 128], attn[:, c * 128:(c + 1) * 128], idb[:]),
                        waits=[("vector", p.n("vector"))])
                p.emit("vector", lambda e, qtl=qtl: e.tensor_copy(
                    attnT[:].rearrange("p (c q) -> p c q", q=512)[:, :, qtl * 128:(qtl + 1) * 128],
                    psT[:].rearrange("p (c q) -> p c q", q=128)),
                    waits=[("tensor", p.n("tensor"))])
            for c in range(16):
                p.emit("tensor", lambda e, h=h, c=c: e.matmul(
                    psC[0:64, :], V[:, c * 1024 + h * 64:c * 1024 + (h + 1) * 64],
                    attnT[:, c * 512:(c + 1) * 512], start=(c == 0), stop=(c == 15)),
                    waits=[("vector", p.n("vector"))])
            p.emit("vector", lambda e, h=h: e.tensor_copy(
                ctxfm[(h % 2) * 64:(h % 2) * 64 + 64, (h // 2) * 512:(h // 2 + 1) * 512], psC[0:64, :]),
                waits=[("tensor", p.n("tensor"))])
        for tsub in range(4):
            t = qb * 4 + tsub
            for oh in range(2):
                pst = psS[:, oh * 512:(oh + 1) * 512]
                for dc in range(8):
                    p.emit("tensor", lambda e, tsub=tsub, oh=oh, dc=dc, pst=pst: e.matmul(
                        pst, ctxfm[:, dc * 512 + tsub * 128:dc * 512 + (tsub + 1) * 128],
                        wt[:, dc * 1024 + oh * 512:dc * 1024 + (oh + 1) * 512],
                        start=(dc == 0), stop=(dc == 7)),
                        waits=[("vector", p.n("vector")), ("scalar", p.n("scalar"))])
            ob = outb[:, (t % 2) * 1024:(t % 2) * 1024 + 1024]
            p.emit("vector", lambda e, ob=ob: e.tensor_tensor(ob, psS[:, 0:1024], bpt[:], op=OP.add),
                   waits=[("tensor", p.n("tensor")), ("gpsimd", max(0, p.n("gpsimd") - 16))])
            p.emit("gpsimd", lambda e, t=t, ob=ob: e.dma_start(
                out=aps["a1"][t * 128:(t + 1) * 128, :], in_=ob),
                waits=[("vector", p.n("vector"))])



def phase_moe(p, io, env):
    aps = io
    idf = env["idf"]
    ones1 = env["ones1"]
    xn = p.sb("xn", [128, 8 * 2048], BF16)
    xf = p.sb("xf", [128, 1024], F32)
    rwt = p.sb("rwt", [128, 8 * 16], F32)
    rstage = p.sb("rstage", [128, 128], F32)
    rbt = p.sb("rbt", [128, 16], F32)
    brow = p.sb("brow", [1, 1024], F32)
    logit = p.sb("logit", [128, 16], F32)
    probs = p.sb("probs", [128, 16], F32)
    pwork = p.sb("pwork", [128, 16], F32)
    msk = p.sb("msk", [128, 16], F32)
    st8 = p.sb("st8", [128, 16], F32)
    gates = p.sb("gates", [128, 16 * 16], F32)
    racct = p.sb("racct", [128, 16], F32)
    epst = p.sb("epst", [128, 16], F32)
    we = p.sb("we", [128, 2 * 8 * 1024], BF16)
    moeacc = p.sb("moeacc", [128, 16 * 1024], F32)
    outb = p.sb("outb", [128, 2 * 1024], F32)

    psR = p.ps("psR", [128, 512], F32)
    psE = p.ps("psE", [128, 1024], F32)
    psF = p.ps("psF", [128, 1024], F32)

    for c in range(8):
        p.emit("sync", lambda e, c=c: e.dma_start(out=xn[:, c * 2048:(c + 1) * 2048],
                                                  in_=aps["xn_bf"][c * 128:(c + 1) * 128, :]))
    for c in range(8):
        p.emit("sync", lambda e, c=c: e.dma_start(out=rstage[0:16, :], in_=aps["router_w"][:, c * 128:(c + 1) * 128]),
               waits=[("tensor", p.n("tensor"))])
        p.emit("tensor", lambda e, c=c: e.transpose(psR[:, 0:16], rstage[0:16, 0:128], idf[0:16, 0:16]),
               waits=[("sync", p.n("sync")), ("vector", p.n("vector"))])
        p.emit("vector", lambda e, c=c: e.tensor_copy(rwt[:, c * 16:(c + 1) * 16], psR[:, 0:16]),
               waits=[("tensor", p.n("tensor"))])
    emit_bcast_row(p, aps["router_b"], 16, rbt, brow, ones1, psR)

    for t in range(16):
        for c in range(8):
            p.emit("sync", lambda e, t=t, c=c: e.dma_start(
                out=xf[:, c * 128:(c + 1) * 128],
                in_=aps["xn_f32"][c * 128:(c + 1) * 128, t * 128:(t + 1) * 128]),
                waits=[("tensor", p.n("tensor"))])
        for c in range(8):
            p.emit("tensor", lambda e, c=c: e.matmul(
                psR[:, 0:16], xf[:, c * 128:(c + 1) * 128], rwt[:, c * 16:(c + 1) * 16],
                start=(c == 0), stop=(c == 7)),
                waits=[("sync", p.n("sync")), ("vector", p.n("vector"))])
        p.emit("vector", lambda e: e.tensor_tensor(logit[:], psR[:, 0:16], rbt[:], op=OP.add),
               waits=[("tensor", p.n("tensor"))])
        p.emit("vector", lambda e: e.tensor_reduce(st8[:, 0:1], logit[:], axis=AX.X, op=OP.max, negate=True))
        p.emit("scalar", lambda e: e.activation(probs[:], logit[:], AF.Exp, bias=st8[:, 0:1], scale=1.0, accum_out=st8[:, 1:2]),
               waits=[("vector", p.n("vector"))])
        p.emit("vector", lambda e: e.reciprocal(st8[:, 2:3], st8[:, 1:2]), waits=[("scalar", p.n("scalar"))])
        p.emit("vector", lambda e: e.tensor_scalar(probs[:], probs[:], st8[:, 2:3], None, op0=OP.mult))
        p.emit("vector", lambda e: e.tensor_copy(pwork[:], probs[:]))
        for it in range(6):
            p.emit("vector", lambda e: e.tensor_reduce(st8[:, 3:4], pwork[:], axis=AX.X, op=OP.max))
            if it == 0:
                p.emit("vector", lambda e: e.tensor_copy(st8[:, 4:5], st8[:, 3:4]))
            else:
                p.emit("vector", lambda e: e.tensor_tensor(st8[:, 4:5], st8[:, 4:5], st8[:, 3:4], op=OP.add))
            if it < 5:
                p.emit("vector", lambda e: e.tensor_scalar(msk[:], pwork[:], st8[:, 3:4], None, op0=OP.is_ge))
                p.emit("vector", lambda e: e.scalar_tensor_tensor(pwork[:], msk[:], -1e30, pwork[:], op0=OP.mult, op1=OP.add))
        p.emit("vector", lambda e: e.tensor_scalar(msk[:], probs[:], st8[:, 3:4], None, op0=OP.is_ge))
        p.emit("vector", lambda e: e.tensor_scalar(st8[:, 5:6], st8[:, 4:5], 1e-10, None, op0=OP.add))
        p.emit("vector", lambda e: e.reciprocal(st8[:, 6:7], st8[:, 5:6]))
        p.emit("vector", lambda e, t=t: e.scalar_tensor_tensor(
            gates[:, t * 16:(t + 1) * 16], probs[:], st8[:, 6:7], msk[:], op0=OP.mult, op1=OP.mult))
        p.emit("vector", lambda e, t=t: e.tensor_reduce(st8[:, 7:8], gates[:, t * 16:(t + 1) * 16], axis=AX.X, op=OP.add))
        p.emit("vector", lambda e: e.tensor_scalar(st8[:, 8:9], st8[:, 7:8], 1e-10, None, op0=OP.max))
        p.emit("vector", lambda e, t=t: e.reciprocal(racct[:, t:t + 1], st8[:, 8:9]))
        p.emit("vector", lambda e, t=t: e.tensor_scalar(epst[:, t:t + 1], st8[:, 7:8], 1e-10, None, op0=OP.mult))

    stt_tok = [0, 0]
    wld_tok = [0, 0]
    for eix in range(2):
        wslot = we[:, (eix % 2) * 8192:(eix % 2) * 8192 + 8192]
        for dc in range(8):
            p.emit("gpsimd", lambda e, eix=eix, dc=dc, wslot=wslot: e.dma_start(
                out=wslot[:, dc * 1024:(dc + 1) * 1024],
                in_=aps["expert_w"][eix, dc * 128:(dc + 1) * 128, :]))
        wld_tok[eix] = p.n("gpsimd")
    for eix in range(NE):
        slot = eix % 2
        wslot = we[:, slot * 8192:slot * 8192 + 8192]
        t_last_mm = 0
        for t in range(16):
            pst, bank = (psE, 0) if t % 2 == 0 else (psF, 1)
            for oh in range(2):
                for dc in range(8):
                    w = []
                    if oh == 0 and dc == 0:
                        w = [("vector", stt_tok[bank])]
                        if t == 0:
                            w.append(("gpsimd", wld_tok[slot]))
                            w.append(("vector", p.n("vector") if eix == 0 else stt_tok[1 - bank]))
                    p.emit("tensor", lambda e, t=t, oh=oh, dc=dc, pst=pst, wslot=wslot: e.matmul(
                        pst[:, oh * 512:(oh + 1) * 512],
                        xn[:, dc * 2048 + t * 128:dc * 2048 + (t + 1) * 128],
                        wslot[:, dc * 1024 + oh * 512:dc * 1024 + (oh + 1) * 512],
                        start=(dc == 0), stop=(dc == 7)), waits=w)
            t_mm = p.n("tensor")
            if eix == 0:
                stt_tok[bank] = p.emit("vector", lambda e, t=t, eix=eix, pst=pst: e.tensor_scalar(
                    moeacc[:, t * 1024:(t + 1) * 1024], pst[:], gates[:, t * 16 + eix:t * 16 + eix + 1], None, op0=OP.mult),
                    waits=[("tensor", t_mm)])
            else:
                stt_tok[bank] = p.emit("vector", lambda e, t=t, eix=eix, pst=pst: e.scalar_tensor_tensor(
                    moeacc[:, t * 1024:(t + 1) * 1024], pst[:], gates[:, t * 16 + eix:t * 16 + eix + 1],
                    moeacc[:, t * 1024:(t + 1) * 1024], op0=OP.mult, op1=OP.add),
                    waits=[("tensor", t_mm)])
        if eix + 2 < NE:
            for dc in range(8):
                p.emit("gpsimd", lambda e, eix=eix, dc=dc, slot=slot: e.dma_start(
                    out=we[:, slot * 8192 + dc * 1024:slot * 8192 + (dc + 1) * 1024],
                    in_=aps["expert_w"][eix + 2, dc * 128:(dc + 1) * 128, :]),
                    waits=([("tensor", p.n("tensor"))] if dc == 0 else []))
            wld_tok[slot] = p.n("gpsimd")
    for t in range(16):
        ob = outb[:, (t % 2) * 1024:(t % 2) * 1024 + 1024]
        p.emit("vector", lambda e, t=t, ob=ob: e.tensor_scalar(
            ob, moeacc[:, t * 1024:(t + 1) * 1024], racct[:, t:t + 1], epst[:, t:t + 1], op0=OP.mult, op1=OP.add),
            waits=[("gpsimd", max(0, p.n("gpsimd") - 16))])
        p.emit("gpsimd", lambda e, t=t, ob=ob: e.dma_start(
            out=aps["m1"][t * 128:(t + 1) * 128, :], in_=ob),
            waits=[("vector", p.n("vector"))])



def _proj_common(p, wt, xn, psS, out_sb, scale, to_tm, tok):
    ready = [("vector", p.n("vector")), ("sync", p.n("sync"))]
    if not to_tm:
        for oc in range(8):
            for qb in range(4):
                bank = qb % 2
                pst = psS[:, bank * 512:bank * 512 + 512]
                for dc in range(8):
                    w = ([("scalar", tok["se_bank"][bank])] + (ready if (oc == 0 and qb == 0) else [])) if dc == 0 else []
                    p.emit("tensor", lambda e, oc=oc, qb=qb, dc=dc, pst=pst: e.matmul(
                        pst, wt[:, dc * 1024 + oc * 128:dc * 1024 + (oc + 1) * 128],
                        xn[:, dc * 2048 + qb * 512:dc * 2048 + (qb + 1) * 512],
                        start=(dc == 0), stop=(dc == 7)), waits=w)
                t_mm = p.n("tensor")
                tok["se_bank"][bank] = p.emit("scalar", lambda e, oc=oc, qb=qb, pst=pst, scale=scale: e.activation(
                    out_sb[:, oc * 2048 + qb * 512:oc * 2048 + (qb + 1) * 512], pst,
                    AF.Copy, bias=0.0, scale=scale), waits=[("tensor", t_mm)])
    else:
        for t in range(16):
            for oh in range(2):
                bank = oh % 2
                pst = psS[:, bank * 512:bank * 512 + 512]
                for dc in range(8):
                    w = ([("scalar", tok["se_bank"][bank])] + (ready if (t == 0 and oh == 0) else [])) if dc == 0 else []
                    p.emit("tensor", lambda e, t=t, oh=oh, dc=dc, pst=pst: e.matmul(
                        pst, xn[:, dc * 2048 + t * 128:dc * 2048 + (t + 1) * 128],
                        wt[:, dc * 1024 + oh * 512:dc * 1024 + (oh + 1) * 512],
                        start=(dc == 0), stop=(dc == 7)), waits=w)
                t_mm = p.n("tensor")
                tok["se_bank"][bank] = p.emit("scalar", lambda e, t=t, oh=oh, pst=pst: e.activation(
                    out_sb[:, t * 1024 + oh * 512:t * 1024 + (oh + 1) * 512], pst,
                    AF.Copy, bias=0.0, scale=1.0), waits=[("tensor", t_mm)])


def _kmax_chain(p, Ksb, Ksq, psS, onec, Krow, col, h, tok):
    r = (h % 2) * 64
    oc = h // 2
    t_sq = p.emit("scalar", lambda e: e.activation(
        Ksq[r:r + 64, :], Ksb[r:r + 64, oc * 2048:(oc + 1) * 2048], AF.Square, bias=0.0, scale=1.0),
        waits=[("scalar", tok.get("k_ready", 0))] if False else [])
    for half in range(2):
        for kb in range(2):
            p.emit("tensor", lambda e, half=half, kb=kb: e.matmul(
                psS[0:1, 1024 + kb * 512:1024 + (kb + 1) * 512],
                onec[r:r + 64, :].bitcast(BF16) if False else onec[r:r + 64, :],
                Ksq[r:r + 64, half * 1024 + kb * 512:half * 1024 + (kb + 1) * 512],
                start=True, stop=True),
                waits=[("scalar", t_sq), ("vector", tok.get("kred", 0))])
        t_mm = p.n("tensor")
        tok["kred"] = p.emit("vector", lambda e, half=half, col=col: e.tensor_reduce(
            Krow[0:1, 16 + half:17 + half], psS[0:1, 1024:2048], axis=AX.X, op=OP.max),
            waits=[("tensor", t_mm)])
    tok["kred"] = p.emit("vector", lambda e, col=col: e.tensor_tensor(
        Krow[0:1, col:col + 1], Krow[0:1, 16:17], Krow[0:1, 17:18], op=OP.max))


def phase_attn0_v2(p, io, env):
    aps = io
    idf = env["idf"]
    idb = env["idb"]
    tri = env["tri"]
    onec = env["onec"]
    ones1 = env["ones1"]

    xn = p.sb("xn", [128, 8 * 2048], BF16)
    wt = p.sb("wt", [128, 8 * 1024], BF16)
    wstage = p.sb("wstage", [128, 1024], F32)
    Q = p.sb("Q", [128, 8 * 2048], BF16)
    K = p.sb("K", [128, 8 * 2048], BF16)
    V = p.sb("V", [128, 16 * 1024], BF16)
    attn = p.sb("attn", [128, 2112], BF16)
    attnT = p.sb("attnT", [128, 16 * 512], BF16)
    ctxfm = p.sb("ctxfm", [128, 8 * 512], BF16)
    Wwin4 = p.sb("Wwin4", [128, 4 * 192], F32)
    QRt4 = p.sb("QRt4", [128, 4 * 65], F32)
    Ptab4 = p.sb("Ptab4", [128, 4 * 319], F32)
    Dg4 = p.sb("Dg4", [128, 4 * 65], BF16)
    Atile4 = p.sb("Atile4", [128, 4 * 65], BF16)
    Arhs4 = p.sb("Arhs4", [128, 512], BF16)
    crn = p.sb("crn", [128, 128], BF16)
    st8 = p.sb("st8", [128, 48], F32)
    Qsq = p.sb("Qsq", [128, 512], BF16)
    KRcols = p.sb("KRcols", [128, 16], F32)
    Krow = p.sb("Krow", [1, 32], F32)
    relkT = p.sb("relkT", [128, 65], BF16)
    relvN = p.sb("relvN", [128, 64], BF16)
    relst = p.sb("relst", [128, 65], F32)
    rsrow = p.sb("rsrow", [4, 128], F32)
    rsrow1 = p.sb("rsrow1", [1, 512], F32)
    rsb = p.sb("rsb", [128, 512], F32)
    bot = p.sb("bot", [128, 1024], F32)
    outb = p.sb("outb", [128, 2 * 1024], F32)
    brow = outb[0:1, 0:1024]
    Ksq = outb[:].bitcast(BF16)[:, 0:2048]

    psS = p.ps("psS", [128, 2048], F32)
    psT = p.ps("psT", [128, 2048], BF16)
    psX = p.ps("psX", [128, 512], F32)
    psC = p.ps("psC", [128, 512], F32)

    for c in range(8):
        p.emit("sync", lambda e, c=c: e.dma_start(out=xn[:, c * 2048:(c + 1) * 2048],
                                                  in_=aps["xn_bf"][c * 128:(c + 1) * 128, :]))

    ST8_S1, ST8_S2, ST8_S3 = 0, 4, 8
    ST8_C0 = 16
    ST8_NM = 20
    ST8_BL, ST8_BR = 24, 28
    ST8_S4, ST8_RS = 32, 36
    ST8_A0, ST8_T1, ST8_T2 = 40, 44, 45

    t_ld = p.emit("sync", lambda e: e.dma_start(out=relst[0:65, 0:64], in_=aps["rel_k"]))
    p.emit("gpsimd", lambda e: e.dma_start(out=relvN[0:65, :], in_=aps["rel_v"]))
    t_tr = p.emit("tensor", lambda e: e.transpose(psX[0:64, 0:65], relst[0:65, 0:64], idf[0:65, 0:65]),
                  waits=[("sync", t_ld)])
    p.emit("vector", lambda e: e.tensor_copy(relkT[0:64, 0:65], psX[0:64, 0:65]),
           waits=[("tensor", t_tr)])
    t_rk = p.emit("vector", lambda e: e.tensor_copy(relkT[64:128, 0:65], psX[0:64, 0:65]))
    p.emit("vector", lambda e: e.tensor_tensor(relst[0:65, 0:65][0:65, 0:64] if False else relst[0:65, 0:64], relst[0:65, 0:64], relst[0:65, 0:64], op=OP.mult))
    t_rs = p.emit("vector", lambda e: e.tensor_reduce(relst[0:65, 64:65], relst[0:65, 0:64], axis=AX.X, op=OP.add))
    t_tr2 = p.emit("tensor", lambda e: e.transpose(psX[0:1, 70:135], relst[0:65, 64:65], idf[0:65, 0:65]),
                   waits=[("vector", t_rs)])
    t_rm = p.emit("vector", lambda e: e.tensor_reduce(Krow[0:1, 20:21], psX[0:1, 70:135], axis=AX.X, op=OP.max),
                  waits=[("tensor", t_tr2)])
    t_rm2 = p.emit("scalar", lambda e: e.activation(Krow[0:1, 20:21], Krow[0:1, 20:21], AF.Ln, bias=0.0, scale=1.0),
                   waits=[("vector", t_rm)])
    t_rm3 = p.emit("scalar", lambda e: e.activation(Krow[0:1, 20:21], Krow[0:1, 20:21], AF.Exp, bias=0.0, scale=0.5))
    emit_bcast_row(p, aps["b_o"], 1024, bot, brow, ones1, psS)
    p.emit("vector", lambda e: e.memset(attn[:, 0:32], 0.0))
    p.emit("vector", lambda e: e.memset(attn[:, 2080:2112], 0.0))

    tok = {"se_bank": [0, 0], "kred": 0}
    emit_weight_T(p, aps["Wq"], D, D, wt, wstage, psS, idf)
    _proj_common(p, wt, xn, psS, Q, 0.125, False, tok)
    emit_weight_T(p, aps["Wk"], D, D, wt, wstage, psS, idf)
    _proj_common(p, wt, xn, psS, K, 1.0, False, tok)
    import os as _os
    TMX = not _os.environ.get("KK_MBOUND")
    for h in ([] if TMX else range(H)):
        _kmax_chain(p, K, Ksq, psS, onec, Krow, h, h, tok)
    emit_weight_T(p, aps["Wv"], D, D, wt, wstage, psS, idf)
    _proj_common(p, wt, xn, psS, V, 1.0, True, tok)
    emit_weight_T(p, aps["Wo"], D, D, wt, wstage, psS, idf)
    if TMX:
        tok_krc = p.n("vector")
    else:
        t1 = p.emit("scalar", lambda e: e.activation(Krow[0:1, 0:16], Krow[0:1, 0:16], AF.Ln, bias=0.0, scale=1.0),
                    waits=[("vector", tok["kred"]), ("scalar", t_rm3)])
        p.emit("scalar", lambda e: e.activation(Krow[0:1, 0:16], Krow[0:1, 0:16], AF.Exp, bias=0.0, scale=0.5))
        t2 = p.emit("vector", lambda e: e.tensor_tensor(Krow[0:1, 0:16], Krow[0:1, 0:16],
                                                        Krow[0:1, 20:21].broadcast_to([1, 16]), op=OP.add),
                    waits=[("scalar", p.n("scalar"))])
        t3 = p.emit("vector", lambda e: e.tensor_scalar(Krow[0:1, 0:16], Krow[0:1, 0:16], -1.0, None, op0=OP.mult))
        t4 = p.emit("tensor", lambda e: e.matmul(psX[:, 140:156], ones1[0:1, :], Krow[0:1, 0:16],
                                                 start=True, stop=True), waits=[("vector", t3)])
        tok_krc = p.emit("vector", lambda e: e.tensor_copy(KRcols[:], psX[:, 140:156]), waits=[("tensor", t4)])

    tk = {"exp": 0, "corner": [0, 0], "transp": [0, 0], "dgw": [0, 0],
          "psT_evac": 0, "psX_ve": tok_krc, "psX_se": 0, "ctxmult": 0,
          "rsbevac": 0, "woevac": 0, "wread": 0, "wadd_last": 0, "atile_prev": 0}
    bufc = [0]
    import os
    NQB = int(os.environ.get("KK_NQB", "4"))
    NH = int(os.environ.get("KK_NH", str(H)))
    for qb in range(NQB):
        for h in range(NH):
            r = (h % 2) * 64
            oc = h // 2
            SKIPQR = bool(os.environ.get("KK_SKIP_QR"))
            if SKIPQR:
                p.emit("vector", lambda e: e.memset(st8[:, ST8_NM:ST8_NM + 4], -10.0))
                p.emit("vector", lambda e: e.memset(st8[:, ST8_BL:ST8_BL + 4], -10.0))
                t_bias = p.emit("vector", lambda e: e.memset(st8[:, ST8_BR:ST8_BR + 4], -10.0))
                tk["wread"] = p.n("sync")
            TRUEMAX0 = not os.environ.get("KK_MBOUND")
            t_qsq = None if (SKIPQR or TRUEMAX0) else p.emit("scalar", lambda e, r=r, oc=oc, qb=qb: e.activation(
                Qsq[r:r + 64, :], Q[r:r + 64, oc * 2048 + qb * 512:oc * 2048 + (qb + 1) * 512],
                AF.Square, bias=0.0, scale=1.0),
                waits=[("tensor", p.n("tensor"))] if (h > 0 or qb > 0) else [])
            for qtl in ([] if SKIPQR else range(4)):
                qt = qb * 4 + qtl
                p.emit("tensor", lambda e, h=h, qt=qt, qtl=qtl: e.matmul(
                    psX[:, qtl * 65:(qtl + 1) * 65], hsl(Q, h, qt * 128, 128),
                    relkT[(h % 2) * 64:(h % 2) * 64 + 64, 0:65], start=True, stop=True),
                    waits=([("vector", tk["psX_ve"]), ("scalar", tk["psX_se"])] if qtl == 0 else []))
            QRLVL = int(os.environ.get("KK_QRLVL", "4"))
            TRUEMAX = not os.environ.get("KK_MBOUND")
            for qtl in ([] if (SKIPQR or QRLVL < 2 or TRUEMAX) else range(4)):
                p.emit("tensor", lambda e, r=r, qtl=qtl: e.matmul(
                    psX[:, 260 + qtl:261 + qtl], Qsq[r:r + 64, qtl * 128:(qtl + 1) * 128],
                    onec[r:r + 64, :], start=True, stop=True),
                    waits=([("scalar", t_qsq)] if qtl == 0 else []))
            t_qr_mm = p.n("tensor")
            if SKIPQR:
                pass
            else:
              t_qrt = p.emit("vector", lambda e: e.tensor_copy(QRt4[:], psX[:, 0:260]),
                           waits=[("tensor", t_qr_mm)])
              if QRLVL < 2:
                p.emit("vector", lambda e: e.memset(st8[:, ST8_NM:ST8_NM + 4], -10.0))
                p.emit("vector", lambda e: e.memset(st8[:, ST8_BL:ST8_BL + 4], -10.0))
                t_bias = p.emit("vector", lambda e: e.memset(st8[:, ST8_BR:ST8_BR + 4], -10.0))
                tk["wread"] = p.n("sync")
                raise_skip = True
              if QRLVL >= 2 and not TRUEMAX:
               t_mln = p.emit("scalar", lambda e: e.activation(st8[:, ST8_T1:ST8_T1 + 4], psX[:, 260:264],
                                                            AF.Ln, bias=0.0, scale=1.0),
                           waits=[("tensor", t_qr_mm)])
               tk["psX_se"] = t_mln
               t_msq = p.emit("scalar", lambda e: e.activation(st8[:, ST8_T1:ST8_T1 + 4], st8[:, ST8_T1:ST8_T1 + 4],
                                                            AF.Exp, bias=0.0, scale=0.5))
               t_nm = p.emit("vector", lambda e, h=h: e.tensor_scalar(
                st8[:, ST8_NM:ST8_NM + 4], st8[:, ST8_T1:ST8_T1 + 4], KRcols[:, h:h + 1], None, op0=OP.mult),
                waits=[("scalar", t_msq)])
              qr4 = QRt4[:].rearrange("p (q r) -> p q r", r=65)
              nm4 = st8[:, ST8_NM:ST8_NM + 4].rearrange("p (q o) -> p q o", o=1)
              if QRLVL >= 3:
               if not TRUEMAX:
                p.emit("vector", lambda e, qr4=qr4, nm4=nm4: e.tensor_tensor(
                 st8[:, ST8_BL:ST8_BL + 4].rearrange("p (q o) -> p q o", o=1), qr4[:, :, 0:1], nm4, op=OP.add))
                t_bias = p.emit("vector", lambda e, qr4=qr4, nm4=nm4: e.tensor_tensor(
                 st8[:, ST8_BR:ST8_BR + 4].rearrange("p (q o) -> p q o", o=1), qr4[:, :, 64:65], nm4, op=OP.add))
               else:
                t_bias = p.n("vector")
               pt4 = Ptab4[:].rearrange("p (q c) -> p q c", c=319)
               p.emit("vector", lambda e, pt4=pt4, qr4=qr4: e.tensor_copy(
                pt4[:, :, 0:127], qr4[:, :, 0:1].broadcast_to([128, 4, 127])))
               p.emit("vector", lambda e, pt4=pt4: e.tensor_copy(
                pt4[:, :, 127:192], QRt4[:].rearrange("p (q r) -> p q r", r=65)))
               t_pb = p.emit("vector", lambda e, pt4=pt4, qr4=qr4: e.tensor_copy(
                pt4[:, :, 192:319], qr4[:, :, 64:65].broadcast_to([128, 4, 127])))
              else:
                t_bias = p.emit("vector", lambda e: e.memset(st8[:, ST8_BL:ST8_BL + 8], -10.0))
              for qtl in (range(4) if QRLVL >= 4 else []):
                off = qtl * 40960
                p.emit("sync", lambda e, qtl=qtl, off=off: e.dma_start(
                    out=aps["pscrP"][off:off + 128 * 319].rearrange("(q w) -> q w", w=319),
                    in_=Ptab4[:, qtl * 319:(qtl + 1) * 319]),
                    waits=([("vector", t_pb)] if qtl == 0 else []))
                p.emit("sync", lambda e, qtl=qtl, off=off: e.dma_start(
                    out=Wwin4[:, qtl * 192:(qtl + 1) * 192],
                    in_=aps["pscrP"][off + 127:off + 127 + 128 * 318].rearrange("(q w) -> q w", w=318)[:, 0:192]),
                    waits=([("vector", tk["wadd_last"])] if qtl == 0 else []))
              tk["wread"] = p.n("sync")
              if QRLVL < 4:
                for qtl in range(4):
                    p.emit("vector", lambda e, qtl=qtl: e.memset(Wwin4[:, qtl * 192:(qtl + 1) * 192], 0.0))
                tk["wread"] = p.n("sync")
            if qb == 0:
                p.emit("vector", lambda e: e.memset(st8[:, ST8_S1:ST8_S1 + 1], 0.0))
                tk["psX_ve"] = p.emit("vector", lambda e: e.memset(psX[:, 264:265], 0.0),
                                      waits=[("tensor", p.n("tensor"))])
            if qb == 3:
                p.emit("vector", lambda e: e.memset(st8[:, ST8_S3 + 3:ST8_S3 + 4], 0.0))
            NQT = int(os.environ.get("KK_NQT", "4"))
            BARE = bool(os.environ.get("KK_BARE"))
            for qtl in range(NQT):
                qt = qb * 4 + qtl
                q0 = qt * 128
                buf = 0
                bufc[0] += 1
                a0_ = 0
                for kb in range(4):
                    w = []
                    if kb == 0:
                        w = [("scalar", tk["exp"])]
                        if h == 0 and qtl == 0:
                            w.append(("vector", tk["woevac"]))
                        if qtl == 0:
                            w.append(("vector", tk["rsbevac"]))
                            w.append(("scalar", tk["rsbevac_se"] if "rsbevac_se" in tk else 0))
                    p.emit("tensor", lambda e, h=h, qt=qt, kb=kb: e.matmul(
                        psS[:, kb * 512:(kb + 1) * 512],
                        hsl(Q, h, qt * 128, 128), hsl(K, h, kb * 512, 512),
                        start=True, stop=True), waits=w)
                t_sc = p.n("tensor")
                if SKIPQR:
                    t_wadd = p.emit("vector", lambda e: e.tensor_copy(st8[:, ST8_T2:ST8_T2 + 1], st8[:, ST8_T2:ST8_T2 + 1]),
                                    waits=[("tensor", t_sc)])
                elif qt == 0:
                    t_wadd = p.emit("vector", lambda e: e.tensor_tensor(
                        psS[:, 0:160], psS[:, 0:160], Wwin4[:, 32:192], op=OP.add),
                        waits=[("sync", tk["wread"]), ("tensor", t_sc)])
                elif qt == 15:
                    t_wadd = p.emit("vector", lambda e: e.tensor_tensor(
                        psS[:, 1888:2048], psS[:, 1888:2048], Wwin4[:, 3 * 192:3 * 192 + 160], op=OP.add),
                        waits=[("sync", tk["wread"]), ("tensor", t_sc)])
                else:
                    t_wadd = p.emit("vector", lambda e, q0=q0, qtl=qtl: e.tensor_tensor(
                        psS[:, q0 - 32:q0 + 160], psS[:, q0 - 32:q0 + 160],
                        Wwin4[:, qtl * 192:(qtl + 1) * 192], op=OP.add),
                        waits=[("sync", tk["wread"]), ("tensor", t_sc), ("vector", t_bias)])
                tk["wadd_last"] = t_wadd
                if TRUEMAX:
                    p.emit("vector", lambda e, qtl=qtl: e.tensor_reduce(
                        st8[:, ST8_NM + qtl:ST8_NM + qtl + 1], psS[:], axis=AX.X, op=OP.max, negate=True))
                    p.emit("vector", lambda e, qtl=qtl: e.tensor_tensor(
                        st8[:, ST8_BL + qtl:ST8_BL + qtl + 1], QRt4[:, qtl * 65:qtl * 65 + 1],
                        st8[:, ST8_NM + qtl:ST8_NM + qtl + 1], op=OP.add))
                    t_wadd = p.emit("vector", lambda e, qtl=qtl: e.tensor_tensor(
                        st8[:, ST8_BR + qtl:ST8_BR + qtl + 1], QRt4[:, qtl * 65 + 64:qtl * 65 + 65],
                        st8[:, ST8_NM + qtl:ST8_NM + qtl + 1], op=OP.add))
                obuf = 1 - buf
                expw = [("vector", t_wadd), ("vector", tk["corner"][buf]),
                        ("tensor", tk["transp"][buf]), ("sync", tk["dgw"][buf])]
                if os.environ.get("KK_EXP1"):
                    p.emit("scalar", lambda e, a0_=a0_, qtl=qtl: e.activation(
                        attn[:, a0_ + 32:a0_ + 32 + 2048], psS[:], AF.Exp,
                        bias=st8[:, ST8_NM + qtl:ST8_NM + qtl + 1], scale=1.0,
                        accum_out=st8[:, ST8_S2 + qtl:ST8_S2 + qtl + 1]), waits=expw)
                    tk["exp"] = p.n("scalar")
                else:
                  if qt > 0:
                    p.emit("scalar", lambda e, q0=q0, a0_=a0_, qtl=qtl: e.activation(
                        attn[:, a0_ + 32:a0_ + 32 + q0 - 32], psS[:, 0:q0 - 32], AF.Exp,
                        bias=st8[:, ST8_BL + qtl:ST8_BL + qtl + 1], scale=1.0,
                        accum_out=st8[:, ST8_S1 + qtl:ST8_S1 + qtl + 1]), waits=expw)
                    expw = []
                  lo = max(0, q0 - 32)
                  hi = min(2048, q0 + 160)
                  p.emit("scalar", lambda e, lo=lo, hi=hi, a0_=a0_, qtl=qtl: e.activation(
                    attn[:, a0_ + 32 + lo:a0_ + 32 + hi], psS[:, lo:hi], AF.Exp,
                    bias=st8[:, ST8_NM + qtl:ST8_NM + qtl + 1], scale=1.0,
                    accum_out=st8[:, ST8_S2 + qtl:ST8_S2 + qtl + 1]), waits=expw)
                  if qt < 15:
                    p.emit("scalar", lambda e, hi=hi, a0_=a0_, qtl=qtl: e.activation(
                        attn[:, a0_ + 32 + hi:a0_ + 32 + 2048], psS[:, hi:2048], AF.Exp,
                        bias=st8[:, ST8_BR + qtl:ST8_BR + qtl + 1], scale=1.0,
                        accum_out=st8[:, ST8_S3 + qtl:ST8_S3 + qtl + 1]), waits=[])
                  tk["exp"] = p.n("scalar")
                if os.environ.get("KK_SKIP_A"):
                    tk["corner"][buf] = p.n("vector")
                    tk["dgw"][buf] = p.n("sync")
                else:
                  p.emit("vector", lambda e, q0=q0, a0_=a0_: e.tensor_tensor(
                    crn[:], attn[:, a0_ + q0:a0_ + q0 + 128], tri[:], op=OP.mult),
                    waits=[("scalar", tk["exp"])])
                  tk["corner"][buf] = p.emit("vector", lambda e, qtl=qtl: e.tensor_reduce(
                    st8[:, ST8_C0 + qtl:ST8_C0 + qtl + 1], crn[:], axis=AX.X, op=OP.add))
                  offb = qtl * 24832
                  p.emit("sync", lambda e, q0=q0, a0_=a0_, offb=offb: e.dma_start(
                    out=aps["pscrB"][offb:offb + 128 * 192].rearrange("(q w) -> q w", w=192),
                    in_=attn[:, a0_ + q0:a0_ + q0 + 192]),
                    waits=[("scalar", tk["exp"])])
                  p.emit("sync", lambda e, qtl=qtl, offb=offb: e.dma_start(
                    out=Dg4[:, qtl * 65:(qtl + 1) * 65],
                    in_=aps["pscrB"][offb:offb + 128 * 193].rearrange("(q w) -> q w", w=193)[:, 0:65]),
                    waits=([("vector", tk["atile_prev"])] if qtl == 0 else []))
                  tk["dgw"][buf] = p.n("sync")
                if BARE:
                    continue
                for c in range(16):
                    w = []
                    if c == 0:
                        w = [("scalar", tk["exp"]), ("vector", tk["psT_evac"])]
                    p.emit("tensor", lambda e, c=c, a0_=a0_: e.transpose(
                        psT[:, c * 128:(c + 1) * 128], attn[:, a0_ + 32 + c * 128:a0_ + 32 + (c + 1) * 128],
                        idb[:]), waits=w)
                tk["transp"][buf] = p.n("tensor")
                tk["psT_evac"] = p.emit("vector", lambda e, qtl=qtl: e.tensor_copy(
                    attnT[:].rearrange("p (c q) -> p c q", q=512)[:, :, qtl * 128:(qtl + 1) * 128],
                    psT[:].rearrange("p (c q) -> p c q", q=128)),
                    waits=[("tensor", tk["transp"][buf])])
            if BARE:
                continue
            t_evacs = tk["psT_evac"]
            SKIPA = bool(os.environ.get("KK_SKIP_A"))
            for qtl in ([] if SKIPA else range(4)):
                qt = qb * 4 + qtl
                nfull = qt - 1
                first = True
                for c in range(max(0, nfull)):
                    p.emit("tensor", lambda e, c=c, qtl=qtl: e.matmul(
                        psX[:, 264 + qtl:265 + qtl],
                        attnT[:, c * 512 + qtl * 128:c * 512 + (qtl + 1) * 128],
                        onec[:], start=(c == 0), stop=False),
                        waits=([("vector", t_evacs)] if first else []))
                    first = False
                if qt >= 1:
                    p.emit("tensor", lambda e, qt=qt, qtl=qtl, st=(nfull <= 0): e.matmul(
                        psX[:, 264 + qtl:265 + qtl],
                        attnT[0:96, (qt - 1) * 512 + qtl * 128:(qt - 1) * 512 + (qtl + 1) * 128],
                        onec[0:96, :], start=st, stop=True),
                        waits=([("vector", t_evacs)] if first else []))
            t_a0 = p.n("tensor")
            p.emit("vector", lambda e: e.tensor_tensor(st8[:, ST8_S4:ST8_S4 + 4], st8[:, ST8_S1:ST8_S1 + 4],
                                                       st8[:, ST8_S2:ST8_S2 + 4], op=OP.add),
                   waits=[("scalar", tk["exp"])])
            p.emit("vector", lambda e: e.tensor_tensor(st8[:, ST8_S4:ST8_S4 + 4], st8[:, ST8_S4:ST8_S4 + 4],
                                                       st8[:, ST8_S3:ST8_S3 + 4], op=OP.add))
            t_rs4 = p.emit("vector", lambda e: e.reciprocal(st8[:, ST8_RS:ST8_RS + 4], st8[:, ST8_S4:ST8_S4 + 4]))
            if SKIPA:
                t_atile = p.n("vector")
                tk["psX_ve"] = max(tk["psX_ve"], t_atile)
                tk["atile_prev"] = t_atile
                t_arhs = tk["psT_evac"]
            dg4v = Dg4[:].rearrange("p (q r) -> p q r", r=65)
            at4v = Atile4[:].rearrange("p (q r) -> p q r", r=65)
            if not SKIPA:
              p.emit("vector", lambda e: e.tensor_tensor(
                st8[:, ST8_A0:ST8_A0 + 4], psX[:, 264:268], st8[:, ST8_C0:ST8_C0 + 4], op=OP.add),
                waits=[("tensor", t_a0), ("sync", tk["dgw"][0]), ("sync", tk["dgw"][1])])
            if not SKIPA:
              p.emit("vector", lambda e, dg4v=dg4v: e.tensor_reduce(
                  st8[:, ST8_T1:ST8_T1 + 4], dg4v[:, :, 1:64], axis=AX.X, op=OP.add))
            if not SKIPA:
              p.emit("vector", lambda e: e.tensor_tensor(st8[:, ST8_T1:ST8_T1 + 4], st8[:, ST8_T1:ST8_T1 + 4],
                                                         st8[:, ST8_A0:ST8_A0 + 4], op=OP.add))
            if not SKIPA:
              p.emit("vector", lambda e: e.tensor_tensor(st8[:, ST8_T1:ST8_T1 + 4], st8[:, ST8_S4:ST8_S4 + 4],
                                                         st8[:, ST8_T1:ST8_T1 + 4], op=OP.subtract))
            if not SKIPA:
              p.emit("vector", lambda e, dg4v=dg4v, at4v=at4v: e.tensor_copy(at4v[:, :, 1:64], dg4v[:, :, 1:64]))
            if not SKIPA:
              p.emit("vector", lambda e, at4v=at4v: e.tensor_copy(
                  at4v[:, :, 0:1], st8[:, ST8_A0:ST8_A0 + 4].rearrange("p (q o) -> p q o", o=1)))
            if not SKIPA:
                t_atile = p.emit("vector", lambda e, at4v=at4v: e.tensor_copy(
                    at4v[:, :, 64:65], st8[:, ST8_T1:ST8_T1 + 4].rearrange("p (q o) -> p q o", o=1)))
                tk["psX_ve"] = t_atile
                tk["atile_prev"] = t_atile
            if not SKIPA:
              for qtl in range(4):
                p.emit("tensor", lambda e, qtl=qtl: e.transpose(
                    psT[0:65, qtl * 128:(qtl + 1) * 128], Atile4[:, qtl * 65:(qtl + 1) * 65], idb[:]),
                    waits=([("vector", t_atile)] if qtl == 0 else []))
              t_atr = p.n("tensor")
              t_arhs = p.emit("vector", lambda e: e.tensor_copy(Arhs4[0:65, :], psT[0:65, 0:512]),
                            waits=[("tensor", t_atr)])
              tk["psT_evac"] = t_arhs
              for qtl in range(4):
                w = [("vector", t_arhs)] if qtl == 0 else []
                if qtl == 0:
                    w.append(("vector", tk["ctxmult"]))
                p.emit("tensor", lambda e, qtl=qtl: e.matmul(
                    psC[0:64, qtl * 128:(qtl + 1) * 128], relvN[0:65, 0:64],
                    Arhs4[0:65, qtl * 128:(qtl + 1) * 128], start=(qtl == 0), stop=False), waits=w)
            if os.environ.get("KK_NOAV"):
                continue
            for c in range(16):
                w = []
                if SKIPA and c == 0:
                    w = [("vector", max(tk["ctxmult"], t_evacs))]
                p.emit("tensor", lambda e, h=h, c=c, st=(SKIPA and c == 0): e.matmul(
                    psC[0:64, :], V[:, c * 1024 + h * 64:c * 1024 + (h + 1) * 64],
                    attnT[:, c * 512:(c + 1) * 512], start=st, stop=(c == 15)), waits=w)
            t_av = p.n("tensor")
            if os.environ.get("KK_SKIP_RS"):
                tk["rsbevac_se"] = p.n("scalar")
                tk["ctxmult"] = p.emit("vector", lambda e, h=h: e.tensor_copy(
                    ctxfm[(h % 2) * 64:(h % 2) * 64 + 64, (h // 2) * 512:(h // 2 + 1) * 512],
                    psC[0:64, :]), waits=[("tensor", t_av)])
                tk["rsbevac"] = tk["ctxmult"]
                continue
            t_rstr = p.emit("tensor", lambda e: e.transpose(psX[0:4, 300:428], st8[:, ST8_RS:ST8_RS + 4],
                                                            idf[:]),
                            waits=[("vector", t_atile)])
            t_rsrow = p.emit("vector", lambda e: e.tensor_copy(rsrow[0:4, :], psX[0:4, 300:428]),
                             waits=[("tensor", t_rstr), ("sync", tk.get("flat_prev", 0))])
            tk["psX_ve"] = t_rsrow
            t_flat = p.emit("sync", lambda e: e.dma_start(out=rsrow1[0:1, :], in_=rsrow[0:4, :]),
                            waits=[("vector", t_rsrow), ("tensor", tk.get("rsmm_prev", 0))])
            tk["flat_prev"] = t_flat
            for qtl in range(4):
                w = [("sync", t_flat)] if qtl == 0 else []
                if qtl == 0:
                    w.append(("scalar", tk["exp"]))
                p.emit("tensor", lambda e, qtl=qtl: e.matmul(
                    psS[:, qtl * 128:(qtl + 1) * 128], ones1[0:1, :],
                    rsrow1[0:1, qtl * 128:(qtl + 1) * 128], start=True, stop=True), waits=w)
            t_rsmm = p.n("tensor")
            tk["rsmm_prev"] = t_rsmm
            tk["rsbevac_se"] = p.emit("scalar", lambda e: e.activation(rsb[:], psS[:, 0:512], AF.Copy,
                                                                       bias=0.0, scale=1.0),
                                      waits=[("tensor", t_rsmm)])
            tk["ctxmult"] = p.emit("vector", lambda e, h=h: e.tensor_tensor(
                ctxfm[(h % 2) * 64:(h % 2) * 64 + 64, (h // 2) * 512:(h // 2 + 1) * 512],
                psC[0:64, :], rsb[0:64, :], op=OP.mult),
                waits=[("scalar", tk["rsbevac_se"]), ("tensor", t_av)])
            tk["rsbevac"] = tk["ctxmult"]
        if os.environ.get("KK_BARE") or os.environ.get("KK_NOAV") or os.environ.get("KK_NOWO"):
            continue
        t_allctx = tk["ctxmult"]
        t_woev = 0
        for tsub in range(4):
            t = qb * 4 + tsub
            for oh in range(2):
                pst = psS[:, oh * 512:(oh + 1) * 512]
                for dc in range(8):
                    w = []
                    if oh == 0 and dc == 0:
                        w = [("vector", max(t_allctx, t_woev))]
                        if tsub == 0:
                            w.append(("scalar", tk["rsbevac_se"]))
                    p.emit("tensor", lambda e, tsub=tsub, oh=oh, dc=dc, pst=pst: e.matmul(
                        pst, ctxfm[:, dc * 512 + tsub * 128:dc * 512 + (tsub + 1) * 128],
                        wt[:, dc * 1024 + oh * 512:dc * 1024 + (oh + 1) * 512],
                        start=(dc == 0), stop=(dc == 7)), waits=w)
            t_wo = p.n("tensor")
            ob = outb[:, (tsub % 2) * 1024:(tsub % 2) * 1024 + 1024]
            p.emit("vector", lambda e, ob=ob: e.tensor_tensor(ob, psS[:, 0:1024], bot[:], op=OP.add),
                   waits=[("tensor", t_wo), ("gpsimd", max(0, p.n("gpsimd") - 16))])
            t_woev = p.n("vector")
            tk["woevac"] = t_woev
            p.emit("gpsimd", lambda e, t=t, ob=ob: e.dma_start(
                out=aps["a0"][t * 128:(t + 1) * 128, :], in_=ob),
                waits=[("vector", tk["woevac"])])



def phase_ffn0_v2(p, io, env):
    aps = io
    ones1 = env["ones1"]
    idf2 = env["idf"]
    xn = p.sb("xn", [128, 8 * 2048], BF16)
    w1q = p.sb("w1q", [128, 2 * 8 * 1024], BF16)
    w2q = p.sb("w2q", [128, 2 * 8 * 1024], BF16)
    hbuf = p.sb("hbuf", [128, 8 * 512], BF16)
    b1c = p.sb("b1c", [128, 32], F32)
    acc = p.sb("acc", [128, 16 * 1024], F32)
    b2t = p.sb("b2t", [128, 1024], F32)
    b1st = p.sb("b1st", [32, 128], F32)
    outb = p.sb("outb", [128, 2 * 1024], F32)
    brow = outb[0:1, 0:1024]
    psA = p.ps("psA", [128, 512], F32)
    psB = p.ps("psB", [128, 512], F32)
    psC = p.ps("psC", [128, 1024], F32)
    psD = p.ps("psD", [128, 1024], F32)

    for c in range(8):
        p.emit("sync", lambda e, c=c: e.dma_start(out=xn[:, c * 2048:(c + 1) * 2048],
                                                  in_=aps["xn_bf"][c * 128:(c + 1) * 128, :]))
    p.emit("sync", lambda e: e.dma_start(out=b1st[:], in_=aps["b1"].rearrange("(ic p) -> ic p", p=128)))
    t0 = p.emit("tensor", lambda e: e.transpose(psC[:, 0:32], b1st[:, 0:128], idf2[0:32, 0:32]),
                waits=[("sync", p.n("sync"))])
    p.emit("vector", lambda e: e.tensor_copy(b1c[:], psC[:, 0:32]), waits=[("tensor", t0)])
    emit_bcast_row(p, aps["b2"], 1024, b2t, brow, ones1, psC)

    t_wld = [0] * 4
    def load_quarter(iq):
        slot = iq % 2
        for dc in range(8):
            p.emit("gpsimd", lambda e, dc=dc, iq=iq, slot=slot: e.dma_start(
                out=w1q[:, slot * 8192 + dc * 1024:slot * 8192 + (dc + 1) * 1024],
                in_=aps["W1"][dc * 128:(dc + 1) * 128, iq * 1024:(iq + 1) * 1024]),
                waits=([("tensor", p.n("tensor"))] if dc == 0 else []))
        for ic in range(8):
            p.emit("gpsimd", lambda e, ic=ic, iq=iq, slot=slot: e.dma_start(
                out=w2q[:, slot * 8192 + ic * 1024:slot * 8192 + (ic + 1) * 1024],
                in_=aps["W2"][iq * 1024 + ic * 128:iq * 1024 + (ic + 1) * 128, :]))
        t_wld[iq] = p.n("gpsimd")

    load_quarter(0)
    load_quarter(1)
    relu_tok = [0, 0]
    acc_tok = [0, 0]
    for iq in range(4):
        slot = iq % 2
        if iq >= 2:
            pass
        for qb in range(4):
            for ic in range(8):
                pst, bank = (psA, 0) if ic % 2 == 0 else (psB, 1)
                for dc in range(8):
                    w = []
                    if dc == 0:
                        w = [("scalar", relu_tok[bank])]
                        if ic == 0 and qb == 0:
                            w.append(("gpsimd", t_wld[iq]))
                            w.append(("sync", p.n("sync")))
                    p.emit("tensor", lambda e, ic=ic, dc=dc, qb=qb, pst=pst, slot=slot: e.matmul(
                        pst[:], w1q[:, slot * 8192 + dc * 1024 + ic * 128:slot * 8192 + dc * 1024 + (ic + 1) * 128],
                        xn[:, dc * 2048 + qb * 512:dc * 2048 + (qb + 1) * 512],
                        start=(dc == 0), stop=(dc == 7)), waits=w)
                t_mm = p.n("tensor")
                icg = iq * 8 + ic
                relu_tok[bank] = p.emit("scalar", lambda e, ic=ic, icg=icg, pst=pst: e.activation(
                    hbuf[:, ic * 512:(ic + 1) * 512], pst[:], AF.Relu,
                    bias=b1c[:, icg:icg + 1], scale=1.0),
                    waits=[("tensor", t_mm)])
            for tsub in range(4):
                t = qb * 4 + tsub
                pst, bank = (psC, 0) if tsub % 2 == 0 else (psD, 1)
                for ic in range(8):
                    for oc in range(2):
                        w = []
                        if ic == 0 and oc == 0:
                            w = [("scalar", relu_tok[0]), ("scalar", relu_tok[1]),
                                 ("vector", acc_tok[bank])]
                        p.emit("tensor", lambda e, ic=ic, oc=oc, tsub=tsub, pst=pst, slot=slot: e.matmul(
                            pst[:, oc * 512:(oc + 1) * 512],
                            hbuf[:, ic * 512 + tsub * 128:ic * 512 + (tsub + 1) * 128],
                            w2q[:, slot * 8192 + ic * 1024 + oc * 512:slot * 8192 + ic * 1024 + (oc + 1) * 512],
                            start=(ic == 0), stop=(ic == 7)), waits=w)
                t_mm2 = p.n("tensor")
                if iq == 0:
                    acc_tok[bank] = p.emit("vector", lambda e, t=t, pst=pst: e.tensor_copy(
                        acc[:, t * 1024:(t + 1) * 1024], pst[:]), waits=[("tensor", t_mm2)])
                else:
                    acc_tok[bank] = p.emit("vector", lambda e, t=t, pst=pst: e.tensor_tensor(
                        acc[:, t * 1024:(t + 1) * 1024], acc[:, t * 1024:(t + 1) * 1024], pst[:], op=OP.add),
                        waits=[("tensor", t_mm2)])
                if iq == 3:
                    ob = outb[:, (t % 2) * 1024:(t % 2) * 1024 + 1024]
                    p.emit("vector", lambda e, ob=ob, t=t: e.tensor_tensor(
                        ob, acc[:, t * 1024:(t + 1) * 1024], b2t[:], op=OP.add),
                        waits=[("gpsimd", max(0, p.n("gpsimd") - 16))])
                    acc_tok[bank] = p.n("vector")
                    p.emit("gpsimd", lambda e, t=t, ob=ob: e.dma_start(
                        out=aps["f0"][t * 128:(t + 1) * 128, :], in_=ob),
                        waits=[("vector", p.n("vector"))])
        if iq + 2 <= 3:
            load_quarter(iq + 2)



def phase_attn1_v2(p, io, env):
    aps = io
    idf = env["idf"]
    idb = env["idb"]
    ones1 = env["ones1"]

    xn = p.sb("xn", [128, 8 * 2048], BF16)
    wt = p.sb("wt", [128, 8 * 1024], BF16)
    wstage = p.sb("wstage", [128, 1024], F32)
    Q = p.sb("Q", [128, 8 * 2048], BF16)
    K = p.sb("K", [128, 8 * 2048], BF16)
    V = p.sb("V", [128, 16 * 1024], BF16)
    attn = p.sb("attn", [128, 2048], BF16)
    attnT = p.sb("attnT", [128, 16 * 512], BF16)
    ctxfm = p.sb("ctxfm", [128, 8 * 512], BF16)
    qkb = p.sb("qkb", [128, 1024], BF16)
    rst = p.sb("rst", [128, 16], F32)
    wnt = p.sb("wnt", [128, 1024], F32)
    st8 = p.sb("st8", [128, 16], F32)
    bqt = p.sb("bqt", [128, 1024], F32)
    bpt = p.sb("bpt", [128, 1024], F32)
    rsrow = p.sb("rsrow", [4, 128], F32)
    rsrow1 = p.sb("rsrow1", [1, 512], F32)
    rsb = p.sb("rsb", [128, 512], F32)
    wrow = p.sb("wrow", [1, 64], F32)
    outb = p.sb("outb", [128, 2 * 1024], F32)
    qkt = outb[:, 0:1024]
    sq = outb[:, 1024:2048]
    brow = outb[0:1, 0:1024]

    psS = p.ps("psS", [128, 2048], F32)
    psT = p.ps("psT", [128, 2048], BF16)
    psC = p.ps("psC", [128, 512], F32)
    psX = p.ps("psX", [128, 512], F32)

    ST8_NM, ST8_S4, ST8_RS = 0, 4, 8

    for c in range(8):
        p.emit("sync", lambda e, c=c: e.dma_start(out=xn[:, c * 2048:(c + 1) * 2048],
                                                  in_=aps["xn_bf"][c * 128:(c + 1) * 128, :]))
    emit_bcast_row(p, aps["bproj"], 1024, bpt, brow, ones1, psS)

    def emit_wn_tile(nm):
        p.emit("sync", lambda e, nm=nm: e.dma_start(out=wrow[0:1, :], in_=aps[nm][0:DK].rearrange("(a b) -> a b", a=1)),
               waits=[("tensor", p.n("tensor"))])
        p.emit("tensor", lambda e: e.matmul(psS[:, 0:64], ones1[0:1, :], wrow[0:1, :], start=True, stop=True),
               waits=[("sync", p.n("sync")), ("vector", p.n("vector"))])
        for j in range(16):
            p.emit("vector", lambda e, j=j: e.tensor_copy(
                wnt[:, j * 64:(j + 1) * 64], psS[:, 0:64]),
                waits=[("tensor", p.n("tensor"))] if j == 0 else [])

    for part, dest in ((0, Q), (1, K), (2, V)):
        emit_bcast_row(p, aps["bqkv"][part * D:(part + 1) * D], 1024, bqt, brow, ones1, psS)
        if part == 0:
            emit_wn_tile("qn_w")
        elif part == 1:
            emit_wn_tile("kn_w")
        emit_weight_T(p, aps["Wqkv"][part * D:(part + 1) * D, :], D, D, wt, wstage, psS, idf)
        t_ps = p.n("vector")
        t_pse = p.n("scalar")
        t_tr_ev = p.n("vector")
        for t in range(16):
            for oh in range(2):
                for dc in range(8):
                    w = []
                    if oh == 0 and dc == 0:
                        w = [("vector", t_ps), ("scalar", t_pse), ("sync", p.n("sync"))]
                    p.emit("tensor", lambda e, t=t, oh=oh, dc=dc: e.matmul(
                        psS[:, oh * 512:oh * 512 + 512],
                        xn[:, dc * 2048 + t * 128:dc * 2048 + (t + 1) * 128],
                        wt[:, dc * 1024 + oh * 512:dc * 1024 + (oh + 1) * 512],
                        start=(dc == 0), stop=(dc == 7)), waits=w)
            t_mm = p.n("tensor")
            if part == 2:
                t_ps = p.emit("vector", lambda e, t=t: e.tensor_tensor(
                    V[:, t * 1024:(t + 1) * 1024], psS[:, 0:1024], bqt[:], op=OP.add),
                    waits=[("tensor", t_mm)])
            else:
                p.emit("vector", lambda e: e.tensor_tensor(qkt[:], psS[:, 0:1024], bqt[:], op=OP.add),
                       waits=[("tensor", t_mm)])
                p.emit("vector", lambda e: e.tensor_tensor(sq[:], qkt[:], qkt[:], op=OP.mult))
                t_ps = p.emit("vector", lambda e: e.tensor_reduce(
                    rst[:, 0:16], sq[:].rearrange("p (g d) -> p g d", d=64), axis=AX.X, op=OP.add))
                eps_r = p.cc(1e-6)
                p.emit("scalar", lambda e, eps_r=eps_r: e.activation(rst[:, 0:16], rst[:, 0:16], AF.Ln, bias=eps_r, scale=1.0 / DK),
                       waits=[("vector", t_ps)])
                t_pse = p.emit("scalar", lambda e: e.activation(rst[:, 0:16], rst[:, 0:16], AF.Exp, bias=0.0, scale=-0.5))
                p.emit("vector", lambda e: e.tensor_tensor(
                    qkt[:].rearrange("p (g d) -> p g d", d=64),
                    qkt[:].rearrange("p (g d) -> p g d", d=64),
                    rst[:, 0:16].rearrange("p (g o) -> p g o", o=1).broadcast_to([128, 16, 64]),
                    op=OP.mult), waits=[("scalar", t_pse)])
                sc2 = 0.125 if part == 0 else 1.0
                t_ps = p.emit("vector", lambda e, sc2=sc2: e.scalar_tensor_tensor(
                    qkb[:], qkt[:], sc2, wnt[:], op0=OP.mult, op1=OP.mult))
                for c in range(8):
                    w = []
                    if c == 0:
                        w = [("vector", t_ps)]
                        if t_tr_ev:
                            w.append(("vector", t_tr_ev))
                    p.emit("tensor", lambda e, c=c: e.transpose(
                        psT[:, c * 128:(c + 1) * 128], qkb[:, c * 128:(c + 1) * 128], idb[:]), waits=w)
                t_trp = p.n("tensor")
                t_tr_ev = p.emit("vector", lambda e, t=t, dest=dest: e.tensor_copy(
                    dest[:].rearrange("p (c s) -> p c s", s=S)[:, :, t * 128:(t + 1) * 128],
                    psT[:, 0:1024].rearrange("p (c q) -> p c q", q=128)),
                    waits=[("tensor", t_trp)])
    emit_weight_T(p, aps["Wproj"], D, D, wt, wstage, psS, idf)

    tk = {"exp": 0, "transp": 0, "psT_evac": p.n("vector"), "ctxmult": 0,
          "rsbevac_se": p.n("scalar"), "woevac": p.n("vector"),
          "flat_prev": p.n("sync"), "rsmm_prev": p.n("tensor")}
    for qb in range(4):
        for h in range(H):
            for qtl in range(4):
                qt = qb * 4 + qtl
                for kb in range(4):
                    w = []
                    if kb == 0:
                        w = [("scalar", tk["exp"])]
                        if qtl == 0:
                            w.append(("vector", tk["woevac"]))
                            w.append(("scalar", tk["rsbevac_se"]))
                    p.emit("tensor", lambda e, h=h, qt=qt, kb=kb: e.matmul(
                        psS[:, kb * 512:(kb + 1) * 512],
                        hsl(Q, h, qt * 128, 128), hsl(K, h, kb * 512, 512),
                        start=True, stop=True), waits=w)
                t_sc = p.n("tensor")
                t_mx = p.emit("vector", lambda e, qtl=qtl: e.tensor_reduce(
                    st8[:, ST8_NM + qtl:ST8_NM + qtl + 1], psS[:], axis=AX.X, op=OP.max, negate=True),
                    waits=[("tensor", t_sc)])
                p.emit("scalar", lambda e, qtl=qtl: e.activation(
                    attn[:], psS[:], AF.Exp, bias=st8[:, ST8_NM + qtl:ST8_NM + qtl + 1], scale=1.0,
                    accum_out=st8[:, ST8_S4 + qtl:ST8_S4 + qtl + 1]),
                    waits=[("vector", t_mx), ("tensor", tk["transp"])])
                tk["exp"] = p.n("scalar")
                for c in range(16):
                    w = []
                    if c == 0:
                        w = [("scalar", tk["exp"]), ("vector", tk["psT_evac"])]
                    p.emit("tensor", lambda e, c=c: e.transpose(
                        psT[:, c * 128:(c + 1) * 128], attn[:, c * 128:(c + 1) * 128], idb[:]), waits=w)
                tk["transp"] = p.n("tensor")
                tk["psT_evac"] = p.emit("vector", lambda e, qtl=qtl: e.tensor_copy(
                    attnT[:].rearrange("p (c q) -> p c q", q=512)[:, :, qtl * 128:(qtl + 1) * 128],
                    psT[:].rearrange("p (c q) -> p c q", q=128)),
                    waits=[("tensor", tk["transp"])])
            t_evacs = tk["psT_evac"]
            t_rs4 = p.emit("vector", lambda e: e.reciprocal(st8[:, ST8_RS:ST8_RS + 4], st8[:, ST8_S4:ST8_S4 + 4]),
                           waits=[("scalar", tk["exp"])])
            for c in range(16):
                w = []
                if c == 0:
                    w = [("vector", max(tk["ctxmult"], t_evacs))]
                p.emit("tensor", lambda e, h=h, c=c, st=(c == 0): e.matmul(
                    psC[0:64, :], V[:, c * 1024 + h * 64:c * 1024 + (h + 1) * 64],
                    attnT[:, c * 512:(c + 1) * 512], start=st, stop=(c == 15)), waits=w)
            t_av = p.n("tensor")
            t_rstr = p.emit("tensor", lambda e: e.transpose(psX[0:4, 300:428], st8[:, ST8_RS:ST8_RS + 4], idf[:]),
                            waits=[("vector", t_rs4)])
            t_rsrow = p.emit("vector", lambda e: e.tensor_copy(rsrow[0:4, :], psX[0:4, 300:428]),
                             waits=[("tensor", t_rstr), ("sync", tk["flat_prev"])])
            t_flat = p.emit("sync", lambda e: e.dma_start(out=rsrow1[0:1, :], in_=rsrow[0:4, :]),
                            waits=[("vector", t_rsrow), ("tensor", tk["rsmm_prev"])])
            tk["flat_prev"] = t_flat
            for qtl in range(4):
                w = []
                if qtl == 0:
                    w = [("sync", t_flat), ("scalar", tk["exp"])]
                p.emit("tensor", lambda e, qtl=qtl: e.matmul(
                    psS[:, qtl * 128:(qtl + 1) * 128], ones1[0:1, :],
                    rsrow1[0:1, qtl * 128:(qtl + 1) * 128], start=True, stop=True), waits=w)
            t_rsmm = p.n("tensor")
            tk["rsmm_prev"] = t_rsmm
            tk["rsbevac_se"] = p.emit("scalar", lambda e: e.activation(rsb[:], psS[:, 0:512], AF.Copy,
                                                                      bias=0.0, scale=1.0),
                                      waits=[("tensor", t_rsmm)])
            tk["ctxmult"] = p.emit("vector", lambda e, h=h: e.tensor_tensor(
                ctxfm[(h % 2) * 64:(h % 2) * 64 + 64, (h // 2) * 512:(h // 2 + 1) * 512],
                psC[0:64, :], rsb[0:64, :], op=OP.mult),
                waits=[("scalar", tk["rsbevac_se"]), ("tensor", t_av)])
        t_allctx = tk["ctxmult"]
        t_woev = 0
        for tsub in range(4):
            t = qb * 4 + tsub
            for oh in range(2):
                pst = psS[:, oh * 512:(oh + 1) * 512]
                for dc in range(8):
                    w = []
                    if oh == 0 and dc == 0:
                        w = [("vector", max(t_allctx, t_woev))]
                        if tsub == 0:
                            w.append(("scalar", tk["rsbevac_se"]))
                    p.emit("tensor", lambda e, tsub=tsub, oh=oh, dc=dc, pst=pst: e.matmul(
                        pst, ctxfm[:, dc * 512 + tsub * 128:dc * 512 + (tsub + 1) * 128],
                        wt[:, dc * 1024 + oh * 512:dc * 1024 + (oh + 1) * 512],
                        start=(dc == 0), stop=(dc == 7)), waits=w)
            t_wo = p.n("tensor")
            ob = outb[:, (tsub % 2) * 1024:(tsub % 2) * 1024 + 1024]
            p.emit("vector", lambda e, ob=ob: e.tensor_tensor(ob, psS[:, 0:1024], bpt[:], op=OP.add),
                   waits=[("tensor", t_wo), ("gpsimd", max(0, p.n("gpsimd") - 16))])
            t_woev = p.n("vector")
            tk["woevac"] = t_woev
            p.emit("gpsimd", lambda e, t=t, ob=ob: e.dma_start(
                out=aps["a1"][t * 128:(t + 1) * 128, :], in_=ob),
                waits=[("vector", t_woev)])



def build_fused(nc, debug=False):
    ins = {
        "x": ((S, D), F32),
        "Wq": ((D, D), F32), "Wk": ((D, D), F32), "Wv": ((D, D), F32), "Wo": ((D, D), F32),
        "b_o": ((D,), F32), "rel_k": ((65, 64), F32), "rel_v": ((65, 64), F32),
        "W1": ((D, INTER), F32), "b1": ((INTER,), F32),
        "W2": ((INTER, D), F32), "b2": ((D,), F32),
        "Wqkv": ((3 * D, D), F32), "bqkv": ((3 * D,), F32),
        "Wproj": ((D, D), F32), "bproj": ((D,), F32),
        "qn_w": ((DK,), F32), "kn_w": ((DK,), F32),
        "router_w": ((NE, D), F32), "router_b": ((NE,), F32),
        "expert_w": ((NE, D, D), F32),
        "ident_f32": ((128, 128), F32), "ident_bf": ((128, 128), BF16),
        "tri_bf": ((128, 128), BF16), "ones_col_bf": ((128, 1), BF16),
        "ones_row_f32": ((1, 128), F32),
    }
    outs = {"out": ((S, D), F32)}
    aps = declare_io(nc, ins, outs)
    for nm, shp, dt in [("xn_d", (D, S), BF16), ("xnf_d", (D, S), F32),
                        ("a_d", (S, D), F32), ("f_d", (S, D), F32),
                        ("x1_d", (S, D), F32), ("a1_d", (S, D), F32),
                        ("m1_d", (S, D), F32),
                        ("pscrP", (4 * 40960 + 256,), F32),
                        ("pscrB", (4 * 24832 + 256,), BF16)]:
        kind = "ExternalOutput" if (debug and nm not in ("pscrP", "pscrB")) else "Internal"
        aps[nm] = nc.dram_tensor(nm, list(shp), dt, kind=kind).ap()

    p = Prog(nc)
    env = {
        "idf": p.sb("idf", [128, 128], F32),
        "idb": p.sb("idb", [128, 128], BF16),
        "tri": p.sb("tri", [128, 128], BF16),
        "onec": p.sb("onec", [128, 1], BF16),
        "ones1": p.sb("ones1", [1, 128], F32),
    }
    for nm, key in [("idf", "ident_f32"), ("idb", "ident_bf"), ("tri", "tri_bf"),
                    ("onec", "ones_col_bf"), ("ones1", "ones_row_f32")]:
        p.emit("sync", lambda e, nm=nm, key=key: e.dma_start(out=env[nm][:], in_=aps[key]))

    with p.phase("ln0"):
        phase_ln(p, {"x": aps["x"], "xn_bf": aps["xn_d"]}, env, want_f32=False)
    with p.phase("attn0"):
        phase_attn0_v2(p, {"xn_bf": aps["xn_d"], "Wq": aps["Wq"], "Wk": aps["Wk"],
                        "Wv": aps["Wv"], "Wo": aps["Wo"], "b_o": aps["b_o"],
                        "rel_k": aps["rel_k"], "rel_v": aps["rel_v"], "a0": aps["a_d"],
                        "pscrP": aps["pscrP"], "pscrB": aps["pscrB"]}, env)
    with p.phase("ffn0"):
        phase_ffn0_v2(p, {"xn_bf": aps["xn_d"], "W1": aps["W1"], "b1": aps["b1"],
                       "W2": aps["W2"], "b2": aps["b2"], "f0": aps["f_d"]}, env)
    with p.phase("comb0"):
        phase_comb(p, {"x": aps["x"], "a": aps["a_d"], "b": aps["f_d"], "o": aps["x1_d"]})
    with p.phase("ln1"):
        phase_ln(p, {"x": aps["x1_d"], "xn_bf": aps["xn_d"], "xn_f32": aps["xnf_d"]},
                 env, want_f32=True)
    with p.phase("attn1"):
        phase_attn1_v2(p, {"xn_bf": aps["xn_d"], "Wqkv": aps["Wqkv"], "bqkv": aps["bqkv"],
                        "Wproj": aps["Wproj"], "bproj": aps["bproj"], "qn_w": aps["qn_w"],
                        "kn_w": aps["kn_w"], "a1": aps["a1_d"]}, env)
    with p.phase("moe"):
        phase_moe(p, {"xn_bf": aps["xn_d"], "xn_f32": aps["xnf_d"],
                      "router_w": aps["router_w"], "router_b": aps["router_b"],
                      "expert_w": aps["expert_w"], "m1": aps["m1_d"]}, env)
    with p.phase("comb1"):
        phase_comb(p, {"x": aps["x1_d"], "a": aps["a1_d"], "b": aps["m1_d"], "o": aps["out"]})
    p.finalize()
    return nc


def kernel(L0_attn_x, L0_attn_Wq, L0_attn_Wk, L0_attn_Wv, L0_attn_Wo, L0_attn_b_o,
           L0_attn_rel_k_emb, L0_attn_rel_v_emb, L0_ffn_W1, L0_ffn_b1, L0_ffn_W2,
           L0_ffn_b2, L1_attn_Wqkv, L1_attn_bqkv, L1_attn_Wproj, L1_attn_bproj,
           L1_attn_qn_w, L1_attn_kn_w, L1_ffn_router_w, L1_ffn_router_b,
           L1_ffn_expert_w):
    C = CONSTS
    x = np.ascontiguousarray(np.asarray(L0_attn_x, dtype=np.float32))
    shared = {
        "Wq": L0_attn_Wq, "Wk": L0_attn_Wk, "Wv": L0_attn_Wv, "Wo": L0_attn_Wo,
        "b_o": L0_attn_b_o, "rel_k": L0_attn_rel_k_emb, "rel_v": L0_attn_rel_v_emb,
        "W1": L0_ffn_W1, "b1": L0_ffn_b1, "W2": L0_ffn_W2, "b2": L0_ffn_b2,
        "Wqkv": L1_attn_Wqkv, "bqkv": L1_attn_bqkv, "Wproj": L1_attn_Wproj,
        "bproj": L1_attn_bproj, "qn_w": L1_attn_qn_w, "kn_w": L1_attn_kn_w,
        "router_w": L1_ffn_router_w, "router_b": L1_ffn_router_b,
        "expert_w": L1_ffn_expert_w,
    }
    shared = {k: np.ascontiguousarray(np.asarray(v, dtype=np.float32)) for k, v in shared.items()}
    shared.update(C)
    in_maps = []
    for b in range(NCORES):
        d = dict(shared)
        d["x"] = np.ascontiguousarray(x[b])
        in_maps.append(d)
    nc = bass.Bass("TRN2", target_bir_lowering=False)
    build_fused(nc)
    res = run_bass_kernel_spmd(nc, in_maps, core_ids=list(range(NCORES)))
    return np.stack([np.asarray(res.results[i]["out"], dtype=np.float32)
                     for i in range(NCORES)], axis=0)


# version 06
# speedup vs baseline: 1.3879x; 1.0055x over previous
import contextlib
import os
import numpy as np
import ml_dtypes
import concourse.bass as bass
import concourse.mybir as mybir
from concourse.alu_op_type import AluOpType
from concourse.bass_utils import run_bass_kernel_spmd

F32 = mybir.dt.float32
BF16 = mybir.dt.bfloat16
AF = mybir.ActivationFunctionType
AX = mybir.AxisListType
OP = AluOpType

D = 1024
S = 2048
H = 16
DK = 64
NCORES = 8
INTER = 4096
NE = 16

ENGS = ("sync", "gpsimd", "vector", "scalar", "tensor")


class Sem:
    __slots__ = ("h", "step", "n")

    def __init__(self, h, step):
        self.h = h
        self.step = step
        self.n = 0


class Prog:

    def __init__(self, nc):
        self.nc = nc
        self.q = {e: [] for e in ENGS}
        self.stack = contextlib.ExitStack()
        self.scopes = [self.stack]
        self._uid = 0
        self._barrier = {}
        self.sem = {}
        for e in ENGS:
            step = 16 if e in ("sync", "gpsimd") else 1
            self.sem[e] = Sem(self.stack.enter_context(nc.semaphore("S" + e)), step)
        self._cpool = self.stack.enter_context(nc.sbuf_tensor("cpool", [128, 16], F32))
        self._cmap = {}

    def n(self, eng):
        return self.sem[eng].n

    def sb(self, name, shape, dtype=F32):
        self._uid += 1
        return self.scopes[-1].enter_context(
            self.nc.sbuf_tensor(f"{name}_{self._uid}", shape, dtype))

    @contextlib.contextmanager
    def phase(self, name):
        st = contextlib.ExitStack()
        self.scopes.append(st)
        try:
            yield
        finally:
            self.scopes.pop()
            st.close()
            snap = [(e, self.sem[e].n) for e in ENGS]
            for e in ENGS:
                self._barrier[e] = snap

    def cc(self, val):
        if val not in self._cmap:
            col = len(self._cmap)
            self._cmap[val] = col
            self.emit("vector", lambda e, col=col, val=val: e.memset(self._cpool[:, col:col + 1], val))
        c = self._cmap[val]
        return self._cpool[:, c:c + 1]

    def ps(self, name, shape, dtype=F32):
        self._uid += 1
        return self.scopes[-1].enter_context(
            self.nc.psum_tensor(f"{name}_{self._uid}", shape, dtype))

    def emit(self, eng, fn, waits=()):
        sem = self.sem[eng]
        waits = list(waits)
        if eng in self._barrier:
            waits.extend((e2, n) for (e2, n) in self._barrier.pop(eng) if e2 != eng)
        wl = []
        for e2, n in waits:
            s2 = self.sem[e2]
            assert n <= s2.n, f"wait on future value of {e2}: {n} > {s2.n}"
            if n > 0:
                wl.append((s2.h, n))
        byh = {}
        for h, n in wl:
            byh[h] = max(n, byh.get(h, 0))
        wl = tuple(byh.items())

        if eng in ("vector", "scalar"):
            def cl(e, fn=fn, wl=wl, sem=sem):
                for h, n in wl:
                    e.wait_ge(h, n)
                fn(e)
                e.drain().then_inc(sem.h, sem.step)
        else:
            def cl(e, fn=fn, wl=wl, sem=sem):
                for h, n in wl:
                    e.wait_ge(h, n)
                fn(e).then_inc(sem.h, sem.step)

        self.q[eng].append(cl)
        sem.n += sem.step
        return sem.n

    def wother(self, eng):
        return [(e, self.sem[e].n) for e in ENGS if e != eng]

    def finalize(self):
        snap = [(e, self.sem[e].n) for e in ENGS]

        def fin(e, snap=snap):
            for eng, n in snap:
                if n > 0:
                    e.wait_ge(self.sem[eng].h, n)
        self.q["gpsimd"].append(fin)
        with self.nc.Block() as block:
            for eng in ENGS:
                lst = self.q[eng]
                if lst:
                    getattr(block, eng)(lambda e, lst=lst: [c(e) for c in lst])
        self.stack.close()


def np_bf16(a):
    return np.asarray(a).astype(ml_dtypes.bfloat16)


def consts_np():
    return {
        "ident_f32": np.eye(128, dtype=np.float32),
        "ident_bf": np_bf16(np.eye(128)),
        "tri_bf": np_bf16(np.tril(np.ones((128, 128)))),
        "ones_col_bf": np_bf16(np.ones((128, 1))),
        "ones_row_f32": np.ones((1, 128), dtype=np.float32),
    }


CONSTS = consts_np()


def declare_io(nc, ins, outs):
    aps = {}
    for n, (shp, dt) in ins.items():
        aps[n] = nc.dram_tensor(n, list(shp), dt, kind="ExternalInput").ap()
    for n, (shp, dt) in outs.items():
        aps[n] = nc.dram_tensor(n, list(shp), dt, kind="ExternalOutput").ap()
    return aps



def emit_weight_T(p, w_dram, O, I, wt_sb, wstage, psT, idf):
    nIC, nOC = I // 128, O // 128
    nbuf = max(1, wstage.shape[1] // I)
    for oc in range(nOC):
        st = wstage[:, (oc % nbuf) * I:(oc % nbuf) * I + I]
        p.emit("sync", lambda e, oc=oc, st=st: e.dma_start(
            out=st, in_=w_dram[oc * 128:(oc + 1) * 128, :]),
            waits=[("tensor", p.n("tensor"))])
        for ic in range(nIC):
            p.emit("tensor", lambda e, ic=ic, st=st: e.transpose(
                psT[:, ic * 128:(ic + 1) * 128], st[:, ic * 128:(ic + 1) * 128], idf[:]),
                waits=[("sync", p.n("sync")), ("vector", p.n("vector")), ("scalar", p.n("scalar"))])
        p.emit("vector", lambda e, oc=oc: e.tensor_copy(
            wt_sb[:].rearrange("p (ic o) -> p ic o", o=O)[:, :, oc * 128:(oc + 1) * 128],
            psT[:, 0:nIC * 128].rearrange("p (ic q) -> p ic q", q=128)),
            waits=[("tensor", p.n("tensor"))])


def emit_bcast_row(p, vec_dram, n, out_tile, row_sb, ones1, psB):
    p.emit("sync", lambda e, n=n: e.dma_start(
        out=row_sb[0:1, 0:n], in_=vec_dram[0:n].rearrange("(a b) -> a b", a=1)),
        waits=[("tensor", p.n("tensor"))])
    for c in range((n + 511) // 512):
        w = min(512, n - c * 512)
        p.emit("tensor", lambda e, c=c, w=w: e.matmul(
            psB[:, c * 512:c * 512 + w], ones1[0:1, :], row_sb[0:1, c * 512:c * 512 + w],
            start=True, stop=True),
            waits=[("sync", p.n("sync")), ("vector", p.n("vector")), ("scalar", p.n("scalar"))])
    p.emit("vector", lambda e, n=n: e.tensor_copy(out_tile[:, 0:n], psB[:, 0:n]),
           waits=[("tensor", p.n("tensor"))])


def hsl(sb, h, c0, w, rows=64):
    oc = h // 2
    r = (h % 2) * 64
    return sb[r:r + rows, oc * 2048 + c0:oc * 2048 + c0 + w]



def phase_ln(p, io, env, want_f32):
    aps = io
    idf = env["idf"]
    xsb = p.sb("xsb", [128, 16 * 1024], F32)
    sc = p.sb("sc", [128, 1024], F32)
    stats = p.sb("stats", [128, 16], F32)
    xnbf = p.sb("xnbf", [128, 8 * 2048], BF16)
    xnf = p.sb("xnf", [128, 8 * 2048], F32) if want_f32 else None
    psA = p.ps("psA", [128, 1024], F32)
    psB = p.ps("psB", [128, 1024], F32)

    ld0 = p.n("sync")
    for t in range(16):
        p.emit("sync", lambda e, t=t: e.dma_start(
            out=xsb[:, t * 1024:(t + 1) * 1024], in_=aps["x"][t * 128:(t + 1) * 128, :]))

    for t in range(16):
        xt = xsb[:, t * 1024:(t + 1) * 1024]
        p.emit("vector", lambda e, xt=xt: e.tensor_reduce(stats[:, 0:1], xt, axis=AX.X, op=OP.add),
               waits=[("sync", ld0 + 16 * (t + 1))])
        p.emit("vector", lambda e, xt=xt: e.tensor_tensor(sc[:], xt, xt, op=OP.mult))
        p.emit("vector", lambda e: e.tensor_reduce(stats[:, 1:2], sc[:], axis=AX.X, op=OP.add))
        p.emit("vector", lambda e: e.tensor_scalar(stats[:, 2:3], stats[:, 0:1], 1.0 / D, None, op0=OP.mult))
        p.emit("vector", lambda e: e.tensor_tensor(stats[:, 3:4], stats[:, 2:3], stats[:, 2:3], op=OP.mult))
        rD = p.cc(1.0 / D)
        p.emit("vector", lambda e, rD=rD: e.tensor_scalar(stats[:, 4:5], stats[:, 1:2], rD, stats[:, 3:4], op0=OP.mult, op1=OP.subtract))
        eps_ln = p.cc(1e-5)
        p.emit("scalar", lambda e, eps_ln=eps_ln: e.activation(stats[:, 5:6], stats[:, 4:5], AF.Ln, bias=eps_ln, scale=1.0),
               waits=[("vector", p.n("vector"))])
        p.emit("scalar", lambda e: e.activation(stats[:, 6:7], stats[:, 5:6], AF.Exp, bias=0.0, scale=-0.5))
        p.emit("vector", lambda e, xt=xt: e.tensor_scalar(
            sc[:], xt, stats[:, 2:3], stats[:, 6:7], op0=OP.subtract, op1=OP.mult),
            waits=[("scalar", p.n("scalar"))])
        pst = psA if t % 2 == 0 else psB
        for c in range(8):
            p.emit("tensor", lambda e, c=c, pst=pst: e.transpose(
                pst[:, c * 128:(c + 1) * 128], sc[:, c * 128:(c + 1) * 128], idf[:]),
                waits=[("vector", p.n("vector"))])
        p.emit("vector", lambda e, t=t, pst=pst: e.tensor_copy(
            xnbf[:].rearrange("p (c s) -> p c s", s=S)[:, :, t * 128:(t + 1) * 128],
            pst[:].rearrange("p (c q) -> p c q", q=128)),
            waits=[("tensor", p.n("tensor"))])
        if want_f32:
            p.emit("vector", lambda e, t=t, pst=pst: e.tensor_copy(
                xnf[:].rearrange("p (c s) -> p c s", s=S)[:, :, t * 128:(t + 1) * 128],
                pst[:].rearrange("p (c q) -> p c q", q=128)))

    for c in range(8):
        p.emit("gpsimd", lambda e, c=c: e.dma_start(
            out=aps["xn_bf"][c * 128:(c + 1) * 128, :], in_=xnbf[:, c * S:(c + 1) * S]),
            waits=[("vector", p.n("vector"))])
        if want_f32:
            p.emit("gpsimd", lambda e, c=c: e.dma_start(
                out=aps["xn_f32"][c * 128:(c + 1) * 128, :], in_=xnf[:, c * S:(c + 1) * S]))



def phase_comb(p, io):
    aps = io
    xa = p.sb("xa", [128, 2 * 1024], F32)
    xb = p.sb("xb", [128, 2 * 1024], F32)
    xc = p.sb("xc", [128, 2 * 1024], F32)
    xo = p.sb("xo", [128, 2 * 1024], F32)
    for t in range(16):
        b = (t % 2) * 1024
        p.emit("sync", lambda e, t=t, b=b: e.dma_start(out=xa[:, b:b + 1024], in_=aps["x"][t * 128:(t + 1) * 128, :]),
               waits=[("vector", p.n("vector"))])
        p.emit("sync", lambda e, t=t, b=b: e.dma_start(out=xb[:, b:b + 1024], in_=aps["a"][t * 128:(t + 1) * 128, :]))
        p.emit("sync", lambda e, t=t, b=b: e.dma_start(out=xc[:, b:b + 1024], in_=aps["b"][t * 128:(t + 1) * 128, :]))
        p.emit("vector", lambda e, b=b: e.tensor_tensor(xo[:, b:b + 1024], xa[:, b:b + 1024], xb[:, b:b + 1024], op=OP.add),
               waits=[("sync", p.n("sync")), ("gpsimd", max(0, p.n("gpsimd") - 16))])
        p.emit("vector", lambda e, b=b: e.tensor_tensor(xo[:, b:b + 1024], xo[:, b:b + 1024], xc[:, b:b + 1024], op=OP.add))
        p.emit("gpsimd", lambda e, t=t, b=b: e.dma_start(out=aps["o"][t * 128:(t + 1) * 128, :], in_=xo[:, b:b + 1024]),
               waits=[("vector", p.n("vector"))])



def phase_ffn0(p, io, env):
    aps = io
    ones1 = env["ones1"]
    xn = p.sb("xn", [128, 8 * 2048], BF16)
    w1h = p.sb("w1h", [128, 8 * 2048], BF16)
    w2h = p.sb("w2h", [128, 16 * 1024], BF16)
    hbuf = p.sb("hbuf", [128, 16 * 512], BF16)
    b1c = p.sb("b1c", [128, 32], F32)
    acc = p.sb("acc", [128, 16 * 1024], F32)
    b2t = p.sb("b2t", [128, 1024], F32)
    brow = p.sb("brow", [1, 1024], F32)
    outb = p.sb("outb", [128, 2 * 1024], F32)
    psA = p.ps("psA", [128, 512], F32)
    psB = p.ps("psB", [128, 512], F32)
    psC = p.ps("psC", [128, 1024], F32)
    psD = p.ps("psD", [128, 1024], F32)

    for c in range(8):
        p.emit("sync", lambda e, c=c: e.dma_start(out=xn[:, c * 2048:(c + 1) * 2048],
                                                  in_=aps["xn_bf"][c * 128:(c + 1) * 128, :]))
    b1st = p.sb("b1st", [32, 128], F32)
    idf2 = env["idf"]
    p.emit("sync", lambda e: e.dma_start(out=b1st[:], in_=aps["b1"].rearrange("(ic p) -> ic p", p=128)))
    p.emit("tensor", lambda e: e.transpose(psC[:, 0:32], b1st[:, 0:128], idf2[0:32, 0:32]),
           waits=[("sync", p.n("sync")), ("gpsimd", p.n("gpsimd"))])
    p.emit("vector", lambda e: e.tensor_copy(b1c[:], psC[:, 0:32]), waits=[("tensor", p.n("tensor"))])
    emit_bcast_row(p, aps["b2"], 1024, b2t, brow, ones1, psC)

    for ih in range(2):
        for dc in range(8):
            p.emit("gpsimd", lambda e, dc=dc, ih=ih: e.dma_start(
                out=w1h[:, dc * 2048:(dc + 1) * 2048],
                in_=aps["W1"][dc * 128:(dc + 1) * 128, ih * 2048:(ih + 1) * 2048]),
                waits=[("tensor", p.n("tensor"))])
        for ic in range(16):
            p.emit("gpsimd", lambda e, ic=ic, ih=ih: e.dma_start(
                out=w2h[:, ic * 1024:(ic + 1) * 1024],
                in_=aps["W2"][ih * 2048 + ic * 128:ih * 2048 + (ic + 1) * 128, :]),
                waits=[("tensor", p.n("tensor"))])
        for qb in range(4):
            for ic in range(16):
                pst = psA if ic % 2 == 0 else psB
                for dc in range(8):
                    p.emit("tensor", lambda e, ic=ic, dc=dc, qb=qb, pst=pst: e.matmul(
                        pst[:], w1h[:, dc * 2048 + ic * 128:dc * 2048 + (ic + 1) * 128],
                        xn[:, dc * 2048 + qb * 512:dc * 2048 + (qb + 1) * 512],
                        start=(dc == 0), stop=(dc == 7)),
                        waits=[("gpsimd", p.n("gpsimd")), ("sync", p.n("sync")), ("scalar", p.n("scalar"))])
                icg = ih * 16 + ic
                p.emit("scalar", lambda e, ic=ic, icg=icg, pst=pst: e.activation(
                    hbuf[:, ic * 512:(ic + 1) * 512], pst[:], AF.Relu,
                    bias=b1c[:, icg:icg + 1], scale=1.0),
                    waits=[("tensor", p.n("tensor"))])
            for tsub in range(4):
                t = qb * 4 + tsub
                pst = psC if tsub % 2 == 0 else psD
                for ic in range(16):
                    for oc in range(2):
                        p.emit("tensor", lambda e, ic=ic, oc=oc, tsub=tsub, pst=pst: e.matmul(
                            pst[:, oc * 512:(oc + 1) * 512],
                            hbuf[:, ic * 512 + tsub * 128:ic * 512 + (tsub + 1) * 128],
                            w2h[:, ic * 1024 + oc * 512:ic * 1024 + (oc + 1) * 512],
                            start=(ic == 0), stop=(ic == 15)),
                            waits=[("scalar", p.n("scalar")), ("vector", p.n("vector"))])
                if ih == 0:
                    p.emit("vector", lambda e, t=t, pst=pst: e.tensor_copy(
                        acc[:, t * 1024:(t + 1) * 1024], pst[:]), waits=[("tensor", p.n("tensor"))])
                else:
                    p.emit("vector", lambda e, t=t, pst=pst: e.tensor_tensor(
                        acc[:, t * 1024:(t + 1) * 1024], acc[:, t * 1024:(t + 1) * 1024], pst[:], op=OP.add),
                        waits=[("tensor", p.n("tensor"))])
                    ob = outb[:, (t % 2) * 1024:(t % 2) * 1024 + 1024]
                    p.emit("vector", lambda e, ob=ob, t=t: e.tensor_tensor(
                        ob, acc[:, t * 1024:(t + 1) * 1024], b2t[:], op=OP.add),
                        waits=[("gpsimd", max(0, p.n("gpsimd") - 16))])
                    p.emit("gpsimd", lambda e, t=t, ob=ob: e.dma_start(
                        out=aps["f0"][t * 128:(t + 1) * 128, :], in_=ob),
                        waits=[("vector", p.n("vector"))])



def phase_attn0(p, io, env):
    aps = io
    pscrP = io["pscrP"]
    pscrB = io["pscrB"]
    idf = env["idf"]
    idb = env["idb"]
    tri = env["tri"]
    onec = env["onec"]
    ones1 = env["ones1"]
    xn = p.sb("xn", [128, 8 * 2048], BF16)
    wt = p.sb("wt", [128, 8 * 1024], BF16)
    wstage = p.sb("wstage", [128, 2 * 1024], F32)
    Q = p.sb("Q", [128, 8 * 2048], BF16)
    K = p.sb("K", [128, 8 * 2048], BF16)
    V = p.sb("V", [128, 16 * 1024], BF16)
    attn = p.sb("attn", [128, 2112], BF16)
    attnT = p.sb("attnT", [128, 16 * 512], BF16)
    ctxfm = p.sb("ctxfm", [128, 8 * 512], BF16)
    Wwin = p.sb("Wwin", [128, 192], F32)
    QRt = p.sb("QRt", [128, 65], F32)
    Ptab = p.sb("Ptab", [128, 319], F32)
    Dg = p.sb("Dg", [128, 65], BF16)
    Atile = p.sb("Atile", [128, 65], BF16)
    Arhs = p.sb("Arhs", [128, 128], BF16)
    crn = p.sb("crn", [128, 128], BF16)
    st8 = p.sb("st8", [128, 16], F32)
    relkT = p.sb("relkT", [128, 65], BF16)
    relvN = p.sb("relvN", [128, 64], BF16)
    relst = p.sb("relst", [128, 64], F32)
    bot = p.sb("bot", [128, 1024], F32)
    brow = p.sb("brow", [1, 1024], F32)
    outb = p.sb("outb", [128, 2 * 1024], F32)

    psS = p.ps("psS", [128, 2048], F32)
    psT = p.ps("psT", [128, 2048], BF16)
    psC = p.ps("psC", [128, 512], F32)
    psX = p.ps("psX", [128, 512], F32)

    p.emit("sync", lambda e: e.dma_start(out=relst[0:65, :], in_=aps["rel_k"]))
    for c in range(8):
        p.emit("sync", lambda e, c=c: e.dma_start(out=xn[:, c * 2048:(c + 1) * 2048],
                                                  in_=aps["xn_bf"][c * 128:(c + 1) * 128, :]))
    p.emit("gpsimd", lambda e: e.dma_start(out=relvN[0:65, :], in_=aps["rel_v"]))
    p.emit("tensor", lambda e: e.transpose(psX[0:64, 0:65], relst[0:65, 0:64], idf[0:65, 0:65]),
           waits=[("sync", p.n("sync"))])
    p.emit("vector", lambda e: e.tensor_copy(relkT[0:64, 0:65], psX[0:64, 0:65]),
           waits=[("tensor", p.n("tensor"))])
    p.emit("vector", lambda e: e.tensor_copy(relkT[64:128, 0:65], psX[0:64, 0:65]))
    emit_bcast_row(p, aps["b_o"], 1024, bot, brow, ones1, psS)

    def proj_fm(out_sb, scale):
        for oc in range(8):
            for qb in range(4):
                pst = psS[:, (qb % 2) * 512:(qb % 2) * 512 + 512]
                for dc in range(8):
                    p.emit("tensor", lambda e, oc=oc, qb=qb, dc=dc, pst=pst: e.matmul(
                        pst, wt[:, dc * 1024 + oc * 128:dc * 1024 + (oc + 1) * 128],
                        xn[:, dc * 2048 + qb * 512:dc * 2048 + (qb + 1) * 512],
                        start=(dc == 0), stop=(dc == 7)),
                        waits=[("vector", p.n("vector")), ("scalar", p.n("scalar")), ("sync", p.n("sync"))])
                p.emit("scalar", lambda e, oc=oc, qb=qb, pst=pst, scale=scale: e.activation(
                    out_sb[:, oc * 2048 + qb * 512:oc * 2048 + (qb + 1) * 512], pst,
                    AF.Copy, bias=0.0, scale=scale),
                    waits=[("tensor", p.n("tensor"))])

    def proj_tm(out_sb):
        for t in range(16):
            for oh in range(2):
                pst = psS[:, (oh % 2) * 512:(oh % 2) * 512 + 512]
                for dc in range(8):
                    p.emit("tensor", lambda e, t=t, oh=oh, dc=dc, pst=pst: e.matmul(
                        pst, xn[:, dc * 2048 + t * 128:dc * 2048 + (t + 1) * 128],
                        wt[:, dc * 1024 + oh * 512:dc * 1024 + (oh + 1) * 512],
                        start=(dc == 0), stop=(dc == 7)),
                        waits=[("vector", p.n("vector")), ("scalar", p.n("scalar")), ("sync", p.n("sync"))])
                p.emit("scalar", lambda e, t=t, oh=oh, pst=pst: e.activation(
                    out_sb[:, t * 1024 + oh * 512:t * 1024 + (oh + 1) * 512], pst,
                    AF.Copy, bias=0.0, scale=1.0),
                    waits=[("tensor", p.n("tensor"))])

    emit_weight_T(p, aps["Wq"], D, D, wt, wstage, psS, idf)
    proj_fm(Q, 0.125)
    emit_weight_T(p, aps["Wk"], D, D, wt, wstage, psS, idf)
    proj_fm(K, 1.0)
    emit_weight_T(p, aps["Wv"], D, D, wt, wstage, psS, idf)
    proj_tm(V)
    emit_weight_T(p, aps["Wo"], D, D, wt, wstage, psS, idf)

    p.emit("vector", lambda e: e.memset(attn[:, 0:32], 0.0))
    p.emit("vector", lambda e: e.memset(attn[:, 2080:2112], 0.0))

    for qb in range(4):
        for h in range(H):
            for qtl in range(4):
                qt = qb * 4 + qtl
                q0 = qt * 128
                for kb in range(4):
                    p.emit("tensor", lambda e, h=h, qt=qt, kb=kb: e.matmul(
                        psS[:, kb * 512:(kb + 1) * 512],
                        hsl(Q, h, qt * 128, 128), hsl(K, h, kb * 512, 512),
                        start=True, stop=True),
                        waits=[("vector", p.n("vector")), ("scalar", p.n("scalar"))])
                p.emit("tensor", lambda e, h=h, qt=qt: e.matmul(
                    psX[:, 0:65], hsl(Q, h, qt * 128, 128),
                    relkT[(h % 2) * 64:(h % 2) * 64 + 64, 0:65],
                    start=True, stop=True))
                p.emit("vector", lambda e: e.tensor_copy(QRt[:], psX[:, 0:65]),
                       waits=[("tensor", p.n("tensor"))])
                p.emit("vector", lambda e: e.tensor_copy(Ptab[:, 0:127], QRt[:, 0:1].broadcast_to([128, 127])))
                p.emit("vector", lambda e: e.tensor_copy(Ptab[:, 127:192], QRt[:]))
                p.emit("vector", lambda e: e.tensor_copy(Ptab[:, 192:319], QRt[:, 64:65].broadcast_to([128, 127])))
                p.emit("sync", lambda e: e.dma_start(
                    out=pscrP[0:128 * 319].rearrange("(q w) -> q w", w=319), in_=Ptab[:]),
                    waits=[("vector", p.n("vector"))])
                p.emit("sync", lambda e: e.dma_start(
                    out=Wwin[:], in_=pscrP[127:127 + 128 * 318].rearrange("(q w) -> q w", w=318)[:, 0:192]))
                wadd = [("sync", p.n("sync")), ("tensor", p.n("tensor"))]
                if qt == 0:
                    p.emit("vector", lambda e: e.tensor_tensor(psS[:, 0:160], psS[:, 0:160], Wwin[:, 32:192], op=OP.add), waits=wadd)
                elif qt == 15:
                    p.emit("vector", lambda e: e.tensor_tensor(psS[:, 1888:2048], psS[:, 1888:2048], Wwin[:, 0:160], op=OP.add), waits=wadd)
                else:
                    p.emit("vector", lambda e, q0=q0: e.tensor_tensor(psS[:, q0 - 32:q0 + 160], psS[:, q0 - 32:q0 + 160], Wwin[:, 0:192], op=OP.add), waits=wadd)
                if qt > 0:
                    p.emit("vector", lambda e, q0=q0: e.tensor_scalar(psS[:, 0:q0 - 32], psS[:, 0:q0 - 32], QRt[:, 0:1], None, op0=OP.add))
                if qt < 15:
                    p.emit("vector", lambda e, q0=q0: e.tensor_scalar(psS[:, q0 + 160:2048], psS[:, q0 + 160:2048], QRt[:, 64:65], None, op0=OP.add))
                p.emit("vector", lambda e: e.tensor_reduce(st8[:, 0:1], psS[:], axis=AX.X, op=OP.max, negate=True))
                p.emit("scalar", lambda e: e.activation(attn[:, 32:2080], psS[:], AF.Exp,
                                                        bias=st8[:, 0:1], scale=1.0, accum_out=st8[:, 1:2]),
                       waits=[("vector", p.n("vector"))])
                p.emit("vector", lambda e: e.reciprocal(st8[:, 2:3], st8[:, 1:2]),
                       waits=[("scalar", p.n("scalar"))])
                p.emit("vector", lambda e: e.tensor_scalar(attn[:, 32:2080], attn[:, 32:2080], st8[:, 2:3], None, op0=OP.mult))
                p.emit("vector", lambda e, q0=q0: e.tensor_tensor(crn[:], attn[:, q0:q0 + 128], tri[:], op=OP.mult))
                p.emit("vector", lambda e: e.tensor_reduce(st8[:, 3:4], crn[:], axis=AX.X, op=OP.add))
                p.emit("sync", lambda e, q0=q0: e.dma_start(
                    out=pscrB[0:128 * 192].rearrange("(q w) -> q w", w=192), in_=attn[:, q0:q0 + 192]),
                    waits=[("vector", p.n("vector"))])
                p.emit("sync", lambda e: e.dma_start(
                    out=Dg[:], in_=pscrB[0:128 * 193].rearrange("(q w) -> q w", w=193)[:, 0:65]))
                for c in range(16):
                    p.emit("tensor", lambda e, c=c: e.transpose(
                        psT[:, c * 128:(c + 1) * 128], attn[:, 32 + c * 128:32 + (c + 1) * 128], idb[:]),
                        waits=[("vector", p.n("vector"))])
                p.emit("vector", lambda e, qtl=qtl: e.tensor_copy(
                    attnT[:].rearrange("p (c q) -> p c q", q=512)[:, :, qtl * 128:(qtl + 1) * 128],
                    psT[:].rearrange("p (c q) -> p c q", q=128)),
                    waits=[("tensor", p.n("tensor"))])
                nfull = qt - 1
                for c in range(max(0, nfull)):
                    p.emit("tensor", lambda e, c=c, qtl=qtl: e.matmul(
                        psX[:, 128:129], attnT[:, c * 512 + qtl * 128:c * 512 + (qtl + 1) * 128],
                        onec[:], start=(c == 0), stop=False),
                        waits=[("vector", p.n("vector"))])
                if qt >= 1:
                    p.emit("tensor", lambda e, qt=qt, qtl=qtl, st=(nfull <= 0): e.matmul(
                        psX[:, 128:129],
                        attnT[0:96, (qt - 1) * 512 + qtl * 128:(qt - 1) * 512 + (qtl + 1) * 128],
                        onec[0:96, :], start=st, stop=True),
                        waits=[("vector", p.n("vector"))])
                aw = [("tensor", p.n("tensor")), ("sync", p.n("sync"))]
                if qt >= 1:
                    p.emit("vector", lambda e: e.tensor_tensor(st8[:, 4:5], psX[:, 128:129], st8[:, 3:4], op=OP.add), waits=aw)
                else:
                    p.emit("vector", lambda e: e.tensor_copy(st8[:, 4:5], st8[:, 3:4]), waits=aw)
                p.emit("vector", lambda e: e.tensor_reduce(st8[:, 5:6], Dg[:, 1:64], axis=AX.X, op=OP.add))
                p.emit("vector", lambda e: e.tensor_tensor(st8[:, 6:7], st8[:, 4:5], st8[:, 5:6], op=OP.add))
                p.emit("vector", lambda e: e.tensor_scalar(st8[:, 7:8], st8[:, 6:7], -1.0, 1.0, op0=OP.mult, op1=OP.add))
                p.emit("vector", lambda e: e.tensor_copy(Atile[:, 1:64], Dg[:, 1:64]))
                p.emit("vector", lambda e: e.tensor_copy(Atile[:, 0:1], st8[:, 4:5]))
                p.emit("vector", lambda e: e.tensor_copy(Atile[:, 64:65], st8[:, 7:8]))
                p.emit("tensor", lambda e: e.transpose(psT[0:65, 0:128], Atile[:], idb[:]),
                       waits=[("vector", p.n("vector"))])
                p.emit("vector", lambda e: e.tensor_copy(Arhs[0:65, 0:128], psT[0:65, 0:128]),
                       waits=[("tensor", p.n("tensor"))])
                p.emit("tensor", lambda e, qtl=qtl: e.matmul(
                    psC[0:64, qtl * 128:(qtl + 1) * 128], relvN[0:65, 0:64], Arhs[0:65, 0:128],
                    start=(qtl == 0), stop=False),
                    waits=[("vector", p.n("vector"))])
            for c in range(16):
                p.emit("tensor", lambda e, h=h, c=c: e.matmul(
                    psC[0:64, :], V[:, c * 1024 + h * 64:c * 1024 + (h + 1) * 64],
                    attnT[:, c * 512:(c + 1) * 512], start=False, stop=(c == 15)),
                    waits=[("vector", p.n("vector"))])
            p.emit("vector", lambda e, h=h: e.tensor_copy(
                ctxfm[(h % 2) * 64:(h % 2) * 64 + 64, (h // 2) * 512:(h // 2 + 1) * 512], psC[0:64, :]),
                waits=[("tensor", p.n("tensor"))])
        for tsub in range(4):
            t = qb * 4 + tsub
            for oh in range(2):
                pst = psS[:, oh * 512:(oh + 1) * 512]
                for dc in range(8):
                    p.emit("tensor", lambda e, tsub=tsub, oh=oh, dc=dc, pst=pst: e.matmul(
                        pst, ctxfm[:, dc * 512 + tsub * 128:dc * 512 + (tsub + 1) * 128],
                        wt[:, dc * 1024 + oh * 512:dc * 1024 + (oh + 1) * 512],
                        start=(dc == 0), stop=(dc == 7)),
                        waits=[("vector", p.n("vector")), ("scalar", p.n("scalar"))])
            ob = outb[:, (t % 2) * 1024:(t % 2) * 1024 + 1024]
            p.emit("vector", lambda e, ob=ob: e.tensor_tensor(ob, psS[:, 0:1024], bot[:], op=OP.add),
                   waits=[("tensor", p.n("tensor")), ("gpsimd", max(0, p.n("gpsimd") - 16))])
            p.emit("gpsimd", lambda e, t=t, ob=ob: e.dma_start(
                out=aps["a0"][t * 128:(t + 1) * 128, :], in_=ob),
                waits=[("vector", p.n("vector"))])



def phase_attn1(p, io, env):
    aps = io
    idf = env["idf"]
    idb = env["idb"]
    ones1 = env["ones1"]
    xn = p.sb("xn", [128, 8 * 2048], BF16)
    wt = p.sb("wt", [128, 8 * 1024], BF16)
    wstage = p.sb("wstage", [128, 1024], F32)
    Q = p.sb("Q", [128, 8 * 2048], BF16)
    K = p.sb("K", [128, 8 * 2048], BF16)
    V = p.sb("V", [128, 16 * 1024], BF16)
    attn = p.sb("attn", [128, 2048], BF16)
    attnT = p.sb("attnT", [128, 16 * 512], BF16)
    ctxfm = p.sb("ctxfm", [128, 8 * 512], BF16)
    qkb = p.sb("qkb", [128, 1024], BF16)
    rst = p.sb("rst", [128, 16], F32)
    wnt = p.sb("wnt", [128, 1024], F32)
    st8 = p.sb("st8", [128, 16], F32)
    bqt = p.sb("bqt", [128, 1024], F32)
    bpt = p.sb("bpt", [128, 1024], F32)
    brow = p.sb("brow", [1, 1024], F32)
    wrow = p.sb("wrow", [1, 64], F32)
    outb = p.sb("outb", [128, 2 * 1024], F32)
    qkt = outb[:, 0:1024]
    sq = outb[:, 1024:2048]

    psS = p.ps("psS", [128, 2048], F32)
    psT = p.ps("psT", [128, 2048], BF16)
    psC = p.ps("psC", [128, 512], F32)

    for c in range(8):
        p.emit("sync", lambda e, c=c: e.dma_start(out=xn[:, c * 2048:(c + 1) * 2048],
                                                  in_=aps["xn_bf"][c * 128:(c + 1) * 128, :]))
    emit_bcast_row(p, aps["bproj"], 1024, bpt, brow, ones1, psS)

    def emit_wn_tile(nm):
        p.emit("sync", lambda e, nm=nm: e.dma_start(out=wrow[0:1, :], in_=aps[nm][0:DK].rearrange("(a b) -> a b", a=1)),
               waits=[("tensor", p.n("tensor"))])
        p.emit("tensor", lambda e: e.matmul(psS[:, 0:64], ones1[0:1, :], wrow[0:1, :], start=True, stop=True),
               waits=[("sync", p.n("sync")), ("vector", p.n("vector"))])
        for j in range(16):
            p.emit("vector", lambda e, j=j: e.tensor_copy(
                wnt[:, j * 64:(j + 1) * 64], psS[:, 0:64]),
                waits=[("tensor", p.n("tensor"))])

    for part, dest in ((0, Q), (1, K), (2, V)):
        emit_bcast_row(p, aps["bqkv"][part * D:(part + 1) * D, 0:1024] if False else aps["bqkv"][part * D:(part + 1) * D], 1024,
                       bqt, brow, ones1, psS)
        if part == 0:
            emit_wn_tile("qn_w")
        elif part == 1:
            emit_wn_tile("kn_w")
        emit_weight_T(p, aps["Wqkv"][part * D:(part + 1) * D, :], D, D, wt, wstage, psS, idf)
        for t in range(16):
            for oh in range(2):
                pst = psS[:, oh * 512:oh * 512 + 512]
                for dc in range(8):
                    p.emit("tensor", lambda e, t=t, oh=oh, dc=dc, pst=pst: e.matmul(
                        pst, xn[:, dc * 2048 + t * 128:dc * 2048 + (t + 1) * 128],
                        wt[:, dc * 1024 + oh * 512:dc * 1024 + (oh + 1) * 512],
                        start=(dc == 0), stop=(dc == 7)),
                        waits=[("vector", p.n("vector")), ("scalar", p.n("scalar")), ("sync", p.n("sync"))])
            if part == 2:
                p.emit("vector", lambda e, t=t: e.tensor_tensor(
                    V[:, t * 1024:(t + 1) * 1024], psS[:, 0:1024], bqt[:], op=OP.add),
                    waits=[("tensor", p.n("tensor"))])
            else:
                p.emit("vector", lambda e: e.tensor_tensor(
                    qkt[:], psS[:, 0:1024], bqt[:], op=OP.add),
                    waits=[("tensor", p.n("tensor"))])
                p.emit("vector", lambda e: e.tensor_tensor(sq[:], qkt[:], qkt[:], op=OP.mult))
                p.emit("vector", lambda e: e.tensor_reduce(
                    rst[:, 0:16], sq[:].rearrange("p (g d) -> p g d", d=64), axis=AX.X, op=OP.add))
                eps_r = p.cc(1e-6)
                p.emit("scalar", lambda e, eps_r=eps_r: e.activation(rst[:, 0:16], rst[:, 0:16], AF.Ln, bias=eps_r, scale=1.0 / DK),
                       waits=[("vector", p.n("vector"))])
                p.emit("scalar", lambda e: e.activation(rst[:, 0:16], rst[:, 0:16], AF.Exp, bias=0.0, scale=-0.5))
                for g in range(16):
                    p.emit("vector", lambda e, g=g: e.tensor_scalar(
                        qkt[:, g * 64:(g + 1) * 64], qkt[:, g * 64:(g + 1) * 64], rst[:, g:g + 1], None, op0=OP.mult),
                        waits=([("scalar", p.n("scalar"))] if g == 0 else []))
                sc2 = 0.125 if part == 0 else 1.0
                p.emit("vector", lambda e, sc2=sc2: e.scalar_tensor_tensor(
                    qkb[:], qkt[:], sc2, wnt[:], op0=OP.mult, op1=OP.mult))
                for c in range(8):
                    p.emit("tensor", lambda e, c=c: e.transpose(
                        psT[:, c * 128:(c + 1) * 128], qkb[:, c * 128:(c + 1) * 128], idb[:]),
                        waits=[("vector", p.n("vector"))])
                p.emit("vector", lambda e, t=t, dest=dest: e.tensor_copy(
                    dest[:].rearrange("p (c s) -> p c s", s=S)[:, :, t * 128:(t + 1) * 128],
                    psT[:, 0:1024].rearrange("p (c q) -> p c q", q=128)),
                    waits=[("tensor", p.n("tensor"))])
    emit_weight_T(p, aps["Wproj"], D, D, wt, wstage, psS, idf)

    for qb in range(4):
        for h in range(H):
            for qtl in range(4):
                qt = qb * 4 + qtl
                for kb in range(4):
                    p.emit("tensor", lambda e, h=h, qt=qt, kb=kb: e.matmul(
                        psS[:, kb * 512:(kb + 1) * 512],
                        hsl(Q, h, qt * 128, 128), hsl(K, h, kb * 512, 512),
                        start=True, stop=True),
                        waits=[("vector", p.n("vector")), ("scalar", p.n("scalar"))])
                p.emit("vector", lambda e: e.tensor_reduce(st8[:, 0:1], psS[:], axis=AX.X, op=OP.max, negate=True),
                       waits=[("tensor", p.n("tensor"))])
                p.emit("scalar", lambda e: e.activation(attn[:], psS[:], AF.Exp,
                                                        bias=st8[:, 0:1], scale=1.0, accum_out=st8[:, 1:2]),
                       waits=[("vector", p.n("vector"))])
                p.emit("vector", lambda e: e.reciprocal(st8[:, 2:3], st8[:, 1:2]),
                       waits=[("scalar", p.n("scalar"))])
                p.emit("vector", lambda e: e.tensor_scalar(attn[:], attn[:], st8[:, 2:3], None, op0=OP.mult))
                for c in range(16):
                    p.emit("tensor", lambda e, c=c: e.transpose(
                        psT[:, c * 128:(c + 1) * 128], attn[:, c * 128:(c + 1) * 128], idb[:]),
                        waits=[("vector", p.n("vector"))])
                p.emit("vector", lambda e, qtl=qtl: e.tensor_copy(
                    attnT[:].rearrange("p (c q) -> p c q", q=512)[:, :, qtl * 128:(qtl + 1) * 128],
                    psT[:].rearrange("p (c q) -> p c q", q=128)),
                    waits=[("tensor", p.n("tensor"))])
            for c in range(16):
                p.emit("tensor", lambda e, h=h, c=c: e.matmul(
                    psC[0:64, :], V[:, c * 1024 + h * 64:c * 1024 + (h + 1) * 64],
                    attnT[:, c * 512:(c + 1) * 512], start=(c == 0), stop=(c == 15)),
                    waits=[("vector", p.n("vector"))])
            p.emit("vector", lambda e, h=h: e.tensor_copy(
                ctxfm[(h % 2) * 64:(h % 2) * 64 + 64, (h // 2) * 512:(h // 2 + 1) * 512], psC[0:64, :]),
                waits=[("tensor", p.n("tensor"))])
        for tsub in range(4):
            t = qb * 4 + tsub
            for oh in range(2):
                pst = psS[:, oh * 512:(oh + 1) * 512]
                for dc in range(8):
                    p.emit("tensor", lambda e, tsub=tsub, oh=oh, dc=dc, pst=pst: e.matmul(
                        pst, ctxfm[:, dc * 512 + tsub * 128:dc * 512 + (tsub + 1) * 128],
                        wt[:, dc * 1024 + oh * 512:dc * 1024 + (oh + 1) * 512],
                        start=(dc == 0), stop=(dc == 7)),
                        waits=[("vector", p.n("vector")), ("scalar", p.n("scalar"))])
            ob = outb[:, (t % 2) * 1024:(t % 2) * 1024 + 1024]
            p.emit("vector", lambda e, ob=ob: e.tensor_tensor(ob, psS[:, 0:1024], bpt[:], op=OP.add),
                   waits=[("tensor", p.n("tensor")), ("gpsimd", max(0, p.n("gpsimd") - 16))])
            p.emit("gpsimd", lambda e, t=t, ob=ob: e.dma_start(
                out=aps["a1"][t * 128:(t + 1) * 128, :], in_=ob),
                waits=[("vector", p.n("vector"))])



def phase_moe(p, io, env):
    aps = io
    idf = env["idf"]
    ones1 = env["ones1"]
    xn = p.sb("xn", [128, 8 * 2048], BF16)
    xf = p.sb("xf", [128, 1024], F32)
    rwt = p.sb("rwt", [128, 8 * 16], F32)
    rstage = p.sb("rstage", [128, 128], F32)
    rbt = p.sb("rbt", [128, 16], F32)
    brow = p.sb("brow", [1, 1024], F32)
    logit = p.sb("logit", [128, 16], F32)
    probs = p.sb("probs", [128, 16], F32)
    pwork = p.sb("pwork", [128, 16], F32)
    msk = p.sb("msk", [128, 16], F32)
    st8 = p.sb("st8", [128, 16], F32)
    gates = p.sb("gates", [128, 16 * 16], F32)
    racct = p.sb("racct", [128, 16], F32)
    epst = p.sb("epst", [128, 16], F32)
    we = p.sb("we", [128, 2 * 8 * 1024], BF16)
    moeacc = p.sb("moeacc", [128, 16 * 1024], F32)
    outb = p.sb("outb", [128, 2 * 1024], F32)

    psR = p.ps("psR", [128, 512], F32)
    psE = p.ps("psE", [128, 1024], F32)
    psF = p.ps("psF", [128, 1024], F32)

    for c in range(8):
        p.emit("sync", lambda e, c=c: e.dma_start(out=xn[:, c * 2048:(c + 1) * 2048],
                                                  in_=aps["xn_bf"][c * 128:(c + 1) * 128, :]))
    for c in range(8):
        p.emit("sync", lambda e, c=c: e.dma_start(out=rstage[0:16, :], in_=aps["router_w"][:, c * 128:(c + 1) * 128]),
               waits=[("tensor", p.n("tensor"))])
        p.emit("tensor", lambda e, c=c: e.transpose(psR[:, 0:16], rstage[0:16, 0:128], idf[0:16, 0:16]),
               waits=[("sync", p.n("sync")), ("vector", p.n("vector"))])
        p.emit("vector", lambda e, c=c: e.tensor_copy(rwt[:, c * 16:(c + 1) * 16], psR[:, 0:16]),
               waits=[("tensor", p.n("tensor"))])
    emit_bcast_row(p, aps["router_b"], 16, rbt, brow, ones1, psR)

    for t in range(16):
        for c in range(8):
            p.emit("sync", lambda e, t=t, c=c: e.dma_start(
                out=xf[:, c * 128:(c + 1) * 128],
                in_=aps["xn_f32"][c * 128:(c + 1) * 128, t * 128:(t + 1) * 128]),
                waits=[("tensor", p.n("tensor"))])
        for c in range(8):
            p.emit("tensor", lambda e, c=c: e.matmul(
                psR[:, 0:16], xf[:, c * 128:(c + 1) * 128], rwt[:, c * 16:(c + 1) * 16],
                start=(c == 0), stop=(c == 7)),
                waits=[("sync", p.n("sync")), ("vector", p.n("vector"))])
        p.emit("vector", lambda e: e.tensor_tensor(logit[:], psR[:, 0:16], rbt[:], op=OP.add),
               waits=[("tensor", p.n("tensor"))])
        p.emit("vector", lambda e: e.tensor_reduce(st8[:, 0:1], logit[:], axis=AX.X, op=OP.max, negate=True))
        p.emit("scalar", lambda e: e.activation(probs[:], logit[:], AF.Exp, bias=st8[:, 0:1], scale=1.0, accum_out=st8[:, 1:2]),
               waits=[("vector", p.n("vector"))])
        p.emit("vector", lambda e: e.reciprocal(st8[:, 2:3], st8[:, 1:2]), waits=[("scalar", p.n("scalar"))])
        p.emit("vector", lambda e: e.tensor_scalar(probs[:], probs[:], st8[:, 2:3], None, op0=OP.mult))
        p.emit("vector", lambda e: e.tensor_copy(pwork[:], probs[:]))
        for it in range(6):
            p.emit("vector", lambda e: e.tensor_reduce(st8[:, 3:4], pwork[:], axis=AX.X, op=OP.max))
            if it == 0:
                p.emit("vector", lambda e: e.tensor_copy(st8[:, 4:5], st8[:, 3:4]))
            else:
                p.emit("vector", lambda e: e.tensor_tensor(st8[:, 4:5], st8[:, 4:5], st8[:, 3:4], op=OP.add))
            if it < 5:
                p.emit("vector", lambda e: e.tensor_scalar(msk[:], pwork[:], st8[:, 3:4], None, op0=OP.is_ge))
                p.emit("vector", lambda e: e.scalar_tensor_tensor(pwork[:], msk[:], -1e30, pwork[:], op0=OP.mult, op1=OP.add))
        p.emit("vector", lambda e: e.tensor_scalar(msk[:], probs[:], st8[:, 3:4], None, op0=OP.is_ge))
        p.emit("vector", lambda e: e.tensor_scalar(st8[:, 5:6], st8[:, 4:5], 1e-10, None, op0=OP.add))
        p.emit("vector", lambda e: e.reciprocal(st8[:, 6:7], st8[:, 5:6]))
        p.emit("vector", lambda e, t=t: e.scalar_tensor_tensor(
            gates[:, t * 16:(t + 1) * 16], probs[:], st8[:, 6:7], msk[:], op0=OP.mult, op1=OP.mult))
        p.emit("vector", lambda e, t=t: e.tensor_reduce(st8[:, 7:8], gates[:, t * 16:(t + 1) * 16], axis=AX.X, op=OP.add))
        p.emit("vector", lambda e: e.tensor_scalar(st8[:, 8:9], st8[:, 7:8], 1e-10, None, op0=OP.max))
        p.emit("vector", lambda e, t=t: e.reciprocal(racct[:, t:t + 1], st8[:, 8:9]))
        p.emit("vector", lambda e, t=t: e.tensor_scalar(epst[:, t:t + 1], st8[:, 7:8], 1e-10, None, op0=OP.mult))

    stt_tok = [0, 0]
    wld_tok = [0, 0]
    for eix in range(2):
        wslot = we[:, (eix % 2) * 8192:(eix % 2) * 8192 + 8192]
        for dc in range(8):
            p.emit("gpsimd", lambda e, eix=eix, dc=dc, wslot=wslot: e.dma_start(
                out=wslot[:, dc * 1024:(dc + 1) * 1024],
                in_=aps["expert_w"][eix, dc * 128:(dc + 1) * 128, :]))
        wld_tok[eix] = p.n("gpsimd")
    for eix in range(NE):
        slot = eix % 2
        wslot = we[:, slot * 8192:slot * 8192 + 8192]
        t_last_mm = 0
        for t in range(16):
            pst, bank = (psE, 0) if t % 2 == 0 else (psF, 1)
            for oh in range(2):
                for dc in range(8):
                    w = []
                    if oh == 0 and dc == 0:
                        w = [("vector", stt_tok[bank])]
                        if t == 0:
                            w.append(("gpsimd", wld_tok[slot]))
                            w.append(("vector", p.n("vector") if eix == 0 else stt_tok[1 - bank]))
                    p.emit("tensor", lambda e, t=t, oh=oh, dc=dc, pst=pst, wslot=wslot: e.matmul(
                        pst[:, oh * 512:(oh + 1) * 512],
                        xn[:, dc * 2048 + t * 128:dc * 2048 + (t + 1) * 128],
                        wslot[:, dc * 1024 + oh * 512:dc * 1024 + (oh + 1) * 512],
                        start=(dc == 0), stop=(dc == 7)), waits=w)
            t_mm = p.n("tensor")
            if eix == 0:
                stt_tok[bank] = p.emit("vector", lambda e, t=t, eix=eix, pst=pst: e.tensor_scalar(
                    moeacc[:, t * 1024:(t + 1) * 1024], pst[:], gates[:, t * 16 + eix:t * 16 + eix + 1], None, op0=OP.mult),
                    waits=[("tensor", t_mm)])
            else:
                stt_tok[bank] = p.emit("vector", lambda e, t=t, eix=eix, pst=pst: e.scalar_tensor_tensor(
                    moeacc[:, t * 1024:(t + 1) * 1024], pst[:], gates[:, t * 16 + eix:t * 16 + eix + 1],
                    moeacc[:, t * 1024:(t + 1) * 1024], op0=OP.mult, op1=OP.add),
                    waits=[("tensor", t_mm)])
        if eix + 2 < NE:
            for dc in range(8):
                p.emit("gpsimd", lambda e, eix=eix, dc=dc, slot=slot: e.dma_start(
                    out=we[:, slot * 8192 + dc * 1024:slot * 8192 + (dc + 1) * 1024],
                    in_=aps["expert_w"][eix + 2, dc * 128:(dc + 1) * 128, :]),
                    waits=([("tensor", p.n("tensor"))] if dc == 0 else []))
            wld_tok[slot] = p.n("gpsimd")
    for t in range(16):
        ob = outb[:, (t % 2) * 1024:(t % 2) * 1024 + 1024]
        p.emit("vector", lambda e, t=t, ob=ob: e.tensor_scalar(
            ob, moeacc[:, t * 1024:(t + 1) * 1024], racct[:, t:t + 1], epst[:, t:t + 1], op0=OP.mult, op1=OP.add),
            waits=[("gpsimd", max(0, p.n("gpsimd") - 16))])
        p.emit("gpsimd", lambda e, t=t, ob=ob: e.dma_start(
            out=aps["m1"][t * 128:(t + 1) * 128, :], in_=ob),
            waits=[("vector", p.n("vector"))])



def _proj_common(p, wt, xn, psS, out_sb, scale, to_tm, tok):
    ready = [("vector", p.n("vector")), ("sync", p.n("sync"))]
    if not to_tm:
        for oc in range(8):
            for qb in range(4):
                bank = qb % 2
                pst = psS[:, bank * 512:bank * 512 + 512]
                for dc in range(8):
                    w = ([("scalar", tok["se_bank"][bank])] + (ready if (oc == 0 and qb == 0) else [])) if dc == 0 else []
                    p.emit("tensor", lambda e, oc=oc, qb=qb, dc=dc, pst=pst: e.matmul(
                        pst, wt[:, dc * 1024 + oc * 128:dc * 1024 + (oc + 1) * 128],
                        xn[:, dc * 2048 + qb * 512:dc * 2048 + (qb + 1) * 512],
                        start=(dc == 0), stop=(dc == 7)), waits=w)
                t_mm = p.n("tensor")
                tok["se_bank"][bank] = p.emit("scalar", lambda e, oc=oc, qb=qb, pst=pst, scale=scale: e.activation(
                    out_sb[:, oc * 2048 + qb * 512:oc * 2048 + (qb + 1) * 512], pst,
                    AF.Copy, bias=0.0, scale=scale), waits=[("tensor", t_mm)])
    else:
        for t in range(16):
            for oh in range(2):
                bank = oh % 2
                pst = psS[:, bank * 512:bank * 512 + 512]
                for dc in range(8):
                    w = ([("scalar", tok["se_bank"][bank])] + (ready if (t == 0 and oh == 0) else [])) if dc == 0 else []
                    p.emit("tensor", lambda e, t=t, oh=oh, dc=dc, pst=pst: e.matmul(
                        pst, xn[:, dc * 2048 + t * 128:dc * 2048 + (t + 1) * 128],
                        wt[:, dc * 1024 + oh * 512:dc * 1024 + (oh + 1) * 512],
                        start=(dc == 0), stop=(dc == 7)), waits=w)
                t_mm = p.n("tensor")
                tok["se_bank"][bank] = p.emit("scalar", lambda e, t=t, oh=oh, pst=pst: e.activation(
                    out_sb[:, t * 1024 + oh * 512:t * 1024 + (oh + 1) * 512], pst,
                    AF.Copy, bias=0.0, scale=1.0), waits=[("tensor", t_mm)])


def _kmax_chain(p, Ksb, Ksq, psS, onec, Krow, col, h, tok):
    r = (h % 2) * 64
    oc = h // 2
    t_sq = p.emit("scalar", lambda e: e.activation(
        Ksq[r:r + 64, :], Ksb[r:r + 64, oc * 2048:(oc + 1) * 2048], AF.Square, bias=0.0, scale=1.0),
        waits=[("tensor", tok.get("kmm%d" % r, 0))])
    for half in range(2):
        for kb in range(2):
            p.emit("tensor", lambda e, half=half, kb=kb: e.matmul(
                psS[0:1, 1024 + kb * 512:1024 + (kb + 1) * 512],
                onec[r:r + 64, :].bitcast(BF16) if False else onec[r:r + 64, :],
                Ksq[r:r + 64, half * 1024 + kb * 512:half * 1024 + (kb + 1) * 512],
                start=True, stop=True),
                waits=[("scalar", t_sq), ("vector", tok.get("kred", 0))])
        t_mm = p.n("tensor")
        tok["kmm%d" % r] = t_mm
        tok["kred"] = p.emit("vector", lambda e, half=half, col=col: e.tensor_reduce(
            Krow[0:1, 16 + half:17 + half], psS[0:1, 1024:2048], axis=AX.X, op=OP.max),
            waits=[("tensor", t_mm)])
    tok["kred"] = p.emit("vector", lambda e, col=col: e.tensor_tensor(
        Krow[0:1, col:col + 1], Krow[0:1, 16:17], Krow[0:1, 17:18], op=OP.max))


def phase_attn0_v2(p, io, env):
    aps = io
    idf = env["idf"]
    idb = env["idb"]
    tri = env["tri"]
    onec = env["onec"]
    ones1 = env["ones1"]

    xn = p.sb("xn", [128, 8 * 2048], BF16)
    wt = p.sb("wt", [128, 8 * 1024], BF16)
    wstage = p.sb("wstage", [128, 1024], F32)
    Q = p.sb("Q", [128, 8 * 2048], BF16)
    K = p.sb("K", [128, 8 * 2048], BF16)
    V = p.sb("V", [128, 16 * 1024], BF16)
    attn = p.sb("attn", [128, 2112], BF16)
    attnT = p.sb("attnT", [128, 16 * 512], BF16)
    ctxfm = p.sb("ctxfm", [128, 8 * 512], BF16)
    Wwin4 = p.sb("Wwin4", [128, 4 * 192], F32)
    QRt4 = p.sb("QRt4", [128, 4 * 65], F32)
    Ptab4 = p.sb("Ptab4", [128, 4 * 319], F32)
    Dg4 = p.sb("Dg4", [128, 4 * 65], BF16)
    Atile4 = p.sb("Atile4", [128, 4 * 65], BF16)
    Arhs4 = p.sb("Arhs4", [128, 512], BF16)
    crn = p.sb("crn", [128, 128], BF16)
    st8 = p.sb("st8", [128, 48], F32)
    Qsq = p.sb("Qsq", [128, 512], BF16)
    KRcols = p.sb("KRcols", [128, 16], F32)
    Krow = p.sb("Krow", [1, 32], F32)
    relkT = p.sb("relkT", [128, 65], BF16)
    relvN = p.sb("relvN", [128, 64], BF16)
    relst = p.sb("relst", [128, 65], F32)
    rsrow = p.sb("rsrow", [4, 128], F32)
    rsrow1 = p.sb("rsrow1", [1, 512], F32)
    rsb = p.sb("rsb", [128, 512], F32)
    bot = p.sb("bot", [128, 1024], F32)
    outb = p.sb("outb", [128, 2 * 1024], F32)
    brow = outb[0:1, 0:1024]
    Ksq = outb[:].bitcast(BF16)[:, 0:2048]

    psS = p.ps("psS", [128, 2048], F32)
    psT = p.ps("psT", [128, 2048], BF16)
    psX = p.ps("psX", [128, 512], F32)
    psC = p.ps("psC", [128, 512], F32)

    for c in range(8):
        p.emit("sync", lambda e, c=c: e.dma_start(out=xn[:, c * 2048:(c + 1) * 2048],
                                                  in_=aps["xn_bf"][c * 128:(c + 1) * 128, :]))

    ST8_S1, ST8_S2, ST8_S3 = 0, 4, 8
    ST8_C0 = 16
    ST8_NM = 20
    ST8_BL, ST8_BR = 24, 28
    ST8_S4, ST8_RS = 32, 36
    ST8_A0, ST8_T1, ST8_T2 = 40, 44, 45

    t_ld = p.emit("sync", lambda e: e.dma_start(out=relst[0:65, 0:64], in_=aps["rel_k"]))
    p.emit("gpsimd", lambda e: e.dma_start(out=relvN[0:65, :], in_=aps["rel_v"]))
    t_tr = p.emit("tensor", lambda e: e.transpose(psX[0:64, 0:65], relst[0:65, 0:64], idf[0:65, 0:65]),
                  waits=[("sync", t_ld)])
    p.emit("vector", lambda e: e.tensor_copy(relkT[0:64, 0:65], psX[0:64, 0:65]),
           waits=[("tensor", t_tr)])
    t_rk = p.emit("vector", lambda e: e.tensor_copy(relkT[64:128, 0:65], psX[0:64, 0:65]))
    p.emit("vector", lambda e: e.tensor_tensor(relst[0:65, 0:65][0:65, 0:64] if False else relst[0:65, 0:64], relst[0:65, 0:64], relst[0:65, 0:64], op=OP.mult))
    t_rs = p.emit("vector", lambda e: e.tensor_reduce(relst[0:65, 64:65], relst[0:65, 0:64], axis=AX.X, op=OP.add))
    t_tr2 = p.emit("tensor", lambda e: e.transpose(psX[0:1, 70:135], relst[0:65, 64:65], idf[0:65, 0:65]),
                   waits=[("vector", t_rs)])
    t_rm = p.emit("vector", lambda e: e.tensor_reduce(Krow[0:1, 20:21], psX[0:1, 70:135], axis=AX.X, op=OP.max),
                  waits=[("tensor", t_tr2)])
    t_rm2 = p.emit("scalar", lambda e: e.activation(Krow[0:1, 20:21], Krow[0:1, 20:21], AF.Ln, bias=0.0, scale=1.0),
                   waits=[("vector", t_rm)])
    t_rm3 = p.emit("scalar", lambda e: e.activation(Krow[0:1, 20:21], Krow[0:1, 20:21], AF.Exp, bias=0.0, scale=0.5))
    emit_bcast_row(p, aps["b_o"], 1024, bot, brow, ones1, psS)
    p.emit("vector", lambda e: e.memset(attn[:, 0:32], 0.0))
    p.emit("vector", lambda e: e.memset(attn[:, 2080:2112], 0.0))

    tok = {"se_bank": [0, 0], "kred": 0}
    emit_weight_T(p, aps["Wq"], D, D, wt, wstage, psS, idf)
    _proj_common(p, wt, xn, psS, Q, 0.125, False, tok)
    emit_weight_T(p, aps["Wk"], D, D, wt, wstage, psS, idf)
    _proj_common(p, wt, xn, psS, K, 1.0, False, tok)
    TMX = not os.environ.get("KK_MBOUND")
    for h in ([] if TMX else range(H)):
        _kmax_chain(p, K, Ksq, psS, onec, Krow, h, h, tok)
    emit_weight_T(p, aps["Wv"], D, D, wt, wstage, psS, idf)
    _proj_common(p, wt, xn, psS, V, 1.0, True, tok)
    emit_weight_T(p, aps["Wo"], D, D, wt, wstage, psS, idf)
    if TMX:
        tok_krc = p.n("vector")
    else:
        t1 = p.emit("scalar", lambda e: e.activation(Krow[0:1, 0:16], Krow[0:1, 0:16], AF.Ln, bias=0.0, scale=1.0),
                    waits=[("vector", tok["kred"]), ("scalar", t_rm3)])
        p.emit("scalar", lambda e: e.activation(Krow[0:1, 0:16], Krow[0:1, 0:16], AF.Exp, bias=0.0, scale=0.5))
        t2 = p.emit("vector", lambda e: e.tensor_tensor(Krow[0:1, 0:16], Krow[0:1, 0:16],
                                                        Krow[0:1, 20:21].broadcast_to([1, 16]), op=OP.add),
                    waits=[("scalar", p.n("scalar"))])
        t3 = p.emit("vector", lambda e: e.tensor_scalar(Krow[0:1, 0:16], Krow[0:1, 0:16], -1.0, None, op0=OP.mult))
        t4 = p.emit("tensor", lambda e: e.matmul(psX[:, 140:156], ones1[0:1, :], Krow[0:1, 0:16],
                                                 start=True, stop=True), waits=[("vector", t3)])
        tok_krc = p.emit("vector", lambda e: e.tensor_copy(KRcols[:], psX[:, 140:156]), waits=[("tensor", t4)])
        if os.environ.get("KK_DUMPKR"):
            p.emit("gpsimd", lambda e: e.dma_start(out=aps["a0"][0:128, 0:16], in_=KRcols[:]),
                   waits=[("vector", tok_krc)])
            p.emit("gpsimd", lambda e: e.dma_start(out=aps["a0"][128:129, 0:32], in_=Krow[0:1, :]))

    tk = {"exp": 0, "corner": [0, 0], "transp": [0, 0], "dgw": [0, 0],
          "psT_evac": 0, "psX_ve": tok_krc, "psX_se": 0, "ctxmult": 0,
          "rsbevac": 0, "woevac": 0, "wread": 0, "wadd_last": 0, "atile_prev": 0}
    bufc = [0]
    NQB = int(os.environ.get("KK_NQB", "4"))
    NH = int(os.environ.get("KK_NH", str(H)))
    for qb in range(NQB):
        for h in range(NH):
            r = (h % 2) * 64
            oc = h // 2
            SKIPQR = bool(os.environ.get("KK_SKIP_QR"))
            if SKIPQR:
                p.emit("vector", lambda e: e.memset(st8[:, ST8_NM:ST8_NM + 4], -10.0))
                p.emit("vector", lambda e: e.memset(st8[:, ST8_BL:ST8_BL + 4], -10.0))
                t_bias = p.emit("vector", lambda e: e.memset(st8[:, ST8_BR:ST8_BR + 4], -10.0))
                tk["wread"] = p.n("sync")
            TRUEMAX0 = not os.environ.get("KK_MBOUND")
            t_qsq = None if (SKIPQR or TRUEMAX0) else p.emit("scalar", lambda e, r=r, oc=oc, qb=qb: e.activation(
                Qsq[r:r + 64, :], Q[r:r + 64, oc * 2048 + qb * 512:oc * 2048 + (qb + 1) * 512],
                AF.Square, bias=0.0, scale=1.0),
                waits=[("tensor", p.n("tensor"))] if (h > 0 or qb > 0) else [])
            for qtl in ([] if SKIPQR else range(4)):
                qt = qb * 4 + qtl
                p.emit("tensor", lambda e, h=h, qt=qt, qtl=qtl: e.matmul(
                    psX[:, qtl * 65:(qtl + 1) * 65], hsl(Q, h, qt * 128, 128),
                    relkT[(h % 2) * 64:(h % 2) * 64 + 64, 0:65], start=True, stop=True),
                    waits=([("vector", tk["psX_ve"]), ("scalar", tk["psX_se"])] if qtl == 0 else []))
            QRLVL = int(os.environ.get("KK_QRLVL", "4"))
            TRUEMAX = not os.environ.get("KK_MBOUND")
            for qtl in ([] if (SKIPQR or QRLVL < 2 or TRUEMAX) else range(4)):
                p.emit("tensor", lambda e, r=r, qtl=qtl: e.matmul(
                    psX[:, 260 + qtl:261 + qtl], Qsq[r:r + 64, qtl * 128:(qtl + 1) * 128],
                    onec[r:r + 64, :], start=True, stop=True),
                    waits=([("scalar", t_qsq)] if qtl == 0 else []))
            t_qr_mm = p.n("tensor")
            if SKIPQR:
                pass
            else:
              t_qrt = p.emit("vector", lambda e: e.tensor_copy(QRt4[:], psX[:, 0:260]),
                           waits=[("tensor", t_qr_mm)])
              if QRLVL < 2:
                p.emit("vector", lambda e: e.memset(st8[:, ST8_NM:ST8_NM + 4], -10.0))
                p.emit("vector", lambda e: e.memset(st8[:, ST8_BL:ST8_BL + 4], -10.0))
                t_bias = p.emit("vector", lambda e: e.memset(st8[:, ST8_BR:ST8_BR + 4], -10.0))
                tk["wread"] = p.n("sync")
                raise_skip = True
              if QRLVL >= 2 and not TRUEMAX:
               t_mln = p.emit("vector", lambda e: e.tensor_copy(st8[:, ST8_T1:ST8_T1 + 4], psX[:, 260:264]),
                              waits=[("tensor", t_qr_mm)])
               tk["psX_se"] = 0
               tk["psX_ve"] = max(tk["psX_ve"], t_mln)
               p.emit("scalar", lambda e: e.activation(st8[:, ST8_T1:ST8_T1 + 4], st8[:, ST8_T1:ST8_T1 + 4],
                                                       AF.Ln, bias=0.0, scale=1.0),
                      waits=[("vector", t_mln)])
               t_msq = p.emit("scalar", lambda e: e.activation(st8[:, ST8_T1:ST8_T1 + 4], st8[:, ST8_T1:ST8_T1 + 4],
                                                            AF.Exp, bias=0.0, scale=0.5))
               t_nm = p.emit("vector", lambda e, h=h: e.tensor_scalar(
                st8[:, ST8_NM:ST8_NM + 4], st8[:, ST8_T1:ST8_T1 + 4], KRcols[:, h:h + 1], None, op0=OP.mult),
                waits=[("scalar", t_msq)])
              qr4 = QRt4[:].rearrange("p (q r) -> p q r", r=65)
              nm4 = st8[:, ST8_NM:ST8_NM + 4].rearrange("p (q o) -> p q o", o=1)
              if QRLVL >= 3:
               if not TRUEMAX:
                p.emit("vector", lambda e, qr4=qr4, nm4=nm4: e.tensor_tensor(
                 st8[:, ST8_BL:ST8_BL + 4].rearrange("p (q o) -> p q o", o=1), qr4[:, :, 0:1], nm4, op=OP.add))
                t_bias = p.emit("vector", lambda e, qr4=qr4, nm4=nm4: e.tensor_tensor(
                 st8[:, ST8_BR:ST8_BR + 4].rearrange("p (q o) -> p q o", o=1), qr4[:, :, 64:65], nm4, op=OP.add))
               else:
                t_bias = p.n("vector")
               pt4 = Ptab4[:].rearrange("p (q c) -> p q c", c=319)
               p.emit("vector", lambda e, pt4=pt4, qr4=qr4: e.tensor_copy(
                pt4[:, :, 0:127], qr4[:, :, 0:1].broadcast_to([128, 4, 127])))
               p.emit("vector", lambda e, pt4=pt4: e.tensor_copy(
                pt4[:, :, 127:192], QRt4[:].rearrange("p (q r) -> p q r", r=65)))
               t_pb = p.emit("vector", lambda e, pt4=pt4, qr4=qr4: e.tensor_copy(
                pt4[:, :, 192:319], qr4[:, :, 64:65].broadcast_to([128, 4, 127])))
              else:
                t_bias = p.emit("vector", lambda e: e.memset(st8[:, ST8_BL:ST8_BL + 8], -10.0))
              for qtl in (range(4) if QRLVL >= 4 else []):
                off = qtl * 40960
                p.emit("sync", lambda e, qtl=qtl, off=off: e.dma_start(
                    out=aps["pscrP"][off:off + 128 * 319].rearrange("(q w) -> q w", w=319),
                    in_=Ptab4[:, qtl * 319:(qtl + 1) * 319]),
                    waits=([("vector", t_pb)] if qtl == 0 else []))
                p.emit("sync", lambda e, qtl=qtl, off=off: e.dma_start(
                    out=Wwin4[:, qtl * 192:(qtl + 1) * 192],
                    in_=aps["pscrP"][off + 127:off + 127 + 128 * 318].rearrange("(q w) -> q w", w=318)[:, 0:192]),
                    waits=([("vector", tk["wadd_last"])] if qtl == 0 else []))
              tk["wread"] = p.n("sync")
              if QRLVL < 4:
                for qtl in range(4):
                    p.emit("vector", lambda e, qtl=qtl: e.memset(Wwin4[:, qtl * 192:(qtl + 1) * 192], 0.0))
                tk["wread"] = p.n("sync")
            if qb == 0:
                p.emit("vector", lambda e: e.memset(st8[:, ST8_S1:ST8_S1 + 1], 0.0))
                tk["psX_ve"] = p.emit("vector", lambda e: e.memset(psX[:, 264:265], 0.0),
                                      waits=[("tensor", p.n("tensor"))])
            if qb == 3:
                p.emit("vector", lambda e: e.memset(st8[:, ST8_S3 + 3:ST8_S3 + 4], 0.0))
            NQT = int(os.environ.get("KK_NQT", "4"))
            BARE = bool(os.environ.get("KK_BARE"))
            for qtl in range(NQT):
                qt = qb * 4 + qtl
                q0 = qt * 128
                buf = 0
                bufc[0] += 1
                a0_ = 0
                for kb in range(4):
                    w = []
                    if kb == 0:
                        w = [("scalar", tk["exp"])]
                        if h == 0 and qtl == 0:
                            w.append(("vector", tk["woevac"]))
                        if qtl == 0:
                            w.append(("vector", tk["rsbevac"]))
                            w.append(("scalar", tk["rsbevac_se"] if "rsbevac_se" in tk else 0))
                    p.emit("tensor", lambda e, h=h, qt=qt, kb=kb: e.matmul(
                        psS[:, kb * 512:(kb + 1) * 512],
                        hsl(Q, h, qt * 128, 128), hsl(K, h, kb * 512, 512),
                        start=True, stop=True), waits=w)
                t_sc = p.n("tensor")
                if SKIPQR:
                    t_wadd = p.emit("vector", lambda e: e.tensor_copy(st8[:, ST8_T2:ST8_T2 + 1], st8[:, ST8_T2:ST8_T2 + 1]),
                                    waits=[("tensor", t_sc)])
                elif qt == 0:
                    t_wadd = p.emit("vector", lambda e: e.tensor_tensor(
                        psS[:, 0:160], psS[:, 0:160], Wwin4[:, 32:192], op=OP.add),
                        waits=[("sync", tk["wread"]), ("tensor", t_sc)])
                elif qt == 15:
                    t_wadd = p.emit("vector", lambda e: e.tensor_tensor(
                        psS[:, 1888:2048], psS[:, 1888:2048], Wwin4[:, 3 * 192:3 * 192 + 160], op=OP.add),
                        waits=[("sync", tk["wread"]), ("tensor", t_sc)])
                else:
                    t_wadd = p.emit("vector", lambda e, q0=q0, qtl=qtl: e.tensor_tensor(
                        psS[:, q0 - 32:q0 + 160], psS[:, q0 - 32:q0 + 160],
                        Wwin4[:, qtl * 192:(qtl + 1) * 192], op=OP.add),
                        waits=[("sync", tk["wread"]), ("tensor", t_sc), ("vector", t_bias)])
                tk["wadd_last"] = t_wadd
                if TRUEMAX:
                    p.emit("vector", lambda e, qtl=qtl: e.tensor_reduce(
                        st8[:, ST8_NM + qtl:ST8_NM + qtl + 1], psS[:], axis=AX.X, op=OP.max, negate=True))
                    p.emit("vector", lambda e, qtl=qtl: e.tensor_tensor(
                        st8[:, ST8_BL + qtl:ST8_BL + qtl + 1], QRt4[:, qtl * 65:qtl * 65 + 1],
                        st8[:, ST8_NM + qtl:ST8_NM + qtl + 1], op=OP.add))
                    t_wadd = p.emit("vector", lambda e, qtl=qtl: e.tensor_tensor(
                        st8[:, ST8_BR + qtl:ST8_BR + qtl + 1], QRt4[:, qtl * 65 + 64:qtl * 65 + 65],
                        st8[:, ST8_NM + qtl:ST8_NM + qtl + 1], op=OP.add))
                obuf = 1 - buf
                expw = [("vector", t_wadd), ("vector", tk["corner"][buf]),
                        ("tensor", tk["transp"][buf]), ("sync", tk["dgw"][buf])]
                if os.environ.get("KK_EXP1"):
                    p.emit("scalar", lambda e, a0_=a0_, qtl=qtl: e.activation(
                        attn[:, a0_ + 32:a0_ + 32 + 2048], psS[:], AF.Exp,
                        bias=st8[:, ST8_NM + qtl:ST8_NM + qtl + 1], scale=1.0,
                        accum_out=st8[:, ST8_S2 + qtl:ST8_S2 + qtl + 1]), waits=expw)
                    tk["exp"] = p.n("scalar")
                else:
                  if qt > 0:
                    p.emit("scalar", lambda e, q0=q0, a0_=a0_, qtl=qtl: e.activation(
                        attn[:, a0_ + 32:a0_ + 32 + q0 - 32], psS[:, 0:q0 - 32], AF.Exp,
                        bias=st8[:, ST8_BL + qtl:ST8_BL + qtl + 1], scale=1.0,
                        accum_out=st8[:, ST8_S1 + qtl:ST8_S1 + qtl + 1]), waits=expw)
                    expw = []
                  lo = max(0, q0 - 32)
                  hi = min(2048, q0 + 160)
                  p.emit("scalar", lambda e, lo=lo, hi=hi, a0_=a0_, qtl=qtl: e.activation(
                    attn[:, a0_ + 32 + lo:a0_ + 32 + hi], psS[:, lo:hi], AF.Exp,
                    bias=st8[:, ST8_NM + qtl:ST8_NM + qtl + 1], scale=1.0,
                    accum_out=st8[:, ST8_S2 + qtl:ST8_S2 + qtl + 1]), waits=expw)
                  if qt < 15:
                    p.emit("scalar", lambda e, hi=hi, a0_=a0_, qtl=qtl: e.activation(
                        attn[:, a0_ + 32 + hi:a0_ + 32 + 2048], psS[:, hi:2048], AF.Exp,
                        bias=st8[:, ST8_BR + qtl:ST8_BR + qtl + 1], scale=1.0,
                        accum_out=st8[:, ST8_S3 + qtl:ST8_S3 + qtl + 1]), waits=[])
                  tk["exp"] = p.n("scalar")
                if os.environ.get("KK_SKIP_A"):
                    tk["corner"][buf] = p.n("vector")
                    tk["dgw"][buf] = p.n("sync")
                else:
                  p.emit("vector", lambda e, q0=q0, a0_=a0_: e.tensor_tensor(
                    crn[:], attn[:, a0_ + q0:a0_ + q0 + 128], tri[:], op=OP.mult),
                    waits=[("scalar", tk["exp"])])
                  tk["corner"][buf] = p.emit("vector", lambda e, qtl=qtl: e.tensor_reduce(
                    st8[:, ST8_C0 + qtl:ST8_C0 + qtl + 1], crn[:], axis=AX.X, op=OP.add))
                  offb = qtl * 24832
                  p.emit("sync", lambda e, q0=q0, a0_=a0_, offb=offb: e.dma_start(
                    out=aps["pscrB"][offb:offb + 128 * 192].rearrange("(q w) -> q w", w=192),
                    in_=attn[:, a0_ + q0:a0_ + q0 + 192]),
                    waits=[("scalar", tk["exp"])])
                  p.emit("sync", lambda e, qtl=qtl, offb=offb: e.dma_start(
                    out=Dg4[:, qtl * 65:(qtl + 1) * 65],
                    in_=aps["pscrB"][offb:offb + 128 * 193].rearrange("(q w) -> q w", w=193)[:, 0:65]),
                    waits=([("vector", tk["atile_prev"])] if qtl == 0 else []))
                  tk["dgw"][buf] = p.n("sync")
                if BARE:
                    continue
                for c in range(16):
                    w = []
                    if c == 0:
                        w = [("scalar", tk["exp"]), ("vector", tk["psT_evac"])]
                    p.emit("tensor", lambda e, c=c, a0_=a0_: e.transpose(
                        psT[:, c * 128:(c + 1) * 128], attn[:, a0_ + 32 + c * 128:a0_ + 32 + (c + 1) * 128],
                        idb[:]), waits=w)
                tk["transp"][buf] = p.n("tensor")
                tk["psT_evac"] = p.emit("vector", lambda e, qtl=qtl: e.tensor_copy(
                    attnT[:].rearrange("p (c q) -> p c q", q=512)[:, :, qtl * 128:(qtl + 1) * 128],
                    psT[:].rearrange("p (c q) -> p c q", q=128)),
                    waits=[("tensor", tk["transp"][buf])])
            if BARE:
                continue
            t_evacs = tk["psT_evac"]
            SKIPA = bool(os.environ.get("KK_SKIP_A"))
            for qtl in ([] if SKIPA else range(4)):
                qt = qb * 4 + qtl
                nfull = qt - 1
                first = True
                for c in range(max(0, nfull)):
                    p.emit("tensor", lambda e, c=c, qtl=qtl: e.matmul(
                        psX[:, 264 + qtl:265 + qtl],
                        attnT[:, c * 512 + qtl * 128:c * 512 + (qtl + 1) * 128],
                        onec[:], start=(c == 0), stop=False),
                        waits=([("vector", t_evacs)] if first else []))
                    first = False
                if qt >= 1:
                    p.emit("tensor", lambda e, qt=qt, qtl=qtl, st=(nfull <= 0): e.matmul(
                        psX[:, 264 + qtl:265 + qtl],
                        attnT[0:96, (qt - 1) * 512 + qtl * 128:(qt - 1) * 512 + (qtl + 1) * 128],
                        onec[0:96, :], start=st, stop=True),
                        waits=([("vector", t_evacs)] if first else []))
            t_a0 = p.n("tensor")
            p.emit("vector", lambda e: e.tensor_tensor(st8[:, ST8_S4:ST8_S4 + 4], st8[:, ST8_S1:ST8_S1 + 4],
                                                       st8[:, ST8_S2:ST8_S2 + 4], op=OP.add),
                   waits=[("scalar", tk["exp"])])
            p.emit("vector", lambda e: e.tensor_tensor(st8[:, ST8_S4:ST8_S4 + 4], st8[:, ST8_S4:ST8_S4 + 4],
                                                       st8[:, ST8_S3:ST8_S3 + 4], op=OP.add))
            t_rs4 = p.emit("vector", lambda e: e.reciprocal(st8[:, ST8_RS:ST8_RS + 4], st8[:, ST8_S4:ST8_S4 + 4]))
            if SKIPA:
                t_atile = p.n("vector")
                tk["psX_ve"] = max(tk["psX_ve"], t_atile)
                tk["atile_prev"] = t_atile
                t_arhs = tk["psT_evac"]
            dg4v = Dg4[:].rearrange("p (q r) -> p q r", r=65)
            at4v = Atile4[:].rearrange("p (q r) -> p q r", r=65)
            if not SKIPA:
              p.emit("vector", lambda e: e.tensor_tensor(
                st8[:, ST8_A0:ST8_A0 + 4], psX[:, 264:268], st8[:, ST8_C0:ST8_C0 + 4], op=OP.add),
                waits=[("tensor", t_a0), ("sync", tk["dgw"][0]), ("sync", tk["dgw"][1])])
            if not SKIPA:
              p.emit("vector", lambda e, dg4v=dg4v: e.tensor_reduce(
                  st8[:, ST8_T1:ST8_T1 + 4], dg4v[:, :, 1:64], axis=AX.X, op=OP.add))
            if not SKIPA:
              p.emit("vector", lambda e: e.tensor_tensor(st8[:, ST8_T1:ST8_T1 + 4], st8[:, ST8_T1:ST8_T1 + 4],
                                                         st8[:, ST8_A0:ST8_A0 + 4], op=OP.add))
            if not SKIPA:
              p.emit("vector", lambda e: e.tensor_tensor(st8[:, ST8_T1:ST8_T1 + 4], st8[:, ST8_S4:ST8_S4 + 4],
                                                         st8[:, ST8_T1:ST8_T1 + 4], op=OP.subtract))
            if not SKIPA:
              p.emit("vector", lambda e, dg4v=dg4v, at4v=at4v: e.tensor_copy(at4v[:, :, 1:64], dg4v[:, :, 1:64]))
            if not SKIPA:
              p.emit("vector", lambda e, at4v=at4v: e.tensor_copy(
                  at4v[:, :, 0:1], st8[:, ST8_A0:ST8_A0 + 4].rearrange("p (q o) -> p q o", o=1)))
            if not SKIPA:
                t_atile = p.emit("vector", lambda e, at4v=at4v: e.tensor_copy(
                    at4v[:, :, 64:65], st8[:, ST8_T1:ST8_T1 + 4].rearrange("p (q o) -> p q o", o=1)))
                tk["psX_ve"] = t_atile
                tk["atile_prev"] = t_atile
            if not SKIPA:
              for qtl in range(4):
                p.emit("tensor", lambda e, qtl=qtl: e.transpose(
                    psT[0:65, qtl * 128:(qtl + 1) * 128], Atile4[:, qtl * 65:(qtl + 1) * 65], idb[:]),
                    waits=([("vector", t_atile)] if qtl == 0 else []))
              t_atr = p.n("tensor")
              t_arhs = p.emit("vector", lambda e: e.tensor_copy(Arhs4[0:65, :], psT[0:65, 0:512]),
                            waits=[("tensor", t_atr)])
              tk["psT_evac"] = t_arhs
              for qtl in range(4):
                w = [("vector", t_arhs)] if qtl == 0 else []
                if qtl == 0:
                    w.append(("vector", tk["ctxmult"]))
                p.emit("tensor", lambda e, qtl=qtl: e.matmul(
                    psC[0:64, qtl * 128:(qtl + 1) * 128], relvN[0:65, 0:64],
                    Arhs4[0:65, qtl * 128:(qtl + 1) * 128], start=(qtl == 0), stop=False), waits=w)
            if os.environ.get("KK_NOAV"):
                continue
            for c in range(16):
                w = []
                if SKIPA and c == 0:
                    w = [("vector", max(tk["ctxmult"], t_evacs))]
                p.emit("tensor", lambda e, h=h, c=c, st=(SKIPA and c == 0): e.matmul(
                    psC[0:64, :], V[:, c * 1024 + h * 64:c * 1024 + (h + 1) * 64],
                    attnT[:, c * 512:(c + 1) * 512], start=st, stop=(c == 15)), waits=w)
            t_av = p.n("tensor")
            if os.environ.get("KK_SKIP_RS"):
                tk["rsbevac_se"] = p.n("scalar")
                tk["ctxmult"] = p.emit("vector", lambda e, h=h: e.tensor_copy(
                    ctxfm[(h % 2) * 64:(h % 2) * 64 + 64, (h // 2) * 512:(h // 2 + 1) * 512],
                    psC[0:64, :]), waits=[("tensor", t_av)])
                tk["rsbevac"] = tk["ctxmult"]
                continue
            t_rstr = p.emit("tensor", lambda e: e.transpose(psX[0:4, 300:428], st8[:, ST8_RS:ST8_RS + 4],
                                                            idf[:]),
                            waits=[("vector", t_atile)])
            t_rsrow = p.emit("vector", lambda e: e.tensor_copy(rsrow[0:4, :], psX[0:4, 300:428]),
                             waits=[("tensor", t_rstr), ("sync", tk.get("flat_prev", 0))])
            tk["psX_ve"] = t_rsrow
            t_flat = p.emit("sync", lambda e: e.dma_start(out=rsrow1[0:1, :], in_=rsrow[0:4, :]),
                            waits=[("vector", t_rsrow), ("tensor", tk.get("rsmm_prev", 0))])
            tk["flat_prev"] = t_flat
            for qtl in range(4):
                w = [("sync", t_flat)] if qtl == 0 else []
                if qtl == 0:
                    w.append(("scalar", tk["exp"]))
                p.emit("tensor", lambda e, qtl=qtl: e.matmul(
                    psS[:, qtl * 128:(qtl + 1) * 128], ones1[0:1, :],
                    rsrow1[0:1, qtl * 128:(qtl + 1) * 128], start=True, stop=True), waits=w)
            t_rsmm = p.n("tensor")
            tk["rsmm_prev"] = t_rsmm
            tk["rsbevac_se"] = p.emit("scalar", lambda e: e.activation(rsb[:], psS[:, 0:512], AF.Copy,
                                                                       bias=0.0, scale=1.0),
                                      waits=[("tensor", t_rsmm)])
            tk["ctxmult"] = p.emit("vector", lambda e, h=h: e.tensor_tensor(
                ctxfm[(h % 2) * 64:(h % 2) * 64 + 64, (h // 2) * 512:(h // 2 + 1) * 512],
                psC[0:64, :], rsb[0:64, :], op=OP.mult),
                waits=[("scalar", tk["rsbevac_se"]), ("tensor", t_av)])
            tk["rsbevac"] = tk["ctxmult"]
        if os.environ.get("KK_BARE") or os.environ.get("KK_NOAV") or os.environ.get("KK_NOWO"):
            continue
        t_allctx = tk["ctxmult"]
        t_woev = 0
        for tsub in range(4):
            t = qb * 4 + tsub
            for oh in range(2):
                pst = psS[:, oh * 512:(oh + 1) * 512]
                for dc in range(8):
                    w = []
                    if oh == 0 and dc == 0:
                        w = [("vector", max(t_allctx, t_woev))]
                        if tsub == 0:
                            w.append(("scalar", tk["rsbevac_se"]))
                    p.emit("tensor", lambda e, tsub=tsub, oh=oh, dc=dc, pst=pst: e.matmul(
                        pst, ctxfm[:, dc * 512 + tsub * 128:dc * 512 + (tsub + 1) * 128],
                        wt[:, dc * 1024 + oh * 512:dc * 1024 + (oh + 1) * 512],
                        start=(dc == 0), stop=(dc == 7)), waits=w)
            t_wo = p.n("tensor")
            ob = outb[:, (tsub % 2) * 1024:(tsub % 2) * 1024 + 1024]
            p.emit("vector", lambda e, ob=ob: e.tensor_tensor(ob, psS[:, 0:1024], bot[:], op=OP.add),
                   waits=[("tensor", t_wo), ("gpsimd", max(0, p.n("gpsimd") - 16))])
            t_woev = p.n("vector")
            tk["woevac"] = t_woev
            p.emit("gpsimd", lambda e, t=t, ob=ob: e.dma_start(
                out=aps["a0"][t * 128:(t + 1) * 128, :], in_=ob),
                waits=[("vector", tk["woevac"])])



def phase_ffn0_v2(p, io, env):
    aps = io
    ones1 = env["ones1"]
    idf2 = env["idf"]
    xn = p.sb("xn", [128, 8 * 2048], BF16)
    w1q = p.sb("w1q", [128, 2 * 8 * 1024], BF16)
    w2q = p.sb("w2q", [128, 2 * 8 * 1024], BF16)
    hbuf = p.sb("hbuf", [128, 8 * 512], BF16)
    b1c = p.sb("b1c", [128, 32], F32)
    acc = p.sb("acc", [128, 16 * 1024], F32)
    b2t = p.sb("b2t", [128, 1024], F32)
    b1st = p.sb("b1st", [32, 128], F32)
    outb = p.sb("outb", [128, 2 * 1024], F32)
    brow = outb[0:1, 0:1024]
    psA = p.ps("psA", [128, 512], F32)
    psB = p.ps("psB", [128, 512], F32)
    psC = p.ps("psC", [128, 1024], F32)
    psD = p.ps("psD", [128, 1024], F32)

    for c in range(8):
        p.emit("sync", lambda e, c=c: e.dma_start(out=xn[:, c * 2048:(c + 1) * 2048],
                                                  in_=aps["xn_bf"][c * 128:(c + 1) * 128, :]))
    p.emit("sync", lambda e: e.dma_start(out=b1st[:], in_=aps["b1"].rearrange("(ic p) -> ic p", p=128)))
    t0 = p.emit("tensor", lambda e: e.transpose(psC[:, 0:32], b1st[:, 0:128], idf2[0:32, 0:32]),
                waits=[("sync", p.n("sync"))])
    p.emit("vector", lambda e: e.tensor_copy(b1c[:], psC[:, 0:32]), waits=[("tensor", t0)])
    emit_bcast_row(p, aps["b2"], 1024, b2t, brow, ones1, psC)

    t_wld = [0] * 4
    def load_quarter(iq):
        slot = iq % 2
        for dc in range(8):
            p.emit("gpsimd", lambda e, dc=dc, iq=iq, slot=slot: e.dma_start(
                out=w1q[:, slot * 8192 + dc * 1024:slot * 8192 + (dc + 1) * 1024],
                in_=aps["W1"][dc * 128:(dc + 1) * 128, iq * 1024:(iq + 1) * 1024]),
                waits=([("tensor", p.n("tensor"))] if dc == 0 else []))
        for ic in range(8):
            p.emit("gpsimd", lambda e, ic=ic, iq=iq, slot=slot: e.dma_start(
                out=w2q[:, slot * 8192 + ic * 1024:slot * 8192 + (ic + 1) * 1024],
                in_=aps["W2"][iq * 1024 + ic * 128:iq * 1024 + (ic + 1) * 128, :]))
        t_wld[iq] = p.n("gpsimd")

    load_quarter(0)
    load_quarter(1)
    relu_tok = [0, 0]
    acc_tok = [0, 0]
    for iq in range(4):
        slot = iq % 2
        if iq >= 2:
            pass
        for qb in range(4):
            for ic in range(8):
                pst, bank = (psA, 0) if ic % 2 == 0 else (psB, 1)
                for dc in range(8):
                    w = []
                    if dc == 0:
                        w = [("scalar", relu_tok[bank])]
                        if ic == 0 and qb == 0:
                            w.append(("gpsimd", t_wld[iq]))
                            w.append(("sync", p.n("sync")))
                    p.emit("tensor", lambda e, ic=ic, dc=dc, qb=qb, pst=pst, slot=slot: e.matmul(
                        pst[:], w1q[:, slot * 8192 + dc * 1024 + ic * 128:slot * 8192 + dc * 1024 + (ic + 1) * 128],
                        xn[:, dc * 2048 + qb * 512:dc * 2048 + (qb + 1) * 512],
                        start=(dc == 0), stop=(dc == 7)), waits=w)
                t_mm = p.n("tensor")
                icg = iq * 8 + ic
                relu_tok[bank] = p.emit("scalar", lambda e, ic=ic, icg=icg, pst=pst: e.activation(
                    hbuf[:, ic * 512:(ic + 1) * 512], pst[:], AF.Relu,
                    bias=b1c[:, icg:icg + 1], scale=1.0),
                    waits=[("tensor", t_mm)])
            for tsub in range(4):
                t = qb * 4 + tsub
                pst, bank = (psC, 0) if tsub % 2 == 0 else (psD, 1)
                for ic in range(8):
                    for oc in range(2):
                        w = []
                        if ic == 0 and oc == 0:
                            w = [("scalar", relu_tok[0]), ("scalar", relu_tok[1]),
                                 ("vector", acc_tok[bank])]
                        p.emit("tensor", lambda e, ic=ic, oc=oc, tsub=tsub, pst=pst, slot=slot: e.matmul(
                            pst[:, oc * 512:(oc + 1) * 512],
                            hbuf[:, ic * 512 + tsub * 128:ic * 512 + (tsub + 1) * 128],
                            w2q[:, slot * 8192 + ic * 1024 + oc * 512:slot * 8192 + ic * 1024 + (oc + 1) * 512],
                            start=(ic == 0), stop=(ic == 7)), waits=w)
                t_mm2 = p.n("tensor")
                if iq == 0:
                    acc_tok[bank] = p.emit("vector", lambda e, t=t, pst=pst: e.tensor_copy(
                        acc[:, t * 1024:(t + 1) * 1024], pst[:]), waits=[("tensor", t_mm2)])
                else:
                    acc_tok[bank] = p.emit("vector", lambda e, t=t, pst=pst: e.tensor_tensor(
                        acc[:, t * 1024:(t + 1) * 1024], acc[:, t * 1024:(t + 1) * 1024], pst[:], op=OP.add),
                        waits=[("tensor", t_mm2)])
                if iq == 3:
                    ob = outb[:, (t % 2) * 1024:(t % 2) * 1024 + 1024]
                    p.emit("vector", lambda e, ob=ob, t=t: e.tensor_tensor(
                        ob, acc[:, t * 1024:(t + 1) * 1024], b2t[:], op=OP.add),
                        waits=[("gpsimd", max(0, p.n("gpsimd") - 16))])
                    acc_tok[bank] = p.n("vector")
                    p.emit("gpsimd", lambda e, t=t, ob=ob: e.dma_start(
                        out=aps["f0"][t * 128:(t + 1) * 128, :], in_=ob),
                        waits=[("vector", p.n("vector"))])
        if iq + 2 <= 3:
            load_quarter(iq + 2)



def phase_attn1_v2(p, io, env):
    aps = io
    idf = env["idf"]
    idb = env["idb"]
    ones1 = env["ones1"]

    xn = p.sb("xn", [128, 8 * 2048], BF16)
    wt = p.sb("wt", [128, 8 * 1024], BF16)
    wstage = p.sb("wstage", [128, 1024], F32)
    Q = p.sb("Q", [128, 8 * 2048], BF16)
    K = p.sb("K", [128, 8 * 2048], BF16)
    V = p.sb("V", [128, 16 * 1024], BF16)
    attn = p.sb("attn", [128, 2048], BF16)
    attnT = p.sb("attnT", [128, 16 * 512], BF16)
    ctxfm = p.sb("ctxfm", [128, 8 * 512], BF16)
    qkb = p.sb("qkb", [128, 1024], BF16)
    rst = p.sb("rst", [128, 16], F32)
    wnt = p.sb("wnt", [128, 1024], F32)
    st8 = p.sb("st8", [128, 16], F32)
    bqt = p.sb("bqt", [128, 1024], F32)
    bpt = p.sb("bpt", [128, 1024], F32)
    rsrow = p.sb("rsrow", [4, 128], F32)
    rsrow1 = p.sb("rsrow1", [1, 512], F32)
    rsb = p.sb("rsb", [128, 512], F32)
    wrow = p.sb("wrow", [1, 64], F32)
    outb = p.sb("outb", [128, 2 * 1024], F32)
    qkt = outb[:, 0:1024]
    sq = outb[:, 1024:2048]
    brow = outb[0:1, 0:1024]

    psS = p.ps("psS", [128, 2048], F32)
    psT = p.ps("psT", [128, 2048], BF16)
    psC = p.ps("psC", [128, 512], F32)
    psX = p.ps("psX", [128, 512], F32)

    ST8_NM, ST8_S4, ST8_RS = 0, 4, 8

    for c in range(8):
        p.emit("sync", lambda e, c=c: e.dma_start(out=xn[:, c * 2048:(c + 1) * 2048],
                                                  in_=aps["xn_bf"][c * 128:(c + 1) * 128, :]))
    emit_bcast_row(p, aps["bproj"], 1024, bpt, brow, ones1, psS)

    def emit_wn_tile(nm):
        p.emit("sync", lambda e, nm=nm: e.dma_start(out=wrow[0:1, :], in_=aps[nm][0:DK].rearrange("(a b) -> a b", a=1)),
               waits=[("tensor", p.n("tensor"))])
        p.emit("tensor", lambda e: e.matmul(psS[:, 0:64], ones1[0:1, :], wrow[0:1, :], start=True, stop=True),
               waits=[("sync", p.n("sync")), ("vector", p.n("vector"))])
        for j in range(16):
            p.emit("vector", lambda e, j=j: e.tensor_copy(
                wnt[:, j * 64:(j + 1) * 64], psS[:, 0:64]),
                waits=[("tensor", p.n("tensor"))] if j == 0 else [])

    for part, dest in ((0, Q), (1, K), (2, V)):
        emit_bcast_row(p, aps["bqkv"][part * D:(part + 1) * D], 1024, bqt, brow, ones1, psS)
        if part == 0:
            emit_wn_tile("qn_w")
        elif part == 1:
            emit_wn_tile("kn_w")
        emit_weight_T(p, aps["Wqkv"][part * D:(part + 1) * D, :], D, D, wt, wstage, psS, idf)
        t_ps = p.n("vector")
        t_pse = p.n("scalar")
        t_tr_ev = p.n("vector")
        for t in range(16):
            for oh in range(2):
                for dc in range(8):
                    w = []
                    if oh == 0 and dc == 0:
                        w = [("vector", t_ps), ("scalar", t_pse), ("sync", p.n("sync"))]
                    p.emit("tensor", lambda e, t=t, oh=oh, dc=dc: e.matmul(
                        psS[:, oh * 512:oh * 512 + 512],
                        xn[:, dc * 2048 + t * 128:dc * 2048 + (t + 1) * 128],
                        wt[:, dc * 1024 + oh * 512:dc * 1024 + (oh + 1) * 512],
                        start=(dc == 0), stop=(dc == 7)), waits=w)
            t_mm = p.n("tensor")
            if part == 2:
                t_ps = p.emit("vector", lambda e, t=t: e.tensor_tensor(
                    V[:, t * 1024:(t + 1) * 1024], psS[:, 0:1024], bqt[:], op=OP.add),
                    waits=[("tensor", t_mm)])
            else:
                p.emit("vector", lambda e: e.tensor_tensor(qkt[:], psS[:, 0:1024], bqt[:], op=OP.add),
                       waits=[("tensor", t_mm)])
                p.emit("vector", lambda e: e.tensor_tensor(sq[:], qkt[:], qkt[:], op=OP.mult))
                t_ps = p.emit("vector", lambda e: e.tensor_reduce(
                    rst[:, 0:16], sq[:].rearrange("p (g d) -> p g d", d=64), axis=AX.X, op=OP.add))
                eps_r = p.cc(1e-6)
                p.emit("scalar", lambda e, eps_r=eps_r: e.activation(rst[:, 0:16], rst[:, 0:16], AF.Ln, bias=eps_r, scale=1.0 / DK),
                       waits=[("vector", t_ps)])
                t_pse = p.emit("scalar", lambda e: e.activation(rst[:, 0:16], rst[:, 0:16], AF.Exp, bias=0.0, scale=-0.5))
                p.emit("vector", lambda e: e.tensor_tensor(
                    qkt[:].rearrange("p (g d) -> p g d", d=64),
                    qkt[:].rearrange("p (g d) -> p g d", d=64),
                    rst[:, 0:16].rearrange("p (g o) -> p g o", o=1).broadcast_to([128, 16, 64]),
                    op=OP.mult), waits=[("scalar", t_pse)])
                sc2 = 0.125 if part == 0 else 1.0
                t_ps = p.emit("vector", lambda e, sc2=sc2: e.scalar_tensor_tensor(
                    qkb[:], qkt[:], sc2, wnt[:], op0=OP.mult, op1=OP.mult))
                for c in range(8):
                    w = []
                    if c == 0:
                        w = [("vector", t_ps)]
                        if t_tr_ev:
                            w.append(("vector", t_tr_ev))
                    p.emit("tensor", lambda e, c=c: e.transpose(
                        psT[:, c * 128:(c + 1) * 128], qkb[:, c * 128:(c + 1) * 128], idb[:]), waits=w)
                t_trp = p.n("tensor")
                t_tr_ev = p.emit("vector", lambda e, t=t, dest=dest: e.tensor_copy(
                    dest[:].rearrange("p (c s) -> p c s", s=S)[:, :, t * 128:(t + 1) * 128],
                    psT[:, 0:1024].rearrange("p (c q) -> p c q", q=128)),
                    waits=[("tensor", t_trp)])
    emit_weight_T(p, aps["Wproj"], D, D, wt, wstage, psS, idf)

    tk = {"exp": 0, "transp": 0, "psT_evac": p.n("vector"), "ctxmult": 0,
          "rsbevac_se": p.n("scalar"), "woevac": p.n("vector"),
          "flat_prev": p.n("sync"), "rsmm_prev": p.n("tensor")}
    for qb in range(4):
        for h in range(H):
            for qtl in range(4):
                qt = qb * 4 + qtl
                for kb in range(4):
                    w = []
                    if kb == 0:
                        w = [("scalar", tk["exp"])]
                        if qtl == 0:
                            w.append(("vector", tk["woevac"]))
                            w.append(("scalar", tk["rsbevac_se"]))
                    p.emit("tensor", lambda e, h=h, qt=qt, kb=kb: e.matmul(
                        psS[:, kb * 512:(kb + 1) * 512],
                        hsl(Q, h, qt * 128, 128), hsl(K, h, kb * 512, 512),
                        start=True, stop=True), waits=w)
                t_sc = p.n("tensor")
                t_mx = p.emit("vector", lambda e, qtl=qtl: e.tensor_reduce(
                    st8[:, ST8_NM + qtl:ST8_NM + qtl + 1], psS[:], axis=AX.X, op=OP.max, negate=True),
                    waits=[("tensor", t_sc)])
                p.emit("scalar", lambda e, qtl=qtl: e.activation(
                    attn[:], psS[:], AF.Exp, bias=st8[:, ST8_NM + qtl:ST8_NM + qtl + 1], scale=1.0,
                    accum_out=st8[:, ST8_S4 + qtl:ST8_S4 + qtl + 1]),
                    waits=[("vector", t_mx), ("tensor", tk["transp"])])
                tk["exp"] = p.n("scalar")
                for c in range(16):
                    w = []
                    if c == 0:
                        w = [("scalar", tk["exp"]), ("vector", tk["psT_evac"])]
                    p.emit("tensor", lambda e, c=c: e.transpose(
                        psT[:, c * 128:(c + 1) * 128], attn[:, c * 128:(c + 1) * 128], idb[:]), waits=w)
                tk["transp"] = p.n("tensor")
                tk["psT_evac"] = p.emit("vector", lambda e, qtl=qtl: e.tensor_copy(
                    attnT[:].rearrange("p (c q) -> p c q", q=512)[:, :, qtl * 128:(qtl + 1) * 128],
                    psT[:].rearrange("p (c q) -> p c q", q=128)),
                    waits=[("tensor", tk["transp"])])
            t_evacs = tk["psT_evac"]
            t_rs4 = p.emit("vector", lambda e: e.reciprocal(st8[:, ST8_RS:ST8_RS + 4], st8[:, ST8_S4:ST8_S4 + 4]),
                           waits=[("scalar", tk["exp"])])
            for c in range(16):
                w = []
                if c == 0:
                    w = [("vector", max(tk["ctxmult"], t_evacs))]
                p.emit("tensor", lambda e, h=h, c=c, st=(c == 0): e.matmul(
                    psC[0:64, :], V[:, c * 1024 + h * 64:c * 1024 + (h + 1) * 64],
                    attnT[:, c * 512:(c + 1) * 512], start=st, stop=(c == 15)), waits=w)
            t_av = p.n("tensor")
            t_rstr = p.emit("tensor", lambda e: e.transpose(psX[0:4, 300:428], st8[:, ST8_RS:ST8_RS + 4], idf[:]),
                            waits=[("vector", t_rs4)])
            t_rsrow = p.emit("vector", lambda e: e.tensor_copy(rsrow[0:4, :], psX[0:4, 300:428]),
                             waits=[("tensor", t_rstr), ("sync", tk["flat_prev"])])
            t_flat = p.emit("sync", lambda e: e.dma_start(out=rsrow1[0:1, :], in_=rsrow[0:4, :]),
                            waits=[("vector", t_rsrow), ("tensor", tk["rsmm_prev"])])
            tk["flat_prev"] = t_flat
            for qtl in range(4):
                w = []
                if qtl == 0:
                    w = [("sync", t_flat), ("scalar", tk["exp"])]
                p.emit("tensor", lambda e, qtl=qtl: e.matmul(
                    psS[:, qtl * 128:(qtl + 1) * 128], ones1[0:1, :],
                    rsrow1[0:1, qtl * 128:(qtl + 1) * 128], start=True, stop=True), waits=w)
            t_rsmm = p.n("tensor")
            tk["rsmm_prev"] = t_rsmm
            tk["rsbevac_se"] = p.emit("scalar", lambda e: e.activation(rsb[:], psS[:, 0:512], AF.Copy,
                                                                      bias=0.0, scale=1.0),
                                      waits=[("tensor", t_rsmm)])
            tk["ctxmult"] = p.emit("vector", lambda e, h=h: e.tensor_tensor(
                ctxfm[(h % 2) * 64:(h % 2) * 64 + 64, (h // 2) * 512:(h // 2 + 1) * 512],
                psC[0:64, :], rsb[0:64, :], op=OP.mult),
                waits=[("scalar", tk["rsbevac_se"]), ("tensor", t_av)])
        t_allctx = tk["ctxmult"]
        t_woev = 0
        for tsub in range(4):
            t = qb * 4 + tsub
            for oh in range(2):
                pst = psS[:, oh * 512:(oh + 1) * 512]
                for dc in range(8):
                    w = []
                    if oh == 0 and dc == 0:
                        w = [("vector", max(t_allctx, t_woev))]
                        if tsub == 0:
                            w.append(("scalar", tk["rsbevac_se"]))
                    p.emit("tensor", lambda e, tsub=tsub, oh=oh, dc=dc, pst=pst: e.matmul(
                        pst, ctxfm[:, dc * 512 + tsub * 128:dc * 512 + (tsub + 1) * 128],
                        wt[:, dc * 1024 + oh * 512:dc * 1024 + (oh + 1) * 512],
                        start=(dc == 0), stop=(dc == 7)), waits=w)
            t_wo = p.n("tensor")
            ob = outb[:, (tsub % 2) * 1024:(tsub % 2) * 1024 + 1024]
            p.emit("vector", lambda e, ob=ob: e.tensor_tensor(ob, psS[:, 0:1024], bpt[:], op=OP.add),
                   waits=[("tensor", t_wo), ("gpsimd", max(0, p.n("gpsimd") - 16))])
            t_woev = p.n("vector")
            tk["woevac"] = t_woev
            p.emit("gpsimd", lambda e, t=t, ob=ob: e.dma_start(
                out=aps["a1"][t * 128:(t + 1) * 128, :], in_=ob),
                waits=[("vector", t_woev)])



def build_fused(nc, debug=False):
    ins = {
        "x": ((S, D), F32),
        "Wq": ((D, D), F32), "Wk": ((D, D), F32), "Wv": ((D, D), F32), "Wo": ((D, D), F32),
        "b_o": ((D,), F32), "rel_k": ((65, 64), F32), "rel_v": ((65, 64), F32),
        "W1": ((D, INTER), F32), "b1": ((INTER,), F32),
        "W2": ((INTER, D), F32), "b2": ((D,), F32),
        "Wqkv": ((3 * D, D), F32), "bqkv": ((3 * D,), F32),
        "Wproj": ((D, D), F32), "bproj": ((D,), F32),
        "qn_w": ((DK,), F32), "kn_w": ((DK,), F32),
        "router_w": ((NE, D), F32), "router_b": ((NE,), F32),
        "expert_w": ((NE, D, D), F32),
        "ident_f32": ((128, 128), F32), "ident_bf": ((128, 128), BF16),
        "tri_bf": ((128, 128), BF16), "ones_col_bf": ((128, 1), BF16),
        "ones_row_f32": ((1, 128), F32),
    }
    outs = {"out": ((S, D), F32)}
    aps = declare_io(nc, ins, outs)
    for nm, shp, dt in [("xn_d", (D, S), BF16), ("xnf_d", (D, S), F32),
                        ("a_d", (S, D), F32), ("f_d", (S, D), F32),
                        ("x1_d", (S, D), F32), ("a1_d", (S, D), F32),
                        ("m1_d", (S, D), F32),
                        ("pscrP", (4 * 40960 + 256,), F32),
                        ("pscrB", (4 * 24832 + 256,), BF16)]:
        kind = "ExternalOutput" if (debug and nm not in ("pscrP", "pscrB")) else "Internal"
        aps[nm] = nc.dram_tensor(nm, list(shp), dt, kind=kind).ap()

    p = Prog(nc)
    env = {
        "idf": p.sb("idf", [128, 128], F32),
        "idb": p.sb("idb", [128, 128], BF16),
        "tri": p.sb("tri", [128, 128], BF16),
        "onec": p.sb("onec", [128, 1], BF16),
        "ones1": p.sb("ones1", [1, 128], F32),
    }
    for nm, key in [("idf", "ident_f32"), ("idb", "ident_bf"), ("tri", "tri_bf"),
                    ("onec", "ones_col_bf"), ("ones1", "ones_row_f32")]:
        p.emit("sync", lambda e, nm=nm, key=key: e.dma_start(out=env[nm][:], in_=aps[key]))

    with p.phase("ln0"):
        phase_ln(p, {"x": aps["x"], "xn_bf": aps["xn_d"]}, env, want_f32=False)
    with p.phase("attn0"):
        phase_attn0_v2(p, {"xn_bf": aps["xn_d"], "Wq": aps["Wq"], "Wk": aps["Wk"],
                        "Wv": aps["Wv"], "Wo": aps["Wo"], "b_o": aps["b_o"],
                        "rel_k": aps["rel_k"], "rel_v": aps["rel_v"], "a0": aps["a_d"],
                        "pscrP": aps["pscrP"], "pscrB": aps["pscrB"]}, env)
    with p.phase("ffn0"):
        phase_ffn0_v2(p, {"xn_bf": aps["xn_d"], "W1": aps["W1"], "b1": aps["b1"],
                       "W2": aps["W2"], "b2": aps["b2"], "f0": aps["f_d"]}, env)
    with p.phase("comb0"):
        phase_comb(p, {"x": aps["x"], "a": aps["a_d"], "b": aps["f_d"], "o": aps["x1_d"]})
    with p.phase("ln1"):
        phase_ln(p, {"x": aps["x1_d"], "xn_bf": aps["xn_d"], "xn_f32": aps["xnf_d"]},
                 env, want_f32=True)
    with p.phase("attn1"):
        phase_attn1_v2(p, {"xn_bf": aps["xn_d"], "Wqkv": aps["Wqkv"], "bqkv": aps["bqkv"],
                        "Wproj": aps["Wproj"], "bproj": aps["bproj"], "qn_w": aps["qn_w"],
                        "kn_w": aps["kn_w"], "a1": aps["a1_d"]}, env)
    with p.phase("moe"):
        phase_moe(p, {"xn_bf": aps["xn_d"], "xn_f32": aps["xnf_d"],
                      "router_w": aps["router_w"], "router_b": aps["router_b"],
                      "expert_w": aps["expert_w"], "m1": aps["m1_d"]}, env)
    with p.phase("comb1"):
        phase_comb(p, {"x": aps["x1_d"], "a": aps["a1_d"], "b": aps["m1_d"], "o": aps["out"]})
    p.finalize()
    return nc


def kernel(L0_attn_x, L0_attn_Wq, L0_attn_Wk, L0_attn_Wv, L0_attn_Wo, L0_attn_b_o,
           L0_attn_rel_k_emb, L0_attn_rel_v_emb, L0_ffn_W1, L0_ffn_b1, L0_ffn_W2,
           L0_ffn_b2, L1_attn_Wqkv, L1_attn_bqkv, L1_attn_Wproj, L1_attn_bproj,
           L1_attn_qn_w, L1_attn_kn_w, L1_ffn_router_w, L1_ffn_router_b,
           L1_ffn_expert_w):
    C = CONSTS
    x = np.ascontiguousarray(np.asarray(L0_attn_x, dtype=np.float32))
    shared = {
        "Wq": L0_attn_Wq, "Wk": L0_attn_Wk, "Wv": L0_attn_Wv, "Wo": L0_attn_Wo,
        "b_o": L0_attn_b_o, "rel_k": L0_attn_rel_k_emb, "rel_v": L0_attn_rel_v_emb,
        "W1": L0_ffn_W1, "b1": L0_ffn_b1, "W2": L0_ffn_W2, "b2": L0_ffn_b2,
        "Wqkv": L1_attn_Wqkv, "bqkv": L1_attn_bqkv, "Wproj": L1_attn_Wproj,
        "bproj": L1_attn_bproj, "qn_w": L1_attn_qn_w, "kn_w": L1_attn_kn_w,
        "router_w": L1_ffn_router_w, "router_b": L1_ffn_router_b,
        "expert_w": L1_ffn_expert_w,
    }
    shared = {k: np.ascontiguousarray(np.asarray(v, dtype=np.float32)) for k, v in shared.items()}
    shared.update(C)
    in_maps = []
    for b in range(NCORES):
        d = dict(shared)
        d["x"] = np.ascontiguousarray(x[b])
        in_maps.append(d)
    nc = bass.Bass("TRN2", target_bir_lowering=False)
    build_fused(nc)
    res = run_bass_kernel_spmd(nc, in_maps, core_ids=list(range(NCORES)))
    return np.stack([np.asarray(res.results[i]["out"], dtype=np.float32)
                     for i in range(NCORES)], axis=0)
